# version 03
# baseline (speedup 1.0000x reference)
import math
import numpy as np
import ml_dtypes
import concourse.bass as bass
import concourse.mybir as mybir
from concourse.bass_utils import run_bass_kernel_spmd

F32 = mybir.dt.float32
BF16 = mybir.dt.bfloat16
I32 = mybir.dt.int32
AF = mybir.ActivationFunctionType
ALU = mybir.AluOpType
AX = mybir.AxisListType
BFNP = ml_dtypes.bfloat16

B, S, D = 2, 8192, 2048
H, DK, DV, SEG, NSEG = 16, 128, 128, 512, 16
NCH, FF, FCH = 16, 4096, 32
NCORE = 8
TC = S * B // NCORE
HL = H // NCORE
NW = 4
WT = TC // NW
EPS_LN = 1e-5
EPS_Q = 1e-5
QB = 127.0
C1 = 1.0 / math.sqrt(D)
C2 = 1.0 / math.sqrt(FF)
RG8 = [[0, 1, 2, 3, 4, 5, 6, 7]]
NATQ = 2 * 2 * 4 * 128 * 512
SCALE_ATT = float(DK ** 0.5)

ENGS = ("sync", "gpsimd", "scalar", "vector", "tensor")
import os as _os
FORCE_SYNC_DMA = _os.environ.get("MK_FORCE_SYNC_DMA", "0") == "1"


def _hadamard(n):
    h = np.array([[1.0]])
    while h.shape[0] < n:
        h = np.block([[h, h], [h, -h]])
    return h


class Prog:

    _fv_ok = frozenset({
        "b", "p", "nc", "e", "xT", "modrow", "h_out", "x_rows", "attT",
        "wo_bf", "wup_pcj", "wdnT", "out_rows", "ident", "H128", "ones_col",
        "ones_row", "catile", "satile", "ca_ap", "sa_ap", "xstg", "xb",
        "xsq", "hwin", "stat_s", "mb", "rb", "mrbc", "t1", "t2", "ps_st",
        "ps_bc", "PS", "PSB", "tPS", "tPSB", "ATW", "XR", "X2T", "H2TOK",
        "H2C", "FA", "FB", "GB", "XQ", "WOS", "WUP", "WDN", "QU", "QZI",
        "QZF", "QC", "QM", "QSW", "QS", "QI", "ST", "OT", "GABC", "GBBC",
        "CBBC", "SBBC", "tGA", "tGB", "tCB", "tSB", "gsc", "row", "rowb",
        "out", "tok", "name", "modoff", "t_eps", "t_half", "wsl", "h_store",
        "gbrow", "gbrow_b", "GBBC0", "width", "src", "tsrc", "dst", "tdst",
        "dst_slices", "cnorm", "tag", "gat", "cbt", "sbt", "h128_dr", "ga_transform", "quant", "tf_ga",
        "tf_cb1", "tf_id", "transform", "XS", "XB", "HW", "nch", "wq_", "HT", "onr", "WSTG", "WQKV", "BROW", "BROWF", "BBC", "MBC", "CMASK", "MEMZ", "MEMZB", "HTB", "Q", "K", "V", "EQ", "SQ", "SK", "VT", "SKT", "DLT", "PT", "PSM", "DEN", "NMS", "RZ", "T1", "T2", "ATT", "ATTT", "mask_dr", "h_full", "wqkv", "betas_s", "att_s", "wb_parts", "wmod_sl", "abs_parts", "SC", "SCF", "WM", "WMF", "MODE", "WT_", "ACC", "SCL", "red", "onesf", "onesfr", "QT", "QA", "QO", "w_up", "w_dn", "wup_sl", "wdn_sl", "wo_sl", "cT", "mod_part", "wqup_sl", "wqdn_sl", "wob_sl", "onesfr0", "FB1", "FLAG128", "IMG", "quant_tile", "rs_mod", "rs_wt", "rs_h", "rs_att", "modg", "wupg", "wdng", "wog", "hg", "attall_loc", "attg2", "sel2", "xspar", "bflag", "store_toks", "modg_v",
        "SELT", "CAND", "hsrc", "hga", "mode_l", "modag", "atq_src", "atg",
        "wupsrc", "wupga", "wdnsrc", "wdnga", "wosrc", "woga", "OH", "GF",
        "onehot8",
    })

    def __init__(self, nc):
        self.nc = nc
        self.ops = []
        self.last_writer = {}
        self.readers = {}

    def fence(self):
        lasts = {}
        dlast = {}
        for i, rec in enumerate(self.ops):
            lasts[rec["engine"]] = i
            if rec["dma"] is not None:
                dlast[rec["dma"]] = i
        deps = set(lasts.values()) | set(dlast.values())
        fids = []
        for e in ENGS:
            i = len(self.ops)
            self.ops.append(dict(engine=e, fn=lambda en: en.nop(), dma=None,
                                 deps=set(deps), marked=False, tag="fence",
                                 extra_waits=(), bcast=False))
            fids.append(i)
        self.last_writer = {t: fids[0] for t in self.last_writer}
        for t in self.readers:
            self.readers[t] = list(fids)

    def op(self, engine, fn, reads=(), writes=(), dma=None, tag="",
           extra_waits=(), bcast=False, mark=False):
        fv = fn.__code__.co_freevars
        bad = [v for v in fv if v not in self._fv_ok]
        if bad:
            raise RuntimeError(f"op fn has late-bound free vars {bad} (tag={tag})")
        if dma is not None and FORCE_SYNC_DMA and engine == "gpsimd":
            engine = "sync"
        i = len(self.ops)
        deps = set()
        for t in reads:
            j = self.last_writer.get(t)
            if j is not None:
                deps.add(j)
        for t in writes:
            j = self.last_writer.get(t)
            if j is not None:
                deps.add(j)
            for r in self.readers.get(t, ()):
                deps.add(r)
        self.ops.append(dict(engine=engine, fn=fn, dma=dma, deps=deps,
                             marked=(dma is not None) or mark, tag=tag,
                             extra_waits=tuple(extra_waits), bcast=bcast))
        for t in reads:
            self.readers.setdefault(t, []).append(i)
        for t in writes:
            self.last_writer[t] = i
            self.readers[t] = []
        return i

    def build(self):
        nc = self.nc
        ops = self.ops
        for rec in ops:
            for j in rec["deps"]:
                o = ops[j]
                o["marked"] = True
        dkeys = []
        seen = set()
        for rec in ops:
            if rec["dma"] is not None and rec["dma"] not in seen:
                seen.add(rec["dma"])
                dkeys.append(rec["dma"])
        self._semctx = []

        def mksem(name):
            cm = nc.semaphore(name)
            h = cm.__enter__()
            self._semctx.append(cm)
            return h

        esem = {e: mksem(f"pe_{e}") for e in ENGS}
        dsem = {k: mksem(f"pd_{i}") for i, k in enumerate(dkeys)}
        ecnt = {e: 0 for e in ENGS}
        dcnt = {k: 0 for k in dkeys}
        for rec in ops:
            rec["dpre"] = None
            if not rec["marked"] and rec["dma"] is None:
                pass
            if rec["dma"] is not None:
                dcnt[rec["dma"]] += 16
                rec["sem"] = ("d", rec["dma"])
                rec["post"] = dcnt[rec["dma"]]
            elif rec["marked"]:
                ecnt[rec["engine"]] += 1
                rec["sem"] = ("e", rec["engine"])
                rec["post"] = ecnt[rec["engine"]]
        self.etotal, self.dtotal = ecnt, dcnt
        run = {k: 0 for k in dkeys}
        for rec in ops:
            rec["dwait"] = {}
            for j in rec["deps"]:
                o = ops[j]
                if o["dma"] is not None:
                    k = o["dma"]
                    rec["dwait"][k] = run[k] + 16
            for k in list(rec["dwait"]):
                rec["dwait"][k] = run[k]
            if rec["dma"] is not None:
                run[rec["dma"]] += 16

        blockcm = nc.Block()
        block = blockcm.__enter__()

        def semh(key):
            return esem[key[1]] if key[0] == "e" else dsem[key[1]]

        def make(engname):
            def body(eng):
                guaranteed = {}
                for rec in ops:
                    if rec["engine"] != engname:
                        continue
                    need = {}
                    for j in rec["deps"]:
                        o = ops[j]
                        if o["dma"] is not None:
                            k = ("d", o["dma"])
                            v = rec["dwait"][o["dma"]]
                            if v > guaranteed.get(k, 0):
                                need[k] = max(need.get(k, 0), v)
                            continue
                        k, v = o["sem"], o["post"]
                        if v > guaranteed.get(k, 0):
                            need[k] = max(need.get(k, 0), v)
                    for k in sorted(need, key=str):
                        eng.wait_ge(semh(k), need[k])
                        guaranteed[k] = need[k]
                    for sh, sv in rec["extra_waits"]:
                        kx = ("x", sh.num)
                        if sv > guaranteed.get(kx, 0):
                            eng.wait_ge(sh, sv)
                            guaranteed[kx] = sv
                    if rec["bcast"]:
                        rec["fn"](eng, dsem[rec["dma"]])
                    else:
                        inst = rec["fn"](eng)
                        if rec["marked"]:
                            inst.then_inc(semh(rec["sem"]),
                                          16 if rec["dma"] is not None else 1)
                if engname == "sync":
                    for k in dkeys:
                        eng.wait_ge(dsem[k], dcnt[k])
                    for e2 in ENGS:
                        if e2 != "sync" and ecnt[e2]:
                            eng.wait_ge(esem[e2], ecnt[e2])
            return body

        block.sync(make("sync"))
        block.gpsimd(make("gpsimd"))
        block.scalar(make("scalar"))
        block.vector(make("vector"))
        block.tensor(make("tensor"))
        blockcm.__exit__(None, None, None)
        for cm in reversed(self._semctx):
            cm.__exit__(None, None, None)


class Bld:

    def __init__(self, nc, p):
        self.nc = nc
        self.p = p
        self.sbuf_bytes = 0
        self._uid = 0
        self._consts = {}
        self.ident = None

    def uid(self):
        self._uid += 1
        return self._uid

    def sb(self, name, shape, dtype):
        sh = getattr(self, "share", None)
        if sh is not None and name in sh:
            return sh[name]
        t = self.nc.alloc_sbuf_tensor(name, list(shape), dtype)
        self.sbuf_bytes += shape[1] * mybir.dt.size(dtype) * 1
        return t

    def ps(self, name, shape, dtype=F32):
        return self.nc.alloc_psum_tensor(name, list(shape), dtype)

    def const_imm(self, val):
        key = ("imm", float(val))
        if key in self._consts:
            return self._consts[key]
        t = self.sb(f"cimm_{self.uid()}", (128, 1), F32)
        self.p.op("vector", lambda e, t=t, v=float(val): e.memset(t[:, :], v),
                  writes=[key], tag="cimm")
        self.nc.const_aps.register(F32, float(val), t[:, 0:1])
        self._consts[key] = key
        return key

    def const_tile(self, val, dtype=F32):
        key = (float(val), dtype)
        if key in self._consts:
            return self._consts[key]
        t = self.sb(f"const_{self.uid()}", (128, 1), dtype)
        self.p.op("vector", lambda e, t=t, v=float(val): e.memset(t[:, :], v),
                  writes=[("const", key)], tag="const")
        self._consts[key] = (t, ("const", key))
        return self._consts[key]

    def need_ident(self):
        if self.ident is None:
            ident_np = np.eye(128, dtype=BFNP)
            dr = self.nc.inline_tensor(np.asarray(ident_np), name="ident128")
            t = self.sb("identsb", (128, 128), BF16)
            self.p.op("sync", lambda e, t=t, dr=dr: e.dma_start(out=t[:, :], in_=dr[:, :]),
                      writes=["ident"], dma="identload", tag="ident")
            self.ident = t
        return self.ident



def emit_phase1(b, xT, modrow, h_store):
    nc, p = b.nc, b.p
    t_eps = b.const_imm(EPS_LN)

    ones_col = b.sb("p1_ones_col", (128, 1), BF16)
    p.op("vector", lambda e: e.memset(ones_col[:, :], 1.0), writes=["p1oc"])
    ones_row = b.sb("p1_ones_row", (1, 128), BF16)
    p.op("vector", lambda e: e.memset(ones_row[:, :], 1.0), writes=["p1or"])

    catile = b.sb("p1_ca", (128, NCH), F32)
    satile = b.sb("p1_sa", (128, NCH), F32)
    if getattr(b, "fused", False):
        SELT = b.sb("p1_modsel", (128, NCH), F32)
        for vv, tile, tok in ((0, satile, "p1sa"), (1, catile, "p1ca")):
            for c1 in range(2):
                a0 = bass.AP(b.modag, 0 * 1536 + vv * 256 + c1 * 128,
                             [[1, 128], [3072, 8]])
                a1 = bass.AP(b.modag, 1 * 1536 + vv * 256 + c1 * 128,
                             [[1, 128], [3072, 8]])
                d0 = bass.AP(tile, c1, [[NCH, 128], [2, 8]])
                d1 = bass.AP(SELT, c1, [[NCH, 128], [2, 8]])
                p.op("sync", lambda e, d0=d0, a0=a0: e.dma_start(
                    out=d0, in_=a0, allow_slow_non_contiguous=True),
                    reads=["modag"], writes=[(tok, c1)],
                    dma=("p1mod", vv, c1, 0))
                p.op("sync", lambda e, d1=d1, a1=a1: e.dma_start(
                    out=d1, in_=a1, allow_slow_non_contiguous=True),
                    reads=["modag"], writes=[("p1sel", c1)],
                    dma=("p1mod", vv, c1, 1))
            p.op("vector", lambda e, tile=tile: e.tensor_sub(
                SELT[:, :], SELT[:, :], tile[:, :]),
                reads=[("p1sel", 0), ("p1sel", 1), (tok, 0), (tok, 1)],
                writes=[("p1sel", 0), ("p1sel", 1)])
            p.op("vector", lambda e, tile=tile: e.scalar_tensor_tensor(
                tile[:, :], SELT[:, :], b.flag128[:, 0:1], tile[:, :],
                ALU.mult, ALU.add),
                reads=[("p1sel", 0), ("p1sel", 1), b.tflag, (tok, 0), (tok, 1)],
                writes=[tok])
    else:
        ca_ap = bass.AP(modrow, D, [[1, 128], [128, NCH]])
        sa_ap = bass.AP(modrow, 0, [[1, 128], [128, NCH]])
        p.op("sync", lambda e: e.dma_start(out=catile[:, :], in_=ca_ap,
                                           allow_slow_non_contiguous=True), writes=["p1ca"], dma="p1modca")
        p.op("sync", lambda e: e.dma_start(out=satile[:, :], in_=sa_ap,
                                           allow_slow_non_contiguous=True), writes=["p1sa"], dma="p1modsa")
    p.op("vector", lambda e: e.tensor_single_scalar(catile[:, :], catile[:, :], 1.0, ALU.add),
         reads=["p1ca"], writes=["p1ca"])

    xstg = [b.sb(f"p1_xstg{i}", (128, NCH * WT), F32) for i in range(2)]
    xb = [b.sb(f"p1_xb{i}", (128, NCH * WT), BF16) for i in range(2)]
    xsq = b.sb("p1_xsq", (128, NCH * WT), BF16)
    hwin = [b.sb(f"p1_hwin{i}", (128, NCH * WT), BF16) for i in range(2)]
    stat_s = b.sb("p1_stats", (1, 6 * WT), F32)
    mb = b.sb("p1_mb", (1, WT), BF16)
    rb = b.sb("p1_rb", (1, WT), BF16)
    mrbc = b.sb("p1_mrbc", (128, 2 * WT), F32)
    t1 = b.sb("p1_t1", (128, WT), F32)
    t2 = b.sb("p1_t2", (128, WT), F32)

    PS = b.PS
    tPS = b.tPS

    for w in range(NW):
        par = w % 2
        xspar = 0 if getattr(b, "fused", False) else par
        XS, XB, HW = xstg[xspar], xb[par], hwin[par]
        tXS, tXB, tHW = ("p1xs", xspar), ("p1xb", par), ("p1hw", par)
        alt1 = "scalar" if getattr(b, "fused", False) else "gpsimd"
        for c in range(NCH):
            eng = "sync" if c % 2 == 0 else alt1
            p.op(eng, lambda e, c=c, w=w, XS=XS: e.dma_start(
                out=XS[:, c * WT:(c + 1) * WT], in_=xT[c, :, w * WT:(w + 1) * WT]),
                writes=[tXS], dma=("p1x", par, eng), tag=f"p1 xload w{w}c{c}")
        p.op("scalar", lambda e, XS=XS, XB=XB: e.copy(XB[:, :], XS[:, :]),
             reads=[tXS], writes=[tXB], tag=f"p1 cast w{w}")
        p.op("scalar", lambda e, XB=XB: e.activation(xsq[:, :], XB[:, :], AF.Square),
             reads=[tXB], writes=["p1xsq"], tag=f"p1 sq w{w}")
        for c in range(NCH):
            p.op("tensor", lambda e, c=c, XB=XB: e.matmul(
                PS[0][0:1, 0:WT], ones_col[:, :], XB[:, c * WT:(c + 1) * WT],
                start=(c == 0), stop=(c == NCH - 1)),
                reads=[tXB, "p1oc"], writes=[tPS[0]], tag=f"p1 sum w{w}c{c}")
        for c in range(NCH):
            p.op("tensor", lambda e, c=c: e.matmul(
                PS[1][0:1, 0:WT], ones_col[:, :], xsq[:, c * WT:(c + 1) * WT],
                start=(c == 0), stop=(c == NCH - 1)),
                reads=["p1xsq", "p1oc"], writes=[tPS[1]], tag=f"p1 sumsq w{w}c{c}")
        p.op("vector", lambda e: e.tensor_single_scalar(
            stat_s[0:1, 0:WT], PS[0][0:1, 0:WT], 1.0 / D, ALU.mult),
            reads=[tPS[0]], writes=["p1stat"])
        p.op("vector", lambda e: e.tensor_mul(
            stat_s[0:1, 2 * WT:3 * WT], stat_s[0:1, 0:WT], stat_s[0:1, 0:WT]),
            reads=["p1stat"], writes=["p1stat"])
        p.op("vector", lambda e: e.scalar_tensor_tensor(
            stat_s[0:1, WT:2 * WT], PS[1][0:1, 0:WT], 1.0 / D,
            stat_s[0:1, 2 * WT:3 * WT], ALU.mult, ALU.subtract),
            reads=[tPS[1], "p1stat"], writes=["p1stat"])
        p.op("scalar", lambda e: e.activation(
            stat_s[0:1, WT:2 * WT], stat_s[0:1, WT:2 * WT], AF.Sqrt,
            bias=EPS_LN, scale=1.0), reads=["p1stat", t_eps], writes=["p1stat"])
        p.op("vector", lambda e: e.reciprocal(
            stat_s[0:1, WT:2 * WT], stat_s[0:1, WT:2 * WT]),
            reads=["p1stat"], writes=["p1stat"])
        p.op("vector", lambda e: e.tensor_copy(mb[0:1, :], stat_s[0:1, 0:WT]),
             reads=["p1stat"], writes=["p1mb"])
        p.op("vector", lambda e: e.tensor_copy(rb[0:1, :], stat_s[0:1, WT:2 * WT]),
             reads=["p1stat"], writes=["p1rb"])
        p.op("tensor", lambda e: e.matmul(PS[2][:, :], ones_row[:, :], mb[0:1, :],
                                          start=True, stop=True),
             reads=["p1mb", "p1or"], writes=[tPS[2]])
        p.op("tensor", lambda e: e.matmul(PS[3][:, :], ones_row[:, :], rb[0:1, :],
                                          start=True, stop=True),
             reads=["p1rb", "p1or"], writes=[tPS[3]])
        p.op("scalar", lambda e: e.copy(mrbc[:, 0:WT], PS[2][:, :]),
             reads=[tPS[2]], writes=["p1mrbc"])
        p.op("scalar", lambda e: e.copy(mrbc[:, WT:2 * WT], PS[3][:, :]),
             reads=[tPS[3]], writes=["p1mrbc"])
        for c in range(NCH):
            p.op("vector", lambda e, c=c, XB=XB: e.tensor_sub(
                t1[:, :], XB[:, c * WT:(c + 1) * WT], mrbc[:, 0:WT]),
                reads=[tXB, "p1mrbc"], writes=["p1t1"])
            p.op("vector", lambda e: e.tensor_mul(t2[:, :], t1[:, :], mrbc[:, WT:2 * WT]),
                 reads=["p1t1", "p1mrbc"], writes=["p1t2"])
            p.op("vector", lambda e, c=c, HW=HW: e.tensor_scalar(
                HW[:, c * WT:(c + 1) * WT], t2[:, :], catile[:, c:c + 1],
                satile[:, c:c + 1], ALU.mult, ALU.add),
                reads=["p1t2", "p1ca", "p1sa"], writes=[tHW])
        h_store(w, HW, tHW)


def build_phase1_v0():
    nc = bass.Bass("TRN2", target_bir_lowering=False, debug=False)
    xT = nc.dram_tensor("xT", [NCH, 128, TC], F32, kind="ExternalInput")
    modrow = nc.dram_tensor("modrow", [1, 6 * D], F32, kind="ExternalInput")
    h_out = nc.dram_tensor("h_out", [NCH, 128, TC], BF16, kind="ExternalOutput")
    p = Prog(nc)
    b = Bld(nc, p)
    b.PS = [b.ps(f"p1_ps{k}", (128, 512), F32) for k in range(4)]
    b.tPS = [("PS", k) for k in range(4)]

    def h_store(w, HW, tHW):
        for c in range(NCH):
            eng = "sync" if c % 2 == 0 else "gpsimd"
            p.op(eng, lambda e, c=c, w=w, HW=HW: e.dma_start(
                out=h_out[c, :, w * WT:(w + 1) * WT],
                in_=HW[:, c * WT:(c + 1) * WT]),
                reads=[tHW], writes=[("hout", w, c)], dma=("p1h", w % 2, eng))

    emit_phase1(b, xT, modrow, h_store)
    p.build()
    return nc



def _bcast_row(b, name, modoff, transform, outdt=BF16):
    nc, p = b.nc, b.p
    if not hasattr(b, "row"):
        b.row = b.sb("shared_row", (1, D), F32)
        b.rowb = b.sb("shared_rowb", (1, D), BF16)
    row, rowb = b.row, b.rowb
    out = b.sb(f"{name}_bc", (128, D), outdt)
    tok = f"{name}bc"
    if getattr(b, "fused", False):
        v = modoff // D
        src0 = bass.AP(b.modag, 0 * 1536 + v * 256, [[3072, 8], [1, 256]])
        src1 = bass.AP(b.modag, 1 * 1536 + v * 256, [[3072, 8], [1, 256]])
        dstr = bass.AP(row, 0, [[D, 1], [256, 8], [1, 256]])
        dstg = bass.AP(b.gsc3_t, 0, [[b.gsc3_t.shape[1], 1], [256, 8], [1, 256]])
        p.op("sync", lambda e, src0=src0, dstr=dstr: e.dma_start(
            out=dstr, in_=src0),
            reads=["modag"], writes=["sharedrow"], dma=("p3crow", name))
        p.op("sync", lambda e, src1=src1, dstg=dstg: e.dma_start(
            out=dstg, in_=src1),
            reads=["modag"], writes=["p3gscrow"], dma=("p3crow2", name))
        p.op("vector", lambda e: e.tensor_sub(b.gsc3[:, :], b.gsc3[:, :], row[:, :]),
             reads=["p3gscrow", "sharedrow"], writes=["p3gscrow"])
        p.op("vector", lambda e: e.scalar_tensor_tensor(
            row[:, :], b.gsc3[:, :], b.flag1[0:1, 0:1], row[:, :],
            ALU.mult, ALU.add),
            reads=["p3gscrow", b.tflag1, "sharedrow"], writes=["sharedrow"])
    else:
        p.op("sync", lambda e: e.dma_start(
            out=row[:, :], in_=bass.AP(b.modrow, modoff, [[D, 1], [1, D]])),
            writes=["sharedrow"], dma=("p3crow", name))
    transform(row, "sharedrow")
    p.op("vector", lambda e: e.tensor_copy(rowb[:, :], row[:, :]),
         reads=["sharedrow"], writes=["sharedrowb"])
    for k in range(4):
        p.op("tensor", lambda e, k=k: e.matmul(
            b.PS[k][:, :], b.ones_row[:, :], rowb[0:1, k * 512:(k + 1) * 512],
            start=True, stop=True), reads=["sharedrowb", "p3or"], writes=[b.tPS[k]])
        p.op("scalar", lambda e, k=k: e.copy(out[:, k * 512:(k + 1) * 512], b.PS[k][:, :]),
             reads=[b.tPS[k]], writes=[tok])
    return out, tok


def emit_phase3(b, x_rows, attT, wo_bf, wup_pcj, wdnT, modrow, out_rows):
    nc, p = b.nc, b.p
    b.modrow = modrow
    t_eps = b.const_imm(EPS_LN)
    t_half = b.const_imm(0.5)
    ident = b.need_ident()

    h128_np = _hadamard(128).astype(BFNP)
    h128_dr = nc.inline_tensor(np.asarray(h128_np), name="h128")
    H128 = b.sb("p3_h128", (128, 128), BF16)
    p.op("sync", lambda e: e.dma_start(out=H128[:, :], in_=h128_dr[:, :]),
         writes=["h128"], dma="p3ch128")
    if getattr(b, "ones_row", None) is None:
        ones_row = b.sb("p3_ones_row", (1, 128), BF16)
        p.op("vector", lambda e: e.memset(ones_row[:, :], 1.0), writes=["p3or"])
        b.ones_row = ones_row
    else:
        ones_row = b.ones_row

    if getattr(b, "PS", None) is not None and len(b.PS) >= 7:
        PS, tPS = b.PS, b.tPS
        PSB, tPSB = b.PSB, b.tPSB
    else:
        PS = [b.ps(f"p3_ps{k}", (128, 512), F32) for k in range(7)]
        PSB = b.ps("p3_psb", (128, 1024), BF16)
        tPS = [("PS", k) for k in range(7)]
        tPSB = ("PSB",)
        b.PS, b.tPS = PS, tPS
        b.PSB, b.tPSB = PSB, tPSB

    gsc = b.sb("p3_gsc", (1, D), F32)

    def tf_ga(row, tok):
        p.op("vector", lambda e, row=row: e.tensor_scalar(
            gsc[:, :], row[:, :], 0.0, 2.0, ALU.is_ge, ALU.mult),
            reads=[tok], writes=["p3gscrow"])
        p.op("vector", lambda e: e.tensor_single_scalar(
            gsc[:, :], gsc[:, :], -1.0, ALU.add), reads=["p3gscrow"], writes=["p3gscrow"])
        p.op("scalar", lambda e, row=row: e.activation(row[:, :], row[:, :], AF.Abs),
             reads=[tok], writes=[tok])
        p.op("vector", lambda e, row=row: e.tensor_single_scalar(
            row[:, :], row[:, :], 0.25, ALU.max), reads=[tok], writes=[tok])
        p.op("vector", lambda e, row=row: e.tensor_mul(row[:, :], row[:, :], gsc[:, :]),
             reads=[tok, "p3gscrow"], writes=[tok])

    def tf_cb1(row, tok):
        p.op("vector", lambda e, row=row: e.tensor_single_scalar(
            row[:, :], row[:, :], 1.0, ALU.add), reads=[tok], writes=[tok])

    def tf_id(row, tok):
        pass

    GABC, tGA = _bcast_row(b, "p3ga", 2 * D, tf_ga)
    GBBC, tGB = _bcast_row(b, "p3gb", 5 * D, tf_ga)
    CBBC, tCB = _bcast_row(b, "p3cb", 4 * D, tf_cb1)
    SBBC, tSB = _bcast_row(b, "p3sb", 3 * D, tf_id)

    ATW = b.sb("p3_atw", (128, NCH * WT), BF16)
    tATW = "xq" if getattr(b, "fused", False) else "atw"
    CAND = [b.sb(f"p3_cand{i}", (128, 512), BF16) for i in range(2)]
    XR = [b.sb(f"p3_xr{i}", (128, 512), F32) for i in range(2)]
    X2T = b.sb("p3_x2t", (128, 4 * D), F32)
    H2TOK = b.sb("p3_h2tok", (128, D), BF16)
    H2C = b.sb("p3_h2c", (128, 128), BF16)
    FA = b.sb("p3_fa", (128, FCH * WT), BF16)
    FB = b.sb("p3_fb", (128, NCH * WT), BF16)
    GB = b.sb("p3_gb", (128, NCH * WT), BF16)
    XQ = b.sb("p3_xq", (128, NCH * WT), BF16)
    WOS = [b.sb(f"p3_wos{i}", (128, 512), BF16) for i in range(2)]
    WUP = [b.sb(f"p3_wup{i}", (128, NCH * 128), BF16) for i in range(2)]
    WDN = [b.sb(f"p3_wdn{i}", (128, 512), BF16) for i in range(2)]
    QU = b.sb("p3_qu", (128, WT), F32)
    QZI = b.sb("p3_qzi", (128, WT), I32)
    QZF = b.sb("p3_qzf", (128, WT), F32)
    QC = b.sb("p3_qc", (128, WT), F32)
    QM = FB
    QSW = b.sb("p3_qsw", (128, WT), BF16)
    QS = b.sb("p3_qs", (128, WT), F32)
    QI = b.sb("p3_qi", (128, WT), F32)
    ST = b.sb("p3_st", (128, 8), F32)
    OT = [b.sb(f"p3_ot{i}", (128, 512), F32) for i in range(2)]

    def quant(nch, width, src, tsrc, dst_slices, tdst, cnorm, wq_, tag):
        p.op("scalar", lambda e: e.activation(
            QM[:, 0:nch * width], src[:, 0:nch * width], AF.Abs),
            reads=[tsrc], writes=["fb"], tag=f"{tag} abs")
        n = nch
        while n > 1:
            n2 = n // 2
            p.op("vector", lambda e, n2=n2, n=n: e.tensor_max(
                QM[:, 0:n2 * width], QM[:, 0:n2 * width], QM[:, n2 * width:n * width]),
                reads=["fb"], writes=["fb"])
            n = n2
        for half in (64, 32):
            pairs = [(0, half)] if half == 64 else [(0, 32), (64, 96)]
            for lo, hi in pairs:
                p.op("sync", lambda e, lo=lo, hi=hi, hh=half: e.dma_start(
                    out=QSW[lo:lo + hh, 0:width], in_=QM[hi:hi + hh, 0:width]),
                    reads=["fb"], writes=["qsw"], dma=("p3swap",))
                p.op("sync", lambda e, lo=lo, hi=hi, hh=half: e.dma_start(
                    out=QSW[hi:hi + hh, 0:width], in_=QM[lo:lo + hh, 0:width]),
                    reads=["fb"], writes=["qsw"], dma=("p3swap",))
            p.op("vector", lambda e: e.tensor_max(
                QM[:, 0:width], QM[:, 0:width], QSW[:, 0:width]),
                reads=["fb", "qsw"], writes=["fb"])
        for bit in (16, 8, 4, 2, 1):
            p.op("vector", lambda e, bit=bit: e.stream_shuffle(
                QSW[:, 0:width], QM[:, 0:width], [i ^ bit for i in range(32)]),
                reads=["fb"], writes=["qsw"])
            p.op("vector", lambda e: e.tensor_max(
                QM[:, 0:width], QM[:, 0:width], QSW[:, 0:width]),
                reads=["fb", "qsw"], writes=["fb"])
        p.op("vector", lambda e: e.tensor_single_scalar(
            QS[:, 0:width], QM[:, 0:width], EPS_Q / cnorm, ALU.max),
            reads=["fb"], writes=["qs"])
        p.op("vector", lambda e: e.tensor_single_scalar(
            QI[:, 0:width], QS[:, 0:width], cnorm / QB, ALU.mult),
            reads=["qs"], writes=["qi"])
        p.op("vector", lambda e: e.reciprocal(QS[:, 0:width], QS[:, 0:width]),
             reads=["qs"], writes=["qs"])
        p.op("vector", lambda e: e.tensor_single_scalar(
            QS[:, 0:width], QS[:, 0:width], QB, ALU.mult), reads=["qs"], writes=["qs"])
        for c in range(nch):
            dt_, dsl = dst_slices(c)
            ssl = slice(c * width, (c + 1) * width)
            p.op("vector", lambda e, ssl=ssl: e.tensor_mul(
                QU[:, 0:width], src[:, ssl], QS[:, 0:width]),
                reads=[tsrc, "qs"], writes=["qu"])
            p.op("scalar", lambda e: e.activation(
                QZI[:, 0:width], QU[:, 0:width], AF.Copy, bias=0.5, scale=1.0),
                reads=["qu", t_half], writes=["qzi"])
            p.op("scalar", lambda e: e.copy(QZF[:, 0:width], QZI[:, 0:width]),
                 reads=["qzi"], writes=["qzf"])
            p.op("vector", lambda e: e.scalar_tensor_tensor(
                QC[:, 0:width], QZF[:, 0:width], 0.5, QU[:, 0:width],
                ALU.subtract, ALU.is_gt),
                reads=["qzf", "qu"], writes=["qc"])
            p.op("vector", lambda e: e.tensor_sub(
                QC[:, 0:width], QZF[:, 0:width], QC[:, 0:width]),
                reads=["qzf", "qc"], writes=["qc"])
            p.op("vector", lambda e, dt_=dt_, dsl=dsl: e.tensor_mul(
                dt_[:, dsl], QC[:, 0:width], QI[:, 0:width]),
                reads=["qc", "qi"], writes=[tdst], tag=f"{tag} out c{c}")

    for w in range(NW):
        wsl = slice(w * WT, (w + 1) * WT)
        if getattr(b, "fused", False):
            for hh in range(H):
                R_, hl_ = hh // HL, hh % HL
                for k8 in range(8):
                    bbx, qq = k8 // 4, k8 % 4
                    src = bass.AP(attT, (qq * 8 + R_) * NATQ
                                  + ((bbx * 2 + hl_) * 4 + w) * (128 * 512),
                                  [[512, 128], [1, 512]])
                    eng = "sync" if k8 % 2 == 0 else "scalar"
                    p.op(eng, lambda e, src=src, k8=k8: e.dma_start(
                        out=CAND[k8 % 2][:, :], in_=src),
                        reads=[("atag", k8 % 4)], writes=[("cand", k8 % 2)],
                        dma=("p3att", k8 % 2), tag=f"p3 atcand w{w}h{hh}k{k8}")
                    if k8 == 0:
                        p.op("vector", lambda e, hh=hh, k8=k8: e.tensor_scalar(
                            ATW[:, hh * WT:(hh + 1) * WT], CAND[k8 % 2][:, :],
                            b.GF[:, k8:k8 + 1], None, ALU.mult),
                            reads=[("cand", k8 % 2), "gf"], writes=[tATW],
                            tag=f"p3 atsel w{w}h{hh}k{k8}")
                    else:
                        p.op("vector", lambda e, hh=hh, k8=k8: e.scalar_tensor_tensor(
                            ATW[:, hh * WT:(hh + 1) * WT], CAND[k8 % 2][:, :],
                            b.GF[:, k8:k8 + 1], ATW[:, hh * WT:(hh + 1) * WT],
                            ALU.mult, ALU.add),
                            reads=[("cand", k8 % 2), "gf", tATW], writes=[tATW],
                            tag=f"p3 atsel w{w}h{hh}k{k8}")
        else:
            for c in range(NCH):
                eng = "sync" if c % 2 == 0 else "gpsimd"
                p.op(eng, lambda e, c=c, wsl=wsl: e.dma_start(
                    out=ATW[:, c * WT:(c + 1) * WT], in_=attT[c, :, wsl]),
                    writes=[tATW], dma=("p3att", eng), tag=f"p3 attload w{w}c{c}")
        for dq in range(4):
            for c1 in range(NCH):
                par = c1 % 2
                if getattr(b, "fused", False):
                    p.op("sync", lambda e, c1=c1, dq=dq, par=par: e.dma_start(
                        out=WOS[par][:, :],
                        in_=wo_bf[c1 // 2, c1 % 2, :, dq * 512:(dq + 1) * 512]),
                        reads=["woag"],
                        writes=[("wos", par)], dma=("p3wo", par),
                        tag=f"p3 woload w{w}q{dq}c{c1}")
                else:
                    p.op("sync", lambda e, c1=c1, dq=dq, par=par: e.dma_start(
                        out=WOS[par][:, :],
                        in_=wo_bf[c1 * 128:(c1 + 1) * 128, dq * 512:(dq + 1) * 512]),
                        writes=[("wos", par)], dma=("p3wo", par),
                        tag=f"p3 woload w{w}q{dq}c{c1}",
                        extra_waits=getattr(b, "xw_wt", ()))
                for ts_ in range(4):
                    if getattr(b, "fused", False):
                        lhs = bass.AP(ATW, ts_ * 4 * SEG + c1,
                                      [[ATW.shape[1], 128], [SEG, 4], [16, 32]])
                        p.op("tensor", lambda e, lhs=lhs, c1=c1, ts_=ts_, par=par: e.matmul(
                            PS[ts_][:, :], lhs, WOS[par][:, :],
                            start=(c1 == 0), stop=(c1 == NCH - 1)),
                            reads=[("wos", par), tATW], writes=[tPS[ts_]],
                            tag=f"p3 o mm w{w}q{dq}c{c1}t{ts_}")
                    else:
                        p.op("tensor", lambda e, c1=c1, ts_=ts_, par=par: e.matmul(
                            PS[ts_][:, :],
                            ATW[:, c1 * WT + ts_ * 128: c1 * WT + (ts_ + 1) * 128],
                            WOS[par][:, :],
                            start=(c1 == 0), stop=(c1 == NCH - 1)),
                            reads=[("wos", par), tATW], writes=[tPS[ts_]],
                            tag=f"p3 o mm w{w}q{dq}c{c1}t{ts_}")
            for ts_ in range(4):
                xpar = ts_ % 2
                osl = slice(dq * 512, (dq + 1) * 512)
                xeng = "scalar" if getattr(b, "fused", False) else "gpsimd"
                p.op(xeng, lambda e, w=w, ts_=ts_, xpar=xpar, osl=osl: e.dma_start(
                    out=XR[xpar][:, :],
                    in_=x_rows[w * WT + ts_ * 128: w * WT + (ts_ + 1) * 128, osl]),
                    writes=[("xr", xpar)], dma=("p3x", xpar),
                    tag=f"p3 xload w{w}q{dq}t{ts_}")
                if getattr(b, "dbg", None) is not None and w == 0 and dq == 0 and ts_ == 0:
                    p.op("sync", lambda e: e.dma_start(out=b.dbg["xr"][:, :], in_=XR[0][:, :]),
                         reads=[("xr", 0)], writes=["dbgxr"], dma=("dbg",))
                    p.op("vector", lambda e: e.tensor_copy(OT[1][:, :], PS[0][:, :]),
                         reads=[tPS[0]], writes=[("ot", 1)])
                    p.op("sync", lambda e: e.dma_start(out=b.dbg["o0"][:, :], in_=OT[1][:, :]),
                         reads=[("ot", 1)], writes=["dbgo0"], dma=("dbg",))
                p.op("vector", lambda e, ts_=ts_, osl=osl: e.tensor_mul(
                    OT[ts_ % 2][:, :], PS[ts_][:, :], GABC[:, osl]),
                    reads=[tPS[ts_], tGA], writes=[("ot", ts_ % 2)],
                    tag=f"p3 oga w{w}q{dq}t{ts_}")
                p.op("vector", lambda e, ts_=ts_, dq=dq, xpar=xpar: e.tensor_add(
                    X2T[:, ts_ * D + dq * 512: ts_ * D + (dq + 1) * 512],
                    OT[ts_ % 2][:, :], XR[xpar][:, :]),
                    reads=[("ot", ts_ % 2), ("xr", xpar)], writes=["x2t"],
                    tag=f"p3 x2 w{w}q{dq}t{ts_}")
        if getattr(b, "dbg", None) is not None and w == 0:
            p.op("sync", lambda e: e.dma_start(out=b.dbg["x2t"][:, :], in_=X2T[:, :]),
                 reads=["x2t"], writes=["dbgx"], dma=("dbg",))
            p.op("sync", lambda e: e.dma_start(out=b.dbg["atw"][:, :], in_=ATW[:, :]),
                 reads=["atw"], writes=["dbga"], dma=("dbg",))
        for ts_ in range(4):
            x2sl = slice(ts_ * D, (ts_ + 1) * D)
            p.op("vector", lambda e, x2sl=x2sl: e.tensor_reduce(
                ST[:, 0:1], X2T[:, x2sl], AX.X, ALU.add),
                reads=["x2t"], writes=["p3st"], tag=f"p3 sum w{w}t{ts_}")
            p.op("scalar", lambda e, x2sl=x2sl: e.activation(
                QM[:, 0:D], X2T[:, x2sl], AF.Square, accum_out=ST[:, 1:2]),
                reads=["x2t"], writes=["fb", "p3st"], tag=f"p3 sumsq w{w}t{ts_}")
            p.op("vector", lambda e: e.tensor_single_scalar(
                ST[:, 2:3], ST[:, 0:1], 1.0 / D, ALU.mult),
                reads=["p3st"], writes=["p3st"])
            p.op("vector", lambda e: e.tensor_mul(ST[:, 3:4], ST[:, 2:3], ST[:, 2:3]),
                 reads=["p3st"], writes=["p3st"])
            p.op("vector", lambda e: e.scalar_tensor_tensor(
                ST[:, 4:5], ST[:, 1:2], 1.0 / D, ST[:, 3:4], ALU.mult, ALU.subtract),
                reads=["p3st"], writes=["p3st"])
            p.op("scalar", lambda e: e.activation(
                ST[:, 4:5], ST[:, 4:5], AF.Sqrt, bias=EPS_LN, scale=1.0),
                reads=["p3st", t_eps], writes=["p3st"])
            p.op("vector", lambda e: e.reciprocal(ST[:, 5:6], ST[:, 4:5]),
                 reads=["p3st"], writes=["p3st"])
            p.op("vector", lambda e, x2sl=x2sl: e.tensor_scalar(
                H2TOK[:, :], X2T[:, x2sl], ST[:, 2:3], ST[:, 5:6],
                ALU.subtract, ALU.mult),
                reads=["x2t", "p3st"], writes=["h2tok"], tag=f"p3 norm w{w}t{ts_}")
            p.op("vector", lambda e: e.tensor_mul(H2TOK[:, :], H2TOK[:, :], CBBC[:, :]),
                 reads=["h2tok", tCB], writes=["h2tok"])
            p.op("vector", lambda e: e.tensor_add(H2TOK[:, :], H2TOK[:, :], SBBC[:, :]),
                 reads=["h2tok", tSB], writes=["h2tok"])
            for c in range(NCH):
                k = 2 + c % 2
                p.op("tensor", lambda e, c=c: e.transpose(
                    PSB[:, 0:128], H2TOK[:, c * 128:(c + 1) * 128], ident[:, :]),
                    reads=["h2tok", "ident"], writes=[tPSB],
                    tag=f"p3 h2t w{w}t{ts_}c{c}")
                p.op("vector", lambda e: e.tensor_copy(H2C[:, :], PSB[:, 0:128]),
                     reads=[tPSB], writes=["h2c"], tag=f"p3 h2tev w{w}t{ts_}c{c}")
                p.op("tensor", lambda e, k=k: e.matmul(
                    PS[k][:, 0:128], H128[:, :], H2C[:, :],
                    start=True, stop=True), reads=["h2c", "h128"], writes=[tPS[k]],
                    tag=f"p3 fw1mm w{w}t{ts_}c{c}")
                p.op("scalar", lambda e, c=c, ts_=ts_, k=k: e.copy(
                    FA[:, c * WT + ts_ * 128: c * WT + (ts_ + 1) * 128],
                    PS[k][:, 0:128]),
                    reads=[tPS[k]], writes=["fa"], tag=f"p3 fw1ev w{w}t{ts_}c{c}")
        bufs = [(FA, "fa"), (FB, "fb")]
        for st_i, stv in enumerate([8, 4, 2, 1]):
            src_t, tsrc = bufs[st_i % 2]
            dst_t, tdst = bufs[(st_i + 1) % 2]
            for g in range(NCH // (2 * stv)):
                for j in range(stv):
                    a = g * 2 * stv + j
                    bi = a + stv
                    asl = slice(a * WT, (a + 1) * WT)
                    bsl = slice(bi * WT, (bi + 1) * WT)
                    p.op("vector", lambda e, asl=asl, bsl=bsl, src_t=src_t, dst_t=dst_t: e.tensor_add(
                        dst_t[:, asl], src_t[:, asl], src_t[:, bsl]),
                        reads=[tsrc], writes=[tdst], tag=f"p3 bf1 w{w}")
                    p.op("vector", lambda e, asl=asl, bsl=bsl, src_t=src_t, dst_t=dst_t: e.tensor_sub(
                        dst_t[:, bsl], src_t[:, asl], src_t[:, bsl]),
                        reads=[tsrc], writes=[tdst], tag=f"p3 bf1 w{w}")
        if getattr(b, "dbg", None) is not None and w == 0:
            p.op("sync", lambda e: e.dma_start(out=b.dbg["fw1"][:, :], in_=FA[:, 0:NCH * WT]),
                 reads=["fa"], writes=["dbgf"], dma=("dbg",))
        quant(NCH, WT, FA, "fa", lambda c: (XQ, slice(c * WT, (c + 1) * WT)),
              "xq", C1, w, f"p3 q1 w{w}")
        if getattr(b, "dbg", None) is not None and w == 0:
            p.op("sync", lambda e: e.dma_start(out=b.dbg["xq"][:, :], in_=XQ[:, :]),
                 reads=["xq"], writes=["dbgq"], dma=("dbg",))
        for f2 in range(FCH):
            par = f2 % 2
            if getattr(b, "fused", False):
                p.op("sync", lambda e, f2=f2, par=par: e.dma_start(
                    out=WUP[par][:, :], in_=wup_pcj[f2 // 4, f2 % 4, :, :]),
                    reads=["wupag"],
                    writes=[("wup", par)], dma=("p3wup", par),
                    tag=f"p3 wupload w{w}f{f2}")
            else:
                p.op("sync", lambda e, f2=f2, par=par: e.dma_start(
                    out=WUP[par][:, :], in_=wup_pcj[f2, :, :]),
                    writes=[("wup", par)], dma=("p3wup", par), tag=f"p3 wupload w{w}f{f2}",
                    extra_waits=getattr(b, "xw_wt", ()))
            k = 2 + f2 % 2
            for c in range(NCH):
                p.op("tensor", lambda e, f2=f2, c=c, k=k, par=par: e.matmul(
                    PS[k][:, :], WUP[par][:, c * 128:(c + 1) * 128],
                    XQ[:, c * WT:(c + 1) * WT],
                    start=(c == 0), stop=(c == NCH - 1)),
                    reads=[("wup", par), "xq"], writes=[tPS[k]],
                    tag=f"p3 up mm w{w}f{f2}c{c}")
            p.op("vector", lambda e, k=k: e.tensor_single_scalar(
                QU[:, :], PS[k][:, :], 0.0, ALU.max),
                reads=[tPS[k]], writes=["qu"], tag=f"p3 relu w{w}f{f2}")
            p.op("scalar", lambda e, f2=f2: e.activation(
                FA[:, f2 * WT:(f2 + 1) * WT], QU[:, :], AF.Square),
                reads=["qu"], writes=["fa"], tag=f"p3 rsq w{w}f{f2}")
        HT = WT // 2
        for hw in range(2):
            for f2 in range(FCH):
                k = 2 + f2 % 2
                p.op("tensor", lambda e, f2=f2, hw=hw, k=k: e.matmul(
                    PS[k][:, 0:HT], H128[:, :],
                    FA[:, f2 * WT + hw * HT: f2 * WT + (hw + 1) * HT],
                    start=True, stop=True), reads=["fa", "h128"], writes=[tPS[k]],
                    tag=f"p3 fw2mm w{w}h{hw}f{f2}")
                p.op("scalar", lambda e, f2=f2, k=k: e.copy(
                    FB[:, f2 * HT:(f2 + 1) * HT], PS[k][:, 0:HT]),
                    reads=[tPS[k]], writes=["fb"], tag=f"p3 fw2ev w{w}h{hw}f{f2}")
            bufs2 = [(FB, "fb"), (GB, "gb")]
            for st_i, stv in enumerate([16, 8, 4, 2, 1]):
                src_t, tsrc = bufs2[st_i % 2]
                dst_t, tdst = bufs2[(st_i + 1) % 2]
                for g in range(FCH // (2 * stv)):
                    for j in range(stv):
                        a = g * 2 * stv + j
                        bi = a + stv
                        asl = slice(a * HT, (a + 1) * HT)
                        bsl = slice(bi * HT, (bi + 1) * HT)
                        p.op("vector", lambda e, asl=asl, bsl=bsl, src_t=src_t, dst_t=dst_t: e.tensor_add(
                            dst_t[:, asl], src_t[:, asl], src_t[:, bsl]),
                            reads=[tsrc], writes=[tdst], tag=f"p3 bf2 w{w}")
                        p.op("vector", lambda e, asl=asl, bsl=bsl, src_t=src_t, dst_t=dst_t: e.tensor_sub(
                            dst_t[:, bsl], src_t[:, asl], src_t[:, bsl]),
                            reads=[tsrc], writes=[tdst], tag=f"p3 bf2 w{w}")
            quant(FCH, HT, GB, "gb",
                  lambda c, hw=hw: (FA, slice(c * WT + hw * HT, c * WT + (hw + 1) * HT)),
                  "fa", C2, w, f"p3 q2 w{w}h{hw}")
        for dq in range(4):
            for f in range(FCH):
                par = f % 2
                if getattr(b, "fused", False):
                    for ii in range(2):
                        p.op("sync", lambda e, f=f, dq=dq, par=par, ii=ii: e.dma_start(
                            out=WDN[par][:, ii * 256:(ii + 1) * 256],
                            in_=wdnT[2 * dq + ii, f * 128:(f + 1) * 128, :]),
                            reads=["wdnag"],
                            writes=[("wdn", par)], dma=("p3wdn", par),
                            tag=f"p3 wdnload w{w}q{dq}f{f}i{ii}")
                else:
                    p.op("sync", lambda e, f=f, dq=dq, par=par: e.dma_start(
                        out=WDN[par][:, :],
                        in_=wdnT[f * 128:(f + 1) * 128, dq * 512:(dq + 1) * 512]),
                        writes=[("wdn", par)], dma=("p3wdn", par),
                        tag=f"p3 wdnload w{w}q{dq}f{f}")
                for ts_ in range(4):
                    p.op("tensor", lambda e, f=f, ts_=ts_, par=par: e.matmul(
                        PS[ts_][:, :],
                        FA[:, f * WT + ts_ * 128: f * WT + (ts_ + 1) * 128],
                        WDN[par][:, :],
                        start=(f == 0), stop=(f == FCH - 1)),
                        reads=[("wdn", par), "fa"], writes=[tPS[ts_]],
                        tag=f"p3 dn mm w{w}q{dq}f{f}t{ts_}")
            for ts_ in range(4):
                osl = slice(dq * 512, (dq + 1) * 512)
                p.op("vector", lambda e, ts_=ts_, osl=osl: e.tensor_mul(
                    OT[ts_ % 2][:, :], PS[ts_][:, :], GBBC[:, osl]),
                    reads=[tPS[ts_], tGB], writes=[("ot", ts_ % 2)],
                    tag=f"p3 dngb w{w}q{dq}t{ts_}")
                p.op("vector", lambda e, ts_=ts_, dq=dq: e.tensor_add(
                    OT[ts_ % 2][:, :], OT[ts_ % 2][:, :],
                    X2T[:, ts_ * D + dq * 512: ts_ * D + (dq + 1) * 512]),
                    reads=[("ot", ts_ % 2), "x2t"], writes=[("ot", ts_ % 2)],
                    tag=f"p3 dnadd w{w}q{dq}t{ts_}")
                alt3 = "scalar" if getattr(b, "fused", False) else "gpsimd"
                eng = "sync" if ts_ % 2 == 0 else alt3
                p.op(eng, lambda e, w=w, ts_=ts_, osl=osl: e.dma_start(
                    out=out_rows[w * WT + ts_ * 128: w * WT + (ts_ + 1) * 128, osl],
                    in_=OT[ts_ % 2][:, :]),
                    reads=[("ot", ts_ % 2)], writes=[("outr", w, ts_, dq)],
                    dma=("p3out", eng), tag=f"p3 out w{w}q{dq}t{ts_}")


def build_phase3_v0(debug=False):
    nc = bass.Bass("TRN2", target_bir_lowering=False, debug=False)
    x_rows = nc.dram_tensor("x_rows", [TC, D], F32, kind="ExternalInput")
    attT = nc.dram_tensor("attT", [NCH, 128, TC], BF16, kind="ExternalInput")
    wo_bf = nc.dram_tensor("wo_bf", [D, D], BF16, kind="ExternalInput")
    wup_pcj = nc.dram_tensor("wup_pcj", [FCH, 128, NCH * 128], BF16, kind="ExternalInput")
    wdnT = nc.dram_tensor("wdnT", [FF, D], BF16, kind="ExternalInput")
    modrow = nc.dram_tensor("modrow", [1, 6 * D], F32, kind="ExternalInput")
    out_rows = nc.dram_tensor("out_rows", [TC, D], F32, kind="ExternalOutput")
    p = Prog(nc)
    b = Bld(nc, p)
    if debug:
        b.dbg = {
            "x2t": nc.dram_tensor("dbg_x2t", [128, 4 * D], F32, kind="ExternalOutput"),
            "fw1": nc.dram_tensor("dbg_fw1", [128, NCH * WT], BF16, kind="ExternalOutput"),
            "xq": nc.dram_tensor("dbg_xq", [128, NCH * WT], BF16, kind="ExternalOutput"),
            "atw": nc.dram_tensor("dbg_atw", [128, NCH * WT], BF16, kind="ExternalOutput"),
            "xr": nc.dram_tensor("dbg_xr", [128, 512], F32, kind="ExternalOutput"),
            "o0": nc.dram_tensor("dbg_o0", [128, 512], F32, kind="ExternalOutput"),
        }
    emit_phase3(b, x_rows, attT, wo_bf, wup_pcj, wdnT, modrow, out_rows)
    p.build()
    return nc



def emit_phase2(b, h_full, wqkv, betas_s, att_s):
    nc, p = b.nc, b.p
    ident = b.need_ident()

    WSTG = b.sb("p2_wstg", (128, 768), F32)
    WQKV = b.sb("p2_wqkv", (128, NCH * 768), BF16)
    for c in range(NCH):
        p.op("sync", lambda e, c=c: e.dma_start(
            out=WSTG[:, :], in_=wqkv[c * 128:(c + 1) * 128, :]),
            writes=["wstg"], dma=("p2w",), tag=f"p2 wload c{c}")
        p.op("scalar", lambda e, c=c: e.copy(WQKV[:, c * 768:(c + 1) * 768], WSTG[:, :]),
             reads=["wstg"], writes=["wqkv"], tag=f"p2 wcast c{c}")

    BROW = b.sb("p2_brow", (1, 2 * 128), BF16)
    BROWF = b.sb("p2_browf", (1, 2 * 128), F32)
    BBC = b.sb("p2_bbc", (128, 2 * 128), F32)
    MBC = b.sb("p2_mbc", (128, 2 * 128), F32)
    for hl in range(HL):
        p.op("sync", lambda e, hl=hl: e.dma_start(
            out=BROWF[0:1, hl * 128:(hl + 1) * 128], in_=betas_s[hl:hl + 1, :]),
            writes=["brow"], dma=("p2wb",), tag=f"p2 bload {hl}")
    p.op("scalar", lambda e: e.activation(BROW[0:1, :], BROWF[0:1, :], AF.Sigmoid),
         reads=["brow"], writes=["brow"])
    for hl in range(HL):
        p.op("tensor", lambda e, hl=hl: e.matmul(
            b.PS[0][:, 0:128], b.ones_row[:, :], BROW[0:1, hl * 128:(hl + 1) * 128],
            start=True, stop=True), reads=["brow", "p3or"], writes=[b.tPS[0]])
        p.op("scalar", lambda e, hl=hl: e.copy(BBC[:, hl * 128:(hl + 1) * 128],
                                               b.PS[0][:, 0:128]),
             reads=[b.tPS[0]], writes=["bbc"])
        p.op("vector", lambda e, hl=hl: e.tensor_scalar(
            MBC[:, hl * 128:(hl + 1) * 128], BBC[:, hl * 128:(hl + 1) * 128],
            -1.0, 1.0, ALU.mult, ALU.add),
            reads=["bbc"], writes=["mbc"])

    mask_np = np.where(np.arange(128)[:, None] >= np.arange(128)[None, :],
                       0.0, -1e9).astype(np.float32)
    mask_dr = nc.inline_tensor(np.asarray(mask_np), name="cmask")
    CMASK = b.sb("p2_cmask", (128, 128), F32)
    p.op("sync", lambda e: e.dma_start(out=CMASK[:, :], in_=mask_dr[:, :]),
         writes=["cmask"], dma=("p2wc",))

    MEMZ = [b.sb(f"p2_memz{i}", (128, 129), F32) for i in range(2 * HL)]
    MEMZB = [b.sb(f"p2_memzb{i}", (128, 129), BF16) for i in range(2 * HL)]
    for pr in range(2 * HL):
        p.op("vector", lambda e, pr=pr: e.memset(MEMZ[pr][:, 0:128], 0.0),
             writes=[("memz", pr)])
        p.op("vector", lambda e, pr=pr: e.memset(MEMZ[pr][:, 128:129], 1.0 / DK),
             writes=[("memz", pr)])
        p.op("scalar", lambda e, pr=pr: e.copy(MEMZB[pr][:, :], MEMZ[pr][:, :]),
             reads=[("memz", pr)], writes=[("memzb", pr)])

    HTB = [b.sb(f"p2_ht{i}", (128, NCH * SEG), BF16) for i in range(2)]
    Q = b.sb("p2_q", (128, SEG), BF16)
    K = b.sb("p2_k", (128, SEG), BF16)
    V = b.sb("p2_v", (128, SEG), BF16)
    EQ = b.sb("p2_eq", (128, SEG), BF16)
    SQ = b.sb("p2_sq", (128, SEG), BF16)
    SK = b.sb("p2_sk", (128, SEG), BF16)
    VT = b.sb("p2_vt", (128, SEG), BF16)
    SKT = b.sb("p2_skt", (128, SEG), BF16)
    DLT = b.sb("p2_dlt", (128, SEG), BF16)
    PT = b.sb("p2_pt", (128, 4 * SEG), BF16)
    PSM = b.sb("p2_psm", (128, SEG), BF16)
    DEN = b.sb("p2_den", (128, 8), F32)
    NMS = b.sb("p2_nms", (128, 8), F32)
    RZ = b.sb("p2_rz", (128, 4), F32)
    T1 = b.sb("p2_t1", (128, 128), F32)
    T2 = b.sb("p2_t2", (128, 128), F32)
    ATT = b.sb("p2_att", (128, 128), BF16)
    ATTT = b.sb("p2_attt", (128, SEG), BF16)

    PS, tPS, PSB, tPSB = b.PS, b.tPS, b.PSB, b.tPSB

    for seg in range(NSEG):
        for bb in range(B):
            par = (seg * B + bb) % 2
            HT_ = HTB[par]
            tHT = ("p2ht", par)
            base = bb * S + seg * SEG
            for c in range(NCH):
                if getattr(b, "fused", False):
                    eng = "sync" if c % 2 == 0 else "scalar"
                    wv, rv = seg % 4, bb * 4 + seg // 4
                    p.op(eng, lambda e, c=c, wv=wv, rv=rv, HT_=HT_: e.dma_start(
                        out=HT_[:, c * SEG:(c + 1) * SEG],
                        in_=h_full[wv, rv, :, c * SEG:(c + 1) * SEG]),
                        reads=[("hag", wv)],
                        writes=[tHT], dma=("p2h", par, eng),
                        tag=f"p2 hload s{seg}b{bb}c{c}")
                else:
                    eng = "sync" if c % 2 == 0 else "gpsimd"
                    p.op(eng, lambda e, c=c, base=base, HT_=HT_: e.dma_start(
                        out=HT_[:, c * SEG:(c + 1) * SEG],
                        in_=h_full[c, :, base:base + SEG]),
                        writes=[tHT], dma=("p2h", par, eng),
                        tag=f"p2 hload s{seg}b{bb}c{c}")
            for hl in range(HL):
                pr = bb * HL + hl
                sfx = f"s{seg}b{bb}h{hl}"
                for pi, (dst, sig, tdst) in enumerate(
                        [(Q, SQ, "q"), (K, SK, "k"), (V, None, "v")]):
                    kk = pi % 2
                    for c in range(NCH):
                        p.op("tensor", lambda e, c=c, pi=pi, hl=hl, kk=kk, HT_=HT_: e.matmul(
                            PS[kk][:, :],
                            WQKV[:, c * 768 + pi * 256 + hl * 128:
                                 c * 768 + pi * 256 + (hl + 1) * 128],
                            HT_[:, c * SEG:(c + 1) * SEG],
                            start=(c == 0), stop=(c == NCH - 1)),
                            reads=["wqkv", tHT], writes=[tPS[kk]],
                            tag=f"p2 {tdst}mm {sfx}c{c}")
                    p.op("scalar", lambda e, dst=dst, kk=kk: e.copy(dst[:, :], PS[kk][:, :]),
                         reads=[tPS[kk]], writes=[tdst], tag=f"p2 {tdst}ev {sfx}")
                    if sig is not None:
                        p.op("scalar", lambda e, kk=kk: e.activation(
                            EQ[:, :], PS[kk][:, :], AF.Exp),
                            reads=[tPS[kk]], writes=["eq"], tag=f"p2 {tdst}exp {sfx}")
                        p.op("vector", lambda e, sig=sig, kk=kk: e.scalar_tensor_tensor(
                            sig[:, :], PS[kk][:, :], 1.0, EQ[:, :], ALU.add, ALU.min),
                            reads=[tPS[kk], "eq"], writes=[tdst + "s"],
                            tag=f"p2 {tdst}sig {sfx}")
                for src, dst, tsrc, tdst in ((V, VT, "v", "vt"), (SK, SKT, "ks", "skt")):
                    for j in range(4):
                        p.op("tensor", lambda e, j=j, src=src: e.transpose(
                            PSB[:, j * 128:(j + 1) * 128],
                            src[:, j * 128:(j + 1) * 128], ident[:, :]),
                            reads=[tsrc, "ident"], writes=[tPSB],
                            tag=f"p2 tr{tdst} {sfx}j{j}")
                    p.op("vector", lambda e, dst=dst: e.tensor_copy(
                        dst[:, :], PSB[:, 0:512]),
                        reads=[tPSB], writes=[tdst], tag=f"p2 trev{tdst} {sfx}")
                for i in range(4):
                    ncol = (i + 1) * 128
                    p.op("tensor", lambda e, i=i, pr=pr: e.matmul(
                        PS[2][:, 0:129], SQ[:, i * 128:(i + 1) * 128],
                        MEMZB[pr][:, :], start=True, stop=True),
                        reads=["qs", ("memzb", pr)], writes=[tPS[2]],
                        tag=f"p2 am {sfx}i{i}")
                    p.op("tensor", lambda e, i=i, ncol=ncol: e.matmul(
                        PS[3][:, 0:ncol], Q[:, i * 128:(i + 1) * 128], K[:, 0:ncol],
                        start=True, stop=True),
                        reads=["q", "k"], writes=[tPS[3]], tag=f"p2 sc {sfx}i{i}")
                    p.op("vector", lambda e, i=i, ncol=ncol: e.tensor_add(
                        PS[3][:, ncol - 128:ncol], PS[3][:, ncol - 128:ncol],
                        CMASK[:, :]),
                        reads=[tPS[3], "cmask"], writes=[tPS[3]],
                        tag=f"p2 mask {sfx}i{i}")
                    p.op("vector", lambda e, i=i, ncol=ncol: e.tensor_reduce(
                        NMS[:, i:i + 1], PS[3][:, 0:ncol], AX.X, ALU.max,
                        negate=True),
                        reads=[tPS[3]], writes=["nms"], tag=f"p2 max {sfx}i{i}")
                    p.op("vector", lambda e, i=i: e.tensor_single_scalar(
                        NMS[:, 4 + i:5 + i], NMS[:, i:i + 1], 1.0 / SCALE_ATT,
                        ALU.mult),
                        reads=["nms"], writes=["nms"])
                    p.op("scalar", lambda e, i=i, ncol=ncol: e.activation(
                        PSM[:, 0:ncol], PS[3][:, 0:ncol], AF.Exp,
                        bias=NMS[:, 4 + i:5 + i], scale=1.0 / SCALE_ATT,
                        accum_out=DEN[:, i:i + 1]),
                        reads=[tPS[3], "nms"], writes=["psm", "den"],
                        tag=f"p2 exp {sfx}i{i}")
                    for j in range(i + 1):
                        p.op("tensor", lambda e, j=j: e.transpose(
                            PSB[:, j * 128:(j + 1) * 128],
                            PSM[:, j * 128:(j + 1) * 128], ident[:, :]),
                            reads=["psm", "ident"], writes=[tPSB],
                            tag=f"p2 trp {sfx}i{i}j{j}")
                    p.op("vector", lambda e, i=i: e.tensor_copy(
                        PT[:, 0:(i + 1) * 128], PSB[:, 0:(i + 1) * 128]),
                        reads=[tPSB], writes=["pt"], tag=f"p2 trpev {sfx}i{i}")
                    for j in range(i + 1):
                        p.op("tensor", lambda e, i=i, j=j: e.matmul(
                            PS[4][:, 0:128], PT[:, j * 128:(j + 1) * 128],
                            VT[:, j * 128:(j + 1) * 128],
                            start=(j == 0), stop=(j == i)),
                            reads=["pt", "vt"], writes=[tPS[4]],
                            tag=f"p2 ad {sfx}i{i}j{j}")
                    p.op("vector", lambda e, i=i: e.reciprocal(RZ[:, 0:1],
                                                               PS[2][:, 128:129]),
                         reads=[tPS[2]], writes=["rz"], tag=f"p2 rz {sfx}i{i}")
                    p.op("vector", lambda e, i=i: e.reciprocal(RZ[:, 1:2],
                                                               DEN[:, i:i + 1]),
                         reads=["den"], writes=["rz"])
                    p.op("vector", lambda e: e.tensor_scalar(
                        T1[:, :], PS[2][:, 0:128], RZ[:, 0:1], None, ALU.mult),
                        reads=[tPS[2], "rz"], writes=["t1x"], tag=f"p2 t1 {sfx}i{i}")
                    p.op("vector", lambda e: e.tensor_scalar(
                        T2[:, :], PS[4][:, 0:128], RZ[:, 1:2], None, ALU.mult),
                        reads=[tPS[4], "rz"], writes=["t2x"], tag=f"p2 t2 {sfx}i{i}")
                    p.op("vector", lambda e: e.tensor_sub(T1[:, :], T1[:, :], T2[:, :]),
                         reads=["t1x", "t2x"], writes=["t1x"])
                    p.op("vector", lambda e, hl=hl: e.tensor_mul(
                        T1[:, :], T1[:, :], BBC[:, hl * 128:(hl + 1) * 128]),
                        reads=["t1x", "bbc"], writes=["t1x"])
                    p.op("vector", lambda e: e.tensor_add(ATT[:, :], T1[:, :], T2[:, :]),
                         reads=["t1x", "t2x"], writes=["attx"], tag=f"p2 cmb {sfx}i{i}")
                    p.op("tensor", lambda e: e.transpose(
                        PSB[:, 512:640], ATT[:, :], ident[:, :]),
                        reads=["attx", "ident"], writes=[tPSB],
                        tag=f"p2 trat {sfx}i{i}")
                    p.op("vector", lambda e, i=i: e.tensor_copy(
                        ATTT[:, i * 128:(i + 1) * 128], PSB[:, 512:640]),
                        reads=[tPSB], writes=["atttx"], tag=f"p2 tratev {sfx}i{i}")
                for j in range(4):
                    p.op("tensor", lambda e, j=j, pr=pr: e.matmul(
                        PS[5][:, 0:129], SK[:, j * 128:(j + 1) * 128],
                        MEMZB[pr][:, :], start=True, stop=True),
                        reads=["ks", ("memzb", pr)], writes=[tPS[5]],
                        tag=f"p2 km {sfx}j{j}")
                    p.op("vector", lambda e: e.reciprocal(RZ[:, 2:3],
                                                          PS[5][:, 128:129]),
                         reads=[tPS[5]], writes=["rz"])
                    p.op("vector", lambda e, j=j: e.scalar_tensor_tensor(
                        DLT[:, j * 128:(j + 1) * 128], PS[5][:, 0:128],
                        RZ[:, 2:3], VT[:, j * 128:(j + 1) * 128],
                        ALU.mult, ALU.subtract),
                        reads=[tPS[5], "rz", "vt"], writes=["dlt"],
                        tag=f"p2 dlt {sfx}j{j}")
                for j in range(4):
                    p.op("tensor", lambda e, j=j: e.matmul(
                        PS[6][:, 0:128], SKT[:, j * 128:(j + 1) * 128],
                        DLT[:, j * 128:(j + 1) * 128],
                        start=(j == 0), stop=(j == 3)),
                        reads=["skt", "dlt"], writes=[tPS[6]],
                        tag=f"p2 upd {sfx}j{j}")
                p.op("vector", lambda e, pr=pr: e.tensor_sub(
                    MEMZ[pr][:, 0:128], MEMZ[pr][:, 0:128], PS[6][:, 0:128]),
                    reads=[tPS[6], ("memz", pr)], writes=[("memz", pr)],
                    tag=f"p2 memup {sfx}")
                p.op("vector", lambda e: e.tensor_reduce(
                    RZ[:, 3:4], SK[:, :], AX.X, ALU.add),
                    reads=["ks"], writes=["rz"], tag=f"p2 zsum {sfx}")
                p.op("vector", lambda e, pr=pr: e.tensor_add(
                    MEMZ[pr][:, 128:129], MEMZ[pr][:, 128:129], RZ[:, 3:4]),
                    reads=["rz", ("memz", pr)], writes=[("memz", pr)],
                    tag=f"p2 zup {sfx}")
                p.op("scalar", lambda e, pr=pr: e.copy(MEMZB[pr][:, :], MEMZ[pr][:, :]),
                     reads=[("memz", pr)], writes=[("memzb", pr)],
                     tag=f"p2 memzb {sfx}")
                if getattr(b, "fused", False):
                    eng = "sync" if pr % 2 == 0 else "scalar"
                    ap = bass.AP(att_s, (seg // 4) * NATQ
                                 + ((bb * 2 + hl) * 4 + seg % 4) * (128 * 512),
                                 [[512, 128], [1, 512]])
                    p.op(eng, lambda e, ap=ap: e.dma_start(
                        out=ap, in_=ATTT[:, :]),
                        reads=["atttx"], writes=[("attout", pr, seg)],
                        dma=("p2att", eng), tag=f"p2 attst {sfx}")
                else:
                    eng = "sync" if pr % 2 == 0 else "gpsimd"
                    p.op(eng, lambda e, hl=hl, bb=bb, base=base, seg=seg: e.dma_start(
                        out=att_s[hl, bb, :, seg * SEG:(seg + 1) * SEG],
                        in_=ATTT[:, :]),
                        reads=["atttx"], writes=[("attout", pr, seg)],
                        dma=("p2att", eng), tag=f"p2 attst {sfx}")
        if getattr(b, "att_qdone", None) is not None and seg % 4 == 3:
            b.att_qdone(seg // 4)


def build_phase2_v0():
    nc = bass.Bass("TRN2", target_bir_lowering=False, debug=False)
    h_full = nc.dram_tensor("h_full", [NCH, 128, B * S], BF16, kind="ExternalInput")
    wqkv = nc.dram_tensor("wqkv", [D, 768], F32, kind="ExternalInput")
    betas_s = nc.dram_tensor("betas_s", [HL, 128], F32, kind="ExternalInput")
    att_s = nc.dram_tensor("att_s", [HL, B, 128, S], BF16, kind="ExternalOutput")
    p = Prog(nc)
    b = Bld(nc, p)
    b.PS = [b.ps(f"p2_ps{k}", (128, 512), F32) for k in range(7)]
    b.tPS = [("PS", k) for k in range(7)]
    b.PSB = b.ps("p2_psb", (128, 1024), BF16)
    b.tPSB = ("PSB",)
    onr = b.sb("p2_ones_row", (1, 128), BF16)
    p.op("vector", lambda e: e.memset(onr[:, :], 1.0), writes=["p3or"])
    b.ones_row = onr
    emit_phase2(b, h_full, wqkv, betas_s, att_s)
    p.build()
    return nc



def emit_phase0(b, w_up, w_dn, wup_sl, wdn_sl, wo_sl, wmod_sl, cT,
                mod_part, wqup_sl, wqdn_sl, wob_sl):
    nc, p = b.nc, b.p
    PS, tPS = b.PS, b.tPS

    onesf = b.sb("p0_onesf", (128, 1), F32)
    p.op("vector", lambda e: e.memset(onesf[:, :], 1.0), writes=["p0of"])

    SC = b.sb("p0_sc", (128, 2 * NCH), BF16)
    SCF = b.sb("p0_scf", (128, 2 * NCH), F32)
    WM = [b.sb(f"p0_wm{i}", (128, 1536), BF16) for i in range(2)]
    WMF = [b.sb(f"p0_wmf{i}", (128, 1536), F32) for i in range(2)]
    MODE = b.sb("p0_mode", (2, 1536), F32)
    for c in range(NCH):
        p.op("sync", lambda e, c=c: e.dma_start(
            out=SCF[:, c * 2:(c + 1) * 2], in_=cT[c, :, :]),
            writes=["p0scf"], dma=("p0c",), tag=f"p0 cload {c}")
    p.op("scalar", lambda e: e.activation(SC[:, :], SCF[:, :], AF.Silu),
         reads=["p0scf"], writes=["p0sc"], tag="p0 silu")
    for c in range(NCH):
        par = c % 2
        p.op("sync", lambda e, c=c, par=par: e.dma_start(
            out=WMF[par][:, :], in_=wmod_sl[c * 128:(c + 1) * 128, :]),
            writes=[("p0wmf", par)], dma=("p0wm", par), tag=f"p0 wmload {c}")
        p.op("scalar", lambda e, par=par: e.copy(WM[par][:, :], WMF[par][:, :]),
             reads=[("p0wmf", par)], writes=[("p0wm", par)], tag=f"p0 wmcast {c}")
        for n3 in range(3):
            p.op("tensor", lambda e, c=c, n3=n3, par=par: e.matmul(
                PS[n3][0:2, :], SC[:, c * 2:(c + 1) * 2],
                WM[par][:, n3 * 512:(n3 + 1) * 512],
                start=(c == 0), stop=(c == NCH - 1)),
                reads=["p0sc", ("p0wm", par)], writes=[tPS[n3]],
                tag=f"p0 modmm c{c}n{n3}")
    for n3 in range(3):
        p.op("vector", lambda e, n3=n3: e.tensor_copy(
            MODE[0:2, n3 * 512:(n3 + 1) * 512], PS[n3][0:2, :]),
            reads=[tPS[n3]], writes=["p0mode"], tag=f"p0 modev {n3}")
    p.op("sync", lambda e: e.dma_start(out=mod_part[:, :], in_=MODE[0:2, :]),
         reads=["p0mode"], writes=["p0modout"], dma=("p0mo",), tag="p0 modst")

    WT_ = [b.sb(f"p0_wt{i}", (128, 4096), F32) for i in range(2)]
    ACC = b.sb("p0_acc", (128, 2), F32)
    SCL = b.sb("p0_scl", (128, 4), F32)
    red = b.sb("p0_red", (128, 1), F32)
    p.op("vector", lambda e: e.memset(ACC[:, :], 0.0), writes=["p0acc"])
    for wi, (wfull, nrow, ncol) in enumerate([(w_up, FF, D), (w_dn, D, FF)]):
        ntile = nrow // 128
        for t in range(ntile):
            par = t % 2
            p.op("sync", lambda e, t=t, par=par, wfull=wfull, ncol=ncol: e.dma_start(
                out=WT_[par][:, 0:ncol], in_=wfull[t * 128:(t + 1) * 128, :]),
                writes=[("p0wt", par)], dma=("p0wr", par), tag=f"p0 wstat {wi}t{t}")
            p.op("vector", lambda e, par=par, ncol=ncol: e.tensor_reduce(
                red[:, :], WT_[par][:, 0:ncol], AX.X, ALU.max if False else ALU.add,
                apply_absolute_value=True),
                reads=[("p0wt", par)], writes=["p0red"], tag=f"p0 wred {wi}t{t}")
            p.op("vector", lambda e, wi=wi: e.tensor_add(
                ACC[:, wi:wi + 1], ACC[:, wi:wi + 1], red[:, :]),
                reads=["p0red", "p0acc"], writes=["p0acc"], tag=f"p0 wacc {wi}t{t}")
    p.op("tensor", lambda e: e.matmul(PS[3][0:1, 0:2], onesf[:, :], ACC[:, :],
                                      start=True, stop=True),
         reads=["p0acc", "p0of"], writes=[tPS[3]], tag="p0 wsum")
    p.op("vector", lambda e: e.tensor_scalar(
        PS[3][0:1, 2:4], PS[3][0:1, 0:2], 1.0 / (FF * D), EPS_Q, ALU.mult, ALU.max),
        reads=[tPS[3]], writes=[tPS[3]], tag="p0 wscale")
    p.op("vector", lambda e: e.tensor_copy(SCL[0:1, 0:2], PS[3][0:1, 2:4]),
         reads=[tPS[3]], writes=["p0scl"], tag="p0 sclev")
    onesfr = b.sb("p0_onesfr", (1, 128), F32)
    p.op("vector", lambda e: e.memset(onesfr[:, :], 1.0), writes=["p0ofr"])
    p.op("tensor", lambda e: e.matmul(PS[4][:, 0:2], onesfr[:, :], SCL[0:1, 0:2],
                                      start=True, stop=True),
         reads=["p0scl", "p0ofr"], writes=[tPS[4]], tag="p0 sbc")
    p.op("vector", lambda e: e.tensor_copy(SCL[:, 0:2], PS[4][:, 0:2]),
         reads=[tPS[4]], writes=["p0scl"])
    p.op("vector", lambda e: e.reciprocal(SCL[:, 2:4], SCL[:, 0:2]),
         reads=["p0scl"], writes=["p0scl"])

    QT = b.sb("p0_qt", (128, 4096), F32)
    QA = b.sb("p0_qa", (128, 4096), F32)
    QO = b.sb("p0_qo", (128, 4096), BF16)
    for wi, (wsl, wqout, nrow, ncol) in enumerate(
            [(wup_sl, wqup_sl, FF // 8, D), (wdn_sl, wqdn_sl, D // 8, FF)]):
        ntile = nrow // 128
        for t in range(ntile):
            par = t % 2
            p.op("sync", lambda e, t=t, par=par, wsl=wsl, ncol=ncol: e.dma_start(
                out=WT_[par][:, 0:ncol], in_=wsl[t * 128:(t + 1) * 128, :]),
                writes=[("p0wt", par)], dma=("p0wq", par), tag=f"p0 wqload {wi}t{t}")
            p.op("vector", lambda e, par=par, wi=wi, ncol=ncol: e.tensor_scalar(
                QT[:, 0:ncol], WT_[par][:, 0:ncol], SCL[:, 2 + wi:3 + wi], None,
                ALU.mult),
                reads=[("p0wt", par), "p0scl"], writes=["p0qt"],
                tag=f"p0 wqs {wi}t{t}")
            p.op("vector", lambda e, ncol=ncol: e.tensor_single_scalar(
                QA[:, 0:ncol], QT[:, 0:ncol], 0.5, ALU.is_gt),
                reads=["p0qt"], writes=["p0qa"], tag=f"p0 wqa {wi}t{t}")
            p.op("vector", lambda e, ncol=ncol: e.tensor_single_scalar(
                QT[:, 0:ncol], QT[:, 0:ncol], -0.5, ALU.is_lt),
                reads=["p0qt"], writes=["p0qt"], tag=f"p0 wqb {wi}t{t}")
            p.op("vector", lambda e, ncol=ncol: e.tensor_sub(
                QA[:, 0:ncol], QA[:, 0:ncol], QT[:, 0:ncol]),
                reads=["p0qa", "p0qt"], writes=["p0qa"], tag=f"p0 wqc {wi}t{t}")
            p.op("vector", lambda e, wi=wi, ncol=ncol: e.tensor_scalar(
                QO[:, 0:ncol], QA[:, 0:ncol], SCL[:, wi:wi + 1], None, ALU.mult),
                reads=["p0qa", "p0scl"], writes=["p0qo"], tag=f"p0 wqd {wi}t{t}")
            p.op("gpsimd", lambda e, t=t, wqout=wqout, ncol=ncol: e.dma_start(
                out=wqout[t * 128:(t + 1) * 128, :], in_=QO[:, 0:ncol]),
                reads=["p0qo"], writes=[("p0qout", wi, t)], dma=("p0qst",),
                tag=f"p0 wqst {wi}t{t}")

    for t in range(2):
        par = t % 2
        p.op("sync", lambda e, t=t, par=par: e.dma_start(
            out=WT_[par][:, 0:D], in_=wo_sl[t * 128:(t + 1) * 128, :]),
            writes=[("p0wt", par)], dma=("p0wo", par), tag=f"p0 woload {t}")
        p.op("scalar", lambda e, par=par: e.copy(QO[:, 0:D], WT_[par][:, 0:D]),
             reads=[("p0wt", par)], writes=["p0qo"], tag=f"p0 wocast {t}")
        p.op("gpsimd", lambda e, t=t: e.dma_start(
            out=wob_sl[t * 128:(t + 1) * 128, :], in_=QO[:, 0:D]),
            reads=["p0qo"], writes=[("p0woout", t)], dma=("p0qst",),
            tag=f"p0 wost {t}")


def build_phase0_v0():
    nc = bass.Bass("TRN2", target_bir_lowering=False, debug=False)
    w_up = nc.dram_tensor("w_up", [FF, D], F32, kind="ExternalInput")
    w_dn = nc.dram_tensor("w_dn", [D, FF], F32, kind="ExternalInput")
    wup_sl = nc.dram_tensor("wup_sl", [FF // 8, D], F32, kind="ExternalInput")
    wdn_sl = nc.dram_tensor("wdn_sl", [D // 8, FF], F32, kind="ExternalInput")
    wo_sl = nc.dram_tensor("wo_sl", [D // 8, D], F32, kind="ExternalInput")
    wmod_sl = nc.dram_tensor("wmod_sl", [D, 1536], F32, kind="ExternalInput")
    cT = nc.dram_tensor("cT", [NCH, 128, 2], F32, kind="ExternalInput")
    mod_part = nc.dram_tensor("mod_part", [2, 1536], F32, kind="ExternalOutput")
    wqup_sl = nc.dram_tensor("wqup_sl", [FF // 8, D], BF16, kind="ExternalOutput")
    wqdn_sl = nc.dram_tensor("wqdn_sl", [D // 8, FF], BF16, kind="ExternalOutput")
    wob_sl = nc.dram_tensor("wob_sl", [D // 8, D], BF16, kind="ExternalOutput")
    p = Prog(nc)
    b = Bld(nc, p)
    b.PS = [b.ps(f"p0_ps{k}", (128, 512), F32) for k in range(5)]
    b.tPS = [("PS", k) for k in range(5)]
    emit_phase0(b, w_up, w_dn, wup_sl, wdn_sl, wo_sl, wmod_sl, cT,
                mod_part, wqup_sl, wqdn_sl, wob_sl)
    p.build()
    return nc



_RUN = [None]


def _runner(nc, in_maps):
    if _RUN[0] is not None:
        return _RUN[0](nc, in_maps)
    return [dict(r) for r in
            run_bass_kernel_spmd(nc, in_maps, core_ids=list(range(NCORE))).results]


def kernel(L0_attn_x, L0_attn_Wq, L0_attn_Wk, L0_attn_Wv, L0_attn_Wo,
           L0_attn_betas, L0_ffn_w_up, L0_ffn_w_down, L0_c, L0_wmod):
    x = np.ascontiguousarray(np.asarray(L0_attn_x, np.float32))
    Wq = np.asarray(L0_attn_Wq, np.float32)
    Wk = np.asarray(L0_attn_Wk, np.float32)
    Wv = np.asarray(L0_attn_Wv, np.float32)
    Wo = np.asarray(L0_attn_Wo, np.float32)
    betas = np.asarray(L0_attn_betas, np.float32)
    w_up = np.ascontiguousarray(np.asarray(L0_ffn_w_up, np.float32))
    w_dn = np.ascontiguousarray(np.asarray(L0_ffn_w_down, np.float32))
    cc = np.asarray(L0_c, np.float32)
    wmod = np.ascontiguousarray(np.asarray(L0_wmod, np.float32))

    cT = np.ascontiguousarray(cc.T).reshape(NCH, 128, 2)

    nc0 = build_phase0_v0()
    in0 = []
    for g in range(NCORE):
        in0.append({
            "w_up": w_up, "w_dn": w_dn,
            "wup_sl": np.ascontiguousarray(w_up[g * FF // 8:(g + 1) * FF // 8]),
            "wdn_sl": np.ascontiguousarray(w_dn[g * D // 8:(g + 1) * D // 8]),
            "wo_sl": np.ascontiguousarray(Wo[g * D // 8:(g + 1) * D // 8]),
            "wmod_sl": np.ascontiguousarray(wmod[:, g * 1536:(g + 1) * 1536]),
            "cT": cT,
        })
    r0 = _runner(nc0, in0)
    mod = np.concatenate([r0[g]["mod_part"] for g in range(NCORE)], axis=1)
    wq_up = np.concatenate([np.asarray(r0[g]["wqup_sl"]).view(BFNP)
                            for g in range(NCORE)], axis=0)
    wq_dn = np.concatenate([np.asarray(r0[g]["wqdn_sl"]).view(BFNP)
                            for g in range(NCORE)], axis=0)
    wo_b = np.concatenate([np.asarray(r0[g]["wob_sl"]).view(BFNP)
                           for g in range(NCORE)], axis=0)

    nc1 = build_phase1_v0()
    in1 = []
    for g in range(NCORE):
        bb, s0 = g // 4, (g % 4) * TC
        xT = np.ascontiguousarray(x[bb, s0:s0 + TC, :].T).reshape(NCH, 128, TC)
        in1.append({"xT": xT, "modrow": np.ascontiguousarray(mod[bb:bb + 1])})
    r1 = _runner(nc1, in1)
    h_full = np.concatenate(
        [np.asarray(r1[g]["h_out"]).view(BFNP) for g in range(NCORE)], axis=2)

    nc2 = build_phase2_v0()
    in2 = []
    for g in range(NCORE):
        wqkv = np.empty((D, 768), np.float32)
        for hl in range(HL):
            hh = HL * g + hl
            wqkv[:, 0 * 256 + hl * 128:0 * 256 + (hl + 1) * 128] = Wq[:, hh * 128:(hh + 1) * 128]
            wqkv[:, 1 * 256 + hl * 128:1 * 256 + (hl + 1) * 128] = Wk[:, hh * 128:(hh + 1) * 128]
            wqkv[:, 2 * 256 + hl * 128:2 * 256 + (hl + 1) * 128] = Wv[:, hh * 128:(hh + 1) * 128]
        bet = np.ascontiguousarray(betas[0, HL * g:HL * g + HL, 0, :])
        in2.append({"h_full": h_full, "wqkv": wqkv, "betas_s": bet})
    r2 = _runner(nc2, in2)

    wup_im = wq_up.reshape(FCH, 128, NCH, 128)
    wup_pcj = np.ascontiguousarray(
        wup_im.transpose(0, 3, 2, 1).reshape(FCH, 128, NCH * 128))
    wdnT = np.ascontiguousarray(wq_dn.T)
    nc3 = build_phase3_v0()
    in3 = []
    att_by_head = [np.asarray(r2[hh // HL]["att_s"]).view(BFNP)[hh % HL]
                   for hh in range(H)]
    for g in range(NCORE):
        bb, s0 = g // 4, (g % 4) * TC
        A = np.stack([a[bb] for a in att_by_head], axis=0)
        A5 = A.reshape(H, 128, NSEG, 32, 16)
        seg0 = s0 // SEG
        Ds = A5[:, :, seg0:seg0 + TC // SEG, :, :]
        attT = np.ascontiguousarray(
            Ds.transpose(4, 1, 2, 0, 3).reshape(NCH, 128, TC))
        in3.append({
            "x_rows": np.ascontiguousarray(x[bb, s0:s0 + TC, :]),
            "attT": attT, "wo_bf": wo_b, "wup_pcj": wup_pcj, "wdnT": wdnT,
            "modrow": np.ascontiguousarray(mod[bb:bb + 1]),
        })
    r3 = _runner(nc3, in3)
    out = np.empty((B, S, D), np.float32)
    for g in range(NCORE):
        bb, s0 = g // 4, (g % 4) * TC
        out[bb, s0:s0 + TC, :] = r3[g]["out_rows"]
    return out



def emit_phase0_fused(b, w_up, w_dn, wup_sl, wdn_sl, wo_sl, wmod_sl, cT,
                      mode_l, modag, wupsrc, wdnsrc, wosrc):
    nc, p = b.nc, b.p
    PS, tPS, PSB, tPSB = b.PS, b.tPS, b.PSB, b.tPSB
    ident = b.need_ident()

    onesf = b.sb("p0_onesf", (128, 1), F32)
    p.op("vector", lambda e: e.memset(onesf[:, :], 1.0), writes=["p0of"])

    SC = b.sb("p0_sc", (128, 2 * NCH), BF16)
    SCF = b.sb("p0_scf", (128, 2 * NCH), F32)
    WT_ = b.sb("p0_wt", (128, 8192), F32)
    WM = [b.sb(f"p0_wm{i}", (128, 1536), BF16) for i in range(2)]
    MODE = b.sb("p0_mode", (2, 1536), F32)
    for c in range(NCH):
        p.op("sync", lambda e, c=c: e.dma_start(
            out=SCF[:, c * 2:(c + 1) * 2], in_=cT[c, :, :]),
            writes=["p0scf"], dma=("p0c",), tag=f"p0 cload {c}")
    p.op("scalar", lambda e: e.activation(SC[:, :], SCF[:, :], AF.Silu),
         reads=["p0scf"], writes=["p0sc"], tag="p0 silu")
    for c in range(NCH):
        par = c % 2
        p.op("sync", lambda e, c=c, par=par: e.dma_start(
            out=WT_[:, par * 4096:par * 4096 + 1536],
            in_=wmod_sl[c * 128:(c + 1) * 128, :]),
            writes=[("p0wt", par)], dma=("p0wm", par), tag=f"p0 wmload {c}")
        p.op("scalar", lambda e, par=par: e.copy(
            WM[par][:, :], WT_[:, par * 4096:par * 4096 + 1536]),
            reads=[("p0wt", par)], writes=[("p0wm", par)], tag=f"p0 wmcast {c}")
        for n3 in range(3):
            p.op("tensor", lambda e, c=c, n3=n3, par=par: e.matmul(
                PS[n3][0:2, :], SC[:, c * 2:(c + 1) * 2],
                WM[par][:, n3 * 512:(n3 + 1) * 512],
                start=(c == 0), stop=(c == NCH - 1)),
                reads=["p0sc", ("p0wm", par)], writes=[tPS[n3]],
                tag=f"p0 modmm c{c}n{n3}")
    for n3 in range(3):
        p.op("vector", lambda e, n3=n3: e.tensor_copy(
            MODE[0:2, n3 * 512:(n3 + 1) * 512], PS[n3][0:2, :]),
            reads=[tPS[n3]], writes=["p0mode"], tag=f"p0 modev {n3}")
    p.op("sync", lambda e: e.dma_start(out=mode_l[:, :], in_=MODE[0:2, :]),
         reads=["p0mode"], writes=["p0modl"], dma=("p0ml",), tag="p0 modst")
    p.op("gpsimd", lambda e: e.collective_compute(
        "AllGather", ALU.bypass, RG8, [mode_l[:, :]], [modag[:, :, :]]),
        reads=["p0modl"], writes=["modag"], mark=True, tag="p0 modag")

    ACC = b.sb("p0_acc", (128, 2), F32)
    SCL = b.sb("p0_scl", (128, 4), F32)
    red = b.sb("p0_red", (128, 1), F32)
    p.op("vector", lambda e: e.memset(ACC[:, :], 0.0), writes=["p0acc"])
    for wi, (wfull, nrow, ncol) in enumerate([(w_up, FF, D), (w_dn, D, FF)]):
        ntile = nrow // 128
        for t in range(ntile):
            par = t % 2
            p.op("sync", lambda e, t=t, par=par, wfull=wfull, ncol=ncol: e.dma_start(
                out=WT_[:, par * 4096:par * 4096 + ncol],
                in_=wfull[t * 128:(t + 1) * 128, :]),
                writes=[("p0wt", par)], dma=("p0wr", par), tag=f"p0 wstat {wi}t{t}")
            p.op("vector", lambda e, par=par, ncol=ncol: e.tensor_reduce(
                red[:, :], WT_[:, par * 4096:par * 4096 + ncol], AX.X, ALU.add,
                apply_absolute_value=True),
                reads=[("p0wt", par)], writes=["p0red"], tag=f"p0 wred {wi}t{t}")
            p.op("vector", lambda e, wi=wi: e.tensor_add(
                ACC[:, wi:wi + 1], ACC[:, wi:wi + 1], red[:, :]),
                reads=["p0red", "p0acc"], writes=["p0acc"], tag=f"p0 wacc {wi}t{t}")
    p.op("tensor", lambda e: e.matmul(PS[3][0:1, 0:2], onesf[:, :], ACC[:, :],
                                      start=True, stop=True),
         reads=["p0acc", "p0of"], writes=[tPS[3]], tag="p0 wsum")
    p.op("vector", lambda e: e.tensor_scalar(
        PS[3][0:1, 2:4], PS[3][0:1, 0:2], 1.0 / (FF * D), EPS_Q, ALU.mult, ALU.max),
        reads=[tPS[3]], writes=[tPS[3]], tag="p0 wscale")
    p.op("vector", lambda e: e.tensor_copy(SCL[0:1, 0:2], PS[3][0:1, 2:4]),
         reads=[tPS[3]], writes=["p0scl"], tag="p0 sclev")
    onesfr = b.sb("p0_onesfr", (1, 128), F32)
    p.op("vector", lambda e: e.memset(onesfr[:, :], 1.0), writes=["p0ofr"])
    p.op("tensor", lambda e: e.matmul(PS[4][:, 0:2], onesfr[:, :], SCL[0:1, 0:2],
                                      start=True, stop=True),
         reads=["p0scl", "p0ofr"], writes=[tPS[4]], tag="p0 sbc")
    p.op("vector", lambda e: e.tensor_copy(SCL[:, 0:2], PS[4][:, 0:2]),
         reads=[tPS[4]], writes=["p0scl"])
    p.op("vector", lambda e: e.reciprocal(SCL[:, 2:4], SCL[:, 0:2]),
         reads=["p0scl"], writes=["p0scl"])

    QT = b.sb("p0_qt", (128, 4096), F32)
    QA = b.sb("p0_qa", (128, 4096), F32)
    QO = [b.sb(f"p0_qo{i}", (128, 4096), BF16) for i in range(2)]
    IMG = [b.sb(f"p0_img{i}", (128, 2048), BF16) for i in range(2)]

    def quant_tile(wi, t, par, wsl, ncol, tag):
        p.op("sync", lambda e, t=t, par=par, wsl=wsl, ncol=ncol: e.dma_start(
            out=WT_[:, par * 4096:par * 4096 + ncol],
            in_=wsl[t * 128:(t + 1) * 128, :]),
            writes=[("p0wt", par)], dma=("p0wq", par), tag=f"{tag} load")
        p.op("vector", lambda e, par=par, wi=wi, ncol=ncol: e.tensor_scalar(
            QT[:, 0:ncol], WT_[:, par * 4096:par * 4096 + ncol],
            SCL[:, 2 + wi:3 + wi], None, ALU.mult),
            reads=[("p0wt", par), "p0scl"], writes=["p0qt"], tag=f"{tag} s")
        p.op("vector", lambda e, ncol=ncol: e.tensor_single_scalar(
            QA[:, 0:ncol], QT[:, 0:ncol], 0.5, ALU.is_gt),
            reads=["p0qt"], writes=["p0qa"], tag=f"{tag} a")
        p.op("vector", lambda e, ncol=ncol: e.tensor_single_scalar(
            QT[:, 0:ncol], QT[:, 0:ncol], -0.5, ALU.is_lt),
            reads=["p0qt"], writes=["p0qt"], tag=f"{tag} b")
        p.op("vector", lambda e, ncol=ncol: e.tensor_sub(
            QA[:, 0:ncol], QA[:, 0:ncol], QT[:, 0:ncol]),
            reads=["p0qa", "p0qt"], writes=["p0qa"], tag=f"{tag} c")
        p.op("vector", lambda e, wi=wi, par=par, ncol=ncol: e.tensor_scalar(
            QO[par][:, 0:ncol], QA[:, 0:ncol], SCL[:, wi:wi + 1], None, ALU.mult),
            reads=["p0qa", "p0scl"], writes=[("p0qo", par)], tag=f"{tag} d")

    for t in range(4):
        par = t % 2
        quant_tile(0, t, par, wup_sl, D, f"p0 up t{t}")
        for c in range(NCH):
            p.op("tensor", lambda e, c=c, par=par: e.transpose(
                PSB[:, (c % 4) * 128:(c % 4 + 1) * 128],
                QO[par][:, c * 128:(c + 1) * 128], ident[:, :]),
                reads=[("p0qo", par), "ident"], writes=[tPSB],
                tag=f"p0 uptr t{t}c{c}")
            p.op("vector", lambda e, c=c, par=par: e.tensor_copy(
                IMG[par][:, c * 128:(c + 1) * 128],
                PSB[:, (c % 4) * 128:(c % 4 + 1) * 128]),
                reads=[tPSB], writes=[("p0img", par)], tag=f"p0 upev t{t}c{c}")
        p.op("sync", lambda e, t=t, par=par: e.dma_start(
            out=wupsrc[t, :, :], in_=IMG[par][:, :]),
            reads=[("p0img", par)], writes=[("wupsrc", t)], dma=("p0st", par),
            tag=f"p0 upst t{t}")

    for t in range(2):
        quant_tile(1, t, t, wdn_sl, FF, f"p0 dn t{t}")
    for fc in range(FCH):
        par = fc % 2
        for t in range(2):
            p.op("tensor", lambda e, fc=fc, t=t, par=par: e.transpose(
                PSB[:, (t + 2 * (fc % 2)) * 128:(t + 2 * (fc % 2) + 1) * 128],
                QO[t][:, fc * 128:(fc + 1) * 128], ident[:, :]),
                reads=[("p0qo", t), "ident"], writes=[tPSB],
                tag=f"p0 dntr f{fc}t{t}")
            p.op("vector", lambda e, fc=fc, t=t, par=par: e.tensor_copy(
                IMG[par][:, t * 128:(t + 1) * 128],
                PSB[:, (t + 2 * (fc % 2)) * 128:(t + 2 * (fc % 2) + 1) * 128]),
                reads=[tPSB], writes=[("p0img", par)], tag=f"p0 dnev f{fc}t{t}")
        p.op("sync", lambda e, fc=fc, par=par: e.dma_start(
            out=wdnsrc[fc * 128:(fc + 1) * 128, :], in_=IMG[par][:, 0:256]),
            reads=[("p0img", par)], writes=[("wdnsrc", fc)], dma=("p0st", par),
            tag=f"p0 dnst f{fc}")

    for t in range(2):
        par = t % 2
        p.op("sync", lambda e, t=t, par=par: e.dma_start(
            out=WT_[:, par * 4096:par * 4096 + D],
            in_=wo_sl[t * 128:(t + 1) * 128, :]),
            writes=[("p0wt", par)], dma=("p0wo", par), tag=f"p0 woload {t}")
        p.op("scalar", lambda e, par=par: e.copy(
            IMG[par][:, :], WT_[:, par * 4096:par * 4096 + D]),
            reads=[("p0wt", par)], writes=[("p0img", par)], tag=f"p0 wocast {t}")
        p.op("sync", lambda e, t=t, par=par: e.dma_start(
            out=wosrc[t, :, :], in_=IMG[par][:, :]),
            reads=[("p0img", par)], writes=[("wosrc", t)], dma=("p0st", par),
            tag=f"p0 wost {t}")


def emit_att_ag_q(b, atq_src, atg, q):
    p = b.p
    toks = [("attout", pr, seg)
            for pr in range(4) for seg in range(4 * q, 4 * q + 4)]
    p.op("gpsimd", lambda e, q=q: e.collective_compute(
        "AllGather", ALU.bypass, RG8, [atq_src[q, :]], [atg[q, :, :]]),
        reads=toks, writes=[("atag", q)], mark=True, tag=f"p2 atag q{q}")


def build_fused():
    nc = bass.Bass("TRN2", target_bir_lowering=False, debug=False,
                   dynamic_dma_scratch_size=8192)
    xT = nc.dram_tensor("xT", [NCH, 128, TC], F32, kind="ExternalInput")
    x_rows = nc.dram_tensor("x_rows", [TC, D], F32, kind="ExternalInput")
    wqkv = nc.dram_tensor("wqkv", [D, 768], F32, kind="ExternalInput")
    betas_s = nc.dram_tensor("betas_s", [HL, 128], F32, kind="ExternalInput")
    w_up = nc.dram_tensor("w_up", [FF, D], F32, kind="ExternalInput")
    w_dn = nc.dram_tensor("w_dn", [D, FF], F32, kind="ExternalInput")
    wup_sl = nc.dram_tensor("wup_sl", [FF // 8, D], F32, kind="ExternalInput")
    wdn_sl = nc.dram_tensor("wdn_sl", [D // 8, FF], F32, kind="ExternalInput")
    wo_sl = nc.dram_tensor("wo_sl", [D // 8, D], F32, kind="ExternalInput")
    wmod_sl = nc.dram_tensor("wmod_sl", [D, 1536], F32, kind="ExternalInput")
    cT = nc.dram_tensor("cT", [NCH, 128, 2], F32, kind="ExternalInput")
    bflag = nc.dram_tensor("bflag", [1, 1], F32, kind="ExternalInput")
    out_rows = nc.dram_tensor("out_rows", [TC, D], F32, kind="ExternalOutput")
    onehot8 = nc.dram_tensor("onehot8", [1, 8], F32, kind="ExternalInput")
    mode_l = nc.dram_tensor("mode_l", [2, 1536], F32, kind="Internal")
    modag = nc.dram_tensor("modag", [8, 2, 1536], F32, kind="Internal")
    hsrc = nc.dram_tensor("hsrc", [4, 128, NCH * WT], BF16, kind="Internal")
    hga = nc.dram_tensor("hga", [4, 8, 128, NCH * WT], BF16, kind="Internal")
    atq_src = nc.dram_tensor("atq_src", [4, NATQ], BF16, kind="Internal")
    atg = nc.dram_tensor("atg", [4, 8, NATQ], BF16, kind="Internal")
    wupsrc = nc.dram_tensor("wupsrc", [4, 128, D], BF16, kind="Internal")
    wupga = nc.dram_tensor("wupga", [8, 4, 128, D], BF16, kind="Internal")
    wdnsrc = nc.dram_tensor("wdnsrc", [FF, 256], BF16, kind="Internal")
    wdnga = nc.dram_tensor("wdnga", [8, FF, 256], BF16, kind="Internal")
    wosrc = nc.dram_tensor("wosrc", [2, 128, D], BF16, kind="Internal")
    woga = nc.dram_tensor("woga", [8, 2, 128, D], BF16, kind="Internal")

    p = Prog(nc)
    b = Bld(nc, p)
    b.fused = True
    b.modag = modag

    A_f32a = nc.alloc_sbuf_tensor("A_f32a", [128, 8192], F32)
    A_bf1 = nc.alloc_sbuf_tensor("A_bf1", [128, 16384], BF16)
    A_bf2 = nc.alloc_sbuf_tensor("A_bf2", [128, 8192], BF16)
    A_bf3 = nc.alloc_sbuf_tensor("A_bf3", [128, 8192], BF16)
    A_bf4 = nc.alloc_sbuf_tensor("A_bf4", [128, 8192], BF16)
    bf1_f32 = A_bf1[:, :].bitcast(F32)

    SCR4K = nc.alloc_sbuf_tensor("SCR4K", [128, 1024], F32)
    T1S = nc.alloc_sbuf_tensor("T1S", [128, 512], F32)
    T2S = nc.alloc_sbuf_tensor("T2S", [128, 512], F32)
    ROWS = nc.alloc_sbuf_tensor("ROWS", [1, 2048], F32)
    b.share = {
        "p3_atw": A_bf4,
        "p1_mrbc": SCR4K,
        "p3_qu": SCR4K[:, 0:512], "p3_qzi": SCR4K[:, 512:1024].bitcast(I32),
        "p1_t1": T1S, "p3_qzf": T1S,
        "p1_t2": T2S, "p3_qc": T2S,
        "p1_stats": ROWS[0:1, 0:1536],
        "p3_gsc": ROWS[0:1, 0:2048],
        "p0_wt": A_f32a,
        "p0_qt": bf1_f32[:, 0:4096], "p0_qa": bf1_f32[:, 4096:8192],
        "p0_qo0": A_bf4[:, 0:4096], "p0_qo1": A_bf4[:, 4096:8192],
        "p0_img0": A_bf2[:, 0:2048], "p0_img1": A_bf2[:, 2048:4096],
        "p0_wm0": A_bf3[:, 0:1536], "p0_wm1": A_bf3[:, 1536:3072],
        "p1_xstg0": A_f32a, "p1_xstg1": bf1_f32,
        "p1_xb0": A_bf2, "p1_xb1": A_bf3, "p1_xsq": A_bf4,
        "p1_hwin0": A_bf1[:, 0:8192], "p1_hwin1": A_bf1[:, 8192:16384],
        "p2_wstg": A_f32a[:, 0:768],
        "p2_wqkv": A_bf1[:, 0:NCH * 768],
        "p2_ht0": A_bf2, "p2_ht1": A_bf3,
        "p2_q": A_bf4[:, 0:512], "p2_k": A_bf4[:, 512:1024],
        "p2_v": A_bf4[:, 1024:1536], "p2_eq": A_bf4[:, 1536:2048],
        "p2_sq": A_bf4[:, 2048:2560], "p2_sk": A_bf4[:, 2560:3072],
        "p2_vt": A_bf4[:, 3072:3584], "p2_skt": A_bf4[:, 3584:4096],
        "p2_dlt": A_bf4[:, 4096:4608], "p2_psm": A_bf4[:, 4608:5120],
        "p2_pt": A_bf4[:, 5120:7168], "p2_attt": A_bf4[:, 7168:7680],
        "p3_x2t": A_f32a, "p3_fa": A_bf1, "p3_fb": A_bf2, "p3_gb": A_bf3,
        "p3_xq": A_bf4,
    }

    b.PS = [b.ps(f"ps{k}", (128, 512), F32) for k in range(7)]
    b.tPS = [("PS", k) for k in range(7)]
    b.PSB = b.ps("psb", (128, 1024), BF16)
    b.tPSB = ("PSB",)
    onr = b.sb("ones_row_b", (1, 128), BF16)
    p.op("vector", lambda e: e.memset(onr[:, :], 1.0), writes=["p3or"])
    b.ones_row = onr

    FB1 = b.sb("fb1", (1, 1), F32)
    FLAG128 = b.sb("flag128", (128, 1), F32)
    onesfr0 = b.sb("onesfr0", (1, 128), F32)
    p.op("vector", lambda e: e.memset(onesfr0[:, :], 1.0), writes=["of0"])
    p.op("sync", lambda e: e.dma_start(out=FB1[:, :], in_=bflag[:, :]),
         writes=["fb1"], dma=("bfl",))
    p.op("tensor", lambda e: e.matmul(b.PS[5][:, 0:1], onesfr0[:, :], FB1[0:1, :],
                                      start=True, stop=True),
         reads=["fb1", "of0"], writes=[b.tPS[5]])
    p.op("vector", lambda e: e.tensor_copy(FLAG128[:, :], b.PS[5][:, 0:1]),
         reads=[b.tPS[5]], writes=["flag128"])
    b.flag128 = FLAG128
    b.tflag = "flag128"
    b.flag1 = FB1
    b.tflag1 = "fb1"
    b.gsc3 = ROWS[0:1, 0:2048]
    b.gsc3_t = ROWS

    OH = b.sb("oh8", (1, 8), F32)
    GF = b.sb("gf8", (128, 8), F32)
    p.op("sync", lambda e: e.dma_start(out=OH[:, :], in_=onehot8[:, :]),
         writes=["oh8"], dma=("oh8",))
    p.op("tensor", lambda e: e.matmul(b.PS[6][:, 0:8], onesfr0[:, :], OH[0:1, :],
                                      start=True, stop=True),
         reads=["oh8", "of0"], writes=[b.tPS[6]])
    p.op("vector", lambda e: e.tensor_copy(GF[:, :], b.PS[6][:, 0:8]),
         reads=[b.tPS[6]], writes=["gf"])
    b.GF = GF

    emit_phase0_fused(b, w_up, w_dn, wup_sl, wdn_sl, wo_sl, wmod_sl, cT,
                      mode_l, modag, wupsrc, wdnsrc, wosrc)
    p.fence()

    def h_store(w, HW, tHW):
        p.op("sync", lambda e, w=w, HW=HW: e.dma_start(
            out=hsrc[w, :, :], in_=HW[:, :]),
            reads=[tHW], writes=[("hsrc", w)], dma=("p1hs", w % 2),
            tag=f"p1 hst w{w}")
        p.op("gpsimd", lambda e, w=w: e.collective_compute(
            "AllGather", ALU.bypass, RG8, [hsrc[w, :, :]], [hga[w, :, :, :]]),
            reads=[("hsrc", w)], writes=[("hag", w)], mark=True, tag=f"p1 hag w{w}")

    emit_phase1(b, xT, None, h_store)
    p.fence()

    p.op("gpsimd", lambda e: e.collective_compute(
        "AllGather", ALU.bypass, RG8, [wupsrc[:, :, :]], [wupga[:, :, :, :]]),
        reads=[("wupsrc", t) for t in range(4)], writes=["wupag"], mark=True, tag="ag wup")
    p.op("gpsimd", lambda e: e.collective_compute(
        "AllGather", ALU.bypass, RG8, [wdnsrc[:, :]], [wdnga[:, :, :]]),
        reads=[("wdnsrc", fc) for fc in range(FCH)], writes=["wdnag"],
        mark=True, tag="ag wdn")
    p.op("gpsimd", lambda e: e.collective_compute(
        "AllGather", ALU.bypass, RG8, [wosrc[:, :, :]], [woga[:, :, :, :]]),
        reads=[("wosrc", t) for t in range(2)], writes=["woag"], mark=True, tag="ag wo")

    b.att_qdone = lambda q: emit_att_ag_q(b, atq_src, atg, q)
    emit_phase2(b, hga, wqkv, betas_s, atq_src)
    b.att_qdone = None
    p.fence()

    emit_phase3(b, x_rows, atg, woga, wupga, wdnga, None, out_rows)
    p.build()
    return nc


def kernel_fused(L0_attn_x, L0_attn_Wq, L0_attn_Wk, L0_attn_Wv, L0_attn_Wo,
                 L0_attn_betas, L0_ffn_w_up, L0_ffn_w_down, L0_c, L0_wmod):
    x = np.ascontiguousarray(np.asarray(L0_attn_x, np.float32))
    Wq = np.asarray(L0_attn_Wq, np.float32)
    Wk = np.asarray(L0_attn_Wk, np.float32)
    Wv = np.asarray(L0_attn_Wv, np.float32)
    Wo = np.ascontiguousarray(np.asarray(L0_attn_Wo, np.float32))
    betas = np.asarray(L0_attn_betas, np.float32)
    w_up = np.ascontiguousarray(np.asarray(L0_ffn_w_up, np.float32))
    w_dn = np.ascontiguousarray(np.asarray(L0_ffn_w_down, np.float32))
    cc = np.asarray(L0_c, np.float32)
    wmod = np.ascontiguousarray(np.asarray(L0_wmod, np.float32))
    cT = np.ascontiguousarray(cc.T).reshape(NCH, 128, 2)
    wmod_v = wmod.reshape(D, 6, 8, 256)
    nc = build_fused()
    in_maps = []
    for g in range(NCORE):
        bb, s0 = g // 4, (g % 4) * TC
        wqkv_g = np.empty((D, 768), np.float32)
        for hl in range(HL):
            hh = HL * g + hl
            wqkv_g[:, 0 * 256 + hl * 128:0 * 256 + (hl + 1) * 128] = Wq[:, hh * 128:(hh + 1) * 128]
            wqkv_g[:, 1 * 256 + hl * 128:1 * 256 + (hl + 1) * 128] = Wk[:, hh * 128:(hh + 1) * 128]
            wqkv_g[:, 2 * 256 + hl * 128:2 * 256 + (hl + 1) * 128] = Wv[:, hh * 128:(hh + 1) * 128]
        in_maps.append({
            "xT": np.ascontiguousarray(x[bb, s0:s0 + TC, :].T).reshape(NCH, 128, TC),
            "x_rows": np.ascontiguousarray(x[bb, s0:s0 + TC, :]),
            "wqkv": wqkv_g,
            "betas_s": np.ascontiguousarray(betas[0, HL * g:HL * g + HL, 0, :]),
            "w_up": w_up, "w_dn": w_dn,
            "wup_sl": np.ascontiguousarray(w_up[g * FF // 8:(g + 1) * FF // 8]),
            "wdn_sl": np.ascontiguousarray(w_dn[g * D // 8:(g + 1) * D // 8]),
            "wo_sl": np.ascontiguousarray(Wo[g * D // 8:(g + 1) * D // 8]),
            "wmod_sl": np.ascontiguousarray(wmod_v[:, :, g, :].reshape(D, 1536)),
            "cT": cT,
            "bflag": np.array([[float(bb)]], np.float32),
            "onehot8": np.eye(8, dtype=np.float32)[g:g + 1],
        })
    res = _runner(nc, in_maps)
    out = np.empty((B, S, D), np.float32)
    for g in range(NCORE):
        bb, s0 = g // 4, (g % 4) * TC
        out[bb, s0:s0 + TC, :] = res[g]["out_rows"]
    return out


_kernel_v0 = kernel
kernel = kernel_fused


# version 04
# speedup vs baseline: 1.0167x; 1.0167x over previous
import math
import numpy as np
import ml_dtypes
import concourse.bass as bass
import concourse.mybir as mybir
from concourse.bass_utils import run_bass_kernel_spmd

F32 = mybir.dt.float32
BF16 = mybir.dt.bfloat16
I32 = mybir.dt.int32
AF = mybir.ActivationFunctionType
ALU = mybir.AluOpType
AX = mybir.AxisListType
BFNP = ml_dtypes.bfloat16

B, S, D = 2, 8192, 2048
H, DK, DV, SEG, NSEG = 16, 128, 128, 512, 16
NCH, FF, FCH = 16, 4096, 32
NCORE = 8
TC = S * B // NCORE
HL = H // NCORE
NW = 4
WT = TC // NW
EPS_LN = 1e-5
EPS_Q = 1e-5
QB = 127.0
C1 = 1.0 / math.sqrt(D)
C2 = 1.0 / math.sqrt(FF)
RG8 = [[0, 1, 2, 3, 4, 5, 6, 7]]
NATQ = 2 * 2 * 4 * 128 * 512
SCALE_ATT = float(DK ** 0.5)

ENGS = ("sync", "gpsimd", "scalar", "vector", "tensor")
import os as _os
FORCE_SYNC_DMA = _os.environ.get("MK_FORCE_SYNC_DMA", "0") == "1"


def _hadamard(n):
    h = np.array([[1.0]])
    while h.shape[0] < n:
        h = np.block([[h, h], [h, -h]])
    return h


class Prog:

    _fv_ok = frozenset({
        "b", "p", "nc", "e", "xT", "modrow", "h_out", "x_rows", "attT",
        "wo_bf", "wup_pcj", "wdnT", "out_rows", "ident", "H128", "ones_col",
        "ones_row", "catile", "satile", "ca_ap", "sa_ap", "xstg", "xb",
        "xsq", "hwin", "stat_s", "mb", "rb", "mrbc", "t1", "t2", "ps_st",
        "ps_bc", "PS", "PSB", "tPS", "tPSB", "ATW", "XR", "X2T", "H2TOK",
        "H2C", "FA", "FB", "GB", "XQ", "WOS", "WUP", "WDN", "QU", "QZI",
        "QZF", "QC", "QM", "QSW", "QS", "QI", "ST", "OT", "GABC", "GBBC",
        "CBBC", "SBBC", "tGA", "tGB", "tCB", "tSB", "gsc", "row", "rowb",
        "out", "tok", "name", "modoff", "t_eps", "t_half", "wsl", "h_store",
        "gbrow", "gbrow_b", "GBBC0", "width", "src", "tsrc", "dst", "tdst",
        "dst_slices", "cnorm", "tag", "gat", "cbt", "sbt", "h128_dr", "ga_transform", "quant", "tf_ga",
        "tf_cb1", "tf_id", "transform", "XS", "XB", "HW", "nch", "wq_", "HT", "onr", "WSTG", "WQKV", "BROW", "BROWF", "BBC", "MBC", "CMASK", "MEMZ", "MEMZB", "HTB", "Q", "K", "V", "EQ", "SQ", "SK", "VT", "SKT", "DLT", "PT", "PSM", "DEN", "NMS", "RZ", "T1", "T2", "ATT", "ATTT", "mask_dr", "h_full", "wqkv", "betas_s", "att_s", "wb_parts", "wmod_sl", "abs_parts", "SC", "SCF", "WM", "WMF", "MODE", "WT_", "ACC", "SCL", "red", "onesf", "onesfr", "QT", "QA", "QO", "w_up", "w_dn", "wup_sl", "wdn_sl", "wo_sl", "cT", "mod_part", "wqup_sl", "wqdn_sl", "wob_sl", "onesfr0", "FB1", "FLAG128", "IMG", "quant_tile", "rs_mod", "rs_wt", "rs_h", "rs_att", "modg", "wupg", "wdng", "wog", "hg", "attall_loc", "attg2", "sel2", "xspar", "bflag", "store_toks", "modg_v",
        "SELT", "CAND", "hsrc", "hga", "mode_l", "modag", "atq_src", "atg",
        "wupsrc", "wupga", "wdnsrc", "wdnga", "wosrc", "woga", "OH", "GF",
        "onehot8",
    })

    def __init__(self, nc):
        self.nc = nc
        self.ops = []
        self.last_writer = {}
        self.readers = {}

    def fence(self):
        lasts = {}
        dlast = {}
        for i, rec in enumerate(self.ops):
            lasts[rec["engine"]] = i
            if rec["dma"] is not None:
                dlast[rec["dma"]] = i
        deps = set(lasts.values()) | set(dlast.values())
        fids = []
        for e in ENGS:
            i = len(self.ops)
            self.ops.append(dict(engine=e, fn=lambda en: en.nop(), dma=None,
                                 deps=set(deps), marked=False, tag="fence",
                                 extra_waits=(), bcast=False))
            fids.append(i)
        self.last_writer = {t: fids[0] for t in self.last_writer}
        for t in self.readers:
            self.readers[t] = list(fids)

    def op(self, engine, fn, reads=(), writes=(), dma=None, tag="",
           extra_waits=(), bcast=False, mark=False):
        fv = fn.__code__.co_freevars
        bad = [v for v in fv if v not in self._fv_ok]
        if bad:
            raise RuntimeError(f"op fn has late-bound free vars {bad} (tag={tag})")
        if dma is not None and FORCE_SYNC_DMA and engine == "gpsimd":
            engine = "sync"
        i = len(self.ops)
        deps = set()
        for t in reads:
            j = self.last_writer.get(t)
            if j is not None:
                deps.add(j)
        for t in writes:
            j = self.last_writer.get(t)
            if j is not None:
                deps.add(j)
            for r in self.readers.get(t, ()):
                deps.add(r)
        self.ops.append(dict(engine=engine, fn=fn, dma=dma, deps=deps,
                             marked=(dma is not None) or mark, tag=tag,
                             extra_waits=tuple(extra_waits), bcast=bcast))
        for t in reads:
            self.readers.setdefault(t, []).append(i)
        for t in writes:
            self.last_writer[t] = i
            self.readers[t] = []
        return i

    def build(self):
        nc = self.nc
        ops = self.ops
        for rec in ops:
            for j in rec["deps"]:
                o = ops[j]
                o["marked"] = True
        dkeys = []
        seen = set()
        for rec in ops:
            if rec["dma"] is not None and rec["dma"] not in seen:
                seen.add(rec["dma"])
                dkeys.append(rec["dma"])
        self._semctx = []

        def mksem(name):
            cm = nc.semaphore(name)
            h = cm.__enter__()
            self._semctx.append(cm)
            return h

        esem = {e: mksem(f"pe_{e}") for e in ENGS}
        dsem = {k: mksem(f"pd_{i}") for i, k in enumerate(dkeys)}
        ecnt = {e: 0 for e in ENGS}
        dcnt = {k: 0 for k in dkeys}
        for rec in ops:
            rec["dpre"] = None
            if not rec["marked"] and rec["dma"] is None:
                pass
            if rec["dma"] is not None:
                dcnt[rec["dma"]] += 16
                rec["sem"] = ("d", rec["dma"])
                rec["post"] = dcnt[rec["dma"]]
            elif rec["marked"]:
                ecnt[rec["engine"]] += 1
                rec["sem"] = ("e", rec["engine"])
                rec["post"] = ecnt[rec["engine"]]
        self.etotal, self.dtotal = ecnt, dcnt
        run = {k: 0 for k in dkeys}
        for rec in ops:
            rec["dwait"] = {}
            for j in rec["deps"]:
                o = ops[j]
                if o["dma"] is not None:
                    k = o["dma"]
                    rec["dwait"][k] = run[k] + 16
            for k in list(rec["dwait"]):
                rec["dwait"][k] = run[k]
            if rec["dma"] is not None:
                run[rec["dma"]] += 16

        blockcm = nc.Block()
        block = blockcm.__enter__()

        def semh(key):
            return esem[key[1]] if key[0] == "e" else dsem[key[1]]

        def make(engname):
            def body(eng):
                guaranteed = {}
                for rec in ops:
                    if rec["engine"] != engname:
                        continue
                    need = {}
                    for j in rec["deps"]:
                        o = ops[j]
                        if o["dma"] is not None:
                            k = ("d", o["dma"])
                            v = rec["dwait"][o["dma"]]
                            if v > guaranteed.get(k, 0):
                                need[k] = max(need.get(k, 0), v)
                            continue
                        k, v = o["sem"], o["post"]
                        if v > guaranteed.get(k, 0):
                            need[k] = max(need.get(k, 0), v)
                    for k in sorted(need, key=str):
                        eng.wait_ge(semh(k), need[k])
                        guaranteed[k] = need[k]
                    for sh, sv in rec["extra_waits"]:
                        kx = ("x", sh.num)
                        if sv > guaranteed.get(kx, 0):
                            eng.wait_ge(sh, sv)
                            guaranteed[kx] = sv
                    if rec["bcast"]:
                        rec["fn"](eng, dsem[rec["dma"]])
                    else:
                        inst = rec["fn"](eng)
                        if rec["marked"]:
                            inst.then_inc(semh(rec["sem"]),
                                          16 if rec["dma"] is not None else 1)
                if engname == "sync":
                    for k in dkeys:
                        eng.wait_ge(dsem[k], dcnt[k])
                    for e2 in ENGS:
                        if e2 != "sync" and ecnt[e2]:
                            eng.wait_ge(esem[e2], ecnt[e2])
            return body

        block.sync(make("sync"))
        block.gpsimd(make("gpsimd"))
        block.scalar(make("scalar"))
        block.vector(make("vector"))
        block.tensor(make("tensor"))
        blockcm.__exit__(None, None, None)
        for cm in reversed(self._semctx):
            cm.__exit__(None, None, None)


class Bld:

    def __init__(self, nc, p):
        self.nc = nc
        self.p = p
        self.sbuf_bytes = 0
        self._uid = 0
        self._consts = {}
        self.ident = None

    def uid(self):
        self._uid += 1
        return self._uid

    def sb(self, name, shape, dtype):
        sh = getattr(self, "share", None)
        if sh is not None and name in sh:
            return sh[name]
        t = self.nc.alloc_sbuf_tensor(name, list(shape), dtype)
        self.sbuf_bytes += shape[1] * mybir.dt.size(dtype) * 1
        return t

    def ps(self, name, shape, dtype=F32):
        return self.nc.alloc_psum_tensor(name, list(shape), dtype)

    def const_imm(self, val):
        key = ("imm", float(val))
        if key in self._consts:
            return self._consts[key]
        t = self.sb(f"cimm_{self.uid()}", (128, 1), F32)
        self.p.op("vector", lambda e, t=t, v=float(val): e.memset(t[:, :], v),
                  writes=[key], tag="cimm")
        self.nc.const_aps.register(F32, float(val), t[:, 0:1])
        self._consts[key] = key
        return key

    def const_tile(self, val, dtype=F32):
        key = (float(val), dtype)
        if key in self._consts:
            return self._consts[key]
        t = self.sb(f"const_{self.uid()}", (128, 1), dtype)
        self.p.op("vector", lambda e, t=t, v=float(val): e.memset(t[:, :], v),
                  writes=[("const", key)], tag="const")
        self._consts[key] = (t, ("const", key))
        return self._consts[key]

    def need_ident(self):
        if self.ident is None:
            ident_np = np.eye(128, dtype=BFNP)
            dr = self.nc.inline_tensor(np.asarray(ident_np), name="ident128")
            t = self.sb("identsb", (128, 128), BF16)
            self.p.op("sync", lambda e, t=t, dr=dr: e.dma_start(out=t[:, :], in_=dr[:, :]),
                      writes=["ident"], dma="identload", tag="ident")
            self.ident = t
        return self.ident



def emit_phase1(b, xT, modrow, h_store):
    nc, p = b.nc, b.p
    t_eps = b.const_imm(EPS_LN)

    ones_col = b.sb("p1_ones_col", (128, 1), BF16)
    p.op("vector", lambda e: e.memset(ones_col[:, :], 1.0), writes=["p1oc"])
    ones_row = b.sb("p1_ones_row", (1, 128), BF16)
    p.op("vector", lambda e: e.memset(ones_row[:, :], 1.0), writes=["p1or"])

    catile = b.sb("p1_ca", (128, NCH), F32)
    satile = b.sb("p1_sa", (128, NCH), F32)
    if getattr(b, "fused", False):
        SELT = b.sb("p1_modsel", (128, NCH), F32)
        for vv, tile, tok in ((0, satile, "p1sa"), (1, catile, "p1ca")):
            for c1 in range(2):
                a0 = bass.AP(b.modag, 0 * 1536 + vv * 256 + c1 * 128,
                             [[1, 128], [3072, 8]])
                a1 = bass.AP(b.modag, 1 * 1536 + vv * 256 + c1 * 128,
                             [[1, 128], [3072, 8]])
                d0 = bass.AP(tile, c1, [[NCH, 128], [2, 8]])
                d1 = bass.AP(SELT, c1, [[NCH, 128], [2, 8]])
                p.op("sync", lambda e, d0=d0, a0=a0: e.dma_start(
                    out=d0, in_=a0, allow_slow_non_contiguous=True),
                    reads=["modag"], writes=[(tok, c1)],
                    dma=("p1mod", vv, c1, 0))
                p.op("sync", lambda e, d1=d1, a1=a1: e.dma_start(
                    out=d1, in_=a1, allow_slow_non_contiguous=True),
                    reads=["modag"], writes=[("p1sel", c1)],
                    dma=("p1mod", vv, c1, 1))
            p.op("vector", lambda e, tile=tile: e.tensor_sub(
                SELT[:, :], SELT[:, :], tile[:, :]),
                reads=[("p1sel", 0), ("p1sel", 1), (tok, 0), (tok, 1)],
                writes=[("p1sel", 0), ("p1sel", 1)])
            p.op("vector", lambda e, tile=tile: e.scalar_tensor_tensor(
                tile[:, :], SELT[:, :], b.flag128[:, 0:1], tile[:, :],
                ALU.mult, ALU.add),
                reads=[("p1sel", 0), ("p1sel", 1), b.tflag, (tok, 0), (tok, 1)],
                writes=[tok])
    else:
        ca_ap = bass.AP(modrow, D, [[1, 128], [128, NCH]])
        sa_ap = bass.AP(modrow, 0, [[1, 128], [128, NCH]])
        p.op("sync", lambda e: e.dma_start(out=catile[:, :], in_=ca_ap,
                                           allow_slow_non_contiguous=True), writes=["p1ca"], dma="p1modca")
        p.op("sync", lambda e: e.dma_start(out=satile[:, :], in_=sa_ap,
                                           allow_slow_non_contiguous=True), writes=["p1sa"], dma="p1modsa")
    p.op("vector", lambda e: e.tensor_single_scalar(catile[:, :], catile[:, :], 1.0, ALU.add),
         reads=["p1ca"], writes=["p1ca"])

    xstg = [b.sb(f"p1_xstg{i}", (128, NCH * WT), F32) for i in range(2)]
    xb = [b.sb(f"p1_xb{i}", (128, NCH * WT), BF16) for i in range(2)]
    xsq = b.sb("p1_xsq", (128, NCH * WT), BF16)
    hwin = [b.sb(f"p1_hwin{i}", (128, NCH * WT), BF16) for i in range(2)]
    stat_s = b.sb("p1_stats", (1, 6 * WT), F32)
    mb = b.sb("p1_mb", (1, WT), BF16)
    rb = b.sb("p1_rb", (1, WT), BF16)
    mrbc = b.sb("p1_mrbc", (128, 2 * WT), F32)
    t1 = b.sb("p1_t1", (128, WT), F32)
    t2 = b.sb("p1_t2", (128, WT), F32)

    PS = b.PS
    tPS = b.tPS

    for w in range(NW):
        par = w % 2
        xspar = 0 if getattr(b, "fused", False) else par
        XS, XB, HW = xstg[xspar], xb[par], hwin[par]
        tXS, tXB, tHW = ("p1xs", xspar), ("p1xb", par), ("p1hw", par)
        alt1 = "scalar" if getattr(b, "fused", False) else "gpsimd"
        for c in range(NCH):
            eng = "sync" if c % 2 == 0 else alt1
            p.op(eng, lambda e, c=c, w=w, XS=XS: e.dma_start(
                out=XS[:, c * WT:(c + 1) * WT], in_=xT[c, :, w * WT:(w + 1) * WT]),
                writes=[tXS], dma=("p1x", par, eng), tag=f"p1 xload w{w}c{c}")
        p.op("scalar", lambda e, XS=XS, XB=XB: e.copy(XB[:, :], XS[:, :]),
             reads=[tXS], writes=[tXB], tag=f"p1 cast w{w}")
        p.op("scalar", lambda e, XB=XB: e.activation(xsq[:, :], XB[:, :], AF.Square),
             reads=[tXB], writes=["p1xsq"], tag=f"p1 sq w{w}")
        for c in range(NCH):
            p.op("tensor", lambda e, c=c, XB=XB: e.matmul(
                PS[0][0:1, 0:WT], ones_col[:, :], XB[:, c * WT:(c + 1) * WT],
                start=(c == 0), stop=(c == NCH - 1)),
                reads=[tXB, "p1oc"], writes=[tPS[0]], tag=f"p1 sum w{w}c{c}")
        for c in range(NCH):
            p.op("tensor", lambda e, c=c: e.matmul(
                PS[1][0:1, 0:WT], ones_col[:, :], xsq[:, c * WT:(c + 1) * WT],
                start=(c == 0), stop=(c == NCH - 1)),
                reads=["p1xsq", "p1oc"], writes=[tPS[1]], tag=f"p1 sumsq w{w}c{c}")
        p.op("vector", lambda e: e.tensor_single_scalar(
            stat_s[0:1, 0:WT], PS[0][0:1, 0:WT], 1.0 / D, ALU.mult),
            reads=[tPS[0]], writes=["p1stat"])
        p.op("vector", lambda e: e.tensor_mul(
            stat_s[0:1, 2 * WT:3 * WT], stat_s[0:1, 0:WT], stat_s[0:1, 0:WT]),
            reads=["p1stat"], writes=["p1stat"])
        p.op("vector", lambda e: e.scalar_tensor_tensor(
            stat_s[0:1, WT:2 * WT], PS[1][0:1, 0:WT], 1.0 / D,
            stat_s[0:1, 2 * WT:3 * WT], ALU.mult, ALU.subtract),
            reads=[tPS[1], "p1stat"], writes=["p1stat"])
        p.op("scalar", lambda e: e.activation(
            stat_s[0:1, WT:2 * WT], stat_s[0:1, WT:2 * WT], AF.Sqrt,
            bias=EPS_LN, scale=1.0), reads=["p1stat", t_eps], writes=["p1stat"])
        p.op("vector", lambda e: e.reciprocal(
            stat_s[0:1, WT:2 * WT], stat_s[0:1, WT:2 * WT]),
            reads=["p1stat"], writes=["p1stat"])
        p.op("vector", lambda e: e.tensor_copy(mb[0:1, :], stat_s[0:1, 0:WT]),
             reads=["p1stat"], writes=["p1mb"])
        p.op("vector", lambda e: e.tensor_copy(rb[0:1, :], stat_s[0:1, WT:2 * WT]),
             reads=["p1stat"], writes=["p1rb"])
        p.op("tensor", lambda e: e.matmul(PS[2][:, :], ones_row[:, :], mb[0:1, :],
                                          start=True, stop=True),
             reads=["p1mb", "p1or"], writes=[tPS[2]])
        p.op("tensor", lambda e: e.matmul(PS[3][:, :], ones_row[:, :], rb[0:1, :],
                                          start=True, stop=True),
             reads=["p1rb", "p1or"], writes=[tPS[3]])
        p.op("scalar", lambda e: e.copy(mrbc[:, 0:WT], PS[2][:, :]),
             reads=[tPS[2]], writes=["p1mrbc"])
        p.op("scalar", lambda e: e.copy(mrbc[:, WT:2 * WT], PS[3][:, :]),
             reads=[tPS[3]], writes=["p1mrbc"])
        for c in range(NCH):
            p.op("vector", lambda e, c=c, XB=XB: e.tensor_sub(
                t1[:, :], XB[:, c * WT:(c + 1) * WT], mrbc[:, 0:WT]),
                reads=[tXB, "p1mrbc"], writes=["p1t1"])
            p.op("vector", lambda e: e.tensor_mul(t2[:, :], t1[:, :], mrbc[:, WT:2 * WT]),
                 reads=["p1t1", "p1mrbc"], writes=["p1t2"])
            p.op("vector", lambda e, c=c, HW=HW: e.tensor_scalar(
                HW[:, c * WT:(c + 1) * WT], t2[:, :], catile[:, c:c + 1],
                satile[:, c:c + 1], ALU.mult, ALU.add),
                reads=["p1t2", "p1ca", "p1sa"], writes=[tHW])
        h_store(w, HW, tHW)


def build_phase1_v0():
    nc = bass.Bass("TRN2", target_bir_lowering=False, debug=False)
    xT = nc.dram_tensor("xT", [NCH, 128, TC], F32, kind="ExternalInput")
    modrow = nc.dram_tensor("modrow", [1, 6 * D], F32, kind="ExternalInput")
    h_out = nc.dram_tensor("h_out", [NCH, 128, TC], BF16, kind="ExternalOutput")
    p = Prog(nc)
    b = Bld(nc, p)
    b.PS = [b.ps(f"p1_ps{k}", (128, 512), F32) for k in range(4)]
    b.tPS = [("PS", k) for k in range(4)]

    def h_store(w, HW, tHW):
        for c in range(NCH):
            eng = "sync" if c % 2 == 0 else "gpsimd"
            p.op(eng, lambda e, c=c, w=w, HW=HW: e.dma_start(
                out=h_out[c, :, w * WT:(w + 1) * WT],
                in_=HW[:, c * WT:(c + 1) * WT]),
                reads=[tHW], writes=[("hout", w, c)], dma=("p1h", w % 2, eng))

    emit_phase1(b, xT, modrow, h_store)
    p.build()
    return nc



def _bcast_row(b, name, modoff, transform, outdt=BF16):
    nc, p = b.nc, b.p
    if not hasattr(b, "row"):
        b.row = b.sb("shared_row", (1, D), F32)
        b.rowb = b.sb("shared_rowb", (1, D), BF16)
    row, rowb = b.row, b.rowb
    out = b.sb(f"{name}_bc", (128, D), outdt)
    tok = f"{name}bc"
    if getattr(b, "fused", False):
        v = modoff // D
        src0 = bass.AP(b.modag, 0 * 1536 + v * 256, [[3072, 8], [1, 256]])
        src1 = bass.AP(b.modag, 1 * 1536 + v * 256, [[3072, 8], [1, 256]])
        dstr = bass.AP(row, 0, [[D, 1], [256, 8], [1, 256]])
        dstg = bass.AP(b.gsc3_t, 0, [[b.gsc3_t.shape[1], 1], [256, 8], [1, 256]])
        p.op("sync", lambda e, src0=src0, dstr=dstr: e.dma_start(
            out=dstr, in_=src0),
            reads=["modag"], writes=["sharedrow"], dma=("p3crow", name))
        p.op("sync", lambda e, src1=src1, dstg=dstg: e.dma_start(
            out=dstg, in_=src1),
            reads=["modag"], writes=["p3gscrow"], dma=("p3crow2", name))
        p.op("vector", lambda e: e.tensor_sub(b.gsc3[:, :], b.gsc3[:, :], row[:, :]),
             reads=["p3gscrow", "sharedrow"], writes=["p3gscrow"])
        p.op("vector", lambda e: e.scalar_tensor_tensor(
            row[:, :], b.gsc3[:, :], b.flag1[0:1, 0:1], row[:, :],
            ALU.mult, ALU.add),
            reads=["p3gscrow", b.tflag1, "sharedrow"], writes=["sharedrow"])
    else:
        p.op("sync", lambda e: e.dma_start(
            out=row[:, :], in_=bass.AP(b.modrow, modoff, [[D, 1], [1, D]])),
            writes=["sharedrow"], dma=("p3crow", name))
    transform(row, "sharedrow")
    p.op("vector", lambda e: e.tensor_copy(rowb[:, :], row[:, :]),
         reads=["sharedrow"], writes=["sharedrowb"])
    for k in range(4):
        p.op("tensor", lambda e, k=k: e.matmul(
            b.PS[k][:, :], b.ones_row[:, :], rowb[0:1, k * 512:(k + 1) * 512],
            start=True, stop=True), reads=["sharedrowb", "p3or"], writes=[b.tPS[k]])
        p.op("scalar", lambda e, k=k: e.copy(out[:, k * 512:(k + 1) * 512], b.PS[k][:, :]),
             reads=[b.tPS[k]], writes=[tok])
    return out, tok


def emit_phase3(b, x_rows, attT, wo_bf, wup_pcj, wdnT, modrow, out_rows):
    nc, p = b.nc, b.p
    b.modrow = modrow
    t_eps = b.const_imm(EPS_LN)
    t_half = b.const_imm(0.5)
    ident = b.need_ident()

    h128_np = _hadamard(128).astype(BFNP)
    h128_dr = nc.inline_tensor(np.asarray(h128_np), name="h128")
    H128 = b.sb("p3_h128", (128, 128), BF16)
    p.op("sync", lambda e: e.dma_start(out=H128[:, :], in_=h128_dr[:, :]),
         writes=["h128"], dma="p3ch128")
    if getattr(b, "ones_row", None) is None:
        ones_row = b.sb("p3_ones_row", (1, 128), BF16)
        p.op("vector", lambda e: e.memset(ones_row[:, :], 1.0), writes=["p3or"])
        b.ones_row = ones_row
    else:
        ones_row = b.ones_row

    if getattr(b, "PS", None) is not None and len(b.PS) >= 7:
        PS, tPS = b.PS, b.tPS
        PSB, tPSB = b.PSB, b.tPSB
    else:
        PS = [b.ps(f"p3_ps{k}", (128, 512), F32) for k in range(7)]
        PSB = b.ps("p3_psb", (128, 1024), BF16)
        tPS = [("PS", k) for k in range(7)]
        tPSB = ("PSB",)
        b.PS, b.tPS = PS, tPS
        b.PSB, b.tPSB = PSB, tPSB

    gsc = b.sb("p3_gsc", (1, D), F32)

    def tf_ga(row, tok):
        p.op("vector", lambda e, row=row: e.tensor_scalar(
            gsc[:, :], row[:, :], 0.0, 2.0, ALU.is_ge, ALU.mult),
            reads=[tok], writes=["p3gscrow"])
        p.op("vector", lambda e: e.tensor_single_scalar(
            gsc[:, :], gsc[:, :], -1.0, ALU.add), reads=["p3gscrow"], writes=["p3gscrow"])
        p.op("scalar", lambda e, row=row: e.activation(row[:, :], row[:, :], AF.Abs),
             reads=[tok], writes=[tok])
        p.op("vector", lambda e, row=row: e.tensor_single_scalar(
            row[:, :], row[:, :], 0.25, ALU.max), reads=[tok], writes=[tok])
        p.op("vector", lambda e, row=row: e.tensor_mul(row[:, :], row[:, :], gsc[:, :]),
             reads=[tok, "p3gscrow"], writes=[tok])

    def tf_cb1(row, tok):
        p.op("vector", lambda e, row=row: e.tensor_single_scalar(
            row[:, :], row[:, :], 1.0, ALU.add), reads=[tok], writes=[tok])

    def tf_id(row, tok):
        pass

    GABC, tGA = _bcast_row(b, "p3ga", 2 * D, tf_ga)
    GBBC, tGB = _bcast_row(b, "p3gb", 5 * D, tf_ga)
    CBBC, tCB = _bcast_row(b, "p3cb", 4 * D, tf_cb1)
    SBBC, tSB = _bcast_row(b, "p3sb", 3 * D, tf_id)

    ATW = b.sb("p3_atw", (128, NCH * WT), BF16)
    tATW = "xq" if getattr(b, "fused", False) else "atw"
    CAND = [b.sb(f"p3_cand{i}", (128, 512), BF16) for i in range(2)]
    XR = [b.sb(f"p3_xr{i}", (128, 512), F32) for i in range(2)]
    X2T = b.sb("p3_x2t", (128, 4 * D), F32)
    H2TOK = b.sb("p3_h2tok", (128, D), BF16)
    H2C = b.sb("p3_h2c", (128, 1024), BF16)
    FA = b.sb("p3_fa", (128, FCH * WT), BF16)
    FB = b.sb("p3_fb", (128, NCH * WT), BF16)
    GB = b.sb("p3_gb", (128, NCH * WT), BF16)
    XQ = b.sb("p3_xq", (128, NCH * WT), BF16)
    WOS = [b.sb(f"p3_wos{i}", (128, 512), BF16) for i in range(2)]
    WUP = [b.sb(f"p3_wup{i}", (128, NCH * 128), BF16) for i in range(2)]
    WDN = [b.sb(f"p3_wdn{i}", (128, 512), BF16) for i in range(2)]
    QU = b.sb("p3_qu", (128, WT), F32)
    QZI = b.sb("p3_qzi", (128, WT), I32)
    QZF = b.sb("p3_qzf", (128, WT), F32)
    QC = b.sb("p3_qc", (128, WT), F32)
    QM = FB
    QSW = b.sb("p3_qsw", (128, WT), BF16)
    QS = b.sb("p3_qs", (128, WT), F32)
    QI = b.sb("p3_qi", (128, WT), F32)
    ST = b.sb("p3_st", (128, 8), F32)
    OT = [b.sb(f"p3_ot{i}", (128, 512), F32) for i in range(2)]

    def quant(nch, width, src, tsrc, dst_slices, tdst, cnorm, wq_, tag):
        p.op("scalar", lambda e: e.activation(
            QM[:, 0:nch * width], src[:, 0:nch * width], AF.Abs),
            reads=[tsrc], writes=["fb"], tag=f"{tag} abs")
        n = nch
        while n > 1:
            n2 = n // 2
            p.op("vector", lambda e, n2=n2, n=n: e.tensor_max(
                QM[:, 0:n2 * width], QM[:, 0:n2 * width], QM[:, n2 * width:n * width]),
                reads=["fb"], writes=["fb"])
            n = n2
        for half in (64, 32):
            pairs = [(0, half)] if half == 64 else [(0, 32), (64, 96)]
            for lo, hi in pairs:
                p.op("sync", lambda e, lo=lo, hi=hi, hh=half: e.dma_start(
                    out=QSW[lo:lo + hh, 0:width], in_=QM[hi:hi + hh, 0:width]),
                    reads=["fb"], writes=["qsw"], dma=("p3swap",))
                p.op("sync", lambda e, lo=lo, hi=hi, hh=half: e.dma_start(
                    out=QSW[hi:hi + hh, 0:width], in_=QM[lo:lo + hh, 0:width]),
                    reads=["fb"], writes=["qsw"], dma=("p3swap",))
            p.op("vector", lambda e: e.tensor_max(
                QM[:, 0:width], QM[:, 0:width], QSW[:, 0:width]),
                reads=["fb", "qsw"], writes=["fb"])
        for bit in (16, 8, 4, 2, 1):
            p.op("vector", lambda e, bit=bit: e.stream_shuffle(
                QSW[:, 0:width], QM[:, 0:width], [i ^ bit for i in range(32)]),
                reads=["fb"], writes=["qsw"])
            p.op("vector", lambda e: e.tensor_max(
                QM[:, 0:width], QM[:, 0:width], QSW[:, 0:width]),
                reads=["fb", "qsw"], writes=["fb"])
        p.op("vector", lambda e: e.tensor_single_scalar(
            QS[:, 0:width], QM[:, 0:width], EPS_Q / cnorm, ALU.max),
            reads=["fb"], writes=["qs"])
        p.op("vector", lambda e: e.tensor_single_scalar(
            QI[:, 0:width], QS[:, 0:width], cnorm / QB, ALU.mult),
            reads=["qs"], writes=["qi"])
        p.op("vector", lambda e: e.reciprocal(QS[:, 0:width], QS[:, 0:width]),
             reads=["qs"], writes=["qs"])
        p.op("vector", lambda e: e.tensor_single_scalar(
            QS[:, 0:width], QS[:, 0:width], QB, ALU.mult), reads=["qs"], writes=["qs"])
        for c in range(nch):
            dt_, dsl = dst_slices(c)
            ssl = slice(c * width, (c + 1) * width)
            p.op("vector", lambda e, ssl=ssl: e.tensor_mul(
                QU[:, 0:width], src[:, ssl], QS[:, 0:width]),
                reads=[tsrc, "qs"], writes=["qu"])
            p.op("scalar", lambda e: e.activation(
                QZI[:, 0:width], QU[:, 0:width], AF.Copy, bias=0.5, scale=1.0),
                reads=["qu", t_half], writes=["qzi"])
            p.op("scalar", lambda e: e.copy(QZF[:, 0:width], QZI[:, 0:width]),
                 reads=["qzi"], writes=["qzf"])
            p.op("vector", lambda e: e.scalar_tensor_tensor(
                QC[:, 0:width], QZF[:, 0:width], 0.5, QU[:, 0:width],
                ALU.subtract, ALU.is_gt),
                reads=["qzf", "qu"], writes=["qc"])
            p.op("vector", lambda e: e.tensor_sub(
                QC[:, 0:width], QZF[:, 0:width], QC[:, 0:width]),
                reads=["qzf", "qc"], writes=["qc"])
            p.op("vector", lambda e, dt_=dt_, dsl=dsl: e.tensor_mul(
                dt_[:, dsl], QC[:, 0:width], QI[:, 0:width]),
                reads=["qc", "qi"], writes=[tdst], tag=f"{tag} out c{c}")

    for w in range(NW):
        wsl = slice(w * WT, (w + 1) * WT)
        if getattr(b, "fused", False):
            for hh in range(H):
                R_, hl_ = hh // HL, hh % HL
                for k8 in range(8):
                    bbx, qq = k8 // 4, k8 % 4
                    src = bass.AP(attT, (qq * 8 + R_) * NATQ
                                  + ((bbx * 2 + hl_) * 4 + w) * (128 * 512),
                                  [[512, 128], [1, 512]])
                    eng = "sync" if k8 % 2 == 0 else "scalar"
                    p.op(eng, lambda e, src=src, k8=k8: e.dma_start(
                        out=CAND[k8 % 2][:, :], in_=src),
                        reads=[("atag", k8 % 4)], writes=[("cand", k8 % 2)],
                        dma=("p3att", k8 % 2), tag=f"p3 atcand w{w}h{hh}k{k8}")
                    if k8 == 0:
                        p.op("vector", lambda e, hh=hh, k8=k8: e.tensor_scalar(
                            ATW[:, hh * WT:(hh + 1) * WT], CAND[k8 % 2][:, :],
                            b.GF[:, k8:k8 + 1], None, ALU.mult),
                            reads=[("cand", k8 % 2), "gf"], writes=[tATW],
                            tag=f"p3 atsel w{w}h{hh}k{k8}")
                    else:
                        p.op("vector", lambda e, hh=hh, k8=k8: e.scalar_tensor_tensor(
                            ATW[:, hh * WT:(hh + 1) * WT], CAND[k8 % 2][:, :],
                            b.GF[:, k8:k8 + 1], ATW[:, hh * WT:(hh + 1) * WT],
                            ALU.mult, ALU.add),
                            reads=[("cand", k8 % 2), "gf", tATW], writes=[tATW],
                            tag=f"p3 atsel w{w}h{hh}k{k8}")
        else:
            for c in range(NCH):
                eng = "sync" if c % 2 == 0 else "gpsimd"
                p.op(eng, lambda e, c=c, wsl=wsl: e.dma_start(
                    out=ATW[:, c * WT:(c + 1) * WT], in_=attT[c, :, wsl]),
                    writes=[tATW], dma=("p3att", eng), tag=f"p3 attload w{w}c{c}")
        for dq in range(4):
            for c1 in range(NCH):
                par = c1 % 2
                if getattr(b, "fused", False):
                    p.op("sync", lambda e, c1=c1, dq=dq, par=par: e.dma_start(
                        out=WOS[par][:, :],
                        in_=wo_bf[c1 // 2, c1 % 2, :, dq * 512:(dq + 1) * 512]),
                        reads=["woag"],
                        writes=[("wos", par)], dma=("p3wo", par),
                        tag=f"p3 woload w{w}q{dq}c{c1}")
                else:
                    p.op("sync", lambda e, c1=c1, dq=dq, par=par: e.dma_start(
                        out=WOS[par][:, :],
                        in_=wo_bf[c1 * 128:(c1 + 1) * 128, dq * 512:(dq + 1) * 512]),
                        writes=[("wos", par)], dma=("p3wo", par),
                        tag=f"p3 woload w{w}q{dq}c{c1}",
                        extra_waits=getattr(b, "xw_wt", ()))
                for ts_ in range(4):
                    if getattr(b, "fused", False):
                        lhs = bass.AP(ATW, ts_ * 4 * SEG + c1,
                                      [[ATW.shape[1], 128], [SEG, 4], [16, 32]])
                        p.op("tensor", lambda e, lhs=lhs, c1=c1, ts_=ts_, par=par: e.matmul(
                            PS[ts_][:, :], lhs, WOS[par][:, :],
                            start=(c1 == 0), stop=(c1 == NCH - 1)),
                            reads=[("wos", par), tATW], writes=[tPS[ts_]],
                            tag=f"p3 o mm w{w}q{dq}c{c1}t{ts_}")
                    else:
                        p.op("tensor", lambda e, c1=c1, ts_=ts_, par=par: e.matmul(
                            PS[ts_][:, :],
                            ATW[:, c1 * WT + ts_ * 128: c1 * WT + (ts_ + 1) * 128],
                            WOS[par][:, :],
                            start=(c1 == 0), stop=(c1 == NCH - 1)),
                            reads=[("wos", par), tATW], writes=[tPS[ts_]],
                            tag=f"p3 o mm w{w}q{dq}c{c1}t{ts_}")
            for ts_ in range(4):
                xpar = ts_ % 2
                osl = slice(dq * 512, (dq + 1) * 512)
                xeng = "scalar" if getattr(b, "fused", False) else "gpsimd"
                p.op(xeng, lambda e, w=w, ts_=ts_, xpar=xpar, osl=osl: e.dma_start(
                    out=XR[xpar][:, :],
                    in_=x_rows[w * WT + ts_ * 128: w * WT + (ts_ + 1) * 128, osl]),
                    writes=[("xr", xpar)], dma=("p3x", xpar),
                    tag=f"p3 xload w{w}q{dq}t{ts_}")
                if getattr(b, "dbg", None) is not None and w == 0 and dq == 0 and ts_ == 0:
                    p.op("sync", lambda e: e.dma_start(out=b.dbg["xr"][:, :], in_=XR[0][:, :]),
                         reads=[("xr", 0)], writes=["dbgxr"], dma=("dbg",))
                    p.op("vector", lambda e: e.tensor_copy(OT[1][:, :], PS[0][:, :]),
                         reads=[tPS[0]], writes=[("ot", 1)])
                    p.op("sync", lambda e: e.dma_start(out=b.dbg["o0"][:, :], in_=OT[1][:, :]),
                         reads=[("ot", 1)], writes=["dbgo0"], dma=("dbg",))
                p.op("vector", lambda e, ts_=ts_, osl=osl: e.tensor_mul(
                    OT[ts_ % 2][:, :], PS[ts_][:, :], GABC[:, osl]),
                    reads=[tPS[ts_], tGA], writes=[("ot", ts_ % 2)],
                    tag=f"p3 oga w{w}q{dq}t{ts_}")
                p.op("vector", lambda e, ts_=ts_, dq=dq, xpar=xpar: e.tensor_add(
                    X2T[:, ts_ * D + dq * 512: ts_ * D + (dq + 1) * 512],
                    OT[ts_ % 2][:, :], XR[xpar][:, :]),
                    reads=[("ot", ts_ % 2), ("xr", xpar)], writes=["x2t"],
                    tag=f"p3 x2 w{w}q{dq}t{ts_}")
        if getattr(b, "dbg", None) is not None and w == 0:
            p.op("sync", lambda e: e.dma_start(out=b.dbg["x2t"][:, :], in_=X2T[:, :]),
                 reads=["x2t"], writes=["dbgx"], dma=("dbg",))
            p.op("sync", lambda e: e.dma_start(out=b.dbg["atw"][:, :], in_=ATW[:, :]),
                 reads=["atw"], writes=["dbga"], dma=("dbg",))
        for ts_ in range(4):
            x2sl = slice(ts_ * D, (ts_ + 1) * D)
            p.op("vector", lambda e, x2sl=x2sl: e.tensor_reduce(
                ST[:, 0:1], X2T[:, x2sl], AX.X, ALU.add),
                reads=["x2t"], writes=["p3st"], tag=f"p3 sum w{w}t{ts_}")
            p.op("scalar", lambda e, x2sl=x2sl: e.activation(
                QM[:, 0:D], X2T[:, x2sl], AF.Square, accum_out=ST[:, 1:2]),
                reads=["x2t"], writes=["fb", "p3st"], tag=f"p3 sumsq w{w}t{ts_}")
            p.op("vector", lambda e: e.tensor_single_scalar(
                ST[:, 2:3], ST[:, 0:1], 1.0 / D, ALU.mult),
                reads=["p3st"], writes=["p3st"])
            p.op("vector", lambda e: e.tensor_mul(ST[:, 3:4], ST[:, 2:3], ST[:, 2:3]),
                 reads=["p3st"], writes=["p3st"])
            p.op("vector", lambda e: e.scalar_tensor_tensor(
                ST[:, 4:5], ST[:, 1:2], 1.0 / D, ST[:, 3:4], ALU.mult, ALU.subtract),
                reads=["p3st"], writes=["p3st"])
            p.op("scalar", lambda e: e.activation(
                ST[:, 4:5], ST[:, 4:5], AF.Sqrt, bias=EPS_LN, scale=1.0),
                reads=["p3st", t_eps], writes=["p3st"])
            p.op("vector", lambda e: e.reciprocal(ST[:, 5:6], ST[:, 4:5]),
                 reads=["p3st"], writes=["p3st"])
            p.op("vector", lambda e, x2sl=x2sl: e.tensor_scalar(
                H2TOK[:, :], X2T[:, x2sl], ST[:, 2:3], ST[:, 5:6],
                ALU.subtract, ALU.mult),
                reads=["x2t", "p3st"], writes=["h2tok"], tag=f"p3 norm w{w}t{ts_}")
            p.op("vector", lambda e: e.tensor_mul(H2TOK[:, :], H2TOK[:, :], CBBC[:, :]),
                 reads=["h2tok", tCB], writes=["h2tok"])
            p.op("vector", lambda e: e.tensor_add(H2TOK[:, :], H2TOK[:, :], SBBC[:, :]),
                 reads=["h2tok", tSB], writes=["h2tok"])
            for half in range(2):
                for cc in range(8):
                    c = half * 8 + cc
                    p.op("tensor", lambda e, c=c, cc=cc: e.transpose(
                        PSB[:, cc * 128:(cc + 1) * 128],
                        H2TOK[:, c * 128:(c + 1) * 128], ident[:, :]),
                        reads=["h2tok", "ident"], writes=[tPSB],
                        tag=f"p3 h2t w{w}t{ts_}c{c}")
                p.op("vector", lambda e: e.tensor_copy(H2C[:, :], PSB[:, 0:1024]),
                     reads=[tPSB], writes=["h2c"], tag=f"p3 h2tev w{w}t{ts_}h{half}")
                for g2 in range(2):
                    k = 2 + g2
                    p.op("tensor", lambda e, g2=g2, k=k: e.matmul(
                        PS[k][:, :], H128[:, :], H2C[:, g2 * 512:(g2 + 1) * 512],
                        start=True, stop=True),
                        reads=["h2c", "h128"], writes=[tPS[k]],
                        tag=f"p3 fw1mm w{w}t{ts_}h{half}g{g2}")
                    for j4 in range(4):
                        c = half * 8 + g2 * 4 + j4
                        p.op("scalar", lambda e, c=c, ts_=ts_, k=k, j4=j4: e.copy(
                            FA[:, c * WT + ts_ * 128: c * WT + (ts_ + 1) * 128],
                            PS[k][:, j4 * 128:(j4 + 1) * 128]),
                            reads=[tPS[k]], writes=["fa"],
                            tag=f"p3 fw1ev w{w}t{ts_}c{c}")
        bufs = [(FA, "fa"), (FB, "fb")]
        for st_i, stv in enumerate([8, 4, 2, 1]):
            src_t, tsrc = bufs[st_i % 2]
            dst_t, tdst = bufs[(st_i + 1) % 2]
            for g in range(NCH // (2 * stv)):
                for j in range(stv):
                    a = g * 2 * stv + j
                    bi = a + stv
                    asl = slice(a * WT, (a + 1) * WT)
                    bsl = slice(bi * WT, (bi + 1) * WT)
                    p.op("vector", lambda e, asl=asl, bsl=bsl, src_t=src_t, dst_t=dst_t: e.tensor_add(
                        dst_t[:, asl], src_t[:, asl], src_t[:, bsl]),
                        reads=[tsrc], writes=[tdst], tag=f"p3 bf1 w{w}")
                    p.op("vector", lambda e, asl=asl, bsl=bsl, src_t=src_t, dst_t=dst_t: e.tensor_sub(
                        dst_t[:, bsl], src_t[:, asl], src_t[:, bsl]),
                        reads=[tsrc], writes=[tdst], tag=f"p3 bf1 w{w}")
        if getattr(b, "dbg", None) is not None and w == 0:
            p.op("sync", lambda e: e.dma_start(out=b.dbg["fw1"][:, :], in_=FA[:, 0:NCH * WT]),
                 reads=["fa"], writes=["dbgf"], dma=("dbg",))
        quant(NCH, WT, FA, "fa", lambda c: (XQ, slice(c * WT, (c + 1) * WT)),
              "xq", C1, w, f"p3 q1 w{w}")
        if getattr(b, "dbg", None) is not None and w == 0:
            p.op("sync", lambda e: e.dma_start(out=b.dbg["xq"][:, :], in_=XQ[:, :]),
                 reads=["xq"], writes=["dbgq"], dma=("dbg",))
        for f2 in range(FCH):
            par = f2 % 2
            if getattr(b, "fused", False):
                p.op("sync", lambda e, f2=f2, par=par: e.dma_start(
                    out=WUP[par][:, :], in_=wup_pcj[f2 // 4, f2 % 4, :, :]),
                    reads=["wupag"],
                    writes=[("wup", par)], dma=("p3wup", par),
                    tag=f"p3 wupload w{w}f{f2}")
            else:
                p.op("sync", lambda e, f2=f2, par=par: e.dma_start(
                    out=WUP[par][:, :], in_=wup_pcj[f2, :, :]),
                    writes=[("wup", par)], dma=("p3wup", par), tag=f"p3 wupload w{w}f{f2}",
                    extra_waits=getattr(b, "xw_wt", ()))
            k = 2 + f2 % 2
            for c in range(NCH):
                p.op("tensor", lambda e, f2=f2, c=c, k=k, par=par: e.matmul(
                    PS[k][:, :], WUP[par][:, c * 128:(c + 1) * 128],
                    XQ[:, c * WT:(c + 1) * WT],
                    start=(c == 0), stop=(c == NCH - 1)),
                    reads=[("wup", par), "xq"], writes=[tPS[k]],
                    tag=f"p3 up mm w{w}f{f2}c{c}")
            p.op("vector", lambda e, k=k: e.tensor_single_scalar(
                QU[:, :], PS[k][:, :], 0.0, ALU.max),
                reads=[tPS[k]], writes=["qu"], tag=f"p3 relu w{w}f{f2}")
            p.op("scalar", lambda e, f2=f2: e.activation(
                FA[:, f2 * WT:(f2 + 1) * WT], QU[:, :], AF.Square),
                reads=["qu"], writes=["fa"], tag=f"p3 rsq w{w}f{f2}")
        HT = WT // 2
        for hw in range(2):
            for f2 in range(FCH):
                k = 2 + f2 % 2
                p.op("tensor", lambda e, f2=f2, hw=hw, k=k: e.matmul(
                    PS[k][:, 0:HT], H128[:, :],
                    FA[:, f2 * WT + hw * HT: f2 * WT + (hw + 1) * HT],
                    start=True, stop=True), reads=["fa", "h128"], writes=[tPS[k]],
                    tag=f"p3 fw2mm w{w}h{hw}f{f2}")
                p.op("scalar", lambda e, f2=f2, k=k: e.copy(
                    FB[:, f2 * HT:(f2 + 1) * HT], PS[k][:, 0:HT]),
                    reads=[tPS[k]], writes=["fb"], tag=f"p3 fw2ev w{w}h{hw}f{f2}")
            bufs2 = [(FB, "fb"), (GB, "gb")]
            for st_i, stv in enumerate([16, 8, 4, 2, 1]):
                src_t, tsrc = bufs2[st_i % 2]
                dst_t, tdst = bufs2[(st_i + 1) % 2]
                for g in range(FCH // (2 * stv)):
                    for j in range(stv):
                        a = g * 2 * stv + j
                        bi = a + stv
                        asl = slice(a * HT, (a + 1) * HT)
                        bsl = slice(bi * HT, (bi + 1) * HT)
                        p.op("vector", lambda e, asl=asl, bsl=bsl, src_t=src_t, dst_t=dst_t: e.tensor_add(
                            dst_t[:, asl], src_t[:, asl], src_t[:, bsl]),
                            reads=[tsrc], writes=[tdst], tag=f"p3 bf2 w{w}")
                        p.op("vector", lambda e, asl=asl, bsl=bsl, src_t=src_t, dst_t=dst_t: e.tensor_sub(
                            dst_t[:, bsl], src_t[:, asl], src_t[:, bsl]),
                            reads=[tsrc], writes=[tdst], tag=f"p3 bf2 w{w}")
            quant(FCH, HT, GB, "gb",
                  lambda c, hw=hw: (FA, slice(c * WT + hw * HT, c * WT + (hw + 1) * HT)),
                  "fa", C2, w, f"p3 q2 w{w}h{hw}")
        for dq in range(4):
            for f in range(FCH):
                par = f % 2
                if getattr(b, "fused", False):
                    for ii in range(2):
                        p.op("sync", lambda e, f=f, dq=dq, par=par, ii=ii: e.dma_start(
                            out=WDN[par][:, ii * 256:(ii + 1) * 256],
                            in_=wdnT[2 * dq + ii, f * 128:(f + 1) * 128, :]),
                            reads=["wdnag"],
                            writes=[("wdn", par)], dma=("p3wdn", par),
                            tag=f"p3 wdnload w{w}q{dq}f{f}i{ii}")
                else:
                    p.op("sync", lambda e, f=f, dq=dq, par=par: e.dma_start(
                        out=WDN[par][:, :],
                        in_=wdnT[f * 128:(f + 1) * 128, dq * 512:(dq + 1) * 512]),
                        writes=[("wdn", par)], dma=("p3wdn", par),
                        tag=f"p3 wdnload w{w}q{dq}f{f}")
                for ts_ in range(4):
                    p.op("tensor", lambda e, f=f, ts_=ts_, par=par: e.matmul(
                        PS[ts_][:, :],
                        FA[:, f * WT + ts_ * 128: f * WT + (ts_ + 1) * 128],
                        WDN[par][:, :],
                        start=(f == 0), stop=(f == FCH - 1)),
                        reads=[("wdn", par), "fa"], writes=[tPS[ts_]],
                        tag=f"p3 dn mm w{w}q{dq}f{f}t{ts_}")
            for ts_ in range(4):
                osl = slice(dq * 512, (dq + 1) * 512)
                p.op("vector", lambda e, ts_=ts_, osl=osl: e.tensor_mul(
                    OT[ts_ % 2][:, :], PS[ts_][:, :], GBBC[:, osl]),
                    reads=[tPS[ts_], tGB], writes=[("ot", ts_ % 2)],
                    tag=f"p3 dngb w{w}q{dq}t{ts_}")
                p.op("vector", lambda e, ts_=ts_, dq=dq: e.tensor_add(
                    OT[ts_ % 2][:, :], OT[ts_ % 2][:, :],
                    X2T[:, ts_ * D + dq * 512: ts_ * D + (dq + 1) * 512]),
                    reads=[("ot", ts_ % 2), "x2t"], writes=[("ot", ts_ % 2)],
                    tag=f"p3 dnadd w{w}q{dq}t{ts_}")
                alt3 = "scalar" if getattr(b, "fused", False) else "gpsimd"
                eng = "sync" if ts_ % 2 == 0 else alt3
                p.op(eng, lambda e, w=w, ts_=ts_, osl=osl: e.dma_start(
                    out=out_rows[w * WT + ts_ * 128: w * WT + (ts_ + 1) * 128, osl],
                    in_=OT[ts_ % 2][:, :]),
                    reads=[("ot", ts_ % 2)], writes=[("outr", w, ts_, dq)],
                    dma=("p3out", eng), tag=f"p3 out w{w}q{dq}t{ts_}")


def build_phase3_v0(debug=False):
    nc = bass.Bass("TRN2", target_bir_lowering=False, debug=False)
    x_rows = nc.dram_tensor("x_rows", [TC, D], F32, kind="ExternalInput")
    attT = nc.dram_tensor("attT", [NCH, 128, TC], BF16, kind="ExternalInput")
    wo_bf = nc.dram_tensor("wo_bf", [D, D], BF16, kind="ExternalInput")
    wup_pcj = nc.dram_tensor("wup_pcj", [FCH, 128, NCH * 128], BF16, kind="ExternalInput")
    wdnT = nc.dram_tensor("wdnT", [FF, D], BF16, kind="ExternalInput")
    modrow = nc.dram_tensor("modrow", [1, 6 * D], F32, kind="ExternalInput")
    out_rows = nc.dram_tensor("out_rows", [TC, D], F32, kind="ExternalOutput")
    p = Prog(nc)
    b = Bld(nc, p)
    if debug:
        b.dbg = {
            "x2t": nc.dram_tensor("dbg_x2t", [128, 4 * D], F32, kind="ExternalOutput"),
            "fw1": nc.dram_tensor("dbg_fw1", [128, NCH * WT], BF16, kind="ExternalOutput"),
            "xq": nc.dram_tensor("dbg_xq", [128, NCH * WT], BF16, kind="ExternalOutput"),
            "atw": nc.dram_tensor("dbg_atw", [128, NCH * WT], BF16, kind="ExternalOutput"),
            "xr": nc.dram_tensor("dbg_xr", [128, 512], F32, kind="ExternalOutput"),
            "o0": nc.dram_tensor("dbg_o0", [128, 512], F32, kind="ExternalOutput"),
        }
    emit_phase3(b, x_rows, attT, wo_bf, wup_pcj, wdnT, modrow, out_rows)
    p.build()
    return nc



def emit_phase2(b, h_full, wqkv, betas_s, att_s):
    nc, p = b.nc, b.p
    ident = b.need_ident()

    WSTG = b.sb("p2_wstg", (128, 768), F32)
    WQKV = b.sb("p2_wqkv", (128, NCH * 768), BF16)
    for c in range(NCH):
        p.op("sync", lambda e, c=c: e.dma_start(
            out=WSTG[:, :], in_=wqkv[c * 128:(c + 1) * 128, :]),
            writes=["wstg"], dma=("p2w",), tag=f"p2 wload c{c}")
        p.op("scalar", lambda e, c=c: e.copy(WQKV[:, c * 768:(c + 1) * 768], WSTG[:, :]),
             reads=["wstg"], writes=["wqkv"], tag=f"p2 wcast c{c}")

    BROW = b.sb("p2_brow", (1, 2 * 128), BF16)
    BROWF = b.sb("p2_browf", (1, 2 * 128), F32)
    BBC = b.sb("p2_bbc", (128, 2 * 128), F32)
    MBC = b.sb("p2_mbc", (128, 2 * 128), F32)
    for hl in range(HL):
        p.op("sync", lambda e, hl=hl: e.dma_start(
            out=BROWF[0:1, hl * 128:(hl + 1) * 128], in_=betas_s[hl:hl + 1, :]),
            writes=["brow"], dma=("p2wb",), tag=f"p2 bload {hl}")
    p.op("scalar", lambda e: e.activation(BROW[0:1, :], BROWF[0:1, :], AF.Sigmoid),
         reads=["brow"], writes=["brow"])
    for hl in range(HL):
        p.op("tensor", lambda e, hl=hl: e.matmul(
            b.PS[0][:, 0:128], b.ones_row[:, :], BROW[0:1, hl * 128:(hl + 1) * 128],
            start=True, stop=True), reads=["brow", "p3or"], writes=[b.tPS[0]])
        p.op("scalar", lambda e, hl=hl: e.copy(BBC[:, hl * 128:(hl + 1) * 128],
                                               b.PS[0][:, 0:128]),
             reads=[b.tPS[0]], writes=["bbc"])
        p.op("vector", lambda e, hl=hl: e.tensor_scalar(
            MBC[:, hl * 128:(hl + 1) * 128], BBC[:, hl * 128:(hl + 1) * 128],
            -1.0, 1.0, ALU.mult, ALU.add),
            reads=["bbc"], writes=["mbc"])

    mask_np = np.where(np.arange(128)[:, None] >= np.arange(128)[None, :],
                       0.0, -1e9).astype(np.float32)
    mask_dr = nc.inline_tensor(np.asarray(mask_np), name="cmask")
    CMASK = b.sb("p2_cmask", (128, 128), F32)
    p.op("sync", lambda e: e.dma_start(out=CMASK[:, :], in_=mask_dr[:, :]),
         writes=["cmask"], dma=("p2wc",))

    MEMZ = [b.sb(f"p2_memz{i}", (128, 129), F32) for i in range(2 * HL)]
    MEMZB = [b.sb(f"p2_memzb{i}", (128, 129), BF16) for i in range(2 * HL)]
    for pr in range(2 * HL):
        p.op("vector", lambda e, pr=pr: e.memset(MEMZ[pr][:, 0:128], 0.0),
             writes=[("memz", pr)])
        p.op("vector", lambda e, pr=pr: e.memset(MEMZ[pr][:, 128:129], 1.0 / DK),
             writes=[("memz", pr)])
        p.op("scalar", lambda e, pr=pr: e.copy(MEMZB[pr][:, :], MEMZ[pr][:, :]),
             reads=[("memz", pr)], writes=[("memzb", pr)])

    HTB = [b.sb(f"p2_ht{i}", (128, NCH * SEG), BF16) for i in range(2)]
    Q = b.sb("p2_q", (128, SEG), BF16)
    K = b.sb("p2_k", (128, SEG), BF16)
    V = b.sb("p2_v", (128, SEG), BF16)
    EQ = b.sb("p2_eq", (128, SEG), BF16)
    SQ = b.sb("p2_sq", (128, SEG), BF16)
    SK = b.sb("p2_sk", (128, SEG), BF16)
    VT = b.sb("p2_vt", (128, SEG), BF16)
    SKT = b.sb("p2_skt", (128, SEG), BF16)
    DLT = b.sb("p2_dlt", (128, SEG), BF16)
    PT = b.sb("p2_pt", (128, 4 * SEG), BF16)
    PSM = b.sb("p2_psm", (128, SEG), BF16)
    DEN = b.sb("p2_den", (128, 8), F32)
    NMS = b.sb("p2_nms", (128, 8), F32)
    RZ = b.sb("p2_rz", (128, 4), F32)
    T1 = b.sb("p2_t1", (128, 128), F32)
    T2 = b.sb("p2_t2", (128, 128), F32)
    ATT = b.sb("p2_att", (128, 128), BF16)
    ATTT = b.sb("p2_attt", (128, SEG), BF16)

    PS, tPS, PSB, tPSB = b.PS, b.tPS, b.PSB, b.tPSB

    for seg in range(NSEG):
        for bb in range(B):
            par = (seg * B + bb) % 2
            HT_ = HTB[par]
            tHT = ("p2ht", par)
            base = bb * S + seg * SEG
            for c in range(NCH):
                if getattr(b, "fused", False):
                    eng = "sync" if c % 2 == 0 else "scalar"
                    wv, rv = seg % 4, bb * 4 + seg // 4
                    p.op(eng, lambda e, c=c, wv=wv, rv=rv, HT_=HT_: e.dma_start(
                        out=HT_[:, c * SEG:(c + 1) * SEG],
                        in_=h_full[wv, rv, :, c * SEG:(c + 1) * SEG]),
                        reads=[("hag", wv)],
                        writes=[tHT], dma=("p2h", par, eng),
                        tag=f"p2 hload s{seg}b{bb}c{c}")
                else:
                    eng = "sync" if c % 2 == 0 else "gpsimd"
                    p.op(eng, lambda e, c=c, base=base, HT_=HT_: e.dma_start(
                        out=HT_[:, c * SEG:(c + 1) * SEG],
                        in_=h_full[c, :, base:base + SEG]),
                        writes=[tHT], dma=("p2h", par, eng),
                        tag=f"p2 hload s{seg}b{bb}c{c}")
            for hl in range(HL):
                pr = bb * HL + hl
                sfx = f"s{seg}b{bb}h{hl}"
                for pi, (dst, sig, tdst) in enumerate(
                        [(Q, SQ, "q"), (K, SK, "k"), (V, None, "v")]):
                    kk = pi % 2
                    for c in range(NCH):
                        p.op("tensor", lambda e, c=c, pi=pi, hl=hl, kk=kk, HT_=HT_: e.matmul(
                            PS[kk][:, :],
                            WQKV[:, c * 768 + pi * 256 + hl * 128:
                                 c * 768 + pi * 256 + (hl + 1) * 128],
                            HT_[:, c * SEG:(c + 1) * SEG],
                            start=(c == 0), stop=(c == NCH - 1)),
                            reads=["wqkv", tHT], writes=[tPS[kk]],
                            tag=f"p2 {tdst}mm {sfx}c{c}")
                    if pi == 0:
                        p.op("scalar", lambda e, dst=dst, kk=kk: e.activation(
                            dst[:, :], PS[kk][:, :], AF.Copy,
                            scale=1.0 / SCALE_ATT),
                            reads=[tPS[kk]], writes=[tdst], tag=f"p2 {tdst}ev {sfx}")
                    else:
                        p.op("scalar", lambda e, dst=dst, kk=kk: e.copy(dst[:, :], PS[kk][:, :]),
                             reads=[tPS[kk]], writes=[tdst], tag=f"p2 {tdst}ev {sfx}")
                    if sig is not None:
                        p.op("scalar", lambda e, kk=kk: e.activation(
                            EQ[:, :], PS[kk][:, :], AF.Exp),
                            reads=[tPS[kk]], writes=["eq"], tag=f"p2 {tdst}exp {sfx}")
                        p.op("vector", lambda e, sig=sig, kk=kk: e.scalar_tensor_tensor(
                            sig[:, :], PS[kk][:, :], 1.0, EQ[:, :], ALU.add, ALU.min),
                            reads=[tPS[kk], "eq"], writes=[tdst + "s"],
                            tag=f"p2 {tdst}sig {sfx}")
                for src, dst, tsrc, tdst in ((V, VT, "v", "vt"), (SK, SKT, "ks", "skt")):
                    for j in range(4):
                        p.op("tensor", lambda e, j=j, src=src: e.transpose(
                            PSB[:, j * 128:(j + 1) * 128],
                            src[:, j * 128:(j + 1) * 128], ident[:, :]),
                            reads=[tsrc, "ident"], writes=[tPSB],
                            tag=f"p2 tr{tdst} {sfx}j{j}")
                    p.op("vector", lambda e, dst=dst: e.tensor_copy(
                        dst[:, :], PSB[:, 0:512]),
                        reads=[tPSB], writes=[tdst], tag=f"p2 trev{tdst} {sfx}")
                for i in range(4):
                    ncol = (i + 1) * 128
                    p.op("tensor", lambda e, i=i, pr=pr: e.matmul(
                        PS[2][:, 0:129], SQ[:, i * 128:(i + 1) * 128],
                        MEMZB[pr][:, :], start=True, stop=True),
                        reads=["qs", ("memzb", pr)], writes=[tPS[2]],
                        tag=f"p2 am {sfx}i{i}")
                    p.op("tensor", lambda e, i=i, ncol=ncol: e.matmul(
                        PS[3][:, 0:ncol], Q[:, i * 128:(i + 1) * 128], K[:, 0:ncol],
                        start=True, stop=True),
                        reads=["q", "k"], writes=[tPS[3]], tag=f"p2 sc {sfx}i{i}")
                    p.op("vector", lambda e, i=i, ncol=ncol: e.tensor_add(
                        PS[3][:, ncol - 128:ncol], PS[3][:, ncol - 128:ncol],
                        CMASK[:, :]),
                        reads=[tPS[3], "cmask"], writes=[tPS[3]],
                        tag=f"p2 mask {sfx}i{i}")
                    p.op("vector", lambda e, i=i, ncol=ncol: e.tensor_reduce(
                        NMS[:, i:i + 1], PS[3][:, 0:ncol], AX.X, ALU.max,
                        negate=True),
                        reads=[tPS[3]], writes=["nms"], tag=f"p2 max {sfx}i{i}")
                    p.op("scalar", lambda e, i=i, ncol=ncol: e.activation(
                        PSM[:, 0:ncol], PS[3][:, 0:ncol], AF.Exp,
                        bias=NMS[:, i:i + 1], scale=1.0,
                        accum_out=DEN[:, i:i + 1]),
                        reads=[tPS[3], "nms"], writes=["psm", "den"],
                        tag=f"p2 exp {sfx}i{i}")
                    for j in range(i + 1):
                        p.op("tensor", lambda e, j=j: e.transpose(
                            PSB[:, j * 128:(j + 1) * 128],
                            PSM[:, j * 128:(j + 1) * 128], ident[:, :]),
                            reads=["psm", "ident"], writes=[tPSB],
                            tag=f"p2 trp {sfx}i{i}j{j}")
                    p.op("vector", lambda e, i=i: e.tensor_copy(
                        PT[:, 0:(i + 1) * 128], PSB[:, 0:(i + 1) * 128]),
                        reads=[tPSB], writes=["pt"], tag=f"p2 trpev {sfx}i{i}")
                    for j in range(i + 1):
                        p.op("tensor", lambda e, i=i, j=j: e.matmul(
                            PS[4][:, 0:128], PT[:, j * 128:(j + 1) * 128],
                            VT[:, j * 128:(j + 1) * 128],
                            start=(j == 0), stop=(j == i)),
                            reads=["pt", "vt"], writes=[tPS[4]],
                            tag=f"p2 ad {sfx}i{i}j{j}")
                    p.op("vector", lambda e, i=i: e.reciprocal(RZ[:, 0:1],
                                                               PS[2][:, 128:129]),
                         reads=[tPS[2]], writes=["rz"], tag=f"p2 rz {sfx}i{i}")
                    p.op("vector", lambda e, i=i: e.reciprocal(RZ[:, 1:2],
                                                               DEN[:, i:i + 1]),
                         reads=["den"], writes=["rz"])
                    p.op("vector", lambda e, hl=hl: e.scalar_tensor_tensor(
                        T1[:, :], PS[2][:, 0:128], RZ[:, 0:1],
                        BBC[:, hl * 128:(hl + 1) * 128], ALU.mult, ALU.mult),
                        reads=[tPS[2], "rz", "bbc"], writes=["t1x"],
                        tag=f"p2 t1 {sfx}i{i}")
                    p.op("vector", lambda e, hl=hl: e.scalar_tensor_tensor(
                        T2[:, :], PS[4][:, 0:128], RZ[:, 1:2],
                        MBC[:, hl * 128:(hl + 1) * 128], ALU.mult, ALU.mult),
                        reads=[tPS[4], "rz", "mbc"], writes=["t2x"],
                        tag=f"p2 t2 {sfx}i{i}")
                    p.op("vector", lambda e: e.tensor_add(ATT[:, :], T1[:, :], T2[:, :]),
                         reads=["t1x", "t2x"], writes=["attx"], tag=f"p2 cmb {sfx}i{i}")
                    p.op("tensor", lambda e: e.transpose(
                        PSB[:, 512:640], ATT[:, :], ident[:, :]),
                        reads=["attx", "ident"], writes=[tPSB],
                        tag=f"p2 trat {sfx}i{i}")
                    p.op("vector", lambda e, i=i: e.tensor_copy(
                        ATTT[:, i * 128:(i + 1) * 128], PSB[:, 512:640]),
                        reads=[tPSB], writes=["atttx"], tag=f"p2 tratev {sfx}i{i}")
                for j in range(4):
                    p.op("tensor", lambda e, j=j, pr=pr: e.matmul(
                        PS[5][:, 0:129], SK[:, j * 128:(j + 1) * 128],
                        MEMZB[pr][:, :], start=True, stop=True),
                        reads=["ks", ("memzb", pr)], writes=[tPS[5]],
                        tag=f"p2 km {sfx}j{j}")
                    p.op("vector", lambda e: e.reciprocal(RZ[:, 2:3],
                                                          PS[5][:, 128:129]),
                         reads=[tPS[5]], writes=["rz"])
                    p.op("vector", lambda e, j=j: e.scalar_tensor_tensor(
                        DLT[:, j * 128:(j + 1) * 128], PS[5][:, 0:128],
                        RZ[:, 2:3], VT[:, j * 128:(j + 1) * 128],
                        ALU.mult, ALU.subtract),
                        reads=[tPS[5], "rz", "vt"], writes=["dlt"],
                        tag=f"p2 dlt {sfx}j{j}")
                for j in range(4):
                    p.op("tensor", lambda e, j=j: e.matmul(
                        PS[6][:, 0:128], SKT[:, j * 128:(j + 1) * 128],
                        DLT[:, j * 128:(j + 1) * 128],
                        start=(j == 0), stop=(j == 3)),
                        reads=["skt", "dlt"], writes=[tPS[6]],
                        tag=f"p2 upd {sfx}j{j}")
                p.op("vector", lambda e, pr=pr: e.tensor_sub(
                    MEMZ[pr][:, 0:128], MEMZ[pr][:, 0:128], PS[6][:, 0:128]),
                    reads=[tPS[6], ("memz", pr)], writes=[("memz", pr)],
                    tag=f"p2 memup {sfx}")
                p.op("vector", lambda e: e.tensor_reduce(
                    RZ[:, 3:4], SK[:, :], AX.X, ALU.add),
                    reads=["ks"], writes=["rz"], tag=f"p2 zsum {sfx}")
                p.op("vector", lambda e, pr=pr: e.tensor_add(
                    MEMZ[pr][:, 128:129], MEMZ[pr][:, 128:129], RZ[:, 3:4]),
                    reads=["rz", ("memz", pr)], writes=[("memz", pr)],
                    tag=f"p2 zup {sfx}")
                p.op("scalar", lambda e, pr=pr: e.copy(MEMZB[pr][:, :], MEMZ[pr][:, :]),
                     reads=[("memz", pr)], writes=[("memzb", pr)],
                     tag=f"p2 memzb {sfx}")
                if getattr(b, "fused", False):
                    eng = "sync" if pr % 2 == 0 else "scalar"
                    ap = bass.AP(att_s, (seg // 4) * NATQ
                                 + ((bb * 2 + hl) * 4 + seg % 4) * (128 * 512),
                                 [[512, 128], [1, 512]])
                    p.op(eng, lambda e, ap=ap: e.dma_start(
                        out=ap, in_=ATTT[:, :]),
                        reads=["atttx"], writes=[("attout", pr, seg)],
                        dma=("p2att", eng), tag=f"p2 attst {sfx}")
                else:
                    eng = "sync" if pr % 2 == 0 else "gpsimd"
                    p.op(eng, lambda e, hl=hl, bb=bb, base=base, seg=seg: e.dma_start(
                        out=att_s[hl, bb, :, seg * SEG:(seg + 1) * SEG],
                        in_=ATTT[:, :]),
                        reads=["atttx"], writes=[("attout", pr, seg)],
                        dma=("p2att", eng), tag=f"p2 attst {sfx}")
        if getattr(b, "att_qdone", None) is not None and seg % 4 == 3:
            b.att_qdone(seg // 4)


def build_phase2_v0():
    nc = bass.Bass("TRN2", target_bir_lowering=False, debug=False)
    h_full = nc.dram_tensor("h_full", [NCH, 128, B * S], BF16, kind="ExternalInput")
    wqkv = nc.dram_tensor("wqkv", [D, 768], F32, kind="ExternalInput")
    betas_s = nc.dram_tensor("betas_s", [HL, 128], F32, kind="ExternalInput")
    att_s = nc.dram_tensor("att_s", [HL, B, 128, S], BF16, kind="ExternalOutput")
    p = Prog(nc)
    b = Bld(nc, p)
    b.PS = [b.ps(f"p2_ps{k}", (128, 512), F32) for k in range(7)]
    b.tPS = [("PS", k) for k in range(7)]
    b.PSB = b.ps("p2_psb", (128, 1024), BF16)
    b.tPSB = ("PSB",)
    onr = b.sb("p2_ones_row", (1, 128), BF16)
    p.op("vector", lambda e: e.memset(onr[:, :], 1.0), writes=["p3or"])
    b.ones_row = onr
    emit_phase2(b, h_full, wqkv, betas_s, att_s)
    p.build()
    return nc



def emit_phase0(b, w_up, w_dn, wup_sl, wdn_sl, wo_sl, wmod_sl, cT,
                mod_part, wqup_sl, wqdn_sl, wob_sl):
    nc, p = b.nc, b.p
    PS, tPS = b.PS, b.tPS

    onesf = b.sb("p0_onesf", (128, 1), F32)
    p.op("vector", lambda e: e.memset(onesf[:, :], 1.0), writes=["p0of"])

    SC = b.sb("p0_sc", (128, 2 * NCH), BF16)
    SCF = b.sb("p0_scf", (128, 2 * NCH), F32)
    WM = [b.sb(f"p0_wm{i}", (128, 1536), BF16) for i in range(2)]
    WMF = [b.sb(f"p0_wmf{i}", (128, 1536), F32) for i in range(2)]
    MODE = b.sb("p0_mode", (2, 1536), F32)
    for c in range(NCH):
        p.op("sync", lambda e, c=c: e.dma_start(
            out=SCF[:, c * 2:(c + 1) * 2], in_=cT[c, :, :]),
            writes=["p0scf"], dma=("p0c",), tag=f"p0 cload {c}")
    p.op("scalar", lambda e: e.activation(SC[:, :], SCF[:, :], AF.Silu),
         reads=["p0scf"], writes=["p0sc"], tag="p0 silu")
    for c in range(NCH):
        par = c % 2
        p.op("sync", lambda e, c=c, par=par: e.dma_start(
            out=WMF[par][:, :], in_=wmod_sl[c * 128:(c + 1) * 128, :]),
            writes=[("p0wmf", par)], dma=("p0wm", par), tag=f"p0 wmload {c}")
        p.op("scalar", lambda e, par=par: e.copy(WM[par][:, :], WMF[par][:, :]),
             reads=[("p0wmf", par)], writes=[("p0wm", par)], tag=f"p0 wmcast {c}")
        for n3 in range(3):
            p.op("tensor", lambda e, c=c, n3=n3, par=par: e.matmul(
                PS[n3][0:2, :], SC[:, c * 2:(c + 1) * 2],
                WM[par][:, n3 * 512:(n3 + 1) * 512],
                start=(c == 0), stop=(c == NCH - 1)),
                reads=["p0sc", ("p0wm", par)], writes=[tPS[n3]],
                tag=f"p0 modmm c{c}n{n3}")
    for n3 in range(3):
        p.op("vector", lambda e, n3=n3: e.tensor_copy(
            MODE[0:2, n3 * 512:(n3 + 1) * 512], PS[n3][0:2, :]),
            reads=[tPS[n3]], writes=["p0mode"], tag=f"p0 modev {n3}")
    p.op("sync", lambda e: e.dma_start(out=mod_part[:, :], in_=MODE[0:2, :]),
         reads=["p0mode"], writes=["p0modout"], dma=("p0mo",), tag="p0 modst")

    WT_ = [b.sb(f"p0_wt{i}", (128, 4096), F32) for i in range(2)]
    ACC = b.sb("p0_acc", (128, 2), F32)
    SCL = b.sb("p0_scl", (128, 4), F32)
    red = b.sb("p0_red", (128, 1), F32)
    p.op("vector", lambda e: e.memset(ACC[:, :], 0.0), writes=["p0acc"])
    for wi, (wfull, nrow, ncol) in enumerate([(w_up, FF, D), (w_dn, D, FF)]):
        ntile = nrow // 128
        for t in range(ntile):
            par = t % 2
            p.op("sync", lambda e, t=t, par=par, wfull=wfull, ncol=ncol: e.dma_start(
                out=WT_[par][:, 0:ncol], in_=wfull[t * 128:(t + 1) * 128, :]),
                writes=[("p0wt", par)], dma=("p0wr", par), tag=f"p0 wstat {wi}t{t}")
            p.op("vector", lambda e, par=par, ncol=ncol: e.tensor_reduce(
                red[:, :], WT_[par][:, 0:ncol], AX.X, ALU.max if False else ALU.add,
                apply_absolute_value=True),
                reads=[("p0wt", par)], writes=["p0red"], tag=f"p0 wred {wi}t{t}")
            p.op("vector", lambda e, wi=wi: e.tensor_add(
                ACC[:, wi:wi + 1], ACC[:, wi:wi + 1], red[:, :]),
                reads=["p0red", "p0acc"], writes=["p0acc"], tag=f"p0 wacc {wi}t{t}")
    p.op("tensor", lambda e: e.matmul(PS[3][0:1, 0:2], onesf[:, :], ACC[:, :],
                                      start=True, stop=True),
         reads=["p0acc", "p0of"], writes=[tPS[3]], tag="p0 wsum")
    p.op("vector", lambda e: e.tensor_scalar(
        PS[3][0:1, 2:4], PS[3][0:1, 0:2], 1.0 / (FF * D), EPS_Q, ALU.mult, ALU.max),
        reads=[tPS[3]], writes=[tPS[3]], tag="p0 wscale")
    p.op("vector", lambda e: e.tensor_copy(SCL[0:1, 0:2], PS[3][0:1, 2:4]),
         reads=[tPS[3]], writes=["p0scl"], tag="p0 sclev")
    onesfr = b.sb("p0_onesfr", (1, 128), F32)
    p.op("vector", lambda e: e.memset(onesfr[:, :], 1.0), writes=["p0ofr"])
    p.op("tensor", lambda e: e.matmul(PS[4][:, 0:2], onesfr[:, :], SCL[0:1, 0:2],
                                      start=True, stop=True),
         reads=["p0scl", "p0ofr"], writes=[tPS[4]], tag="p0 sbc")
    p.op("vector", lambda e: e.tensor_copy(SCL[:, 0:2], PS[4][:, 0:2]),
         reads=[tPS[4]], writes=["p0scl"])
    p.op("vector", lambda e: e.reciprocal(SCL[:, 2:4], SCL[:, 0:2]),
         reads=["p0scl"], writes=["p0scl"])

    QT = b.sb("p0_qt", (128, 4096), F32)
    QA = b.sb("p0_qa", (128, 4096), F32)
    QO = b.sb("p0_qo", (128, 4096), BF16)
    for wi, (wsl, wqout, nrow, ncol) in enumerate(
            [(wup_sl, wqup_sl, FF // 8, D), (wdn_sl, wqdn_sl, D // 8, FF)]):
        ntile = nrow // 128
        for t in range(ntile):
            par = t % 2
            p.op("sync", lambda e, t=t, par=par, wsl=wsl, ncol=ncol: e.dma_start(
                out=WT_[par][:, 0:ncol], in_=wsl[t * 128:(t + 1) * 128, :]),
                writes=[("p0wt", par)], dma=("p0wq", par), tag=f"p0 wqload {wi}t{t}")
            p.op("vector", lambda e, par=par, wi=wi, ncol=ncol: e.tensor_scalar(
                QT[:, 0:ncol], WT_[par][:, 0:ncol], SCL[:, 2 + wi:3 + wi], None,
                ALU.mult),
                reads=[("p0wt", par), "p0scl"], writes=["p0qt"],
                tag=f"p0 wqs {wi}t{t}")
            p.op("vector", lambda e, ncol=ncol: e.tensor_single_scalar(
                QA[:, 0:ncol], QT[:, 0:ncol], 0.5, ALU.is_gt),
                reads=["p0qt"], writes=["p0qa"], tag=f"p0 wqa {wi}t{t}")
            p.op("vector", lambda e, ncol=ncol: e.tensor_single_scalar(
                QT[:, 0:ncol], QT[:, 0:ncol], -0.5, ALU.is_lt),
                reads=["p0qt"], writes=["p0qt"], tag=f"p0 wqb {wi}t{t}")
            p.op("vector", lambda e, ncol=ncol: e.tensor_sub(
                QA[:, 0:ncol], QA[:, 0:ncol], QT[:, 0:ncol]),
                reads=["p0qa", "p0qt"], writes=["p0qa"], tag=f"p0 wqc {wi}t{t}")
            p.op("vector", lambda e, wi=wi, ncol=ncol: e.tensor_scalar(
                QO[:, 0:ncol], QA[:, 0:ncol], SCL[:, wi:wi + 1], None, ALU.mult),
                reads=["p0qa", "p0scl"], writes=["p0qo"], tag=f"p0 wqd {wi}t{t}")
            p.op("gpsimd", lambda e, t=t, wqout=wqout, ncol=ncol: e.dma_start(
                out=wqout[t * 128:(t + 1) * 128, :], in_=QO[:, 0:ncol]),
                reads=["p0qo"], writes=[("p0qout", wi, t)], dma=("p0qst",),
                tag=f"p0 wqst {wi}t{t}")

    for t in range(2):
        par = t % 2
        p.op("sync", lambda e, t=t, par=par: e.dma_start(
            out=WT_[par][:, 0:D], in_=wo_sl[t * 128:(t + 1) * 128, :]),
            writes=[("p0wt", par)], dma=("p0wo", par), tag=f"p0 woload {t}")
        p.op("scalar", lambda e, par=par: e.copy(QO[:, 0:D], WT_[par][:, 0:D]),
             reads=[("p0wt", par)], writes=["p0qo"], tag=f"p0 wocast {t}")
        p.op("gpsimd", lambda e, t=t: e.dma_start(
            out=wob_sl[t * 128:(t + 1) * 128, :], in_=QO[:, 0:D]),
            reads=["p0qo"], writes=[("p0woout", t)], dma=("p0qst",),
            tag=f"p0 wost {t}")


def build_phase0_v0():
    nc = bass.Bass("TRN2", target_bir_lowering=False, debug=False)
    w_up = nc.dram_tensor("w_up", [FF, D], F32, kind="ExternalInput")
    w_dn = nc.dram_tensor("w_dn", [D, FF], F32, kind="ExternalInput")
    wup_sl = nc.dram_tensor("wup_sl", [FF // 8, D], F32, kind="ExternalInput")
    wdn_sl = nc.dram_tensor("wdn_sl", [D // 8, FF], F32, kind="ExternalInput")
    wo_sl = nc.dram_tensor("wo_sl", [D // 8, D], F32, kind="ExternalInput")
    wmod_sl = nc.dram_tensor("wmod_sl", [D, 1536], F32, kind="ExternalInput")
    cT = nc.dram_tensor("cT", [NCH, 128, 2], F32, kind="ExternalInput")
    mod_part = nc.dram_tensor("mod_part", [2, 1536], F32, kind="ExternalOutput")
    wqup_sl = nc.dram_tensor("wqup_sl", [FF // 8, D], BF16, kind="ExternalOutput")
    wqdn_sl = nc.dram_tensor("wqdn_sl", [D // 8, FF], BF16, kind="ExternalOutput")
    wob_sl = nc.dram_tensor("wob_sl", [D // 8, D], BF16, kind="ExternalOutput")
    p = Prog(nc)
    b = Bld(nc, p)
    b.PS = [b.ps(f"p0_ps{k}", (128, 512), F32) for k in range(5)]
    b.tPS = [("PS", k) for k in range(5)]
    emit_phase0(b, w_up, w_dn, wup_sl, wdn_sl, wo_sl, wmod_sl, cT,
                mod_part, wqup_sl, wqdn_sl, wob_sl)
    p.build()
    return nc



_RUN = [None]


def _runner(nc, in_maps):
    if _RUN[0] is not None:
        return _RUN[0](nc, in_maps)
    return [dict(r) for r in
            run_bass_kernel_spmd(nc, in_maps, core_ids=list(range(NCORE))).results]


def kernel(L0_attn_x, L0_attn_Wq, L0_attn_Wk, L0_attn_Wv, L0_attn_Wo,
           L0_attn_betas, L0_ffn_w_up, L0_ffn_w_down, L0_c, L0_wmod):
    x = np.ascontiguousarray(np.asarray(L0_attn_x, np.float32))
    Wq = np.asarray(L0_attn_Wq, np.float32)
    Wk = np.asarray(L0_attn_Wk, np.float32)
    Wv = np.asarray(L0_attn_Wv, np.float32)
    Wo = np.asarray(L0_attn_Wo, np.float32)
    betas = np.asarray(L0_attn_betas, np.float32)
    w_up = np.ascontiguousarray(np.asarray(L0_ffn_w_up, np.float32))
    w_dn = np.ascontiguousarray(np.asarray(L0_ffn_w_down, np.float32))
    cc = np.asarray(L0_c, np.float32)
    wmod = np.ascontiguousarray(np.asarray(L0_wmod, np.float32))

    cT = np.ascontiguousarray(cc.T).reshape(NCH, 128, 2)

    nc0 = build_phase0_v0()
    in0 = []
    for g in range(NCORE):
        in0.append({
            "w_up": w_up, "w_dn": w_dn,
            "wup_sl": np.ascontiguousarray(w_up[g * FF // 8:(g + 1) * FF // 8]),
            "wdn_sl": np.ascontiguousarray(w_dn[g * D // 8:(g + 1) * D // 8]),
            "wo_sl": np.ascontiguousarray(Wo[g * D // 8:(g + 1) * D // 8]),
            "wmod_sl": np.ascontiguousarray(wmod[:, g * 1536:(g + 1) * 1536]),
            "cT": cT,
        })
    r0 = _runner(nc0, in0)
    mod = np.concatenate([r0[g]["mod_part"] for g in range(NCORE)], axis=1)
    wq_up = np.concatenate([np.asarray(r0[g]["wqup_sl"]).view(BFNP)
                            for g in range(NCORE)], axis=0)
    wq_dn = np.concatenate([np.asarray(r0[g]["wqdn_sl"]).view(BFNP)
                            for g in range(NCORE)], axis=0)
    wo_b = np.concatenate([np.asarray(r0[g]["wob_sl"]).view(BFNP)
                           for g in range(NCORE)], axis=0)

    nc1 = build_phase1_v0()
    in1 = []
    for g in range(NCORE):
        bb, s0 = g // 4, (g % 4) * TC
        xT = np.ascontiguousarray(x[bb, s0:s0 + TC, :].T).reshape(NCH, 128, TC)
        in1.append({"xT": xT, "modrow": np.ascontiguousarray(mod[bb:bb + 1])})
    r1 = _runner(nc1, in1)
    h_full = np.concatenate(
        [np.asarray(r1[g]["h_out"]).view(BFNP) for g in range(NCORE)], axis=2)

    nc2 = build_phase2_v0()
    in2 = []
    for g in range(NCORE):
        wqkv = np.empty((D, 768), np.float32)
        for hl in range(HL):
            hh = HL * g + hl
            wqkv[:, 0 * 256 + hl * 128:0 * 256 + (hl + 1) * 128] = Wq[:, hh * 128:(hh + 1) * 128]
            wqkv[:, 1 * 256 + hl * 128:1 * 256 + (hl + 1) * 128] = Wk[:, hh * 128:(hh + 1) * 128]
            wqkv[:, 2 * 256 + hl * 128:2 * 256 + (hl + 1) * 128] = Wv[:, hh * 128:(hh + 1) * 128]
        bet = np.ascontiguousarray(betas[0, HL * g:HL * g + HL, 0, :])
        in2.append({"h_full": h_full, "wqkv": wqkv, "betas_s": bet})
    r2 = _runner(nc2, in2)

    wup_im = wq_up.reshape(FCH, 128, NCH, 128)
    wup_pcj = np.ascontiguousarray(
        wup_im.transpose(0, 3, 2, 1).reshape(FCH, 128, NCH * 128))
    wdnT = np.ascontiguousarray(wq_dn.T)
    nc3 = build_phase3_v0()
    in3 = []
    att_by_head = [np.asarray(r2[hh // HL]["att_s"]).view(BFNP)[hh % HL]
                   for hh in range(H)]
    for g in range(NCORE):
        bb, s0 = g // 4, (g % 4) * TC
        A = np.stack([a[bb] for a in att_by_head], axis=0)
        A5 = A.reshape(H, 128, NSEG, 32, 16)
        seg0 = s0 // SEG
        Ds = A5[:, :, seg0:seg0 + TC // SEG, :, :]
        attT = np.ascontiguousarray(
            Ds.transpose(4, 1, 2, 0, 3).reshape(NCH, 128, TC))
        in3.append({
            "x_rows": np.ascontiguousarray(x[bb, s0:s0 + TC, :]),
            "attT": attT, "wo_bf": wo_b, "wup_pcj": wup_pcj, "wdnT": wdnT,
            "modrow": np.ascontiguousarray(mod[bb:bb + 1]),
        })
    r3 = _runner(nc3, in3)
    out = np.empty((B, S, D), np.float32)
    for g in range(NCORE):
        bb, s0 = g // 4, (g % 4) * TC
        out[bb, s0:s0 + TC, :] = r3[g]["out_rows"]
    return out



def emit_phase0_fused(b, w_up, w_dn, wup_sl, wdn_sl, wo_sl, wmod_sl, cT,
                      mode_l, modag, wupsrc, wdnsrc, wosrc):
    nc, p = b.nc, b.p
    PS, tPS, PSB, tPSB = b.PS, b.tPS, b.PSB, b.tPSB
    ident = b.need_ident()

    onesf = b.sb("p0_onesf", (128, 1), F32)
    p.op("vector", lambda e: e.memset(onesf[:, :], 1.0), writes=["p0of"])

    SC = b.sb("p0_sc", (128, 2 * NCH), BF16)
    SCF = b.sb("p0_scf", (128, 2 * NCH), F32)
    WT_ = b.sb("p0_wt", (128, 8192), F32)
    WM = [b.sb(f"p0_wm{i}", (128, 1536), BF16) for i in range(2)]
    MODE = b.sb("p0_mode", (2, 1536), F32)
    for c in range(NCH):
        p.op("sync", lambda e, c=c: e.dma_start(
            out=SCF[:, c * 2:(c + 1) * 2], in_=cT[c, :, :]),
            writes=["p0scf"], dma=("p0c",), tag=f"p0 cload {c}")
    p.op("scalar", lambda e: e.activation(SC[:, :], SCF[:, :], AF.Silu),
         reads=["p0scf"], writes=["p0sc"], tag="p0 silu")
    for c in range(NCH):
        par = c % 2
        p.op("sync", lambda e, c=c, par=par: e.dma_start(
            out=WT_[:, par * 4096:par * 4096 + 1536],
            in_=wmod_sl[c * 128:(c + 1) * 128, :]),
            writes=[("p0wt", par)], dma=("p0wm", par), tag=f"p0 wmload {c}")
        p.op("scalar", lambda e, par=par: e.copy(
            WM[par][:, :], WT_[:, par * 4096:par * 4096 + 1536]),
            reads=[("p0wt", par)], writes=[("p0wm", par)], tag=f"p0 wmcast {c}")
        for n3 in range(3):
            p.op("tensor", lambda e, c=c, n3=n3, par=par: e.matmul(
                PS[n3][0:2, :], SC[:, c * 2:(c + 1) * 2],
                WM[par][:, n3 * 512:(n3 + 1) * 512],
                start=(c == 0), stop=(c == NCH - 1)),
                reads=["p0sc", ("p0wm", par)], writes=[tPS[n3]],
                tag=f"p0 modmm c{c}n{n3}")
    for n3 in range(3):
        p.op("vector", lambda e, n3=n3: e.tensor_copy(
            MODE[0:2, n3 * 512:(n3 + 1) * 512], PS[n3][0:2, :]),
            reads=[tPS[n3]], writes=["p0mode"], tag=f"p0 modev {n3}")
    p.op("sync", lambda e: e.dma_start(out=mode_l[:, :], in_=MODE[0:2, :]),
         reads=["p0mode"], writes=["p0modl"], dma=("p0ml",), tag="p0 modst")
    p.op("gpsimd", lambda e: e.collective_compute(
        "AllGather", ALU.bypass, RG8, [mode_l[:, :]], [modag[:, :, :]]),
        reads=["p0modl"], writes=["modag"], mark=True, tag="p0 modag")

    ACC = b.sb("p0_acc", (128, 2), F32)
    SCL = b.sb("p0_scl", (128, 4), F32)
    red = b.sb("p0_red", (128, 1), F32)
    p.op("vector", lambda e: e.memset(ACC[:, :], 0.0), writes=["p0acc"])
    for wi, (wfull, nrow, ncol) in enumerate([(w_up, FF, D), (w_dn, D, FF)]):
        ntile = nrow // 128
        for t in range(ntile):
            par = t % 2
            p.op("sync", lambda e, t=t, par=par, wfull=wfull, ncol=ncol: e.dma_start(
                out=WT_[:, par * 4096:par * 4096 + ncol],
                in_=wfull[t * 128:(t + 1) * 128, :]),
                writes=[("p0wt", par)], dma=("p0wr", par), tag=f"p0 wstat {wi}t{t}")
            p.op("vector", lambda e, par=par, ncol=ncol: e.tensor_reduce(
                red[:, :], WT_[:, par * 4096:par * 4096 + ncol], AX.X, ALU.add,
                apply_absolute_value=True),
                reads=[("p0wt", par)], writes=["p0red"], tag=f"p0 wred {wi}t{t}")
            p.op("vector", lambda e, wi=wi: e.tensor_add(
                ACC[:, wi:wi + 1], ACC[:, wi:wi + 1], red[:, :]),
                reads=["p0red", "p0acc"], writes=["p0acc"], tag=f"p0 wacc {wi}t{t}")
    p.op("tensor", lambda e: e.matmul(PS[3][0:1, 0:2], onesf[:, :], ACC[:, :],
                                      start=True, stop=True),
         reads=["p0acc", "p0of"], writes=[tPS[3]], tag="p0 wsum")
    p.op("vector", lambda e: e.tensor_scalar(
        PS[3][0:1, 2:4], PS[3][0:1, 0:2], 1.0 / (FF * D), EPS_Q, ALU.mult, ALU.max),
        reads=[tPS[3]], writes=[tPS[3]], tag="p0 wscale")
    p.op("vector", lambda e: e.tensor_copy(SCL[0:1, 0:2], PS[3][0:1, 2:4]),
         reads=[tPS[3]], writes=["p0scl"], tag="p0 sclev")
    onesfr = b.sb("p0_onesfr", (1, 128), F32)
    p.op("vector", lambda e: e.memset(onesfr[:, :], 1.0), writes=["p0ofr"])
    p.op("tensor", lambda e: e.matmul(PS[4][:, 0:2], onesfr[:, :], SCL[0:1, 0:2],
                                      start=True, stop=True),
         reads=["p0scl", "p0ofr"], writes=[tPS[4]], tag="p0 sbc")
    p.op("vector", lambda e: e.tensor_copy(SCL[:, 0:2], PS[4][:, 0:2]),
         reads=[tPS[4]], writes=["p0scl"])
    p.op("vector", lambda e: e.reciprocal(SCL[:, 2:4], SCL[:, 0:2]),
         reads=["p0scl"], writes=["p0scl"])

    QT = b.sb("p0_qt", (128, 4096), F32)
    QA = b.sb("p0_qa", (128, 4096), F32)
    QO = [b.sb(f"p0_qo{i}", (128, 4096), BF16) for i in range(2)]
    IMG = [b.sb(f"p0_img{i}", (128, 2048), BF16) for i in range(2)]

    def quant_tile(wi, t, par, wsl, ncol, tag):
        p.op("sync", lambda e, t=t, par=par, wsl=wsl, ncol=ncol: e.dma_start(
            out=WT_[:, par * 4096:par * 4096 + ncol],
            in_=wsl[t * 128:(t + 1) * 128, :]),
            writes=[("p0wt", par)], dma=("p0wq", par), tag=f"{tag} load")
        p.op("vector", lambda e, par=par, wi=wi, ncol=ncol: e.tensor_scalar(
            QT[:, 0:ncol], WT_[:, par * 4096:par * 4096 + ncol],
            SCL[:, 2 + wi:3 + wi], None, ALU.mult),
            reads=[("p0wt", par), "p0scl"], writes=["p0qt"], tag=f"{tag} s")
        p.op("vector", lambda e, ncol=ncol: e.tensor_single_scalar(
            QA[:, 0:ncol], QT[:, 0:ncol], 0.5, ALU.is_gt),
            reads=["p0qt"], writes=["p0qa"], tag=f"{tag} a")
        p.op("vector", lambda e, ncol=ncol: e.tensor_single_scalar(
            QT[:, 0:ncol], QT[:, 0:ncol], -0.5, ALU.is_lt),
            reads=["p0qt"], writes=["p0qt"], tag=f"{tag} b")
        p.op("vector", lambda e, ncol=ncol: e.tensor_sub(
            QA[:, 0:ncol], QA[:, 0:ncol], QT[:, 0:ncol]),
            reads=["p0qa", "p0qt"], writes=["p0qa"], tag=f"{tag} c")
        p.op("vector", lambda e, wi=wi, par=par, ncol=ncol: e.tensor_scalar(
            QO[par][:, 0:ncol], QA[:, 0:ncol], SCL[:, wi:wi + 1], None, ALU.mult),
            reads=["p0qa", "p0scl"], writes=[("p0qo", par)], tag=f"{tag} d")

    for t in range(4):
        par = t % 2
        quant_tile(0, t, par, wup_sl, D, f"p0 up t{t}")
        for c in range(NCH):
            p.op("tensor", lambda e, c=c, par=par: e.transpose(
                PSB[:, (c % 4) * 128:(c % 4 + 1) * 128],
                QO[par][:, c * 128:(c + 1) * 128], ident[:, :]),
                reads=[("p0qo", par), "ident"], writes=[tPSB],
                tag=f"p0 uptr t{t}c{c}")
            p.op("vector", lambda e, c=c, par=par: e.tensor_copy(
                IMG[par][:, c * 128:(c + 1) * 128],
                PSB[:, (c % 4) * 128:(c % 4 + 1) * 128]),
                reads=[tPSB], writes=[("p0img", par)], tag=f"p0 upev t{t}c{c}")
        p.op("sync", lambda e, t=t, par=par: e.dma_start(
            out=wupsrc[t, :, :], in_=IMG[par][:, :]),
            reads=[("p0img", par)], writes=[("wupsrc", t)], dma=("p0st", par),
            tag=f"p0 upst t{t}")

    for t in range(2):
        quant_tile(1, t, t, wdn_sl, FF, f"p0 dn t{t}")
    for fc in range(FCH):
        par = fc % 2
        for t in range(2):
            p.op("tensor", lambda e, fc=fc, t=t, par=par: e.transpose(
                PSB[:, (t + 2 * (fc % 2)) * 128:(t + 2 * (fc % 2) + 1) * 128],
                QO[t][:, fc * 128:(fc + 1) * 128], ident[:, :]),
                reads=[("p0qo", t), "ident"], writes=[tPSB],
                tag=f"p0 dntr f{fc}t{t}")
            p.op("vector", lambda e, fc=fc, t=t, par=par: e.tensor_copy(
                IMG[par][:, t * 128:(t + 1) * 128],
                PSB[:, (t + 2 * (fc % 2)) * 128:(t + 2 * (fc % 2) + 1) * 128]),
                reads=[tPSB], writes=[("p0img", par)], tag=f"p0 dnev f{fc}t{t}")
        p.op("sync", lambda e, fc=fc, par=par: e.dma_start(
            out=wdnsrc[fc * 128:(fc + 1) * 128, :], in_=IMG[par][:, 0:256]),
            reads=[("p0img", par)], writes=[("wdnsrc", fc)], dma=("p0st", par),
            tag=f"p0 dnst f{fc}")

    for t in range(2):
        par = t % 2
        p.op("sync", lambda e, t=t, par=par: e.dma_start(
            out=WT_[:, par * 4096:par * 4096 + D],
            in_=wo_sl[t * 128:(t + 1) * 128, :]),
            writes=[("p0wt", par)], dma=("p0wo", par), tag=f"p0 woload {t}")
        p.op("scalar", lambda e, par=par: e.copy(
            IMG[par][:, :], WT_[:, par * 4096:par * 4096 + D]),
            reads=[("p0wt", par)], writes=[("p0img", par)], tag=f"p0 wocast {t}")
        p.op("sync", lambda e, t=t, par=par: e.dma_start(
            out=wosrc[t, :, :], in_=IMG[par][:, :]),
            reads=[("p0img", par)], writes=[("wosrc", t)], dma=("p0st", par),
            tag=f"p0 wost {t}")


def emit_att_ag_q(b, atq_src, atg, q):
    p = b.p
    toks = [("attout", pr, seg)
            for pr in range(4) for seg in range(4 * q, 4 * q + 4)]
    p.op("gpsimd", lambda e, q=q: e.collective_compute(
        "AllGather", ALU.bypass, RG8, [atq_src[q, :]], [atg[q, :, :]]),
        reads=toks, writes=[("atag", q)], mark=True, tag=f"p2 atag q{q}")


def build_fused():
    nc = bass.Bass("TRN2", target_bir_lowering=False, debug=False,
                   dynamic_dma_scratch_size=8192)
    xT = nc.dram_tensor("xT", [NCH, 128, TC], F32, kind="ExternalInput")
    x_rows = nc.dram_tensor("x_rows", [TC, D], F32, kind="ExternalInput")
    wqkv = nc.dram_tensor("wqkv", [D, 768], F32, kind="ExternalInput")
    betas_s = nc.dram_tensor("betas_s", [HL, 128], F32, kind="ExternalInput")
    w_up = nc.dram_tensor("w_up", [FF, D], F32, kind="ExternalInput")
    w_dn = nc.dram_tensor("w_dn", [D, FF], F32, kind="ExternalInput")
    wup_sl = nc.dram_tensor("wup_sl", [FF // 8, D], F32, kind="ExternalInput")
    wdn_sl = nc.dram_tensor("wdn_sl", [D // 8, FF], F32, kind="ExternalInput")
    wo_sl = nc.dram_tensor("wo_sl", [D // 8, D], F32, kind="ExternalInput")
    wmod_sl = nc.dram_tensor("wmod_sl", [D, 1536], F32, kind="ExternalInput")
    cT = nc.dram_tensor("cT", [NCH, 128, 2], F32, kind="ExternalInput")
    bflag = nc.dram_tensor("bflag", [1, 1], F32, kind="ExternalInput")
    out_rows = nc.dram_tensor("out_rows", [TC, D], F32, kind="ExternalOutput")
    onehot8 = nc.dram_tensor("onehot8", [1, 8], F32, kind="ExternalInput")
    mode_l = nc.dram_tensor("mode_l", [2, 1536], F32, kind="Internal")
    modag = nc.dram_tensor("modag", [8, 2, 1536], F32, kind="Internal")
    hsrc = nc.dram_tensor("hsrc", [4, 128, NCH * WT], BF16, kind="Internal")
    hga = nc.dram_tensor("hga", [4, 8, 128, NCH * WT], BF16, kind="Internal")
    atq_src = nc.dram_tensor("atq_src", [4, NATQ], BF16, kind="Internal")
    atg = nc.dram_tensor("atg", [4, 8, NATQ], BF16, kind="Internal")
    wupsrc = nc.dram_tensor("wupsrc", [4, 128, D], BF16, kind="Internal")
    wupga = nc.dram_tensor("wupga", [8, 4, 128, D], BF16, kind="Internal")
    wdnsrc = nc.dram_tensor("wdnsrc", [FF, 256], BF16, kind="Internal")
    wdnga = nc.dram_tensor("wdnga", [8, FF, 256], BF16, kind="Internal")
    wosrc = nc.dram_tensor("wosrc", [2, 128, D], BF16, kind="Internal")
    woga = nc.dram_tensor("woga", [8, 2, 128, D], BF16, kind="Internal")

    p = Prog(nc)
    b = Bld(nc, p)
    b.fused = True
    b.modag = modag

    A_f32a = nc.alloc_sbuf_tensor("A_f32a", [128, 8192], F32)
    A_bf1 = nc.alloc_sbuf_tensor("A_bf1", [128, 16384], BF16)
    A_bf2 = nc.alloc_sbuf_tensor("A_bf2", [128, 8192], BF16)
    A_bf3 = nc.alloc_sbuf_tensor("A_bf3", [128, 8192], BF16)
    A_bf4 = nc.alloc_sbuf_tensor("A_bf4", [128, 8192], BF16)
    bf1_f32 = A_bf1[:, :].bitcast(F32)

    SCR4K = nc.alloc_sbuf_tensor("SCR4K", [128, 1024], F32)
    T1S = nc.alloc_sbuf_tensor("T1S", [128, 512], F32)
    T2S = nc.alloc_sbuf_tensor("T2S", [128, 512], F32)
    ROWS = nc.alloc_sbuf_tensor("ROWS", [1, 2048], F32)
    b.share = {
        "p3_atw": A_bf4,
        "p1_mrbc": SCR4K,
        "p3_qu": SCR4K[:, 0:512], "p3_qzi": SCR4K[:, 512:1024].bitcast(I32),
        "p1_t1": T1S, "p3_qzf": T1S,
        "p1_t2": T2S, "p3_qc": T2S,
        "p1_stats": ROWS[0:1, 0:1536],
        "p3_gsc": ROWS[0:1, 0:2048],
        "p0_wt": A_f32a,
        "p0_qt": bf1_f32[:, 0:4096], "p0_qa": bf1_f32[:, 4096:8192],
        "p0_qo0": A_bf4[:, 0:4096], "p0_qo1": A_bf4[:, 4096:8192],
        "p0_img0": A_bf2[:, 0:2048], "p0_img1": A_bf2[:, 2048:4096],
        "p0_wm0": A_bf3[:, 0:1536], "p0_wm1": A_bf3[:, 1536:3072],
        "p1_xstg0": A_f32a, "p1_xstg1": bf1_f32,
        "p1_xb0": A_bf2, "p1_xb1": A_bf3, "p1_xsq": A_bf4,
        "p1_hwin0": A_bf1[:, 0:8192], "p1_hwin1": A_bf1[:, 8192:16384],
        "p2_wstg": A_f32a[:, 0:768],
        "p2_wqkv": A_bf1[:, 0:NCH * 768],
        "p2_ht0": A_bf2, "p2_ht1": A_bf3,
        "p2_q": A_bf4[:, 0:512], "p2_k": A_bf4[:, 512:1024],
        "p2_v": A_bf4[:, 1024:1536], "p2_eq": A_bf4[:, 1536:2048],
        "p2_sq": A_bf4[:, 2048:2560], "p2_sk": A_bf4[:, 2560:3072],
        "p2_vt": A_bf4[:, 3072:3584], "p2_skt": A_bf4[:, 3584:4096],
        "p2_dlt": A_bf4[:, 4096:4608], "p2_psm": A_bf4[:, 4608:5120],
        "p2_pt": A_bf4[:, 5120:7168], "p2_attt": A_bf4[:, 7168:7680],
        "p3_x2t": A_f32a, "p3_fa": A_bf1, "p3_fb": A_bf2, "p3_gb": A_bf3,
        "p3_xq": A_bf4,
    }

    b.PS = [b.ps(f"ps{k}", (128, 512), F32) for k in range(7)]
    b.tPS = [("PS", k) for k in range(7)]
    b.PSB = b.ps("psb", (128, 1024), BF16)
    b.tPSB = ("PSB",)
    onr = b.sb("ones_row_b", (1, 128), BF16)
    p.op("vector", lambda e: e.memset(onr[:, :], 1.0), writes=["p3or"])
    b.ones_row = onr

    FB1 = b.sb("fb1", (1, 1), F32)
    FLAG128 = b.sb("flag128", (128, 1), F32)
    onesfr0 = b.sb("onesfr0", (1, 128), F32)
    p.op("vector", lambda e: e.memset(onesfr0[:, :], 1.0), writes=["of0"])
    p.op("sync", lambda e: e.dma_start(out=FB1[:, :], in_=bflag[:, :]),
         writes=["fb1"], dma=("bfl",))
    p.op("tensor", lambda e: e.matmul(b.PS[5][:, 0:1], onesfr0[:, :], FB1[0:1, :],
                                      start=True, stop=True),
         reads=["fb1", "of0"], writes=[b.tPS[5]])
    p.op("vector", lambda e: e.tensor_copy(FLAG128[:, :], b.PS[5][:, 0:1]),
         reads=[b.tPS[5]], writes=["flag128"])
    b.flag128 = FLAG128
    b.tflag = "flag128"
    b.flag1 = FB1
    b.tflag1 = "fb1"
    b.gsc3 = ROWS[0:1, 0:2048]
    b.gsc3_t = ROWS

    OH = b.sb("oh8", (1, 8), F32)
    GF = b.sb("gf8", (128, 8), F32)
    p.op("sync", lambda e: e.dma_start(out=OH[:, :], in_=onehot8[:, :]),
         writes=["oh8"], dma=("oh8",))
    p.op("tensor", lambda e: e.matmul(b.PS[6][:, 0:8], onesfr0[:, :], OH[0:1, :],
                                      start=True, stop=True),
         reads=["oh8", "of0"], writes=[b.tPS[6]])
    p.op("vector", lambda e: e.tensor_copy(GF[:, :], b.PS[6][:, 0:8]),
         reads=[b.tPS[6]], writes=["gf"])
    b.GF = GF

    emit_phase0_fused(b, w_up, w_dn, wup_sl, wdn_sl, wo_sl, wmod_sl, cT,
                      mode_l, modag, wupsrc, wdnsrc, wosrc)
    p.fence()

    def h_store(w, HW, tHW):
        p.op("sync", lambda e, w=w, HW=HW: e.dma_start(
            out=hsrc[w, :, :], in_=HW[:, :]),
            reads=[tHW], writes=[("hsrc", w)], dma=("p1hs", w % 2),
            tag=f"p1 hst w{w}")
        p.op("gpsimd", lambda e, w=w: e.collective_compute(
            "AllGather", ALU.bypass, RG8, [hsrc[w, :, :]], [hga[w, :, :, :]]),
            reads=[("hsrc", w)], writes=[("hag", w)], mark=True, tag=f"p1 hag w{w}")

    emit_phase1(b, xT, None, h_store)
    p.fence()

    p.op("gpsimd", lambda e: e.collective_compute(
        "AllGather", ALU.bypass, RG8, [wupsrc[:, :, :]], [wupga[:, :, :, :]]),
        reads=[("wupsrc", t) for t in range(4)], writes=["wupag"], mark=True, tag="ag wup")
    p.op("gpsimd", lambda e: e.collective_compute(
        "AllGather", ALU.bypass, RG8, [wdnsrc[:, :]], [wdnga[:, :, :]]),
        reads=[("wdnsrc", fc) for fc in range(FCH)], writes=["wdnag"],
        mark=True, tag="ag wdn")
    p.op("gpsimd", lambda e: e.collective_compute(
        "AllGather", ALU.bypass, RG8, [wosrc[:, :, :]], [woga[:, :, :, :]]),
        reads=[("wosrc", t) for t in range(2)], writes=["woag"], mark=True, tag="ag wo")

    b.att_qdone = lambda q: emit_att_ag_q(b, atq_src, atg, q)
    emit_phase2(b, hga, wqkv, betas_s, atq_src)
    b.att_qdone = None
    p.fence()

    emit_phase3(b, x_rows, atg, woga, wupga, wdnga, None, out_rows)
    p.build()
    return nc


def kernel_fused(L0_attn_x, L0_attn_Wq, L0_attn_Wk, L0_attn_Wv, L0_attn_Wo,
                 L0_attn_betas, L0_ffn_w_up, L0_ffn_w_down, L0_c, L0_wmod):
    x = np.ascontiguousarray(np.asarray(L0_attn_x, np.float32))
    Wq = np.asarray(L0_attn_Wq, np.float32)
    Wk = np.asarray(L0_attn_Wk, np.float32)
    Wv = np.asarray(L0_attn_Wv, np.float32)
    Wo = np.ascontiguousarray(np.asarray(L0_attn_Wo, np.float32))
    betas = np.asarray(L0_attn_betas, np.float32)
    w_up = np.ascontiguousarray(np.asarray(L0_ffn_w_up, np.float32))
    w_dn = np.ascontiguousarray(np.asarray(L0_ffn_w_down, np.float32))
    cc = np.asarray(L0_c, np.float32)
    wmod = np.ascontiguousarray(np.asarray(L0_wmod, np.float32))
    cT = np.ascontiguousarray(cc.T).reshape(NCH, 128, 2)
    wmod_v = wmod.reshape(D, 6, 8, 256)
    nc = build_fused()
    in_maps = []
    for g in range(NCORE):
        bb, s0 = g // 4, (g % 4) * TC
        wqkv_g = np.empty((D, 768), np.float32)
        for hl in range(HL):
            hh = HL * g + hl
            wqkv_g[:, 0 * 256 + hl * 128:0 * 256 + (hl + 1) * 128] = Wq[:, hh * 128:(hh + 1) * 128]
            wqkv_g[:, 1 * 256 + hl * 128:1 * 256 + (hl + 1) * 128] = Wk[:, hh * 128:(hh + 1) * 128]
            wqkv_g[:, 2 * 256 + hl * 128:2 * 256 + (hl + 1) * 128] = Wv[:, hh * 128:(hh + 1) * 128]
        in_maps.append({
            "xT": np.ascontiguousarray(x[bb, s0:s0 + TC, :].T).reshape(NCH, 128, TC),
            "x_rows": np.ascontiguousarray(x[bb, s0:s0 + TC, :]),
            "wqkv": wqkv_g,
            "betas_s": np.ascontiguousarray(betas[0, HL * g:HL * g + HL, 0, :]),
            "w_up": w_up, "w_dn": w_dn,
            "wup_sl": np.ascontiguousarray(w_up[g * FF // 8:(g + 1) * FF // 8]),
            "wdn_sl": np.ascontiguousarray(w_dn[g * D // 8:(g + 1) * D // 8]),
            "wo_sl": np.ascontiguousarray(Wo[g * D // 8:(g + 1) * D // 8]),
            "wmod_sl": np.ascontiguousarray(wmod_v[:, :, g, :].reshape(D, 1536)),
            "cT": cT,
            "bflag": np.array([[float(bb)]], np.float32),
            "onehot8": np.eye(8, dtype=np.float32)[g:g + 1],
        })
    res = _runner(nc, in_maps)
    out = np.empty((B, S, D), np.float32)
    for g in range(NCORE):
        bb, s0 = g // 4, (g % 4) * TC
        out[bb, s0:s0 + TC, :] = res[g]["out_rows"]
    return out


_kernel_v0 = kernel
kernel = kernel_fused


# version 05
# speedup vs baseline: 1.1825x; 1.1630x over previous
import math
import numpy as np
import ml_dtypes
import concourse.bass as bass
import concourse.mybir as mybir
from concourse.bass_utils import run_bass_kernel_spmd

F32 = mybir.dt.float32
BF16 = mybir.dt.bfloat16
I32 = mybir.dt.int32
AF = mybir.ActivationFunctionType
ALU = mybir.AluOpType
AX = mybir.AxisListType
BFNP = ml_dtypes.bfloat16

B, S, D = 2, 8192, 2048
H, DK, DV, SEG, NSEG = 16, 128, 128, 512, 16
NCH, FF, FCH = 16, 4096, 32
NCORE = 8
TC = S * B // NCORE
HL = H // NCORE
NW = 4
WT = TC // NW
EPS_LN = 1e-5
EPS_Q = 1e-5
QB = 127.0
C1 = 1.0 / math.sqrt(D)
C2 = 1.0 / math.sqrt(FF)
RG8 = [[0, 1, 2, 3, 4, 5, 6, 7]]
NATQ = 2 * 2 * 4 * 128 * 512
SCALE_ATT = float(DK ** 0.5)

ENGS = ("sync", "gpsimd", "scalar", "vector", "tensor")
import os as _os
FORCE_SYNC_DMA = _os.environ.get("MK_FORCE_SYNC_DMA", "0") == "1"


def _hadamard(n):
    h = np.array([[1.0]])
    while h.shape[0] < n:
        h = np.block([[h, h], [h, -h]])
    return h


class Prog:

    _fv_ok = frozenset({
        "b", "p", "nc", "e", "xT", "modrow", "h_out", "x_rows", "attT",
        "wo_bf", "wup_pcj", "wdnT", "out_rows", "ident", "H128", "ones_col",
        "ones_row", "catile", "satile", "ca_ap", "sa_ap", "xstg", "xb",
        "xsq", "hwin", "stat_s", "mb", "rb", "mrbc", "t1", "t2", "ps_st",
        "ps_bc", "PS", "PSB", "tPS", "tPSB", "ATW", "XR", "X2T", "H2TOK",
        "H2C", "FA", "FB", "GB", "XQ", "WOS", "WUP", "WDN", "QU", "QZI",
        "QZF", "QC", "QM", "QSW", "QS", "QI", "ST", "OT", "GABC", "GBBC",
        "CBBC", "SBBC", "tGA", "tGB", "tCB", "tSB", "gsc", "row", "rowb",
        "out", "tok", "name", "modoff", "t_eps", "t_half", "wsl", "h_store",
        "gbrow", "gbrow_b", "GBBC0", "width", "src", "tsrc", "dst", "tdst",
        "dst_slices", "cnorm", "tag", "gat", "cbt", "sbt", "h128_dr", "ga_transform", "quant", "tf_ga",
        "tf_cb1", "tf_id", "transform", "XS", "XB", "HW", "nch", "wq_", "HT", "onr", "WSTG", "WQKV", "BROW", "BROWF", "BBC", "MBC", "CMASK", "MEMZ", "MEMZB", "HTB", "Q", "K", "V", "EQ", "SQ", "SK", "VT", "SKT", "DLT", "PT", "PSM", "DEN", "NMS", "RZ", "T1", "T2", "ATT", "ATTT", "mask_dr", "h_full", "wqkv", "betas_s", "att_s", "wb_parts", "wmod_sl", "abs_parts", "SC", "SCF", "WM", "WMF", "MODE", "WT_", "ACC", "SCL", "red", "onesf", "onesfr", "QT", "QA", "QO", "w_up", "w_dn", "wup_sl", "wdn_sl", "wo_sl", "cT", "mod_part", "wqup_sl", "wqdn_sl", "wob_sl", "onesfr0", "FB1", "FLAG128", "IMG", "quant_tile", "rs_mod", "rs_wt", "rs_h", "rs_att", "modg", "wupg", "wdng", "wog", "hg", "attall_loc", "attg2", "sel2", "xspar", "bflag", "store_toks", "modg_v",
        "SELT", "CAND", "hsrc", "hga", "mode_l", "modag", "atq_src", "atg",
        "wupsrc", "wupga", "wdnsrc", "wdnga", "wosrc", "woga", "OH", "GF",
        "onehot8",
    })

    def __init__(self, nc):
        self.nc = nc
        self.ops = []
        self.last_writer = {}
        self.readers = {}

    def fence(self):
        lasts = {}
        dlast = {}
        for i, rec in enumerate(self.ops):
            lasts[rec["engine"]] = i
            if rec["dma"] is not None:
                dlast[rec["dma"]] = i
        deps = set(lasts.values()) | set(dlast.values())
        fids = []
        for e in ENGS:
            i = len(self.ops)
            self.ops.append(dict(engine=e, fn=lambda en: en.nop(), dma=None,
                                 deps=set(deps), marked=False, tag="fence",
                                 extra_waits=(), bcast=False))
            fids.append(i)
        self.last_writer = {t: fids[0] for t in self.last_writer}
        for t in self.readers:
            self.readers[t] = list(fids)

    def op(self, engine, fn, reads=(), writes=(), dma=None, tag="",
           extra_waits=(), bcast=False, mark=False):
        fv = fn.__code__.co_freevars
        bad = [v for v in fv if v not in self._fv_ok]
        if bad:
            raise RuntimeError(f"op fn has late-bound free vars {bad} (tag={tag})")
        if dma is not None and FORCE_SYNC_DMA and engine == "gpsimd":
            engine = "sync"
        i = len(self.ops)
        deps = set()
        for t in reads:
            j = self.last_writer.get(t)
            if j is not None:
                deps.add(j)
        for t in writes:
            j = self.last_writer.get(t)
            if j is not None:
                deps.add(j)
            for r in self.readers.get(t, ()):
                deps.add(r)
        self.ops.append(dict(engine=engine, fn=fn, dma=dma, deps=deps,
                             marked=(dma is not None) or mark, tag=tag,
                             extra_waits=tuple(extra_waits), bcast=bcast))
        for t in reads:
            self.readers.setdefault(t, []).append(i)
        for t in writes:
            self.last_writer[t] = i
            self.readers[t] = []
        return i

    def build(self):
        nc = self.nc
        ops = self.ops
        for rec in ops:
            for j in rec["deps"]:
                o = ops[j]
                o["marked"] = True
        dkeys = []
        seen = set()
        for rec in ops:
            if rec["dma"] is not None and rec["dma"] not in seen:
                seen.add(rec["dma"])
                dkeys.append(rec["dma"])
        self._semctx = []

        def mksem(name):
            cm = nc.semaphore(name)
            h = cm.__enter__()
            self._semctx.append(cm)
            return h

        esem = {e: mksem(f"pe_{e}") for e in ENGS}
        dsem = {k: mksem(f"pd_{i}") for i, k in enumerate(dkeys)}
        ecnt = {e: 0 for e in ENGS}
        dcnt = {k: 0 for k in dkeys}
        for rec in ops:
            rec["dpre"] = None
            if not rec["marked"] and rec["dma"] is None:
                pass
            if rec["dma"] is not None:
                dcnt[rec["dma"]] += 16
                rec["sem"] = ("d", rec["dma"])
                rec["post"] = dcnt[rec["dma"]]
            elif rec["marked"]:
                ecnt[rec["engine"]] += 1
                rec["sem"] = ("e", rec["engine"])
                rec["post"] = ecnt[rec["engine"]]
        self.etotal, self.dtotal = ecnt, dcnt
        run = {k: 0 for k in dkeys}
        for rec in ops:
            rec["dwait"] = {}
            for j in rec["deps"]:
                o = ops[j]
                if o["dma"] is not None:
                    k = o["dma"]
                    rec["dwait"][k] = run[k] + 16
            for k in list(rec["dwait"]):
                rec["dwait"][k] = run[k]
            if rec["dma"] is not None:
                run[rec["dma"]] += 16

        blockcm = nc.Block()
        block = blockcm.__enter__()

        def semh(key):
            return esem[key[1]] if key[0] == "e" else dsem[key[1]]

        def make(engname):
            def body(eng):
                guaranteed = {}
                for rec in ops:
                    if rec["engine"] != engname:
                        continue
                    need = {}
                    for j in rec["deps"]:
                        o = ops[j]
                        if o["dma"] is not None:
                            k = ("d", o["dma"])
                            v = rec["dwait"][o["dma"]]
                            if v > guaranteed.get(k, 0):
                                need[k] = max(need.get(k, 0), v)
                            continue
                        k, v = o["sem"], o["post"]
                        if v > guaranteed.get(k, 0):
                            need[k] = max(need.get(k, 0), v)
                    for k in sorted(need, key=str):
                        eng.wait_ge(semh(k), need[k])
                        guaranteed[k] = need[k]
                    for sh, sv in rec["extra_waits"]:
                        kx = ("x", sh.num)
                        if sv > guaranteed.get(kx, 0):
                            eng.wait_ge(sh, sv)
                            guaranteed[kx] = sv
                    if rec["bcast"]:
                        rec["fn"](eng, dsem[rec["dma"]])
                    else:
                        inst = rec["fn"](eng)
                        if rec["marked"]:
                            inst.then_inc(semh(rec["sem"]),
                                          16 if rec["dma"] is not None else 1)
                if engname == "sync":
                    for k in dkeys:
                        eng.wait_ge(dsem[k], dcnt[k])
                    for e2 in ENGS:
                        if e2 != "sync" and ecnt[e2]:
                            eng.wait_ge(esem[e2], ecnt[e2])
            return body

        block.sync(make("sync"))
        block.gpsimd(make("gpsimd"))
        block.scalar(make("scalar"))
        block.vector(make("vector"))
        block.tensor(make("tensor"))
        blockcm.__exit__(None, None, None)
        for cm in reversed(self._semctx):
            cm.__exit__(None, None, None)


class Bld:

    def __init__(self, nc, p):
        self.nc = nc
        self.p = p
        self.sbuf_bytes = 0
        self._uid = 0
        self._consts = {}
        self.ident = None

    def uid(self):
        self._uid += 1
        return self._uid

    def sb(self, name, shape, dtype):
        sh = getattr(self, "share", None)
        if sh is not None and name in sh:
            return sh[name]
        t = self.nc.alloc_sbuf_tensor(name, list(shape), dtype)
        self.sbuf_bytes += shape[1] * mybir.dt.size(dtype) * 1
        return t

    def ps(self, name, shape, dtype=F32):
        return self.nc.alloc_psum_tensor(name, list(shape), dtype)

    def const_imm(self, val):
        key = ("imm", float(val))
        if key in self._consts:
            return self._consts[key]
        t = self.sb(f"cimm_{self.uid()}", (128, 1), F32)
        self.p.op("vector", lambda e, t=t, v=float(val): e.memset(t[:, :], v),
                  writes=[key], tag="cimm")
        self.nc.const_aps.register(F32, float(val), t[:, 0:1])
        self._consts[key] = key
        return key

    def const_tile(self, val, dtype=F32):
        key = (float(val), dtype)
        if key in self._consts:
            return self._consts[key]
        t = self.sb(f"const_{self.uid()}", (128, 1), dtype)
        self.p.op("vector", lambda e, t=t, v=float(val): e.memset(t[:, :], v),
                  writes=[("const", key)], tag="const")
        self._consts[key] = (t, ("const", key))
        return self._consts[key]

    def need_ident(self):
        if self.ident is None:
            ident_np = np.eye(128, dtype=BFNP)
            dr = self.nc.inline_tensor(np.asarray(ident_np), name="ident128")
            t = self.sb("identsb", (128, 128), BF16)
            self.p.op("sync", lambda e, t=t, dr=dr: e.dma_start(out=t[:, :], in_=dr[:, :]),
                      writes=["ident"], dma="identload", tag="ident")
            self.ident = t
        return self.ident



def emit_phase1(b, xT, modrow, h_store):
    nc, p = b.nc, b.p
    t_eps = b.const_imm(EPS_LN)

    ones_col = b.sb("p1_ones_col", (128, 1), BF16)
    p.op("vector", lambda e: e.memset(ones_col[:, :], 1.0), writes=["p1oc"])
    ones_row = b.sb("p1_ones_row", (1, 128), BF16)
    p.op("vector", lambda e: e.memset(ones_row[:, :], 1.0), writes=["p1or"])

    catile = b.sb("p1_ca", (128, NCH), F32)
    satile = b.sb("p1_sa", (128, NCH), F32)
    if getattr(b, "fused", False):
        SELT = b.sb("p1_modsel", (128, NCH), F32)
        for vv, tile, tok in ((0, satile, "p1sa"), (1, catile, "p1ca")):
            for c1 in range(2):
                a0 = bass.AP(b.modag, 0 * 1536 + vv * 256 + c1 * 128,
                             [[1, 128], [3072, 8]])
                a1 = bass.AP(b.modag, 1 * 1536 + vv * 256 + c1 * 128,
                             [[1, 128], [3072, 8]])
                d0 = bass.AP(tile, c1, [[NCH, 128], [2, 8]])
                d1 = bass.AP(SELT, c1, [[NCH, 128], [2, 8]])
                p.op("sync", lambda e, d0=d0, a0=a0: e.dma_start(
                    out=d0, in_=a0, allow_slow_non_contiguous=True),
                    reads=["modag"], writes=[(tok, c1)],
                    dma=("p1mod", vv, c1, 0))
                p.op("sync", lambda e, d1=d1, a1=a1: e.dma_start(
                    out=d1, in_=a1, allow_slow_non_contiguous=True),
                    reads=["modag"], writes=[("p1sel", c1)],
                    dma=("p1mod", vv, c1, 1))
            p.op("vector", lambda e, tile=tile: e.tensor_sub(
                SELT[:, :], SELT[:, :], tile[:, :]),
                reads=[("p1sel", 0), ("p1sel", 1), (tok, 0), (tok, 1)],
                writes=[("p1sel", 0), ("p1sel", 1)])
            p.op("vector", lambda e, tile=tile: e.scalar_tensor_tensor(
                tile[:, :], SELT[:, :], b.flag128[:, 0:1], tile[:, :],
                ALU.mult, ALU.add),
                reads=[("p1sel", 0), ("p1sel", 1), b.tflag, (tok, 0), (tok, 1)],
                writes=[tok])
    else:
        ca_ap = bass.AP(modrow, D, [[1, 128], [128, NCH]])
        sa_ap = bass.AP(modrow, 0, [[1, 128], [128, NCH]])
        p.op("sync", lambda e: e.dma_start(out=catile[:, :], in_=ca_ap,
                                           allow_slow_non_contiguous=True), writes=["p1ca"], dma="p1modca")
        p.op("sync", lambda e: e.dma_start(out=satile[:, :], in_=sa_ap,
                                           allow_slow_non_contiguous=True), writes=["p1sa"], dma="p1modsa")
    p.op("vector", lambda e: e.tensor_single_scalar(catile[:, :], catile[:, :], 1.0, ALU.add),
         reads=["p1ca"], writes=["p1ca"])

    xstg = [b.sb(f"p1_xstg{i}", (128, NCH * WT), F32) for i in range(2)]
    xb = [b.sb(f"p1_xb{i}", (128, NCH * WT), BF16) for i in range(2)]
    xsq = b.sb("p1_xsq", (128, NCH * WT), BF16)
    hwin = [b.sb(f"p1_hwin{i}", (128, NCH * WT), BF16) for i in range(2)]
    stat_s = b.sb("p1_stats", (1, 6 * WT), F32)
    mb = b.sb("p1_mb", (1, WT), BF16)
    rb = b.sb("p1_rb", (1, WT), BF16)
    mrbc = b.sb("p1_mrbc", (128, 2 * WT), F32)
    t1 = b.sb("p1_t1", (128, WT), F32)
    t2 = b.sb("p1_t2", (128, WT), F32)

    PS = b.PS
    tPS = b.tPS

    for w in range(NW):
        par = w % 2
        xspar = 0 if getattr(b, "fused", False) else par
        XS, XB, HW = xstg[xspar], xb[par], hwin[par]
        tXS, tXB, tHW = ("p1xs", xspar), ("p1xb", par), ("p1hw", par)
        alt1 = "scalar" if getattr(b, "fused", False) else "gpsimd"
        for c in range(NCH):
            eng = "sync" if c % 2 == 0 else alt1
            p.op(eng, lambda e, c=c, w=w, XS=XS: e.dma_start(
                out=XS[:, c * WT:(c + 1) * WT], in_=xT[c, :, w * WT:(w + 1) * WT]),
                writes=[tXS], dma=("p1x", par, eng), tag=f"p1 xload w{w}c{c}")
        p.op("scalar", lambda e, XS=XS, XB=XB: e.copy(XB[:, :], XS[:, :]),
             reads=[tXS], writes=[tXB], tag=f"p1 cast w{w}")
        p.op("scalar", lambda e, XB=XB: e.activation(xsq[:, :], XB[:, :], AF.Square),
             reads=[tXB], writes=["p1xsq"], tag=f"p1 sq w{w}")
        for c in range(NCH):
            p.op("tensor", lambda e, c=c, XB=XB: e.matmul(
                PS[0][0:1, 0:WT], ones_col[:, :], XB[:, c * WT:(c + 1) * WT],
                start=(c == 0), stop=(c == NCH - 1)),
                reads=[tXB, "p1oc"], writes=[tPS[0]], tag=f"p1 sum w{w}c{c}")
        for c in range(NCH):
            p.op("tensor", lambda e, c=c: e.matmul(
                PS[1][0:1, 0:WT], ones_col[:, :], xsq[:, c * WT:(c + 1) * WT],
                start=(c == 0), stop=(c == NCH - 1)),
                reads=["p1xsq", "p1oc"], writes=[tPS[1]], tag=f"p1 sumsq w{w}c{c}")
        p.op("vector", lambda e: e.tensor_single_scalar(
            stat_s[0:1, 0:WT], PS[0][0:1, 0:WT], 1.0 / D, ALU.mult),
            reads=[tPS[0]], writes=["p1stat"])
        p.op("vector", lambda e: e.tensor_mul(
            stat_s[0:1, 2 * WT:3 * WT], stat_s[0:1, 0:WT], stat_s[0:1, 0:WT]),
            reads=["p1stat"], writes=["p1stat"])
        p.op("vector", lambda e: e.scalar_tensor_tensor(
            stat_s[0:1, WT:2 * WT], PS[1][0:1, 0:WT], 1.0 / D,
            stat_s[0:1, 2 * WT:3 * WT], ALU.mult, ALU.subtract),
            reads=[tPS[1], "p1stat"], writes=["p1stat"])
        p.op("scalar", lambda e: e.activation(
            stat_s[0:1, WT:2 * WT], stat_s[0:1, WT:2 * WT], AF.Sqrt,
            bias=EPS_LN, scale=1.0), reads=["p1stat", t_eps], writes=["p1stat"])
        p.op("vector", lambda e: e.reciprocal(
            stat_s[0:1, WT:2 * WT], stat_s[0:1, WT:2 * WT]),
            reads=["p1stat"], writes=["p1stat"])
        p.op("vector", lambda e: e.tensor_copy(mb[0:1, :], stat_s[0:1, 0:WT]),
             reads=["p1stat"], writes=["p1mb"])
        p.op("vector", lambda e: e.tensor_copy(rb[0:1, :], stat_s[0:1, WT:2 * WT]),
             reads=["p1stat"], writes=["p1rb"])
        p.op("tensor", lambda e: e.matmul(PS[2][:, :], ones_row[:, :], mb[0:1, :],
                                          start=True, stop=True),
             reads=["p1mb", "p1or"], writes=[tPS[2]])
        p.op("tensor", lambda e: e.matmul(PS[3][:, :], ones_row[:, :], rb[0:1, :],
                                          start=True, stop=True),
             reads=["p1rb", "p1or"], writes=[tPS[3]])
        p.op("scalar", lambda e: e.copy(mrbc[:, 0:WT], PS[2][:, :]),
             reads=[tPS[2]], writes=["p1mrbc"])
        p.op("scalar", lambda e: e.copy(mrbc[:, WT:2 * WT], PS[3][:, :]),
             reads=[tPS[3]], writes=["p1mrbc"])
        for c in range(NCH):
            p.op("vector", lambda e, c=c, XB=XB: e.tensor_sub(
                t1[:, :], XB[:, c * WT:(c + 1) * WT], mrbc[:, 0:WT]),
                reads=[tXB, "p1mrbc"], writes=["p1t1"])
            p.op("vector", lambda e: e.tensor_mul(t2[:, :], t1[:, :], mrbc[:, WT:2 * WT]),
                 reads=["p1t1", "p1mrbc"], writes=["p1t2"])
            p.op("vector", lambda e, c=c, HW=HW: e.tensor_scalar(
                HW[:, c * WT:(c + 1) * WT], t2[:, :], catile[:, c:c + 1],
                satile[:, c:c + 1], ALU.mult, ALU.add),
                reads=["p1t2", "p1ca", "p1sa"], writes=[tHW])
        h_store(w, HW, tHW)


def build_phase1_v0():
    nc = bass.Bass("TRN2", target_bir_lowering=False, debug=False)
    xT = nc.dram_tensor("xT", [NCH, 128, TC], F32, kind="ExternalInput")
    modrow = nc.dram_tensor("modrow", [1, 6 * D], F32, kind="ExternalInput")
    h_out = nc.dram_tensor("h_out", [NCH, 128, TC], BF16, kind="ExternalOutput")
    p = Prog(nc)
    b = Bld(nc, p)
    b.PS = [b.ps(f"p1_ps{k}", (128, 512), F32) for k in range(4)]
    b.tPS = [("PS", k) for k in range(4)]

    def h_store(w, HW, tHW):
        for c in range(NCH):
            eng = "sync" if c % 2 == 0 else "gpsimd"
            p.op(eng, lambda e, c=c, w=w, HW=HW: e.dma_start(
                out=h_out[c, :, w * WT:(w + 1) * WT],
                in_=HW[:, c * WT:(c + 1) * WT]),
                reads=[tHW], writes=[("hout", w, c)], dma=("p1h", w % 2, eng))

    emit_phase1(b, xT, modrow, h_store)
    p.build()
    return nc



def _bcast_row(b, name, modoff, transform, outdt=BF16):
    nc, p = b.nc, b.p
    if not hasattr(b, "row"):
        b.row = b.sb("shared_row", (1, D), F32)
        b.rowb = b.sb("shared_rowb", (1, D), BF16)
    row, rowb = b.row, b.rowb
    out = b.sb(f"{name}_bc", (128, D), outdt)
    tok = f"{name}bc"
    if getattr(b, "fused", False):
        v = modoff // D
        src0 = bass.AP(b.modag, 0 * 1536 + v * 256, [[3072, 8], [1, 256]])
        src1 = bass.AP(b.modag, 1 * 1536 + v * 256, [[3072, 8], [1, 256]])
        dstr = bass.AP(row, 0, [[D, 1], [256, 8], [1, 256]])
        dstg = bass.AP(b.gsc3_t, 0, [[b.gsc3_t.shape[1], 1], [256, 8], [1, 256]])
        p.op("sync", lambda e, src0=src0, dstr=dstr: e.dma_start(
            out=dstr, in_=src0),
            reads=["modag"], writes=["sharedrow"], dma=("p3crow", name))
        p.op("sync", lambda e, src1=src1, dstg=dstg: e.dma_start(
            out=dstg, in_=src1),
            reads=["modag"], writes=["p3gscrow"], dma=("p3crow2", name))
        p.op("vector", lambda e: e.tensor_sub(b.gsc3[:, :], b.gsc3[:, :], row[:, :]),
             reads=["p3gscrow", "sharedrow"], writes=["p3gscrow"])
        p.op("vector", lambda e: e.scalar_tensor_tensor(
            row[:, :], b.gsc3[:, :], b.flag1[0:1, 0:1], row[:, :],
            ALU.mult, ALU.add),
            reads=["p3gscrow", b.tflag1, "sharedrow"], writes=["sharedrow"])
    else:
        p.op("sync", lambda e: e.dma_start(
            out=row[:, :], in_=bass.AP(b.modrow, modoff, [[D, 1], [1, D]])),
            writes=["sharedrow"], dma=("p3crow", name))
    transform(row, "sharedrow")
    p.op("vector", lambda e: e.tensor_copy(rowb[:, :], row[:, :]),
         reads=["sharedrow"], writes=["sharedrowb"])
    for k in range(4):
        p.op("tensor", lambda e, k=k: e.matmul(
            b.PS[k][:, :], b.ones_row[:, :], rowb[0:1, k * 512:(k + 1) * 512],
            start=True, stop=True), reads=["sharedrowb", "p3or"], writes=[b.tPS[k]])
        p.op("scalar", lambda e, k=k: e.copy(out[:, k * 512:(k + 1) * 512], b.PS[k][:, :]),
             reads=[b.tPS[k]], writes=[tok])
    return out, tok


def emit_phase3(b, x_rows, attT, wo_bf, wup_pcj, wdnT, modrow, out_rows):
    nc, p = b.nc, b.p
    b.modrow = modrow
    t_eps = b.const_imm(EPS_LN)
    t_half = b.const_imm(0.5)
    ident = b.need_ident()

    h128_np = _hadamard(128).astype(BFNP)
    h128_dr = nc.inline_tensor(np.asarray(h128_np), name="h128")
    H128 = b.sb("p3_h128", (128, 128), BF16)
    p.op("sync", lambda e: e.dma_start(out=H128[:, :], in_=h128_dr[:, :]),
         writes=["h128"], dma="p3ch128")
    if getattr(b, "ones_row", None) is None:
        ones_row = b.sb("p3_ones_row", (1, 128), BF16)
        p.op("vector", lambda e: e.memset(ones_row[:, :], 1.0), writes=["p3or"])
        b.ones_row = ones_row
    else:
        ones_row = b.ones_row

    if getattr(b, "PS", None) is not None and len(b.PS) >= 7:
        PS, tPS = b.PS, b.tPS
        PSB, tPSB = b.PSB, b.tPSB
    else:
        PS = [b.ps(f"p3_ps{k}", (128, 512), F32) for k in range(7)]
        PSB = b.ps("p3_psb", (128, 1024), BF16)
        tPS = [("PS", k) for k in range(7)]
        tPSB = ("PSB",)
        b.PS, b.tPS = PS, tPS
        b.PSB, b.tPSB = PSB, tPSB

    gsc = b.sb("p3_gsc", (1, D), F32)

    def tf_ga(row, tok):
        p.op("vector", lambda e, row=row: e.tensor_scalar(
            gsc[:, :], row[:, :], 0.0, 2.0, ALU.is_ge, ALU.mult),
            reads=[tok], writes=["p3gscrow"])
        p.op("vector", lambda e: e.tensor_single_scalar(
            gsc[:, :], gsc[:, :], -1.0, ALU.add), reads=["p3gscrow"], writes=["p3gscrow"])
        p.op("scalar", lambda e, row=row: e.activation(row[:, :], row[:, :], AF.Abs),
             reads=[tok], writes=[tok])
        p.op("vector", lambda e, row=row: e.tensor_single_scalar(
            row[:, :], row[:, :], 0.25, ALU.max), reads=[tok], writes=[tok])
        p.op("vector", lambda e, row=row: e.tensor_mul(row[:, :], row[:, :], gsc[:, :]),
             reads=[tok, "p3gscrow"], writes=[tok])

    def tf_cb1(row, tok):
        p.op("vector", lambda e, row=row: e.tensor_single_scalar(
            row[:, :], row[:, :], 1.0, ALU.add), reads=[tok], writes=[tok])

    def tf_id(row, tok):
        pass

    GABC, tGA = _bcast_row(b, "p3ga", 2 * D, tf_ga)
    GBBC, tGB = _bcast_row(b, "p3gb", 5 * D, tf_ga)
    CBBC, tCB = _bcast_row(b, "p3cb", 4 * D, tf_cb1)
    SBBC, tSB = _bcast_row(b, "p3sb", 3 * D, tf_id)

    ATW = b.sb("p3_atw", (128, NCH * WT), BF16)
    tATW = "xq" if getattr(b, "fused", False) else "atw"
    CAND = [b.sb(f"p3_cand{i}", (128, 512), BF16) for i in range(2)]
    XR = [b.sb(f"p3_xr{i}", (128, 512), F32) for i in range(2)]
    X2T = b.sb("p3_x2t", (128, 4 * D), F32)
    H2TOK = b.sb("p3_h2tok", (128, D), BF16)
    H2C = b.sb("p3_h2c", (128, 1024), BF16)
    FA = b.sb("p3_fa", (128, FCH * WT), BF16)
    FB = b.sb("p3_fb", (128, NCH * WT), BF16)
    GB = b.sb("p3_gb", (128, NCH * WT), BF16)
    XQ = b.sb("p3_xq", (128, NCH * WT), BF16)
    WOS = [b.sb(f"p3_wos{i}", (128, 512), BF16) for i in range(2)]
    WUP = [b.sb(f"p3_wup{i}", (128, NCH * 128), BF16) for i in range(2)]
    WDN = [b.sb(f"p3_wdn{i}", (128, 512), BF16) for i in range(2)]
    QU = b.sb("p3_qu", (128, WT), F32)
    QZI = b.sb("p3_qzi", (128, WT), I32)
    QZF = b.sb("p3_qzf", (128, WT), F32)
    QC = b.sb("p3_qc", (128, WT), F32)
    QM = FB
    QSW = b.sb("p3_qsw", (128, WT), BF16)
    QS = b.sb("p3_qs", (128, WT), F32)
    QI = b.sb("p3_qi", (128, WT), F32)
    ST = b.sb("p3_st", (128, 8), F32)
    OT = [b.sb(f"p3_ot{i}", (128, 512), F32) for i in range(2)]

    def quant(nch, width, src, tsrc, dst_slices, tdst, cnorm, wq_, tag):
        p.op("scalar", lambda e: e.activation(
            QM[:, 0:nch * width], src[:, 0:nch * width], AF.Abs),
            reads=[tsrc], writes=["fb"], tag=f"{tag} abs")
        n = nch
        while n > 1:
            n2 = n // 2
            p.op("vector", lambda e, n2=n2, n=n: e.tensor_max(
                QM[:, 0:n2 * width], QM[:, 0:n2 * width], QM[:, n2 * width:n * width]),
                reads=["fb"], writes=["fb"])
            n = n2
        for half in (64, 32):
            pairs = [(0, half)] if half == 64 else [(0, 32), (64, 96)]
            for lo, hi in pairs:
                p.op("sync", lambda e, lo=lo, hi=hi, hh=half: e.dma_start(
                    out=QSW[lo:lo + hh, 0:width], in_=QM[hi:hi + hh, 0:width]),
                    reads=["fb"], writes=["qsw"], dma=("p3swap",))
                p.op("sync", lambda e, lo=lo, hi=hi, hh=half: e.dma_start(
                    out=QSW[hi:hi + hh, 0:width], in_=QM[lo:lo + hh, 0:width]),
                    reads=["fb"], writes=["qsw"], dma=("p3swap",))
            p.op("vector", lambda e: e.tensor_max(
                QM[:, 0:width], QM[:, 0:width], QSW[:, 0:width]),
                reads=["fb", "qsw"], writes=["fb"])
        for bit in (16, 8, 4, 2, 1):
            p.op("vector", lambda e, bit=bit: e.stream_shuffle(
                QSW[:, 0:width], QM[:, 0:width], [i ^ bit for i in range(32)]),
                reads=["fb"], writes=["qsw"])
            p.op("vector", lambda e: e.tensor_max(
                QM[:, 0:width], QM[:, 0:width], QSW[:, 0:width]),
                reads=["fb", "qsw"], writes=["fb"])
        p.op("vector", lambda e: e.tensor_single_scalar(
            QS[:, 0:width], QM[:, 0:width], EPS_Q / cnorm, ALU.max),
            reads=["fb"], writes=["qs"])
        p.op("vector", lambda e: e.tensor_single_scalar(
            QI[:, 0:width], QS[:, 0:width], cnorm / QB, ALU.mult),
            reads=["qs"], writes=["qi"])
        p.op("vector", lambda e: e.reciprocal(QS[:, 0:width], QS[:, 0:width]),
             reads=["qs"], writes=["qs"])
        p.op("vector", lambda e: e.tensor_single_scalar(
            QS[:, 0:width], QS[:, 0:width], QB, ALU.mult), reads=["qs"], writes=["qs"])
        for c in range(nch):
            dt_, dsl = dst_slices(c)
            ssl = slice(c * width, (c + 1) * width)
            p.op("vector", lambda e, ssl=ssl: e.tensor_mul(
                QU[:, 0:width], src[:, ssl], QS[:, 0:width]),
                reads=[tsrc, "qs"], writes=["qu"])
            p.op("scalar", lambda e: e.activation(
                QZI[:, 0:width], QU[:, 0:width], AF.Copy, bias=0.5, scale=1.0),
                reads=["qu", t_half], writes=["qzi"])
            p.op("scalar", lambda e: e.copy(QZF[:, 0:width], QZI[:, 0:width]),
                 reads=["qzi"], writes=["qzf"])
            p.op("vector", lambda e: e.scalar_tensor_tensor(
                QC[:, 0:width], QZF[:, 0:width], 0.5, QU[:, 0:width],
                ALU.subtract, ALU.is_gt),
                reads=["qzf", "qu"], writes=["qc"])
            p.op("vector", lambda e: e.tensor_sub(
                QC[:, 0:width], QZF[:, 0:width], QC[:, 0:width]),
                reads=["qzf", "qc"], writes=["qc"])
            p.op("vector", lambda e, dt_=dt_, dsl=dsl: e.tensor_mul(
                dt_[:, dsl], QC[:, 0:width], QI[:, 0:width]),
                reads=["qc", "qi"], writes=[tdst], tag=f"{tag} out c{c}")

    for w in range(NW):
        wsl = slice(w * WT, (w + 1) * WT)
        if getattr(b, "fused", False):
            for hh in range(H):
                R_, hl_ = hh // HL, hh % HL
                for k8 in range(8):
                    bbx, qq = k8 // 4, k8 % 4
                    src = bass.AP(attT, (qq * 8 + R_) * NATQ
                                  + ((bbx * 2 + hl_) * 4 + w) * (128 * 512),
                                  [[512, 128], [1, 512]])
                    eng = "sync" if k8 % 2 == 0 else "scalar"
                    p.op(eng, lambda e, src=src, k8=k8: e.dma_start(
                        out=CAND[k8 % 2][:, :], in_=src),
                        reads=[("atag", k8 % 4)], writes=[("cand", k8 % 2)],
                        dma=("p3att", k8 % 2), tag=f"p3 atcand w{w}h{hh}k{k8}")
                    if k8 == 0:
                        p.op("vector", lambda e, hh=hh, k8=k8: e.tensor_scalar(
                            ATW[:, hh * WT:(hh + 1) * WT], CAND[k8 % 2][:, :],
                            b.GF[:, k8:k8 + 1], None, ALU.mult),
                            reads=[("cand", k8 % 2), "gf"], writes=[tATW],
                            tag=f"p3 atsel w{w}h{hh}k{k8}")
                    else:
                        p.op("vector", lambda e, hh=hh, k8=k8: e.scalar_tensor_tensor(
                            ATW[:, hh * WT:(hh + 1) * WT], CAND[k8 % 2][:, :],
                            b.GF[:, k8:k8 + 1], ATW[:, hh * WT:(hh + 1) * WT],
                            ALU.mult, ALU.add),
                            reads=[("cand", k8 % 2), "gf", tATW], writes=[tATW],
                            tag=f"p3 atsel w{w}h{hh}k{k8}")
        else:
            for c in range(NCH):
                eng = "sync" if c % 2 == 0 else "gpsimd"
                p.op(eng, lambda e, c=c, wsl=wsl: e.dma_start(
                    out=ATW[:, c * WT:(c + 1) * WT], in_=attT[c, :, wsl]),
                    writes=[tATW], dma=("p3att", eng), tag=f"p3 attload w{w}c{c}")
        for dq in range(4):
            for c1 in range(NCH):
                par = c1 % 2
                if getattr(b, "fused", False):
                    p.op("sync", lambda e, c1=c1, dq=dq, par=par: e.dma_start(
                        out=WOS[par][:, :],
                        in_=wo_bf[c1 // 2, c1 % 2, :, dq * 512:(dq + 1) * 512]),
                        reads=["woag"],
                        writes=[("wos", par)], dma=("p3wo", par),
                        tag=f"p3 woload w{w}q{dq}c{c1}")
                else:
                    p.op("sync", lambda e, c1=c1, dq=dq, par=par: e.dma_start(
                        out=WOS[par][:, :],
                        in_=wo_bf[c1 * 128:(c1 + 1) * 128, dq * 512:(dq + 1) * 512]),
                        writes=[("wos", par)], dma=("p3wo", par),
                        tag=f"p3 woload w{w}q{dq}c{c1}",
                        extra_waits=getattr(b, "xw_wt", ()))
                for ts_ in range(4):
                    if getattr(b, "fused", False):
                        lhs = bass.AP(ATW, ts_ * 4 * SEG + c1,
                                      [[ATW.shape[1], 128], [SEG, 4], [16, 32]])
                        p.op("tensor", lambda e, lhs=lhs, c1=c1, ts_=ts_, par=par: e.matmul(
                            PS[ts_][:, :], lhs, WOS[par][:, :],
                            start=(c1 == 0), stop=(c1 == NCH - 1)),
                            reads=[("wos", par), tATW], writes=[tPS[ts_]],
                            tag=f"p3 o mm w{w}q{dq}c{c1}t{ts_}")
                    else:
                        p.op("tensor", lambda e, c1=c1, ts_=ts_, par=par: e.matmul(
                            PS[ts_][:, :],
                            ATW[:, c1 * WT + ts_ * 128: c1 * WT + (ts_ + 1) * 128],
                            WOS[par][:, :],
                            start=(c1 == 0), stop=(c1 == NCH - 1)),
                            reads=[("wos", par), tATW], writes=[tPS[ts_]],
                            tag=f"p3 o mm w{w}q{dq}c{c1}t{ts_}")
            for ts_ in range(4):
                xpar = ts_ % 2
                osl = slice(dq * 512, (dq + 1) * 512)
                xeng = "scalar" if getattr(b, "fused", False) else "gpsimd"
                p.op(xeng, lambda e, w=w, ts_=ts_, xpar=xpar, osl=osl: e.dma_start(
                    out=XR[xpar][:, :],
                    in_=x_rows[w * WT + ts_ * 128: w * WT + (ts_ + 1) * 128, osl]),
                    writes=[("xr", xpar)], dma=("p3x", xpar),
                    tag=f"p3 xload w{w}q{dq}t{ts_}")
                if getattr(b, "dbg", None) is not None and w == 0 and dq == 0 and ts_ == 0:
                    p.op("sync", lambda e: e.dma_start(out=b.dbg["xr"][:, :], in_=XR[0][:, :]),
                         reads=[("xr", 0)], writes=["dbgxr"], dma=("dbg",))
                    p.op("vector", lambda e: e.tensor_copy(OT[1][:, :], PS[0][:, :]),
                         reads=[tPS[0]], writes=[("ot", 1)])
                    p.op("sync", lambda e: e.dma_start(out=b.dbg["o0"][:, :], in_=OT[1][:, :]),
                         reads=[("ot", 1)], writes=["dbgo0"], dma=("dbg",))
                p.op("vector", lambda e, ts_=ts_, osl=osl: e.tensor_mul(
                    OT[ts_ % 2][:, :], PS[ts_][:, :], GABC[:, osl]),
                    reads=[tPS[ts_], tGA], writes=[("ot", ts_ % 2)],
                    tag=f"p3 oga w{w}q{dq}t{ts_}")
                p.op("vector", lambda e, ts_=ts_, dq=dq, xpar=xpar: e.tensor_add(
                    X2T[:, ts_ * D + dq * 512: ts_ * D + (dq + 1) * 512],
                    OT[ts_ % 2][:, :], XR[xpar][:, :]),
                    reads=[("ot", ts_ % 2), ("xr", xpar)], writes=["x2t"],
                    tag=f"p3 x2 w{w}q{dq}t{ts_}")
        if getattr(b, "dbg", None) is not None and w == 0:
            p.op("sync", lambda e: e.dma_start(out=b.dbg["x2t"][:, :], in_=X2T[:, :]),
                 reads=["x2t"], writes=["dbgx"], dma=("dbg",))
            p.op("sync", lambda e: e.dma_start(out=b.dbg["atw"][:, :], in_=ATW[:, :]),
                 reads=["atw"], writes=["dbga"], dma=("dbg",))
        for ts_ in range(4):
            x2sl = slice(ts_ * D, (ts_ + 1) * D)
            p.op("vector", lambda e, x2sl=x2sl: e.tensor_reduce(
                ST[:, 0:1], X2T[:, x2sl], AX.X, ALU.add),
                reads=["x2t"], writes=["p3st"], tag=f"p3 sum w{w}t{ts_}")
            p.op("scalar", lambda e, x2sl=x2sl: e.activation(
                QM[:, 0:D], X2T[:, x2sl], AF.Square, accum_out=ST[:, 1:2]),
                reads=["x2t"], writes=["fb", "p3st"], tag=f"p3 sumsq w{w}t{ts_}")
            p.op("vector", lambda e: e.tensor_single_scalar(
                ST[:, 2:3], ST[:, 0:1], 1.0 / D, ALU.mult),
                reads=["p3st"], writes=["p3st"])
            p.op("vector", lambda e: e.tensor_mul(ST[:, 3:4], ST[:, 2:3], ST[:, 2:3]),
                 reads=["p3st"], writes=["p3st"])
            p.op("vector", lambda e: e.scalar_tensor_tensor(
                ST[:, 4:5], ST[:, 1:2], 1.0 / D, ST[:, 3:4], ALU.mult, ALU.subtract),
                reads=["p3st"], writes=["p3st"])
            p.op("scalar", lambda e: e.activation(
                ST[:, 4:5], ST[:, 4:5], AF.Sqrt, bias=EPS_LN, scale=1.0),
                reads=["p3st", t_eps], writes=["p3st"])
            p.op("vector", lambda e: e.reciprocal(ST[:, 5:6], ST[:, 4:5]),
                 reads=["p3st"], writes=["p3st"])
            p.op("vector", lambda e, x2sl=x2sl: e.tensor_scalar(
                H2TOK[:, :], X2T[:, x2sl], ST[:, 2:3], ST[:, 5:6],
                ALU.subtract, ALU.mult),
                reads=["x2t", "p3st"], writes=["h2tok"], tag=f"p3 norm w{w}t{ts_}")
            p.op("vector", lambda e: e.tensor_mul(H2TOK[:, :], H2TOK[:, :], CBBC[:, :]),
                 reads=["h2tok", tCB], writes=["h2tok"])
            p.op("vector", lambda e: e.tensor_add(H2TOK[:, :], H2TOK[:, :], SBBC[:, :]),
                 reads=["h2tok", tSB], writes=["h2tok"])
            for half in range(2):
                for cc in range(8):
                    c = half * 8 + cc
                    p.op("tensor", lambda e, c=c, cc=cc: e.transpose(
                        PSB[:, cc * 128:(cc + 1) * 128],
                        H2TOK[:, c * 128:(c + 1) * 128], ident[:, :]),
                        reads=["h2tok", "ident"], writes=[tPSB],
                        tag=f"p3 h2t w{w}t{ts_}c{c}")
                p.op("vector", lambda e: e.tensor_copy(H2C[:, :], PSB[:, 0:1024]),
                     reads=[tPSB], writes=["h2c"], tag=f"p3 h2tev w{w}t{ts_}h{half}")
                for g2 in range(2):
                    k = 2 + g2
                    p.op("tensor", lambda e, g2=g2, k=k: e.matmul(
                        PS[k][:, :], H128[:, :], H2C[:, g2 * 512:(g2 + 1) * 512],
                        start=True, stop=True),
                        reads=["h2c", "h128"], writes=[tPS[k]],
                        tag=f"p3 fw1mm w{w}t{ts_}h{half}g{g2}")
                    for j4 in range(4):
                        c = half * 8 + g2 * 4 + j4
                        p.op("scalar", lambda e, c=c, ts_=ts_, k=k, j4=j4: e.copy(
                            FA[:, c * WT + ts_ * 128: c * WT + (ts_ + 1) * 128],
                            PS[k][:, j4 * 128:(j4 + 1) * 128]),
                            reads=[tPS[k]], writes=["fa"],
                            tag=f"p3 fw1ev w{w}t{ts_}c{c}")
        bufs = [(FA, "fa"), (FB, "fb")]
        for st_i, stv in enumerate([8, 4, 2, 1]):
            src_t, tsrc = bufs[st_i % 2]
            dst_t, tdst = bufs[(st_i + 1) % 2]
            for g in range(NCH // (2 * stv)):
                for j in range(stv):
                    a = g * 2 * stv + j
                    bi = a + stv
                    asl = slice(a * WT, (a + 1) * WT)
                    bsl = slice(bi * WT, (bi + 1) * WT)
                    p.op("vector", lambda e, asl=asl, bsl=bsl, src_t=src_t, dst_t=dst_t: e.tensor_add(
                        dst_t[:, asl], src_t[:, asl], src_t[:, bsl]),
                        reads=[tsrc], writes=[tdst], tag=f"p3 bf1 w{w}")
                    p.op("vector", lambda e, asl=asl, bsl=bsl, src_t=src_t, dst_t=dst_t: e.tensor_sub(
                        dst_t[:, bsl], src_t[:, asl], src_t[:, bsl]),
                        reads=[tsrc], writes=[tdst], tag=f"p3 bf1 w{w}")
        if getattr(b, "dbg", None) is not None and w == 0:
            p.op("sync", lambda e: e.dma_start(out=b.dbg["fw1"][:, :], in_=FA[:, 0:NCH * WT]),
                 reads=["fa"], writes=["dbgf"], dma=("dbg",))
        quant(NCH, WT, FA, "fa", lambda c: (XQ, slice(c * WT, (c + 1) * WT)),
              "xq", C1, w, f"p3 q1 w{w}")
        if getattr(b, "dbg", None) is not None and w == 0:
            p.op("sync", lambda e: e.dma_start(out=b.dbg["xq"][:, :], in_=XQ[:, :]),
                 reads=["xq"], writes=["dbgq"], dma=("dbg",))
        for f2 in range(FCH):
            par = f2 % 2
            if getattr(b, "fused", False):
                p.op("sync", lambda e, f2=f2, par=par: e.dma_start(
                    out=WUP[par][:, :], in_=wup_pcj[f2 // 4, f2 % 4, :, :]),
                    reads=["wupag"],
                    writes=[("wup", par)], dma=("p3wup", par),
                    tag=f"p3 wupload w{w}f{f2}")
            else:
                p.op("sync", lambda e, f2=f2, par=par: e.dma_start(
                    out=WUP[par][:, :], in_=wup_pcj[f2, :, :]),
                    writes=[("wup", par)], dma=("p3wup", par), tag=f"p3 wupload w{w}f{f2}",
                    extra_waits=getattr(b, "xw_wt", ()))
            k = 2 + f2 % 2
            for c in range(NCH):
                p.op("tensor", lambda e, f2=f2, c=c, k=k, par=par: e.matmul(
                    PS[k][:, :], WUP[par][:, c * 128:(c + 1) * 128],
                    XQ[:, c * WT:(c + 1) * WT],
                    start=(c == 0), stop=(c == NCH - 1)),
                    reads=[("wup", par), "xq"], writes=[tPS[k]],
                    tag=f"p3 up mm w{w}f{f2}c{c}")
            p.op("vector", lambda e, k=k: e.tensor_single_scalar(
                QU[:, :], PS[k][:, :], 0.0, ALU.max),
                reads=[tPS[k]], writes=["qu"], tag=f"p3 relu w{w}f{f2}")
            p.op("scalar", lambda e, f2=f2: e.activation(
                FA[:, f2 * WT:(f2 + 1) * WT], QU[:, :], AF.Square),
                reads=["qu"], writes=["fa"], tag=f"p3 rsq w{w}f{f2}")
        HT = WT // 2
        for hw in range(2):
            for f2 in range(FCH):
                k = 2 + f2 % 2
                p.op("tensor", lambda e, f2=f2, hw=hw, k=k: e.matmul(
                    PS[k][:, 0:HT], H128[:, :],
                    FA[:, f2 * WT + hw * HT: f2 * WT + (hw + 1) * HT],
                    start=True, stop=True), reads=["fa", "h128"], writes=[tPS[k]],
                    tag=f"p3 fw2mm w{w}h{hw}f{f2}")
                p.op("scalar", lambda e, f2=f2, k=k: e.copy(
                    FB[:, f2 * HT:(f2 + 1) * HT], PS[k][:, 0:HT]),
                    reads=[tPS[k]], writes=["fb"], tag=f"p3 fw2ev w{w}h{hw}f{f2}")
            bufs2 = [(FB, "fb"), (GB, "gb")]
            for st_i, stv in enumerate([16, 8, 4, 2, 1]):
                src_t, tsrc = bufs2[st_i % 2]
                dst_t, tdst = bufs2[(st_i + 1) % 2]
                for g in range(FCH // (2 * stv)):
                    for j in range(stv):
                        a = g * 2 * stv + j
                        bi = a + stv
                        asl = slice(a * HT, (a + 1) * HT)
                        bsl = slice(bi * HT, (bi + 1) * HT)
                        p.op("vector", lambda e, asl=asl, bsl=bsl, src_t=src_t, dst_t=dst_t: e.tensor_add(
                            dst_t[:, asl], src_t[:, asl], src_t[:, bsl]),
                            reads=[tsrc], writes=[tdst], tag=f"p3 bf2 w{w}")
                        p.op("vector", lambda e, asl=asl, bsl=bsl, src_t=src_t, dst_t=dst_t: e.tensor_sub(
                            dst_t[:, bsl], src_t[:, asl], src_t[:, bsl]),
                            reads=[tsrc], writes=[tdst], tag=f"p3 bf2 w{w}")
            quant(FCH, HT, GB, "gb",
                  lambda c, hw=hw: (FA, slice(c * WT + hw * HT, c * WT + (hw + 1) * HT)),
                  "fa", C2, w, f"p3 q2 w{w}h{hw}")
        for dq in range(4):
            for f in range(FCH):
                par = f % 2
                if getattr(b, "fused", False):
                    wsrc = bass.AP(wdnT, (2 * dq) * FF * 256 + f * 128 * 256,
                                   [[256, 128], [FF * 256, 2], [1, 256]])
                    wdst = bass.AP(WDN[par], 0, [[512, 128], [256, 2], [1, 256]])
                    p.op("sync", lambda e, wsrc=wsrc, wdst=wdst: e.dma_start(
                        out=wdst, in_=wsrc),
                        reads=["wdnag"],
                        writes=[("wdn", par)], dma=("p3wdn", par),
                        tag=f"p3 wdnload w{w}q{dq}f{f}")
                else:
                    p.op("sync", lambda e, f=f, dq=dq, par=par: e.dma_start(
                        out=WDN[par][:, :],
                        in_=wdnT[f * 128:(f + 1) * 128, dq * 512:(dq + 1) * 512]),
                        writes=[("wdn", par)], dma=("p3wdn", par),
                        tag=f"p3 wdnload w{w}q{dq}f{f}")
                for ts_ in range(4):
                    p.op("tensor", lambda e, f=f, ts_=ts_, par=par: e.matmul(
                        PS[ts_][:, :],
                        FA[:, f * WT + ts_ * 128: f * WT + (ts_ + 1) * 128],
                        WDN[par][:, :],
                        start=(f == 0), stop=(f == FCH - 1)),
                        reads=[("wdn", par), "fa"], writes=[tPS[ts_]],
                        tag=f"p3 dn mm w{w}q{dq}f{f}t{ts_}")
            for ts_ in range(4):
                osl = slice(dq * 512, (dq + 1) * 512)
                p.op("vector", lambda e, ts_=ts_, osl=osl: e.tensor_mul(
                    OT[ts_ % 2][:, :], PS[ts_][:, :], GBBC[:, osl]),
                    reads=[tPS[ts_], tGB], writes=[("ot", ts_ % 2)],
                    tag=f"p3 dngb w{w}q{dq}t{ts_}")
                p.op("vector", lambda e, ts_=ts_, dq=dq: e.tensor_add(
                    OT[ts_ % 2][:, :], OT[ts_ % 2][:, :],
                    X2T[:, ts_ * D + dq * 512: ts_ * D + (dq + 1) * 512]),
                    reads=[("ot", ts_ % 2), "x2t"], writes=[("ot", ts_ % 2)],
                    tag=f"p3 dnadd w{w}q{dq}t{ts_}")
                alt3 = "scalar" if getattr(b, "fused", False) else "gpsimd"
                eng = "sync" if ts_ % 2 == 0 else alt3
                p.op(eng, lambda e, w=w, ts_=ts_, osl=osl: e.dma_start(
                    out=out_rows[w * WT + ts_ * 128: w * WT + (ts_ + 1) * 128, osl],
                    in_=OT[ts_ % 2][:, :]),
                    reads=[("ot", ts_ % 2)], writes=[("outr", w, ts_, dq)],
                    dma=("p3out", eng), tag=f"p3 out w{w}q{dq}t{ts_}")


def build_phase3_v0(debug=False):
    nc = bass.Bass("TRN2", target_bir_lowering=False, debug=False)
    x_rows = nc.dram_tensor("x_rows", [TC, D], F32, kind="ExternalInput")
    attT = nc.dram_tensor("attT", [NCH, 128, TC], BF16, kind="ExternalInput")
    wo_bf = nc.dram_tensor("wo_bf", [D, D], BF16, kind="ExternalInput")
    wup_pcj = nc.dram_tensor("wup_pcj", [FCH, 128, NCH * 128], BF16, kind="ExternalInput")
    wdnT = nc.dram_tensor("wdnT", [FF, D], BF16, kind="ExternalInput")
    modrow = nc.dram_tensor("modrow", [1, 6 * D], F32, kind="ExternalInput")
    out_rows = nc.dram_tensor("out_rows", [TC, D], F32, kind="ExternalOutput")
    p = Prog(nc)
    b = Bld(nc, p)
    if debug:
        b.dbg = {
            "x2t": nc.dram_tensor("dbg_x2t", [128, 4 * D], F32, kind="ExternalOutput"),
            "fw1": nc.dram_tensor("dbg_fw1", [128, NCH * WT], BF16, kind="ExternalOutput"),
            "xq": nc.dram_tensor("dbg_xq", [128, NCH * WT], BF16, kind="ExternalOutput"),
            "atw": nc.dram_tensor("dbg_atw", [128, NCH * WT], BF16, kind="ExternalOutput"),
            "xr": nc.dram_tensor("dbg_xr", [128, 512], F32, kind="ExternalOutput"),
            "o0": nc.dram_tensor("dbg_o0", [128, 512], F32, kind="ExternalOutput"),
        }
    emit_phase3(b, x_rows, attT, wo_bf, wup_pcj, wdnT, modrow, out_rows)
    p.build()
    return nc



def emit_phase2(b, h_full, wqkv, betas_s, att_s):
    nc, p = b.nc, b.p
    ident = b.need_ident()

    WSTG = b.sb("p2_wstg", (128, 768), F32)
    WQKV = b.sb("p2_wqkv", (128, NCH * 768), BF16)
    for c in range(NCH):
        p.op("sync", lambda e, c=c: e.dma_start(
            out=WSTG[:, :], in_=wqkv[c * 128:(c + 1) * 128, :]),
            writes=["wstg"], dma=("p2w",), tag=f"p2 wload c{c}")
        p.op("scalar", lambda e, c=c: e.copy(WQKV[:, c * 768:(c + 1) * 768], WSTG[:, :]),
             reads=["wstg"], writes=["wqkv"], tag=f"p2 wcast c{c}")

    BROW = b.sb("p2_brow", (1, 2 * 128), BF16)
    BROWF = b.sb("p2_browf", (1, 2 * 128), F32)
    BBC = b.sb("p2_bbc", (128, 2 * 128), F32)
    MBC = b.sb("p2_mbc", (128, 2 * 128), F32)
    for hl in range(HL):
        p.op("sync", lambda e, hl=hl: e.dma_start(
            out=BROWF[0:1, hl * 128:(hl + 1) * 128], in_=betas_s[hl:hl + 1, :]),
            writes=["brow"], dma=("p2wb",), tag=f"p2 bload {hl}")
    p.op("scalar", lambda e: e.activation(BROW[0:1, :], BROWF[0:1, :], AF.Sigmoid),
         reads=["brow"], writes=["brow"])
    for hl in range(HL):
        p.op("tensor", lambda e, hl=hl: e.matmul(
            b.PS[0][:, 0:128], b.ones_row[:, :], BROW[0:1, hl * 128:(hl + 1) * 128],
            start=True, stop=True), reads=["brow", "p3or"], writes=[b.tPS[0]])
        p.op("scalar", lambda e, hl=hl: e.copy(BBC[:, hl * 128:(hl + 1) * 128],
                                               b.PS[0][:, 0:128]),
             reads=[b.tPS[0]], writes=["bbc"])
        p.op("vector", lambda e, hl=hl: e.tensor_scalar(
            MBC[:, hl * 128:(hl + 1) * 128], BBC[:, hl * 128:(hl + 1) * 128],
            -1.0, 1.0, ALU.mult, ALU.add),
            reads=["bbc"], writes=["mbc"])

    mask_np = np.where(np.arange(128)[:, None] >= np.arange(128)[None, :],
                       0.0, -1e9).astype(np.float32)
    mask_dr = nc.inline_tensor(np.asarray(mask_np), name="cmask")
    CMASK = b.sb("p2_cmask", (128, 128), F32)
    p.op("sync", lambda e: e.dma_start(out=CMASK[:, :], in_=mask_dr[:, :]),
         writes=["cmask"], dma=("p2wc",))

    MEMZ = [b.sb(f"p2_memz{i}", (128, 129), F32) for i in range(2 * HL)]
    MEMZB = [b.sb(f"p2_memzb{i}", (128, 129), BF16) for i in range(2 * HL)]
    for pr in range(2 * HL):
        p.op("vector", lambda e, pr=pr: e.memset(MEMZ[pr][:, 0:128], 0.0),
             writes=[("memz", pr)])
        p.op("vector", lambda e, pr=pr: e.memset(MEMZ[pr][:, 128:129], 1.0 / DK),
             writes=[("memz", pr)])
        p.op("scalar", lambda e, pr=pr: e.copy(MEMZB[pr][:, :], MEMZ[pr][:, :]),
             reads=[("memz", pr)], writes=[("memzb", pr)])

    HTB = [b.sb(f"p2_ht{i}", (128, NCH * SEG), BF16) for i in range(2)]
    Q = b.sb("p2_q", (128, SEG), BF16)
    K = b.sb("p2_k", (128, SEG), BF16)
    V = b.sb("p2_v", (128, SEG), BF16)
    EQ = b.sb("p2_eq", (128, SEG), BF16)
    SQ = b.sb("p2_sq", (128, SEG), BF16)
    SK = b.sb("p2_sk", (128, SEG), BF16)
    VT = b.sb("p2_vt", (128, SEG), BF16)
    SKT = b.sb("p2_skt", (128, SEG), BF16)
    DLT = b.sb("p2_dlt", (128, SEG), BF16)
    PT = b.sb("p2_pt", (128, 4 * SEG), BF16)
    PSM = b.sb("p2_psm", (128, SEG), BF16)
    DEN = b.sb("p2_den", (128, 8), F32)
    NMS = b.sb("p2_nms", (128, 8), F32)
    RZ = b.sb("p2_rz", (128, 4), F32)
    T1 = b.sb("p2_t1", (128, 128), F32)
    T2 = b.sb("p2_t2", (128, 128), F32)
    ATT = b.sb("p2_att", (128, 128), BF16)
    ATTT = b.sb("p2_attt", (128, SEG), BF16)

    PS, tPS, PSB, tPSB = b.PS, b.tPS, b.PSB, b.tPSB

    for seg in range(NSEG):
        for bb in range(B):
            par = (seg * B + bb) % 2
            HT_ = HTB[par]
            tHT = ("p2ht", par)
            base = bb * S + seg * SEG
            if getattr(b, "fused", False):
                wv, rv = seg % 4, bb * 4 + seg // 4
                eng = "sync" if par == 0 else "scalar"
                p.op(eng, lambda e, wv=wv, rv=rv, HT_=HT_: e.dma_start(
                    out=HT_[:, :], in_=h_full[wv, rv, :, :]),
                    reads=[("hag", wv)],
                    writes=[tHT], dma=("p2h", par, eng),
                    tag=f"p2 hload s{seg}b{bb}")
            else:
                for c in range(NCH):
                    eng = "sync" if c % 2 == 0 else "gpsimd"
                    p.op(eng, lambda e, c=c, base=base, HT_=HT_: e.dma_start(
                        out=HT_[:, c * SEG:(c + 1) * SEG],
                        in_=h_full[c, :, base:base + SEG]),
                        writes=[tHT], dma=("p2h", par, eng),
                        tag=f"p2 hload s{seg}b{bb}c{c}")
            for hl in range(HL):
                pr = bb * HL + hl
                sfx = f"s{seg}b{bb}h{hl}"
                for pi, (dst, sig, tdst) in enumerate(
                        [(Q, SQ, "q"), (K, SK, "k"), (V, None, "v")]):
                    kk = pi % 2
                    for c in range(NCH):
                        p.op("tensor", lambda e, c=c, pi=pi, hl=hl, kk=kk, HT_=HT_: e.matmul(
                            PS[kk][:, :],
                            WQKV[:, c * 768 + pi * 256 + hl * 128:
                                 c * 768 + pi * 256 + (hl + 1) * 128],
                            HT_[:, c * SEG:(c + 1) * SEG],
                            start=(c == 0), stop=(c == NCH - 1)),
                            reads=["wqkv", tHT], writes=[tPS[kk]],
                            tag=f"p2 {tdst}mm {sfx}c{c}")
                    if pi == 0:
                        p.op("scalar", lambda e, dst=dst, kk=kk: e.activation(
                            dst[:, :], PS[kk][:, :], AF.Copy,
                            scale=1.0 / SCALE_ATT),
                            reads=[tPS[kk]], writes=[tdst], tag=f"p2 {tdst}ev {sfx}")
                    else:
                        p.op("scalar", lambda e, dst=dst, kk=kk: e.copy(dst[:, :], PS[kk][:, :]),
                             reads=[tPS[kk]], writes=[tdst], tag=f"p2 {tdst}ev {sfx}")
                    if sig is not None:
                        p.op("scalar", lambda e, kk=kk: e.activation(
                            EQ[:, :], PS[kk][:, :], AF.Exp),
                            reads=[tPS[kk]], writes=["eq"], tag=f"p2 {tdst}exp {sfx}")
                        p.op("vector", lambda e, sig=sig, kk=kk: e.scalar_tensor_tensor(
                            sig[:, :], PS[kk][:, :], 1.0, EQ[:, :], ALU.add, ALU.min),
                            reads=[tPS[kk], "eq"], writes=[tdst + "s"],
                            tag=f"p2 {tdst}sig {sfx}")
                for src, dst, tsrc, tdst in ((V, VT, "v", "vt"), (SK, SKT, "ks", "skt")):
                    for j in range(4):
                        p.op("tensor", lambda e, j=j, src=src: e.transpose(
                            PSB[:, j * 128:(j + 1) * 128],
                            src[:, j * 128:(j + 1) * 128], ident[:, :]),
                            reads=[tsrc, "ident"], writes=[tPSB],
                            tag=f"p2 tr{tdst} {sfx}j{j}")
                    p.op("vector", lambda e, dst=dst: e.tensor_copy(
                        dst[:, :], PSB[:, 0:512]),
                        reads=[tPSB], writes=[tdst], tag=f"p2 trev{tdst} {sfx}")
                for i in range(4):
                    ncol = (i + 1) * 128
                    p.op("tensor", lambda e, i=i, pr=pr: e.matmul(
                        PS[2][:, 0:129], SQ[:, i * 128:(i + 1) * 128],
                        MEMZB[pr][:, :], start=True, stop=True),
                        reads=["qs", ("memzb", pr)], writes=[tPS[2]],
                        tag=f"p2 am {sfx}i{i}")
                    p.op("tensor", lambda e, i=i, ncol=ncol: e.matmul(
                        PS[3][:, 0:ncol], Q[:, i * 128:(i + 1) * 128], K[:, 0:ncol],
                        start=True, stop=True),
                        reads=["q", "k"], writes=[tPS[3]], tag=f"p2 sc {sfx}i{i}")
                    p.op("vector", lambda e, i=i, ncol=ncol: e.tensor_add(
                        PS[3][:, ncol - 128:ncol], PS[3][:, ncol - 128:ncol],
                        CMASK[:, :]),
                        reads=[tPS[3], "cmask"], writes=[tPS[3]],
                        tag=f"p2 mask {sfx}i{i}")
                    p.op("vector", lambda e, i=i, ncol=ncol: e.tensor_reduce(
                        NMS[:, i:i + 1], PS[3][:, 0:ncol], AX.X, ALU.max,
                        negate=True),
                        reads=[tPS[3]], writes=["nms"], tag=f"p2 max {sfx}i{i}")
                    p.op("scalar", lambda e, i=i, ncol=ncol: e.activation(
                        PSM[:, 0:ncol], PS[3][:, 0:ncol], AF.Exp,
                        bias=NMS[:, i:i + 1], scale=1.0,
                        accum_out=DEN[:, i:i + 1]),
                        reads=[tPS[3], "nms"], writes=["psm", "den"],
                        tag=f"p2 exp {sfx}i{i}")
                    for j in range(i + 1):
                        p.op("tensor", lambda e, j=j: e.transpose(
                            PSB[:, j * 128:(j + 1) * 128],
                            PSM[:, j * 128:(j + 1) * 128], ident[:, :]),
                            reads=["psm", "ident"], writes=[tPSB],
                            tag=f"p2 trp {sfx}i{i}j{j}")
                    p.op("vector", lambda e, i=i: e.tensor_copy(
                        PT[:, 0:(i + 1) * 128], PSB[:, 0:(i + 1) * 128]),
                        reads=[tPSB], writes=["pt"], tag=f"p2 trpev {sfx}i{i}")
                    for j in range(i + 1):
                        p.op("tensor", lambda e, i=i, j=j: e.matmul(
                            PS[4][:, 0:128], PT[:, j * 128:(j + 1) * 128],
                            VT[:, j * 128:(j + 1) * 128],
                            start=(j == 0), stop=(j == i)),
                            reads=["pt", "vt"], writes=[tPS[4]],
                            tag=f"p2 ad {sfx}i{i}j{j}")
                    p.op("vector", lambda e, i=i: e.reciprocal(RZ[:, 0:1],
                                                               PS[2][:, 128:129]),
                         reads=[tPS[2]], writes=["rz"], tag=f"p2 rz {sfx}i{i}")
                    p.op("vector", lambda e, i=i: e.reciprocal(RZ[:, 1:2],
                                                               DEN[:, i:i + 1]),
                         reads=["den"], writes=["rz"])
                    p.op("vector", lambda e, hl=hl: e.scalar_tensor_tensor(
                        T1[:, :], PS[2][:, 0:128], RZ[:, 0:1],
                        BBC[:, hl * 128:(hl + 1) * 128], ALU.mult, ALU.mult),
                        reads=[tPS[2], "rz", "bbc"], writes=["t1x"],
                        tag=f"p2 t1 {sfx}i{i}")
                    p.op("vector", lambda e, hl=hl: e.scalar_tensor_tensor(
                        T2[:, :], PS[4][:, 0:128], RZ[:, 1:2],
                        MBC[:, hl * 128:(hl + 1) * 128], ALU.mult, ALU.mult),
                        reads=[tPS[4], "rz", "mbc"], writes=["t2x"],
                        tag=f"p2 t2 {sfx}i{i}")
                    p.op("vector", lambda e: e.tensor_add(ATT[:, :], T1[:, :], T2[:, :]),
                         reads=["t1x", "t2x"], writes=["attx"], tag=f"p2 cmb {sfx}i{i}")
                    p.op("tensor", lambda e: e.transpose(
                        PSB[:, 512:640], ATT[:, :], ident[:, :]),
                        reads=["attx", "ident"], writes=[tPSB],
                        tag=f"p2 trat {sfx}i{i}")
                    p.op("vector", lambda e, i=i: e.tensor_copy(
                        ATTT[:, i * 128:(i + 1) * 128], PSB[:, 512:640]),
                        reads=[tPSB], writes=["atttx"], tag=f"p2 tratev {sfx}i{i}")
                for j in range(4):
                    p.op("tensor", lambda e, j=j, pr=pr: e.matmul(
                        PS[5][:, 0:129], SK[:, j * 128:(j + 1) * 128],
                        MEMZB[pr][:, :], start=True, stop=True),
                        reads=["ks", ("memzb", pr)], writes=[tPS[5]],
                        tag=f"p2 km {sfx}j{j}")
                    p.op("vector", lambda e: e.reciprocal(RZ[:, 2:3],
                                                          PS[5][:, 128:129]),
                         reads=[tPS[5]], writes=["rz"])
                    p.op("vector", lambda e, j=j: e.scalar_tensor_tensor(
                        DLT[:, j * 128:(j + 1) * 128], PS[5][:, 0:128],
                        RZ[:, 2:3], VT[:, j * 128:(j + 1) * 128],
                        ALU.mult, ALU.subtract),
                        reads=[tPS[5], "rz", "vt"], writes=["dlt"],
                        tag=f"p2 dlt {sfx}j{j}")
                for j in range(4):
                    p.op("tensor", lambda e, j=j: e.matmul(
                        PS[6][:, 0:128], SKT[:, j * 128:(j + 1) * 128],
                        DLT[:, j * 128:(j + 1) * 128],
                        start=(j == 0), stop=(j == 3)),
                        reads=["skt", "dlt"], writes=[tPS[6]],
                        tag=f"p2 upd {sfx}j{j}")
                p.op("vector", lambda e, pr=pr: e.tensor_sub(
                    MEMZ[pr][:, 0:128], MEMZ[pr][:, 0:128], PS[6][:, 0:128]),
                    reads=[tPS[6], ("memz", pr)], writes=[("memz", pr)],
                    tag=f"p2 memup {sfx}")
                p.op("vector", lambda e: e.tensor_reduce(
                    RZ[:, 3:4], SK[:, :], AX.X, ALU.add),
                    reads=["ks"], writes=["rz"], tag=f"p2 zsum {sfx}")
                p.op("vector", lambda e, pr=pr: e.tensor_add(
                    MEMZ[pr][:, 128:129], MEMZ[pr][:, 128:129], RZ[:, 3:4]),
                    reads=["rz", ("memz", pr)], writes=[("memz", pr)],
                    tag=f"p2 zup {sfx}")
                p.op("scalar", lambda e, pr=pr: e.copy(MEMZB[pr][:, :], MEMZ[pr][:, :]),
                     reads=[("memz", pr)], writes=[("memzb", pr)],
                     tag=f"p2 memzb {sfx}")
                if getattr(b, "fused", False):
                    eng = "sync" if pr % 2 == 0 else "scalar"
                    ap = bass.AP(att_s, (seg // 4) * NATQ
                                 + ((bb * 2 + hl) * 4 + seg % 4) * (128 * 512),
                                 [[512, 128], [1, 512]])
                    p.op(eng, lambda e, ap=ap: e.dma_start(
                        out=ap, in_=ATTT[:, :]),
                        reads=["atttx"], writes=[("attout", pr, seg)],
                        dma=("p2att", eng), tag=f"p2 attst {sfx}")
                else:
                    eng = "sync" if pr % 2 == 0 else "gpsimd"
                    p.op(eng, lambda e, hl=hl, bb=bb, base=base, seg=seg: e.dma_start(
                        out=att_s[hl, bb, :, seg * SEG:(seg + 1) * SEG],
                        in_=ATTT[:, :]),
                        reads=["atttx"], writes=[("attout", pr, seg)],
                        dma=("p2att", eng), tag=f"p2 attst {sfx}")
        if getattr(b, "att_qdone", None) is not None and seg % 4 == 3:
            b.att_qdone(seg // 4)


def build_phase2_v0():
    nc = bass.Bass("TRN2", target_bir_lowering=False, debug=False)
    h_full = nc.dram_tensor("h_full", [NCH, 128, B * S], BF16, kind="ExternalInput")
    wqkv = nc.dram_tensor("wqkv", [D, 768], F32, kind="ExternalInput")
    betas_s = nc.dram_tensor("betas_s", [HL, 128], F32, kind="ExternalInput")
    att_s = nc.dram_tensor("att_s", [HL, B, 128, S], BF16, kind="ExternalOutput")
    p = Prog(nc)
    b = Bld(nc, p)
    b.PS = [b.ps(f"p2_ps{k}", (128, 512), F32) for k in range(7)]
    b.tPS = [("PS", k) for k in range(7)]
    b.PSB = b.ps("p2_psb", (128, 1024), BF16)
    b.tPSB = ("PSB",)
    onr = b.sb("p2_ones_row", (1, 128), BF16)
    p.op("vector", lambda e: e.memset(onr[:, :], 1.0), writes=["p3or"])
    b.ones_row = onr
    emit_phase2(b, h_full, wqkv, betas_s, att_s)
    p.build()
    return nc



def emit_phase0(b, w_up, w_dn, wup_sl, wdn_sl, wo_sl, wmod_sl, cT,
                mod_part, wqup_sl, wqdn_sl, wob_sl):
    nc, p = b.nc, b.p
    PS, tPS = b.PS, b.tPS

    onesf = b.sb("p0_onesf", (128, 1), F32)
    p.op("vector", lambda e: e.memset(onesf[:, :], 1.0), writes=["p0of"])

    SC = b.sb("p0_sc", (128, 2 * NCH), BF16)
    SCF = b.sb("p0_scf", (128, 2 * NCH), F32)
    WM = [b.sb(f"p0_wm{i}", (128, 1536), BF16) for i in range(2)]
    WMF = [b.sb(f"p0_wmf{i}", (128, 1536), F32) for i in range(2)]
    MODE = b.sb("p0_mode", (2, 1536), F32)
    for c in range(NCH):
        p.op("sync", lambda e, c=c: e.dma_start(
            out=SCF[:, c * 2:(c + 1) * 2], in_=cT[c, :, :]),
            writes=["p0scf"], dma=("p0c",), tag=f"p0 cload {c}")
    p.op("scalar", lambda e: e.activation(SC[:, :], SCF[:, :], AF.Silu),
         reads=["p0scf"], writes=["p0sc"], tag="p0 silu")
    for c in range(NCH):
        par = c % 2
        p.op("sync", lambda e, c=c, par=par: e.dma_start(
            out=WMF[par][:, :], in_=wmod_sl[c * 128:(c + 1) * 128, :]),
            writes=[("p0wmf", par)], dma=("p0wm", par), tag=f"p0 wmload {c}")
        p.op("scalar", lambda e, par=par: e.copy(WM[par][:, :], WMF[par][:, :]),
             reads=[("p0wmf", par)], writes=[("p0wm", par)], tag=f"p0 wmcast {c}")
        for n3 in range(3):
            p.op("tensor", lambda e, c=c, n3=n3, par=par: e.matmul(
                PS[n3][0:2, :], SC[:, c * 2:(c + 1) * 2],
                WM[par][:, n3 * 512:(n3 + 1) * 512],
                start=(c == 0), stop=(c == NCH - 1)),
                reads=["p0sc", ("p0wm", par)], writes=[tPS[n3]],
                tag=f"p0 modmm c{c}n{n3}")
    for n3 in range(3):
        p.op("vector", lambda e, n3=n3: e.tensor_copy(
            MODE[0:2, n3 * 512:(n3 + 1) * 512], PS[n3][0:2, :]),
            reads=[tPS[n3]], writes=["p0mode"], tag=f"p0 modev {n3}")
    p.op("sync", lambda e: e.dma_start(out=mod_part[:, :], in_=MODE[0:2, :]),
         reads=["p0mode"], writes=["p0modout"], dma=("p0mo",), tag="p0 modst")

    WT_ = [b.sb(f"p0_wt{i}", (128, 4096), F32) for i in range(2)]
    ACC = b.sb("p0_acc", (128, 2), F32)
    SCL = b.sb("p0_scl", (128, 4), F32)
    red = b.sb("p0_red", (128, 1), F32)
    p.op("vector", lambda e: e.memset(ACC[:, :], 0.0), writes=["p0acc"])
    for wi, (wfull, nrow, ncol) in enumerate([(w_up, FF, D), (w_dn, D, FF)]):
        ntile = nrow // 128
        for t in range(ntile):
            par = t % 2
            p.op("sync", lambda e, t=t, par=par, wfull=wfull, ncol=ncol: e.dma_start(
                out=WT_[par][:, 0:ncol], in_=wfull[t * 128:(t + 1) * 128, :]),
                writes=[("p0wt", par)], dma=("p0wr", par), tag=f"p0 wstat {wi}t{t}")
            p.op("vector", lambda e, par=par, ncol=ncol: e.tensor_reduce(
                red[:, :], WT_[par][:, 0:ncol], AX.X, ALU.max if False else ALU.add,
                apply_absolute_value=True),
                reads=[("p0wt", par)], writes=["p0red"], tag=f"p0 wred {wi}t{t}")
            p.op("vector", lambda e, wi=wi: e.tensor_add(
                ACC[:, wi:wi + 1], ACC[:, wi:wi + 1], red[:, :]),
                reads=["p0red", "p0acc"], writes=["p0acc"], tag=f"p0 wacc {wi}t{t}")
    p.op("tensor", lambda e: e.matmul(PS[3][0:1, 0:2], onesf[:, :], ACC[:, :],
                                      start=True, stop=True),
         reads=["p0acc", "p0of"], writes=[tPS[3]], tag="p0 wsum")
    p.op("vector", lambda e: e.tensor_scalar(
        PS[3][0:1, 2:4], PS[3][0:1, 0:2], 1.0 / (FF * D), EPS_Q, ALU.mult, ALU.max),
        reads=[tPS[3]], writes=[tPS[3]], tag="p0 wscale")
    p.op("vector", lambda e: e.tensor_copy(SCL[0:1, 0:2], PS[3][0:1, 2:4]),
         reads=[tPS[3]], writes=["p0scl"], tag="p0 sclev")
    onesfr = b.sb("p0_onesfr", (1, 128), F32)
    p.op("vector", lambda e: e.memset(onesfr[:, :], 1.0), writes=["p0ofr"])
    p.op("tensor", lambda e: e.matmul(PS[4][:, 0:2], onesfr[:, :], SCL[0:1, 0:2],
                                      start=True, stop=True),
         reads=["p0scl", "p0ofr"], writes=[tPS[4]], tag="p0 sbc")
    p.op("vector", lambda e: e.tensor_copy(SCL[:, 0:2], PS[4][:, 0:2]),
         reads=[tPS[4]], writes=["p0scl"])
    p.op("vector", lambda e: e.reciprocal(SCL[:, 2:4], SCL[:, 0:2]),
         reads=["p0scl"], writes=["p0scl"])

    QT = b.sb("p0_qt", (128, 4096), F32)
    QA = b.sb("p0_qa", (128, 4096), F32)
    QO = b.sb("p0_qo", (128, 4096), BF16)
    for wi, (wsl, wqout, nrow, ncol) in enumerate(
            [(wup_sl, wqup_sl, FF // 8, D), (wdn_sl, wqdn_sl, D // 8, FF)]):
        ntile = nrow // 128
        for t in range(ntile):
            par = t % 2
            p.op("sync", lambda e, t=t, par=par, wsl=wsl, ncol=ncol: e.dma_start(
                out=WT_[par][:, 0:ncol], in_=wsl[t * 128:(t + 1) * 128, :]),
                writes=[("p0wt", par)], dma=("p0wq", par), tag=f"p0 wqload {wi}t{t}")
            p.op("vector", lambda e, par=par, wi=wi, ncol=ncol: e.tensor_scalar(
                QT[:, 0:ncol], WT_[par][:, 0:ncol], SCL[:, 2 + wi:3 + wi], None,
                ALU.mult),
                reads=[("p0wt", par), "p0scl"], writes=["p0qt"],
                tag=f"p0 wqs {wi}t{t}")
            p.op("vector", lambda e, ncol=ncol: e.tensor_single_scalar(
                QA[:, 0:ncol], QT[:, 0:ncol], 0.5, ALU.is_gt),
                reads=["p0qt"], writes=["p0qa"], tag=f"p0 wqa {wi}t{t}")
            p.op("vector", lambda e, ncol=ncol: e.tensor_single_scalar(
                QT[:, 0:ncol], QT[:, 0:ncol], -0.5, ALU.is_lt),
                reads=["p0qt"], writes=["p0qt"], tag=f"p0 wqb {wi}t{t}")
            p.op("vector", lambda e, ncol=ncol: e.tensor_sub(
                QA[:, 0:ncol], QA[:, 0:ncol], QT[:, 0:ncol]),
                reads=["p0qa", "p0qt"], writes=["p0qa"], tag=f"p0 wqc {wi}t{t}")
            p.op("vector", lambda e, wi=wi, ncol=ncol: e.tensor_scalar(
                QO[:, 0:ncol], QA[:, 0:ncol], SCL[:, wi:wi + 1], None, ALU.mult),
                reads=["p0qa", "p0scl"], writes=["p0qo"], tag=f"p0 wqd {wi}t{t}")
            p.op("gpsimd", lambda e, t=t, wqout=wqout, ncol=ncol: e.dma_start(
                out=wqout[t * 128:(t + 1) * 128, :], in_=QO[:, 0:ncol]),
                reads=["p0qo"], writes=[("p0qout", wi, t)], dma=("p0qst",),
                tag=f"p0 wqst {wi}t{t}")

    for t in range(2):
        par = t % 2
        p.op("sync", lambda e, t=t, par=par: e.dma_start(
            out=WT_[par][:, 0:D], in_=wo_sl[t * 128:(t + 1) * 128, :]),
            writes=[("p0wt", par)], dma=("p0wo", par), tag=f"p0 woload {t}")
        p.op("scalar", lambda e, par=par: e.copy(QO[:, 0:D], WT_[par][:, 0:D]),
             reads=[("p0wt", par)], writes=["p0qo"], tag=f"p0 wocast {t}")
        p.op("gpsimd", lambda e, t=t: e.dma_start(
            out=wob_sl[t * 128:(t + 1) * 128, :], in_=QO[:, 0:D]),
            reads=["p0qo"], writes=[("p0woout", t)], dma=("p0qst",),
            tag=f"p0 wost {t}")


def build_phase0_v0():
    nc = bass.Bass("TRN2", target_bir_lowering=False, debug=False)
    w_up = nc.dram_tensor("w_up", [FF, D], F32, kind="ExternalInput")
    w_dn = nc.dram_tensor("w_dn", [D, FF], F32, kind="ExternalInput")
    wup_sl = nc.dram_tensor("wup_sl", [FF // 8, D], F32, kind="ExternalInput")
    wdn_sl = nc.dram_tensor("wdn_sl", [D // 8, FF], F32, kind="ExternalInput")
    wo_sl = nc.dram_tensor("wo_sl", [D // 8, D], F32, kind="ExternalInput")
    wmod_sl = nc.dram_tensor("wmod_sl", [D, 1536], F32, kind="ExternalInput")
    cT = nc.dram_tensor("cT", [NCH, 128, 2], F32, kind="ExternalInput")
    mod_part = nc.dram_tensor("mod_part", [2, 1536], F32, kind="ExternalOutput")
    wqup_sl = nc.dram_tensor("wqup_sl", [FF // 8, D], BF16, kind="ExternalOutput")
    wqdn_sl = nc.dram_tensor("wqdn_sl", [D // 8, FF], BF16, kind="ExternalOutput")
    wob_sl = nc.dram_tensor("wob_sl", [D // 8, D], BF16, kind="ExternalOutput")
    p = Prog(nc)
    b = Bld(nc, p)
    b.PS = [b.ps(f"p0_ps{k}", (128, 512), F32) for k in range(5)]
    b.tPS = [("PS", k) for k in range(5)]
    emit_phase0(b, w_up, w_dn, wup_sl, wdn_sl, wo_sl, wmod_sl, cT,
                mod_part, wqup_sl, wqdn_sl, wob_sl)
    p.build()
    return nc



_RUN = [None]


def _runner(nc, in_maps):
    if _RUN[0] is not None:
        return _RUN[0](nc, in_maps)
    return [dict(r) for r in
            run_bass_kernel_spmd(nc, in_maps, core_ids=list(range(NCORE))).results]


def kernel(L0_attn_x, L0_attn_Wq, L0_attn_Wk, L0_attn_Wv, L0_attn_Wo,
           L0_attn_betas, L0_ffn_w_up, L0_ffn_w_down, L0_c, L0_wmod):
    x = np.ascontiguousarray(np.asarray(L0_attn_x, np.float32))
    Wq = np.asarray(L0_attn_Wq, np.float32)
    Wk = np.asarray(L0_attn_Wk, np.float32)
    Wv = np.asarray(L0_attn_Wv, np.float32)
    Wo = np.asarray(L0_attn_Wo, np.float32)
    betas = np.asarray(L0_attn_betas, np.float32)
    w_up = np.ascontiguousarray(np.asarray(L0_ffn_w_up, np.float32))
    w_dn = np.ascontiguousarray(np.asarray(L0_ffn_w_down, np.float32))
    cc = np.asarray(L0_c, np.float32)
    wmod = np.ascontiguousarray(np.asarray(L0_wmod, np.float32))

    cT = np.ascontiguousarray(cc.T).reshape(NCH, 128, 2)

    nc0 = build_phase0_v0()
    in0 = []
    for g in range(NCORE):
        in0.append({
            "w_up": w_up, "w_dn": w_dn,
            "wup_sl": np.ascontiguousarray(w_up[g * FF // 8:(g + 1) * FF // 8]),
            "wdn_sl": np.ascontiguousarray(w_dn[g * D // 8:(g + 1) * D // 8]),
            "wo_sl": np.ascontiguousarray(Wo[g * D // 8:(g + 1) * D // 8]),
            "wmod_sl": np.ascontiguousarray(wmod[:, g * 1536:(g + 1) * 1536]),
            "cT": cT,
        })
    r0 = _runner(nc0, in0)
    mod = np.concatenate([r0[g]["mod_part"] for g in range(NCORE)], axis=1)
    wq_up = np.concatenate([np.asarray(r0[g]["wqup_sl"]).view(BFNP)
                            for g in range(NCORE)], axis=0)
    wq_dn = np.concatenate([np.asarray(r0[g]["wqdn_sl"]).view(BFNP)
                            for g in range(NCORE)], axis=0)
    wo_b = np.concatenate([np.asarray(r0[g]["wob_sl"]).view(BFNP)
                           for g in range(NCORE)], axis=0)

    nc1 = build_phase1_v0()
    in1 = []
    for g in range(NCORE):
        bb, s0 = g // 4, (g % 4) * TC
        xT = np.ascontiguousarray(x[bb, s0:s0 + TC, :].T).reshape(NCH, 128, TC)
        in1.append({"xT": xT, "modrow": np.ascontiguousarray(mod[bb:bb + 1])})
    r1 = _runner(nc1, in1)
    h_full = np.concatenate(
        [np.asarray(r1[g]["h_out"]).view(BFNP) for g in range(NCORE)], axis=2)

    nc2 = build_phase2_v0()
    in2 = []
    for g in range(NCORE):
        wqkv = np.empty((D, 768), np.float32)
        for hl in range(HL):
            hh = HL * g + hl
            wqkv[:, 0 * 256 + hl * 128:0 * 256 + (hl + 1) * 128] = Wq[:, hh * 128:(hh + 1) * 128]
            wqkv[:, 1 * 256 + hl * 128:1 * 256 + (hl + 1) * 128] = Wk[:, hh * 128:(hh + 1) * 128]
            wqkv[:, 2 * 256 + hl * 128:2 * 256 + (hl + 1) * 128] = Wv[:, hh * 128:(hh + 1) * 128]
        bet = np.ascontiguousarray(betas[0, HL * g:HL * g + HL, 0, :])
        in2.append({"h_full": h_full, "wqkv": wqkv, "betas_s": bet})
    r2 = _runner(nc2, in2)

    wup_im = wq_up.reshape(FCH, 128, NCH, 128)
    wup_pcj = np.ascontiguousarray(
        wup_im.transpose(0, 3, 2, 1).reshape(FCH, 128, NCH * 128))
    wdnT = np.ascontiguousarray(wq_dn.T)
    nc3 = build_phase3_v0()
    in3 = []
    att_by_head = [np.asarray(r2[hh // HL]["att_s"]).view(BFNP)[hh % HL]
                   for hh in range(H)]
    for g in range(NCORE):
        bb, s0 = g // 4, (g % 4) * TC
        A = np.stack([a[bb] for a in att_by_head], axis=0)
        A5 = A.reshape(H, 128, NSEG, 32, 16)
        seg0 = s0 // SEG
        Ds = A5[:, :, seg0:seg0 + TC // SEG, :, :]
        attT = np.ascontiguousarray(
            Ds.transpose(4, 1, 2, 0, 3).reshape(NCH, 128, TC))
        in3.append({
            "x_rows": np.ascontiguousarray(x[bb, s0:s0 + TC, :]),
            "attT": attT, "wo_bf": wo_b, "wup_pcj": wup_pcj, "wdnT": wdnT,
            "modrow": np.ascontiguousarray(mod[bb:bb + 1]),
        })
    r3 = _runner(nc3, in3)
    out = np.empty((B, S, D), np.float32)
    for g in range(NCORE):
        bb, s0 = g // 4, (g % 4) * TC
        out[bb, s0:s0 + TC, :] = r3[g]["out_rows"]
    return out



def emit_phase0_fused(b, w_up, w_dn, wup_sl, wdn_sl, wo_sl, wmod_sl, cT,
                      mode_l, modag, wupsrc, wdnsrc, wosrc):
    nc, p = b.nc, b.p
    PS, tPS, PSB, tPSB = b.PS, b.tPS, b.PSB, b.tPSB
    ident = b.need_ident()

    onesf = b.sb("p0_onesf", (128, 1), F32)
    p.op("vector", lambda e: e.memset(onesf[:, :], 1.0), writes=["p0of"])

    SC = b.sb("p0_sc", (128, 2 * NCH), BF16)
    SCF = b.sb("p0_scf", (128, 2 * NCH), F32)
    WT_ = b.sb("p0_wt", (128, 8192), F32)
    WM = [b.sb(f"p0_wm{i}", (128, 1536), BF16) for i in range(2)]
    MODE = b.sb("p0_mode", (2, 1536), F32)
    for c in range(NCH):
        p.op("sync", lambda e, c=c: e.dma_start(
            out=SCF[:, c * 2:(c + 1) * 2], in_=cT[c, :, :]),
            writes=["p0scf"], dma=("p0c",), tag=f"p0 cload {c}")
    p.op("scalar", lambda e: e.activation(SC[:, :], SCF[:, :], AF.Silu),
         reads=["p0scf"], writes=["p0sc"], tag="p0 silu")
    for c in range(NCH):
        par = c % 2
        p.op("sync", lambda e, c=c, par=par: e.dma_start(
            out=WT_[:, par * 4096:par * 4096 + 1536],
            in_=wmod_sl[c * 128:(c + 1) * 128, :]),
            writes=[("p0wt", par)], dma=("p0wm", par), tag=f"p0 wmload {c}")
        p.op("scalar", lambda e, par=par: e.copy(
            WM[par][:, :], WT_[:, par * 4096:par * 4096 + 1536]),
            reads=[("p0wt", par)], writes=[("p0wm", par)], tag=f"p0 wmcast {c}")
        for n3 in range(3):
            p.op("tensor", lambda e, c=c, n3=n3, par=par: e.matmul(
                PS[n3][0:2, :], SC[:, c * 2:(c + 1) * 2],
                WM[par][:, n3 * 512:(n3 + 1) * 512],
                start=(c == 0), stop=(c == NCH - 1)),
                reads=["p0sc", ("p0wm", par)], writes=[tPS[n3]],
                tag=f"p0 modmm c{c}n{n3}")
    for n3 in range(3):
        p.op("vector", lambda e, n3=n3: e.tensor_copy(
            MODE[0:2, n3 * 512:(n3 + 1) * 512], PS[n3][0:2, :]),
            reads=[tPS[n3]], writes=["p0mode"], tag=f"p0 modev {n3}")
    p.op("sync", lambda e: e.dma_start(out=mode_l[:, :], in_=MODE[0:2, :]),
         reads=["p0mode"], writes=["p0modl"], dma=("p0ml",), tag="p0 modst")
    p.op("gpsimd", lambda e: e.collective_compute(
        "AllGather", ALU.bypass, RG8, [mode_l[:, :]], [modag[:, :, :]]),
        reads=["p0modl"], writes=["modag"], mark=True, tag="p0 modag")

    ACC = b.sb("p0_acc", (128, 2), F32)
    SCL = b.sb("p0_scl", (128, 4), F32)
    red = b.sb("p0_red", (128, 1), F32)
    p.op("vector", lambda e: e.memset(ACC[:, :], 0.0), writes=["p0acc"])
    for wi, (wfull, nrow, ncol) in enumerate([(w_up, FF, D), (w_dn, D, FF)]):
        ntile = nrow // 128
        for t in range(ntile):
            par = t % 2
            p.op("sync", lambda e, t=t, par=par, wfull=wfull, ncol=ncol: e.dma_start(
                out=WT_[:, par * 4096:par * 4096 + ncol],
                in_=wfull[t * 128:(t + 1) * 128, :]),
                writes=[("p0wt", par)], dma=("p0wr", par), tag=f"p0 wstat {wi}t{t}")
            p.op("vector", lambda e, par=par, ncol=ncol: e.tensor_reduce(
                red[:, :], WT_[:, par * 4096:par * 4096 + ncol], AX.X, ALU.add,
                apply_absolute_value=True),
                reads=[("p0wt", par)], writes=["p0red"], tag=f"p0 wred {wi}t{t}")
            p.op("vector", lambda e, wi=wi: e.tensor_add(
                ACC[:, wi:wi + 1], ACC[:, wi:wi + 1], red[:, :]),
                reads=["p0red", "p0acc"], writes=["p0acc"], tag=f"p0 wacc {wi}t{t}")
    p.op("tensor", lambda e: e.matmul(PS[3][0:1, 0:2], onesf[:, :], ACC[:, :],
                                      start=True, stop=True),
         reads=["p0acc", "p0of"], writes=[tPS[3]], tag="p0 wsum")
    p.op("vector", lambda e: e.tensor_scalar(
        PS[3][0:1, 2:4], PS[3][0:1, 0:2], 1.0 / (FF * D), EPS_Q, ALU.mult, ALU.max),
        reads=[tPS[3]], writes=[tPS[3]], tag="p0 wscale")
    p.op("vector", lambda e: e.tensor_copy(SCL[0:1, 0:2], PS[3][0:1, 2:4]),
         reads=[tPS[3]], writes=["p0scl"], tag="p0 sclev")
    onesfr = b.sb("p0_onesfr", (1, 128), F32)
    p.op("vector", lambda e: e.memset(onesfr[:, :], 1.0), writes=["p0ofr"])
    p.op("tensor", lambda e: e.matmul(PS[4][:, 0:2], onesfr[:, :], SCL[0:1, 0:2],
                                      start=True, stop=True),
         reads=["p0scl", "p0ofr"], writes=[tPS[4]], tag="p0 sbc")
    p.op("vector", lambda e: e.tensor_copy(SCL[:, 0:2], PS[4][:, 0:2]),
         reads=[tPS[4]], writes=["p0scl"])
    p.op("vector", lambda e: e.reciprocal(SCL[:, 2:4], SCL[:, 0:2]),
         reads=["p0scl"], writes=["p0scl"])

    QT = b.sb("p0_qt", (128, 4096), F32)
    QA = b.sb("p0_qa", (128, 4096), F32)
    QO = [b.sb(f"p0_qo{i}", (128, 4096), BF16) for i in range(2)]
    IMG = [b.sb(f"p0_img{i}", (128, 2048), BF16) for i in range(2)]

    def quant_tile(wi, t, par, wsl, ncol, tag):
        p.op("sync", lambda e, t=t, par=par, wsl=wsl, ncol=ncol: e.dma_start(
            out=WT_[:, par * 4096:par * 4096 + ncol],
            in_=wsl[t * 128:(t + 1) * 128, :]),
            writes=[("p0wt", par)], dma=("p0wq", par), tag=f"{tag} load")
        p.op("vector", lambda e, par=par, wi=wi, ncol=ncol: e.tensor_scalar(
            QT[:, 0:ncol], WT_[:, par * 4096:par * 4096 + ncol],
            SCL[:, 2 + wi:3 + wi], None, ALU.mult),
            reads=[("p0wt", par), "p0scl"], writes=["p0qt"], tag=f"{tag} s")
        p.op("vector", lambda e, ncol=ncol: e.tensor_single_scalar(
            QA[:, 0:ncol], QT[:, 0:ncol], 0.5, ALU.is_gt),
            reads=["p0qt"], writes=["p0qa"], tag=f"{tag} a")
        p.op("vector", lambda e, ncol=ncol: e.tensor_single_scalar(
            QT[:, 0:ncol], QT[:, 0:ncol], -0.5, ALU.is_lt),
            reads=["p0qt"], writes=["p0qt"], tag=f"{tag} b")
        p.op("vector", lambda e, ncol=ncol: e.tensor_sub(
            QA[:, 0:ncol], QA[:, 0:ncol], QT[:, 0:ncol]),
            reads=["p0qa", "p0qt"], writes=["p0qa"], tag=f"{tag} c")
        p.op("vector", lambda e, wi=wi, par=par, ncol=ncol: e.tensor_scalar(
            QO[par][:, 0:ncol], QA[:, 0:ncol], SCL[:, wi:wi + 1], None, ALU.mult),
            reads=["p0qa", "p0scl"], writes=[("p0qo", par)], tag=f"{tag} d")

    for t in range(4):
        par = t % 2
        quant_tile(0, t, par, wup_sl, D, f"p0 up t{t}")
        for c in range(NCH):
            p.op("tensor", lambda e, c=c, par=par: e.transpose(
                PSB[:, (c % 4) * 128:(c % 4 + 1) * 128],
                QO[par][:, c * 128:(c + 1) * 128], ident[:, :]),
                reads=[("p0qo", par), "ident"], writes=[tPSB],
                tag=f"p0 uptr t{t}c{c}")
            p.op("vector", lambda e, c=c, par=par: e.tensor_copy(
                IMG[par][:, c * 128:(c + 1) * 128],
                PSB[:, (c % 4) * 128:(c % 4 + 1) * 128]),
                reads=[tPSB], writes=[("p0img", par)], tag=f"p0 upev t{t}c{c}")
        p.op("sync", lambda e, t=t, par=par: e.dma_start(
            out=wupsrc[t, :, :], in_=IMG[par][:, :]),
            reads=[("p0img", par)], writes=[("wupsrc", t)], dma=("p0st", par),
            tag=f"p0 upst t{t}")

    for t in range(2):
        quant_tile(1, t, t, wdn_sl, FF, f"p0 dn t{t}")
    for fc in range(FCH):
        par = fc % 2
        for t in range(2):
            p.op("tensor", lambda e, fc=fc, t=t, par=par: e.transpose(
                PSB[:, (t + 2 * (fc % 2)) * 128:(t + 2 * (fc % 2) + 1) * 128],
                QO[t][:, fc * 128:(fc + 1) * 128], ident[:, :]),
                reads=[("p0qo", t), "ident"], writes=[tPSB],
                tag=f"p0 dntr f{fc}t{t}")
            p.op("vector", lambda e, fc=fc, t=t, par=par: e.tensor_copy(
                IMG[par][:, t * 128:(t + 1) * 128],
                PSB[:, (t + 2 * (fc % 2)) * 128:(t + 2 * (fc % 2) + 1) * 128]),
                reads=[tPSB], writes=[("p0img", par)], tag=f"p0 dnev f{fc}t{t}")
        p.op("sync", lambda e, fc=fc, par=par: e.dma_start(
            out=wdnsrc[fc * 128:(fc + 1) * 128, :], in_=IMG[par][:, 0:256]),
            reads=[("p0img", par)], writes=[("wdnsrc", fc)], dma=("p0st", par),
            tag=f"p0 dnst f{fc}")

    for t in range(2):
        par = t % 2
        p.op("sync", lambda e, t=t, par=par: e.dma_start(
            out=WT_[:, par * 4096:par * 4096 + D],
            in_=wo_sl[t * 128:(t + 1) * 128, :]),
            writes=[("p0wt", par)], dma=("p0wo", par), tag=f"p0 woload {t}")
        p.op("scalar", lambda e, par=par: e.copy(
            IMG[par][:, :], WT_[:, par * 4096:par * 4096 + D]),
            reads=[("p0wt", par)], writes=[("p0img", par)], tag=f"p0 wocast {t}")
        p.op("sync", lambda e, t=t, par=par: e.dma_start(
            out=wosrc[t, :, :], in_=IMG[par][:, :]),
            reads=[("p0img", par)], writes=[("wosrc", t)], dma=("p0st", par),
            tag=f"p0 wost {t}")


def emit_att_ag_q(b, atq_src, atg, q):
    p = b.p
    toks = [("attout", pr, seg)
            for pr in range(4) for seg in range(4 * q, 4 * q + 4)]
    p.op("gpsimd", lambda e, q=q: e.collective_compute(
        "AllGather", ALU.bypass, RG8, [atq_src[q, :]], [atg[q, :, :]]),
        reads=toks, writes=[("atag", q)], mark=True, tag=f"p2 atag q{q}")


def build_fused():
    nc = bass.Bass("TRN2", target_bir_lowering=False, debug=False,
                   dynamic_dma_scratch_size=8192)
    xT = nc.dram_tensor("xT", [NCH, 128, TC], F32, kind="ExternalInput")
    x_rows = nc.dram_tensor("x_rows", [TC, D], F32, kind="ExternalInput")
    wqkv = nc.dram_tensor("wqkv", [D, 768], F32, kind="ExternalInput")
    betas_s = nc.dram_tensor("betas_s", [HL, 128], F32, kind="ExternalInput")
    w_up = nc.dram_tensor("w_up", [FF, D], F32, kind="ExternalInput")
    w_dn = nc.dram_tensor("w_dn", [D, FF], F32, kind="ExternalInput")
    wup_sl = nc.dram_tensor("wup_sl", [FF // 8, D], F32, kind="ExternalInput")
    wdn_sl = nc.dram_tensor("wdn_sl", [D // 8, FF], F32, kind="ExternalInput")
    wo_sl = nc.dram_tensor("wo_sl", [D // 8, D], F32, kind="ExternalInput")
    wmod_sl = nc.dram_tensor("wmod_sl", [D, 1536], F32, kind="ExternalInput")
    cT = nc.dram_tensor("cT", [NCH, 128, 2], F32, kind="ExternalInput")
    bflag = nc.dram_tensor("bflag", [1, 1], F32, kind="ExternalInput")
    out_rows = nc.dram_tensor("out_rows", [TC, D], F32, kind="ExternalOutput")
    onehot8 = nc.dram_tensor("onehot8", [1, 8], F32, kind="ExternalInput")
    mode_l = nc.dram_tensor("mode_l", [2, 1536], F32, kind="Internal")
    modag = nc.dram_tensor("modag", [8, 2, 1536], F32, kind="Internal")
    hsrc = nc.dram_tensor("hsrc", [4, 128, NCH * WT], BF16, kind="Internal")
    hga = nc.dram_tensor("hga", [4, 8, 128, NCH * WT], BF16, kind="Internal")
    atq_src = nc.dram_tensor("atq_src", [4, NATQ], BF16, kind="Internal")
    atg = nc.dram_tensor("atg", [4, 8, NATQ], BF16, kind="Internal")
    wupsrc = nc.dram_tensor("wupsrc", [4, 128, D], BF16, kind="Internal")
    wupga = nc.dram_tensor("wupga", [8, 4, 128, D], BF16, kind="Internal")
    wdnsrc = nc.dram_tensor("wdnsrc", [FF, 256], BF16, kind="Internal")
    wdnga = nc.dram_tensor("wdnga", [8, FF, 256], BF16, kind="Internal")
    wosrc = nc.dram_tensor("wosrc", [2, 128, D], BF16, kind="Internal")
    woga = nc.dram_tensor("woga", [8, 2, 128, D], BF16, kind="Internal")

    p = Prog(nc)
    b = Bld(nc, p)
    b.fused = True
    b.modag = modag

    A_f32a = nc.alloc_sbuf_tensor("A_f32a", [128, 8192], F32)
    A_bf1 = nc.alloc_sbuf_tensor("A_bf1", [128, 16384], BF16)
    A_bf2 = nc.alloc_sbuf_tensor("A_bf2", [128, 8192], BF16)
    A_bf3 = nc.alloc_sbuf_tensor("A_bf3", [128, 8192], BF16)
    A_bf4 = nc.alloc_sbuf_tensor("A_bf4", [128, 8192], BF16)
    bf1_f32 = A_bf1[:, :].bitcast(F32)

    SCR4K = nc.alloc_sbuf_tensor("SCR4K", [128, 1024], F32)
    T1S = nc.alloc_sbuf_tensor("T1S", [128, 512], F32)
    T2S = nc.alloc_sbuf_tensor("T2S", [128, 512], F32)
    ROWS = nc.alloc_sbuf_tensor("ROWS", [1, 2048], F32)
    b.share = {
        "p3_atw": A_bf4,
        "p1_mrbc": SCR4K,
        "p3_qu": SCR4K[:, 0:512], "p3_qzi": SCR4K[:, 512:1024].bitcast(I32),
        "p1_t1": T1S, "p3_qzf": T1S,
        "p1_t2": T2S, "p3_qc": T2S,
        "p1_stats": ROWS[0:1, 0:1536],
        "p3_gsc": ROWS[0:1, 0:2048],
        "p0_wt": A_f32a,
        "p0_qt": bf1_f32[:, 0:4096], "p0_qa": bf1_f32[:, 4096:8192],
        "p0_qo0": A_bf4[:, 0:4096], "p0_qo1": A_bf4[:, 4096:8192],
        "p0_img0": A_bf2[:, 0:2048], "p0_img1": A_bf2[:, 2048:4096],
        "p0_wm0": A_bf3[:, 0:1536], "p0_wm1": A_bf3[:, 1536:3072],
        "p1_xstg0": A_f32a, "p1_xstg1": bf1_f32,
        "p1_xb0": A_bf2, "p1_xb1": A_bf3, "p1_xsq": A_bf4,
        "p1_hwin0": A_bf1[:, 0:8192], "p1_hwin1": A_bf1[:, 8192:16384],
        "p2_wstg": A_f32a[:, 0:768],
        "p2_wqkv": A_bf1[:, 0:NCH * 768],
        "p2_ht0": A_bf2, "p2_ht1": A_bf3,
        "p2_q": A_bf4[:, 0:512], "p2_k": A_bf4[:, 512:1024],
        "p2_v": A_bf4[:, 1024:1536], "p2_eq": A_bf4[:, 1536:2048],
        "p2_sq": A_bf4[:, 2048:2560], "p2_sk": A_bf4[:, 2560:3072],
        "p2_vt": A_bf4[:, 3072:3584], "p2_skt": A_bf4[:, 3584:4096],
        "p2_dlt": A_bf4[:, 4096:4608], "p2_psm": A_bf4[:, 4608:5120],
        "p2_pt": A_bf4[:, 5120:7168], "p2_attt": A_bf4[:, 7168:7680],
        "p3_x2t": A_f32a, "p3_fa": A_bf1, "p3_fb": A_bf2, "p3_gb": A_bf3,
        "p3_xq": A_bf4,
    }

    b.PS = [b.ps(f"ps{k}", (128, 512), F32) for k in range(7)]
    b.tPS = [("PS", k) for k in range(7)]
    b.PSB = b.ps("psb", (128, 1024), BF16)
    b.tPSB = ("PSB",)
    onr = b.sb("ones_row_b", (1, 128), BF16)
    p.op("vector", lambda e: e.memset(onr[:, :], 1.0), writes=["p3or"])
    b.ones_row = onr

    FB1 = b.sb("fb1", (1, 1), F32)
    FLAG128 = b.sb("flag128", (128, 1), F32)
    onesfr0 = b.sb("onesfr0", (1, 128), F32)
    p.op("vector", lambda e: e.memset(onesfr0[:, :], 1.0), writes=["of0"])
    p.op("sync", lambda e: e.dma_start(out=FB1[:, :], in_=bflag[:, :]),
         writes=["fb1"], dma=("bfl",))
    p.op("tensor", lambda e: e.matmul(b.PS[5][:, 0:1], onesfr0[:, :], FB1[0:1, :],
                                      start=True, stop=True),
         reads=["fb1", "of0"], writes=[b.tPS[5]])
    p.op("vector", lambda e: e.tensor_copy(FLAG128[:, :], b.PS[5][:, 0:1]),
         reads=[b.tPS[5]], writes=["flag128"])
    b.flag128 = FLAG128
    b.tflag = "flag128"
    b.flag1 = FB1
    b.tflag1 = "fb1"
    b.gsc3 = ROWS[0:1, 0:2048]
    b.gsc3_t = ROWS

    OH = b.sb("oh8", (1, 8), F32)
    GF = b.sb("gf8", (128, 8), F32)
    p.op("sync", lambda e: e.dma_start(out=OH[:, :], in_=onehot8[:, :]),
         writes=["oh8"], dma=("oh8",))
    p.op("tensor", lambda e: e.matmul(b.PS[6][:, 0:8], onesfr0[:, :], OH[0:1, :],
                                      start=True, stop=True),
         reads=["oh8", "of0"], writes=[b.tPS[6]])
    p.op("vector", lambda e: e.tensor_copy(GF[:, :], b.PS[6][:, 0:8]),
         reads=[b.tPS[6]], writes=["gf"])
    b.GF = GF

    emit_phase0_fused(b, w_up, w_dn, wup_sl, wdn_sl, wo_sl, wmod_sl, cT,
                      mode_l, modag, wupsrc, wdnsrc, wosrc)
    p.fence()

    def h_store(w, HW, tHW):
        p.op("sync", lambda e, w=w, HW=HW: e.dma_start(
            out=hsrc[w, :, :], in_=HW[:, :]),
            reads=[tHW], writes=[("hsrc", w)], dma=("p1hs", w % 2),
            tag=f"p1 hst w{w}")
        p.op("gpsimd", lambda e, w=w: e.collective_compute(
            "AllGather", ALU.bypass, RG8, [hsrc[w, :, :]], [hga[w, :, :, :]]),
            reads=[("hsrc", w)], writes=[("hag", w)], mark=True, tag=f"p1 hag w{w}")

    emit_phase1(b, xT, None, h_store)
    p.fence()

    p.op("gpsimd", lambda e: e.collective_compute(
        "AllGather", ALU.bypass, RG8, [wupsrc[:, :, :]], [wupga[:, :, :, :]]),
        reads=[("wupsrc", t) for t in range(4)], writes=["wupag"], mark=True, tag="ag wup")
    p.op("gpsimd", lambda e: e.collective_compute(
        "AllGather", ALU.bypass, RG8, [wdnsrc[:, :]], [wdnga[:, :, :]]),
        reads=[("wdnsrc", fc) for fc in range(FCH)], writes=["wdnag"],
        mark=True, tag="ag wdn")
    p.op("gpsimd", lambda e: e.collective_compute(
        "AllGather", ALU.bypass, RG8, [wosrc[:, :, :]], [woga[:, :, :, :]]),
        reads=[("wosrc", t) for t in range(2)], writes=["woag"], mark=True, tag="ag wo")

    b.att_qdone = lambda q: emit_att_ag_q(b, atq_src, atg, q)
    emit_phase2(b, hga, wqkv, betas_s, atq_src)
    b.att_qdone = None
    p.fence()

    emit_phase3(b, x_rows, atg, woga, wupga, wdnga, None, out_rows)
    p.build()
    return nc


def kernel_fused(L0_attn_x, L0_attn_Wq, L0_attn_Wk, L0_attn_Wv, L0_attn_Wo,
                 L0_attn_betas, L0_ffn_w_up, L0_ffn_w_down, L0_c, L0_wmod):
    x = np.ascontiguousarray(np.asarray(L0_attn_x, np.float32))
    Wq = np.asarray(L0_attn_Wq, np.float32)
    Wk = np.asarray(L0_attn_Wk, np.float32)
    Wv = np.asarray(L0_attn_Wv, np.float32)
    Wo = np.ascontiguousarray(np.asarray(L0_attn_Wo, np.float32))
    betas = np.asarray(L0_attn_betas, np.float32)
    w_up = np.ascontiguousarray(np.asarray(L0_ffn_w_up, np.float32))
    w_dn = np.ascontiguousarray(np.asarray(L0_ffn_w_down, np.float32))
    cc = np.asarray(L0_c, np.float32)
    wmod = np.ascontiguousarray(np.asarray(L0_wmod, np.float32))
    cT = np.ascontiguousarray(cc.T).reshape(NCH, 128, 2)
    wmod_v = wmod.reshape(D, 6, 8, 256)
    nc = build_fused()
    in_maps = []
    for g in range(NCORE):
        bb, s0 = g // 4, (g % 4) * TC
        wqkv_g = np.empty((D, 768), np.float32)
        for hl in range(HL):
            hh = HL * g + hl
            wqkv_g[:, 0 * 256 + hl * 128:0 * 256 + (hl + 1) * 128] = Wq[:, hh * 128:(hh + 1) * 128]
            wqkv_g[:, 1 * 256 + hl * 128:1 * 256 + (hl + 1) * 128] = Wk[:, hh * 128:(hh + 1) * 128]
            wqkv_g[:, 2 * 256 + hl * 128:2 * 256 + (hl + 1) * 128] = Wv[:, hh * 128:(hh + 1) * 128]
        in_maps.append({
            "xT": np.ascontiguousarray(x[bb, s0:s0 + TC, :].T).reshape(NCH, 128, TC),
            "x_rows": np.ascontiguousarray(x[bb, s0:s0 + TC, :]),
            "wqkv": wqkv_g,
            "betas_s": np.ascontiguousarray(betas[0, HL * g:HL * g + HL, 0, :]),
            "w_up": w_up, "w_dn": w_dn,
            "wup_sl": np.ascontiguousarray(w_up[g * FF // 8:(g + 1) * FF // 8]),
            "wdn_sl": np.ascontiguousarray(w_dn[g * D // 8:(g + 1) * D // 8]),
            "wo_sl": np.ascontiguousarray(Wo[g * D // 8:(g + 1) * D // 8]),
            "wmod_sl": np.ascontiguousarray(wmod_v[:, :, g, :].reshape(D, 1536)),
            "cT": cT,
            "bflag": np.array([[float(bb)]], np.float32),
            "onehot8": np.eye(8, dtype=np.float32)[g:g + 1],
        })
    res = _runner(nc, in_maps)
    out = np.empty((B, S, D), np.float32)
    for g in range(NCORE):
        bb, s0 = g // 4, (g % 4) * TC
        out[bb, s0:s0 + TC, :] = res[g]["out_rows"]
    return out


_kernel_v0 = kernel
kernel = kernel_fused


# version 06
# speedup vs baseline: 1.2549x; 1.0612x over previous
import math
import numpy as np
import ml_dtypes
import concourse.bass as bass
import concourse.mybir as mybir
from concourse.bass_utils import run_bass_kernel_spmd

F32 = mybir.dt.float32
BF16 = mybir.dt.bfloat16
I32 = mybir.dt.int32
AF = mybir.ActivationFunctionType
ALU = mybir.AluOpType
AX = mybir.AxisListType
BFNP = ml_dtypes.bfloat16

B, S, D = 2, 8192, 2048
H, DK, DV, SEG, NSEG = 16, 128, 128, 512, 16
NCH, FF, FCH = 16, 4096, 32
NCORE = 8
TC = S * B // NCORE
HL = H // NCORE
NW = 4
WT = TC // NW
EPS_LN = 1e-5
EPS_Q = 1e-5
QB = 127.0
C1 = 1.0 / math.sqrt(D)
C2 = 1.0 / math.sqrt(FF)
RG8 = [[0, 1, 2, 3, 4, 5, 6, 7]]
NATQ = 2 * 2 * 4 * 128 * 512
SCALE_ATT = float(DK ** 0.5)

ENGS = ("sync", "gpsimd", "scalar", "vector", "tensor")
import os as _os
FORCE_SYNC_DMA = _os.environ.get("MK_FORCE_SYNC_DMA", "0") == "1"


def _hadamard(n):
    h = np.array([[1.0]])
    while h.shape[0] < n:
        h = np.block([[h, h], [h, -h]])
    return h


class Prog:

    _fv_ok = frozenset({
        "b", "p", "nc", "e", "xT", "modrow", "h_out", "x_rows", "attT",
        "wo_bf", "wup_pcj", "wdnT", "out_rows", "ident", "H128", "ones_col",
        "ones_row", "catile", "satile", "ca_ap", "sa_ap", "xstg", "xb",
        "xsq", "hwin", "stat_s", "mb", "rb", "mrbc", "t1", "t2", "ps_st",
        "ps_bc", "PS", "PSB", "tPS", "tPSB", "ATW", "XR", "X2T", "H2TOK",
        "H2C", "FA", "FB", "GB", "XQ", "WOS", "WUP", "WDN", "QU", "QZI",
        "QZF", "QC", "QM", "QSW", "QS", "QI", "ST", "OT", "GABC", "GBBC",
        "CBBC", "SBBC", "tGA", "tGB", "tCB", "tSB", "gsc", "row", "rowb",
        "out", "tok", "name", "modoff", "t_eps", "t_half", "wsl", "h_store",
        "gbrow", "gbrow_b", "GBBC0", "width", "src", "tsrc", "dst", "tdst",
        "dst_slices", "cnorm", "tag", "gat", "cbt", "sbt", "h128_dr", "ga_transform", "quant", "tf_ga",
        "tf_cb1", "tf_id", "transform", "XS", "XB", "HW", "nch", "wq_", "HT", "onr", "WSTG", "WQKV", "BROW", "BROWF", "BBC", "MBC", "CMASK", "MEMZ", "MEMZB", "HTB", "Q", "K", "V", "EQ", "SQ", "SK", "VT", "SKT", "DLT", "PT", "PSM", "DEN", "NMS", "RZ", "T1", "T2", "ATT", "ATTT", "mask_dr", "h_full", "wqkv", "betas_s", "att_s", "wb_parts", "wmod_sl", "abs_parts", "SC", "SCF", "WM", "WMF", "MODE", "WT_", "ACC", "SCL", "red", "onesf", "onesfr", "QT", "QA", "QO", "w_up", "w_dn", "wup_sl", "wdn_sl", "wo_sl", "cT", "mod_part", "wqup_sl", "wqdn_sl", "wob_sl", "onesfr0", "FB1", "FLAG128", "IMG", "quant_tile", "rs_mod", "rs_wt", "rs_h", "rs_att", "modg", "wupg", "wdng", "wog", "hg", "attall_loc", "attg2", "sel2", "xspar", "bflag", "store_toks", "modg_v",
        "SELT", "CAND", "CAND8", "ACR", "hsrc", "hga", "mode_l", "modag", "atq_src", "atg",
        "wupsrc", "wupga", "wdnsrc", "wdnga", "wosrc", "woga", "OH", "GF",
        "onehot8",
    })

    def __init__(self, nc):
        self.nc = nc
        self.ops = []
        self.last_writer = {}
        self.readers = {}

    def fence(self):
        lasts = {}
        dlast = {}
        for i, rec in enumerate(self.ops):
            lasts[rec["engine"]] = i
            if rec["dma"] is not None:
                dlast[rec["dma"]] = i
        deps = set(lasts.values()) | set(dlast.values())
        fids = []
        for e in ENGS:
            i = len(self.ops)
            self.ops.append(dict(engine=e, fn=lambda en: en.nop(), dma=None,
                                 deps=set(deps), marked=False, tag="fence",
                                 extra_waits=(), bcast=False))
            fids.append(i)
        self.last_writer = {t: fids[0] for t in self.last_writer}
        for t in self.readers:
            self.readers[t] = list(fids)

    def op(self, engine, fn, reads=(), writes=(), dma=None, tag="",
           extra_waits=(), bcast=False, mark=False):
        fv = fn.__code__.co_freevars
        bad = [v for v in fv if v not in self._fv_ok]
        if bad:
            raise RuntimeError(f"op fn has late-bound free vars {bad} (tag={tag})")
        if dma is not None and FORCE_SYNC_DMA and engine == "gpsimd":
            engine = "sync"
        i = len(self.ops)
        deps = set()
        for t in reads:
            j = self.last_writer.get(t)
            if j is not None:
                deps.add(j)
        for t in writes:
            j = self.last_writer.get(t)
            if j is not None:
                deps.add(j)
            for r in self.readers.get(t, ()):
                deps.add(r)
        self.ops.append(dict(engine=engine, fn=fn, dma=dma, deps=deps,
                             marked=(dma is not None) or mark, tag=tag,
                             extra_waits=tuple(extra_waits), bcast=bcast))
        for t in reads:
            self.readers.setdefault(t, []).append(i)
        for t in writes:
            self.last_writer[t] = i
            self.readers[t] = []
        return i

    def build(self):
        nc = self.nc
        ops = self.ops
        for rec in ops:
            for j in rec["deps"]:
                o = ops[j]
                o["marked"] = True
        dkeys = []
        seen = set()
        for rec in ops:
            if rec["dma"] is not None and rec["dma"] not in seen:
                seen.add(rec["dma"])
                dkeys.append(rec["dma"])
        self._semctx = []

        def mksem(name):
            cm = nc.semaphore(name)
            h = cm.__enter__()
            self._semctx.append(cm)
            return h

        esem = {e: mksem(f"pe_{e}") for e in ENGS}
        dsem = {k: mksem(f"pd_{i}") for i, k in enumerate(dkeys)}
        ecnt = {e: 0 for e in ENGS}
        dcnt = {k: 0 for k in dkeys}
        for rec in ops:
            rec["dpre"] = None
            if not rec["marked"] and rec["dma"] is None:
                pass
            if rec["dma"] is not None:
                dcnt[rec["dma"]] += 16
                rec["sem"] = ("d", rec["dma"])
                rec["post"] = dcnt[rec["dma"]]
            elif rec["marked"]:
                ecnt[rec["engine"]] += 1
                rec["sem"] = ("e", rec["engine"])
                rec["post"] = ecnt[rec["engine"]]
        self.etotal, self.dtotal = ecnt, dcnt
        run = {k: 0 for k in dkeys}
        for rec in ops:
            rec["dwait"] = {}
            for j in rec["deps"]:
                o = ops[j]
                if o["dma"] is not None:
                    k = o["dma"]
                    rec["dwait"][k] = run[k] + 16
            for k in list(rec["dwait"]):
                rec["dwait"][k] = run[k]
            if rec["dma"] is not None:
                run[rec["dma"]] += 16

        blockcm = nc.Block()
        block = blockcm.__enter__()

        def semh(key):
            return esem[key[1]] if key[0] == "e" else dsem[key[1]]

        def make(engname):
            def body(eng):
                guaranteed = {}
                for rec in ops:
                    if rec["engine"] != engname:
                        continue
                    need = {}
                    for j in rec["deps"]:
                        o = ops[j]
                        if o["dma"] is not None:
                            k = ("d", o["dma"])
                            v = rec["dwait"][o["dma"]]
                            if v > guaranteed.get(k, 0):
                                need[k] = max(need.get(k, 0), v)
                            continue
                        k, v = o["sem"], o["post"]
                        if v > guaranteed.get(k, 0):
                            need[k] = max(need.get(k, 0), v)
                    for k in sorted(need, key=str):
                        eng.wait_ge(semh(k), need[k])
                        guaranteed[k] = need[k]
                    for sh, sv in rec["extra_waits"]:
                        kx = ("x", sh.num)
                        if sv > guaranteed.get(kx, 0):
                            eng.wait_ge(sh, sv)
                            guaranteed[kx] = sv
                    if rec["bcast"]:
                        rec["fn"](eng, dsem[rec["dma"]])
                    else:
                        inst = rec["fn"](eng)
                        if rec["marked"]:
                            inst.then_inc(semh(rec["sem"]),
                                          16 if rec["dma"] is not None else 1)
                if engname == "sync":
                    for k in dkeys:
                        eng.wait_ge(dsem[k], dcnt[k])
                    for e2 in ENGS:
                        if e2 != "sync" and ecnt[e2]:
                            eng.wait_ge(esem[e2], ecnt[e2])
            return body

        block.sync(make("sync"))
        block.gpsimd(make("gpsimd"))
        block.scalar(make("scalar"))
        block.vector(make("vector"))
        block.tensor(make("tensor"))
        blockcm.__exit__(None, None, None)
        for cm in reversed(self._semctx):
            cm.__exit__(None, None, None)


class Bld:

    def __init__(self, nc, p):
        self.nc = nc
        self.p = p
        self.sbuf_bytes = 0
        self._uid = 0
        self._consts = {}
        self.ident = None

    def uid(self):
        self._uid += 1
        return self._uid

    def sb(self, name, shape, dtype):
        sh = getattr(self, "share", None)
        if sh is not None and name in sh:
            return sh[name]
        t = self.nc.alloc_sbuf_tensor(name, list(shape), dtype)
        self.sbuf_bytes += shape[1] * mybir.dt.size(dtype) * 1
        return t

    def ps(self, name, shape, dtype=F32):
        return self.nc.alloc_psum_tensor(name, list(shape), dtype)

    def const_imm(self, val):
        key = ("imm", float(val))
        if key in self._consts:
            return self._consts[key]
        t = self.sb(f"cimm_{self.uid()}", (128, 1), F32)
        self.p.op("vector", lambda e, t=t, v=float(val): e.memset(t[:, :], v),
                  writes=[key], tag="cimm")
        self.nc.const_aps.register(F32, float(val), t[:, 0:1])
        self._consts[key] = key
        return key

    def const_tile(self, val, dtype=F32):
        key = (float(val), dtype)
        if key in self._consts:
            return self._consts[key]
        t = self.sb(f"const_{self.uid()}", (128, 1), dtype)
        self.p.op("vector", lambda e, t=t, v=float(val): e.memset(t[:, :], v),
                  writes=[("const", key)], tag="const")
        self._consts[key] = (t, ("const", key))
        return self._consts[key]

    def need_ident(self):
        if self.ident is None:
            ident_np = np.eye(128, dtype=BFNP)
            dr = self.nc.inline_tensor(np.asarray(ident_np), name="ident128")
            t = self.sb("identsb", (128, 128), BF16)
            self.p.op("sync", lambda e, t=t, dr=dr: e.dma_start(out=t[:, :], in_=dr[:, :]),
                      writes=["ident"], dma="identload", tag="ident")
            self.ident = t
        return self.ident



def emit_phase1(b, xT, modrow, h_store):
    nc, p = b.nc, b.p
    t_eps = b.const_imm(EPS_LN)

    ones_col = b.sb("p1_ones_col", (128, 1), BF16)
    p.op("vector", lambda e: e.memset(ones_col[:, :], 1.0), writes=["p1oc"])
    ones_row = b.sb("p1_ones_row", (1, 128), BF16)
    p.op("vector", lambda e: e.memset(ones_row[:, :], 1.0), writes=["p1or"])

    catile = b.sb("p1_ca", (128, NCH), F32)
    satile = b.sb("p1_sa", (128, NCH), F32)
    if getattr(b, "fused", False):
        SELT = b.sb("p1_modsel", (128, NCH), F32)
        for vv, tile, tok in ((0, satile, "p1sa"), (1, catile, "p1ca")):
            for c1 in range(2):
                a0 = bass.AP(b.modag, 0 * 1536 + vv * 256 + c1 * 128,
                             [[1, 128], [3072, 8]])
                a1 = bass.AP(b.modag, 1 * 1536 + vv * 256 + c1 * 128,
                             [[1, 128], [3072, 8]])
                d0 = bass.AP(tile, c1, [[NCH, 128], [2, 8]])
                d1 = bass.AP(SELT, c1, [[NCH, 128], [2, 8]])
                p.op("sync", lambda e, d0=d0, a0=a0: e.dma_start(
                    out=d0, in_=a0, allow_slow_non_contiguous=True),
                    reads=["modag"], writes=[(tok, c1)],
                    dma=("p1mod", vv, c1, 0))
                p.op("sync", lambda e, d1=d1, a1=a1: e.dma_start(
                    out=d1, in_=a1, allow_slow_non_contiguous=True),
                    reads=["modag"], writes=[("p1sel", c1)],
                    dma=("p1mod", vv, c1, 1))
            p.op("vector", lambda e, tile=tile: e.tensor_sub(
                SELT[:, :], SELT[:, :], tile[:, :]),
                reads=[("p1sel", 0), ("p1sel", 1), (tok, 0), (tok, 1)],
                writes=[("p1sel", 0), ("p1sel", 1)])
            p.op("vector", lambda e, tile=tile: e.scalar_tensor_tensor(
                tile[:, :], SELT[:, :], b.flag128[:, 0:1], tile[:, :],
                ALU.mult, ALU.add),
                reads=[("p1sel", 0), ("p1sel", 1), b.tflag, (tok, 0), (tok, 1)],
                writes=[tok])
    else:
        ca_ap = bass.AP(modrow, D, [[1, 128], [128, NCH]])
        sa_ap = bass.AP(modrow, 0, [[1, 128], [128, NCH]])
        p.op("sync", lambda e: e.dma_start(out=catile[:, :], in_=ca_ap,
                                           allow_slow_non_contiguous=True), writes=["p1ca"], dma="p1modca")
        p.op("sync", lambda e: e.dma_start(out=satile[:, :], in_=sa_ap,
                                           allow_slow_non_contiguous=True), writes=["p1sa"], dma="p1modsa")
    p.op("vector", lambda e: e.tensor_single_scalar(catile[:, :], catile[:, :], 1.0, ALU.add),
         reads=["p1ca"], writes=["p1ca"])

    xstg = [b.sb(f"p1_xstg{i}", (128, NCH * WT), F32) for i in range(2)]
    xb = [b.sb(f"p1_xb{i}", (128, NCH * WT), BF16) for i in range(2)]
    xsq = b.sb("p1_xsq", (128, NCH * WT), BF16)
    hwin = [b.sb(f"p1_hwin{i}", (128, NCH * WT), BF16) for i in range(2)]
    stat_s = b.sb("p1_stats", (1, 6 * WT), F32)
    mb = b.sb("p1_mb", (1, WT), BF16)
    rb = b.sb("p1_rb", (1, WT), BF16)
    mrbc = b.sb("p1_mrbc", (128, 2 * WT), F32)
    t1 = b.sb("p1_t1", (128, WT), F32)
    t2 = b.sb("p1_t2", (128, WT), F32)

    PS = b.PS
    tPS = b.tPS

    for w in range(NW):
        par = w % 2
        xspar = 0 if getattr(b, "fused", False) else par
        XS, XB, HW = xstg[xspar], xb[par], hwin[par]
        tXS, tXB, tHW = ("p1xs", xspar), ("p1xb", par), ("p1hw", par)
        if getattr(b, "fused", False):
            xsrc = bass.AP(xT, w * WT, [[TC, 128], [128 * TC, NCH], [1, WT]])
            xdst = bass.AP(XS, 0, [[NCH * WT, 128], [WT, NCH], [1, WT]])
            eng = "sync" if w % 2 == 0 else "scalar"
            p.op(eng, lambda e, xsrc=xsrc, xdst=xdst: e.dma_start(
                out=xdst, in_=xsrc),
                writes=[tXS], dma=("p1x", par, eng), tag=f"p1 xload w{w}")
        else:
            for c in range(NCH):
                eng = "sync" if c % 2 == 0 else "gpsimd"
                p.op(eng, lambda e, c=c, w=w, XS=XS: e.dma_start(
                    out=XS[:, c * WT:(c + 1) * WT], in_=xT[c, :, w * WT:(w + 1) * WT]),
                    writes=[tXS], dma=("p1x", par, eng), tag=f"p1 xload w{w}c{c}")
        p.op("scalar", lambda e, XS=XS, XB=XB: e.copy(XB[:, :], XS[:, :]),
             reads=[tXS], writes=[tXB], tag=f"p1 cast w{w}")
        p.op("scalar", lambda e, XB=XB: e.activation(xsq[:, :], XB[:, :], AF.Square),
             reads=[tXB], writes=["p1xsq"], tag=f"p1 sq w{w}")
        for c in range(NCH):
            p.op("tensor", lambda e, c=c, XB=XB: e.matmul(
                PS[0][0:1, 0:WT], ones_col[:, :], XB[:, c * WT:(c + 1) * WT],
                start=(c == 0), stop=(c == NCH - 1)),
                reads=[tXB, "p1oc"], writes=[tPS[0]], tag=f"p1 sum w{w}c{c}")
        for c in range(NCH):
            p.op("tensor", lambda e, c=c: e.matmul(
                PS[1][0:1, 0:WT], ones_col[:, :], xsq[:, c * WT:(c + 1) * WT],
                start=(c == 0), stop=(c == NCH - 1)),
                reads=["p1xsq", "p1oc"], writes=[tPS[1]], tag=f"p1 sumsq w{w}c{c}")
        p.op("vector", lambda e: e.tensor_single_scalar(
            stat_s[0:1, 0:WT], PS[0][0:1, 0:WT], 1.0 / D, ALU.mult),
            reads=[tPS[0]], writes=["p1stat"])
        p.op("vector", lambda e: e.tensor_mul(
            stat_s[0:1, 2 * WT:3 * WT], stat_s[0:1, 0:WT], stat_s[0:1, 0:WT]),
            reads=["p1stat"], writes=["p1stat"])
        p.op("vector", lambda e: e.scalar_tensor_tensor(
            stat_s[0:1, WT:2 * WT], PS[1][0:1, 0:WT], 1.0 / D,
            stat_s[0:1, 2 * WT:3 * WT], ALU.mult, ALU.subtract),
            reads=[tPS[1], "p1stat"], writes=["p1stat"])
        p.op("scalar", lambda e: e.activation(
            stat_s[0:1, WT:2 * WT], stat_s[0:1, WT:2 * WT], AF.Sqrt,
            bias=EPS_LN, scale=1.0), reads=["p1stat", t_eps], writes=["p1stat"])
        p.op("vector", lambda e: e.reciprocal(
            stat_s[0:1, WT:2 * WT], stat_s[0:1, WT:2 * WT]),
            reads=["p1stat"], writes=["p1stat"])
        p.op("vector", lambda e: e.tensor_copy(mb[0:1, :], stat_s[0:1, 0:WT]),
             reads=["p1stat"], writes=["p1mb"])
        p.op("vector", lambda e: e.tensor_copy(rb[0:1, :], stat_s[0:1, WT:2 * WT]),
             reads=["p1stat"], writes=["p1rb"])
        p.op("tensor", lambda e: e.matmul(PS[2][:, :], ones_row[:, :], mb[0:1, :],
                                          start=True, stop=True),
             reads=["p1mb", "p1or"], writes=[tPS[2]])
        p.op("tensor", lambda e: e.matmul(PS[3][:, :], ones_row[:, :], rb[0:1, :],
                                          start=True, stop=True),
             reads=["p1rb", "p1or"], writes=[tPS[3]])
        p.op("scalar", lambda e: e.copy(mrbc[:, 0:WT], PS[2][:, :]),
             reads=[tPS[2]], writes=["p1mrbc"])
        p.op("scalar", lambda e: e.copy(mrbc[:, WT:2 * WT], PS[3][:, :]),
             reads=[tPS[3]], writes=["p1mrbc"])
        for c in range(NCH):
            p.op("vector", lambda e, c=c, XB=XB: e.tensor_sub(
                t1[:, :], XB[:, c * WT:(c + 1) * WT], mrbc[:, 0:WT]),
                reads=[tXB, "p1mrbc"], writes=["p1t1"])
            p.op("vector", lambda e: e.tensor_mul(t2[:, :], t1[:, :], mrbc[:, WT:2 * WT]),
                 reads=["p1t1", "p1mrbc"], writes=["p1t2"])
            p.op("vector", lambda e, c=c, HW=HW: e.tensor_scalar(
                HW[:, c * WT:(c + 1) * WT], t2[:, :], catile[:, c:c + 1],
                satile[:, c:c + 1], ALU.mult, ALU.add),
                reads=["p1t2", "p1ca", "p1sa"], writes=[tHW])
        h_store(w, HW, tHW)


def build_phase1_v0():
    nc = bass.Bass("TRN2", target_bir_lowering=False, debug=False)
    xT = nc.dram_tensor("xT", [NCH, 128, TC], F32, kind="ExternalInput")
    modrow = nc.dram_tensor("modrow", [1, 6 * D], F32, kind="ExternalInput")
    h_out = nc.dram_tensor("h_out", [NCH, 128, TC], BF16, kind="ExternalOutput")
    p = Prog(nc)
    b = Bld(nc, p)
    b.PS = [b.ps(f"p1_ps{k}", (128, 512), F32) for k in range(4)]
    b.tPS = [("PS", k) for k in range(4)]

    def h_store(w, HW, tHW):
        for c in range(NCH):
            eng = "sync" if c % 2 == 0 else "gpsimd"
            p.op(eng, lambda e, c=c, w=w, HW=HW: e.dma_start(
                out=h_out[c, :, w * WT:(w + 1) * WT],
                in_=HW[:, c * WT:(c + 1) * WT]),
                reads=[tHW], writes=[("hout", w, c)], dma=("p1h", w % 2, eng))

    emit_phase1(b, xT, modrow, h_store)
    p.build()
    return nc



def _bcast_row(b, name, modoff, transform, outdt=BF16):
    nc, p = b.nc, b.p
    if not hasattr(b, "row"):
        b.row = b.sb("shared_row", (1, D), F32)
        b.rowb = b.sb("shared_rowb", (1, D), BF16)
    row, rowb = b.row, b.rowb
    out = b.sb(f"{name}_bc", (128, D), outdt)
    tok = f"{name}bc"
    if getattr(b, "fused", False):
        v = modoff // D
        src0 = bass.AP(b.modag, 0 * 1536 + v * 256, [[3072, 8], [1, 256]])
        src1 = bass.AP(b.modag, 1 * 1536 + v * 256, [[3072, 8], [1, 256]])
        dstr = bass.AP(row, 0, [[D, 1], [256, 8], [1, 256]])
        dstg = bass.AP(b.gsc3_t, 0, [[b.gsc3_t.shape[1], 1], [256, 8], [1, 256]])
        p.op("sync", lambda e, src0=src0, dstr=dstr: e.dma_start(
            out=dstr, in_=src0),
            reads=["modag"], writes=["sharedrow"], dma=("p3crow", name))
        p.op("sync", lambda e, src1=src1, dstg=dstg: e.dma_start(
            out=dstg, in_=src1),
            reads=["modag"], writes=["p3gscrow"], dma=("p3crow2", name))
        p.op("vector", lambda e: e.tensor_sub(b.gsc3[:, :], b.gsc3[:, :], row[:, :]),
             reads=["p3gscrow", "sharedrow"], writes=["p3gscrow"])
        p.op("vector", lambda e: e.scalar_tensor_tensor(
            row[:, :], b.gsc3[:, :], b.flag1[0:1, 0:1], row[:, :],
            ALU.mult, ALU.add),
            reads=["p3gscrow", b.tflag1, "sharedrow"], writes=["sharedrow"])
    else:
        p.op("sync", lambda e: e.dma_start(
            out=row[:, :], in_=bass.AP(b.modrow, modoff, [[D, 1], [1, D]])),
            writes=["sharedrow"], dma=("p3crow", name))
    transform(row, "sharedrow")
    p.op("vector", lambda e: e.tensor_copy(rowb[:, :], row[:, :]),
         reads=["sharedrow"], writes=["sharedrowb"])
    for k in range(4):
        p.op("tensor", lambda e, k=k: e.matmul(
            b.PS[k][:, :], b.ones_row[:, :], rowb[0:1, k * 512:(k + 1) * 512],
            start=True, stop=True), reads=["sharedrowb", "p3or"], writes=[b.tPS[k]])
        p.op("scalar", lambda e, k=k: e.copy(out[:, k * 512:(k + 1) * 512], b.PS[k][:, :]),
             reads=[b.tPS[k]], writes=[tok])
    return out, tok


def emit_phase3(b, x_rows, attT, wo_bf, wup_pcj, wdnT, modrow, out_rows):
    nc, p = b.nc, b.p
    b.modrow = modrow
    t_eps = b.const_imm(EPS_LN)
    t_half = b.const_imm(0.5)
    ident = b.need_ident()

    h128_np = _hadamard(128).astype(BFNP)
    h128_dr = nc.inline_tensor(np.asarray(h128_np), name="h128")
    H128 = b.sb("p3_h128", (128, 128), BF16)
    p.op("sync", lambda e: e.dma_start(out=H128[:, :], in_=h128_dr[:, :]),
         writes=["h128"], dma="p3ch128")
    if getattr(b, "ones_row", None) is None:
        ones_row = b.sb("p3_ones_row", (1, 128), BF16)
        p.op("vector", lambda e: e.memset(ones_row[:, :], 1.0), writes=["p3or"])
        b.ones_row = ones_row
    else:
        ones_row = b.ones_row

    if getattr(b, "PS", None) is not None and len(b.PS) >= 7:
        PS, tPS = b.PS, b.tPS
        PSB, tPSB = b.PSB, b.tPSB
    else:
        PS = [b.ps(f"p3_ps{k}", (128, 512), F32) for k in range(7)]
        PSB = b.ps("p3_psb", (128, 1024), BF16)
        tPS = [("PS", k) for k in range(7)]
        tPSB = ("PSB",)
        b.PS, b.tPS = PS, tPS
        b.PSB, b.tPSB = PSB, tPSB

    gsc = b.sb("p3_gsc", (1, D), F32)

    def tf_ga(row, tok):
        p.op("vector", lambda e, row=row: e.tensor_scalar(
            gsc[:, :], row[:, :], 0.0, 2.0, ALU.is_ge, ALU.mult),
            reads=[tok], writes=["p3gscrow"])
        p.op("vector", lambda e: e.tensor_single_scalar(
            gsc[:, :], gsc[:, :], -1.0, ALU.add), reads=["p3gscrow"], writes=["p3gscrow"])
        p.op("scalar", lambda e, row=row: e.activation(row[:, :], row[:, :], AF.Abs),
             reads=[tok], writes=[tok])
        p.op("vector", lambda e, row=row: e.tensor_single_scalar(
            row[:, :], row[:, :], 0.25, ALU.max), reads=[tok], writes=[tok])
        p.op("vector", lambda e, row=row: e.tensor_mul(row[:, :], row[:, :], gsc[:, :]),
             reads=[tok, "p3gscrow"], writes=[tok])

    def tf_cb1(row, tok):
        p.op("vector", lambda e, row=row: e.tensor_single_scalar(
            row[:, :], row[:, :], 1.0, ALU.add), reads=[tok], writes=[tok])

    def tf_id(row, tok):
        pass

    GABC, tGA = _bcast_row(b, "p3ga", 2 * D, tf_ga)
    GBBC, tGB = _bcast_row(b, "p3gb", 5 * D, tf_ga)
    CBBC, tCB = _bcast_row(b, "p3cb", 4 * D, tf_cb1)
    SBBC, tSB = _bcast_row(b, "p3sb", 3 * D, tf_id)

    ATW = b.sb("p3_atw", (128, NCH * WT), BF16)
    tATW = "xq" if getattr(b, "fused", False) else "atw"
    CAND8 = [b.sb(f"p3_cand8{i}", (128, 8 * 512), BF16) for i in range(2)]
    tCAND = getattr(b, "cand_toks", [("cand", 0), ("cand", 1)])
    XR = [b.sb(f"p3_xr{i}", (128, 512), F32) for i in range(2)]
    X2T = b.sb("p3_x2t", (128, 4 * D), F32)
    H2TOK = b.sb("p3_h2tok", (128, D), BF16)
    H2C = b.sb("p3_h2c", (128, 1024), BF16)
    FA = b.sb("p3_fa", (128, FCH * WT), BF16)
    FB = b.sb("p3_fb", (128, NCH * WT), BF16)
    GB = b.sb("p3_gb", (128, NCH * WT), BF16)
    XQ = b.sb("p3_xq", (128, NCH * WT), BF16)
    WOS = [b.sb(f"p3_wos{i}", (128, 512), BF16) for i in range(2)]
    WUP = [b.sb(f"p3_wup{i}", (128, NCH * 128), BF16) for i in range(2)]
    WDN = [b.sb(f"p3_wdn{i}", (128, 512), BF16) for i in range(2)]
    QU = b.sb("p3_qu", (128, WT), F32)
    QZI = b.sb("p3_qzi", (128, WT), I32)
    QZF = b.sb("p3_qzf", (128, WT), F32)
    QC = b.sb("p3_qc", (128, WT), F32)
    QM = FB
    QSW = b.sb("p3_qsw", (128, WT), BF16)
    QS = b.sb("p3_qs", (128, WT), F32)
    QI = b.sb("p3_qi", (128, WT), F32)
    ST = b.sb("p3_st", (128, 8), F32)
    OT = [b.sb(f"p3_ot{i}", (128, 512), F32) for i in range(2)]

    def quant(nch, width, src, tsrc, dst_slices, tdst, cnorm, wq_, tag):
        p.op("scalar", lambda e: e.activation(
            QM[:, 0:nch * width], src[:, 0:nch * width], AF.Abs),
            reads=[tsrc], writes=["fb"], tag=f"{tag} abs")
        n = nch
        while n > 1:
            n2 = n // 2
            p.op("vector", lambda e, n2=n2, n=n: e.tensor_max(
                QM[:, 0:n2 * width], QM[:, 0:n2 * width], QM[:, n2 * width:n * width]),
                reads=["fb"], writes=["fb"])
            n = n2
        for half in (64, 32):
            pairs = [(0, half)] if half == 64 else [(0, 32), (64, 96)]
            for lo, hi in pairs:
                p.op("sync", lambda e, lo=lo, hi=hi, hh=half: e.dma_start(
                    out=QSW[lo:lo + hh, 0:width], in_=QM[hi:hi + hh, 0:width]),
                    reads=["fb"], writes=["qsw"], dma=("p3swap",))
                p.op("sync", lambda e, lo=lo, hi=hi, hh=half: e.dma_start(
                    out=QSW[hi:hi + hh, 0:width], in_=QM[lo:lo + hh, 0:width]),
                    reads=["fb"], writes=["qsw"], dma=("p3swap",))
            p.op("vector", lambda e: e.tensor_max(
                QM[:, 0:width], QM[:, 0:width], QSW[:, 0:width]),
                reads=["fb", "qsw"], writes=["fb"])
        for bit in (16, 8, 4, 2, 1):
            p.op("vector", lambda e, bit=bit: e.stream_shuffle(
                QSW[:, 0:width], QM[:, 0:width], [i ^ bit for i in range(32)]),
                reads=["fb"], writes=["qsw"])
            p.op("vector", lambda e: e.tensor_max(
                QM[:, 0:width], QM[:, 0:width], QSW[:, 0:width]),
                reads=["fb", "qsw"], writes=["fb"])
        p.op("vector", lambda e: e.tensor_single_scalar(
            QS[:, 0:width], QM[:, 0:width], EPS_Q / cnorm, ALU.max),
            reads=["fb"], writes=["qs"])
        p.op("vector", lambda e: e.tensor_single_scalar(
            QI[:, 0:width], QS[:, 0:width], cnorm / QB, ALU.mult),
            reads=["qs"], writes=["qi"])
        p.op("vector", lambda e: e.reciprocal(QS[:, 0:width], QS[:, 0:width]),
             reads=["qs"], writes=["qs"])
        p.op("vector", lambda e: e.tensor_single_scalar(
            QS[:, 0:width], QS[:, 0:width], QB, ALU.mult), reads=["qs"], writes=["qs"])
        for c in range(nch):
            dt_, dsl = dst_slices(c)
            ssl = slice(c * width, (c + 1) * width)
            p.op("vector", lambda e, ssl=ssl: e.tensor_mul(
                QU[:, 0:width], src[:, ssl], QS[:, 0:width]),
                reads=[tsrc, "qs"], writes=["qu"])
            p.op("scalar", lambda e: e.activation(
                QZI[:, 0:width], QU[:, 0:width], AF.Copy, bias=0.5, scale=1.0),
                reads=["qu", t_half], writes=["qzi"])
            p.op("scalar", lambda e: e.copy(QZF[:, 0:width], QZI[:, 0:width]),
                 reads=["qzi"], writes=["qzf"])
            p.op("vector", lambda e: e.scalar_tensor_tensor(
                QC[:, 0:width], QZF[:, 0:width], 0.5, QU[:, 0:width],
                ALU.subtract, ALU.is_gt),
                reads=["qzf", "qu"], writes=["qc"])
            p.op("vector", lambda e: e.tensor_sub(
                QC[:, 0:width], QZF[:, 0:width], QC[:, 0:width]),
                reads=["qzf", "qc"], writes=["qc"])
            p.op("vector", lambda e, dt_=dt_, dsl=dsl: e.tensor_mul(
                dt_[:, dsl], QC[:, 0:width], QI[:, 0:width]),
                reads=["qc", "qi"], writes=[tdst], tag=f"{tag} out c{c}")

    for w in range(NW):
        wsl = slice(w * WT, (w + 1) * WT)
        if getattr(b, "fused", False):
            for hh in range(H):
                R_, hl_ = hh // HL, hh % HL
                cpar = hh % 2
                ct = CAND8[cpar].tensor if hasattr(CAND8[cpar], "tensor") \
                    else CAND8[cpar]
                coff = CAND8[cpar].offset if hasattr(CAND8[cpar], "offset") else 0
                crow = int(np.prod(ct.shape[1:]))
                eng = "sync" if hh % 2 == 0 else "scalar"
                for bbx in range(2):
                    src = bass.AP(attT, R_ * NATQ
                                  + ((bbx * 2 + hl_) * 4 + w) * (128 * 512),
                                  [[512, 128], [8 * NATQ, 4], [1, 512]])
                    dst = bass.AP(ct, coff + bbx * 4 * 512,
                                  [[crow, 128], [512, 4], [1, 512]])
                    p.op(eng, lambda e, src=src, dst=dst: e.dma_start(
                        out=dst, in_=src),
                        reads=[("atag", 0), ("atag", 1), ("atag", 2), ("atag", 3)],
                        writes=[tCAND[cpar]],
                        dma=("p3att", cpar, bbx), tag=f"p3 atcand w{w}h{hh}b{bbx}")
                for k8 in range(8):
                    if k8 == 0:
                        p.op("vector", lambda e, hh=hh, k8=k8, cpar=cpar: e.tensor_scalar(
                            ATW[:, hh * WT:(hh + 1) * WT],
                            CAND8[cpar][:, k8 * 512:(k8 + 1) * 512],
                            b.GF[:, k8:k8 + 1], None, ALU.mult),
                            reads=[tCAND[cpar], "gf"], writes=[tATW],
                            tag=f"p3 atsel w{w}h{hh}k{k8}")
                    else:
                        p.op("vector", lambda e, hh=hh, k8=k8, cpar=cpar: e.scalar_tensor_tensor(
                            ATW[:, hh * WT:(hh + 1) * WT],
                            CAND8[cpar][:, k8 * 512:(k8 + 1) * 512],
                            b.GF[:, k8:k8 + 1], ATW[:, hh * WT:(hh + 1) * WT],
                            ALU.mult, ALU.add),
                            reads=[tCAND[cpar], "gf", tATW], writes=[tATW],
                            tag=f"p3 atsel w{w}h{hh}k{k8}")
        else:
            for c in range(NCH):
                eng = "sync" if c % 2 == 0 else "gpsimd"
                p.op(eng, lambda e, c=c, wsl=wsl: e.dma_start(
                    out=ATW[:, c * WT:(c + 1) * WT], in_=attT[c, :, wsl]),
                    writes=[tATW], dma=("p3att", eng), tag=f"p3 attload w{w}c{c}")
        for dq in range(4):
            for c1 in range(NCH):
                par = c1 % 2
                if getattr(b, "fused", False):
                    p.op("sync", lambda e, c1=c1, dq=dq, par=par: e.dma_start(
                        out=WOS[par][:, :],
                        in_=wo_bf[c1 // 2, c1 % 2, :, dq * 512:(dq + 1) * 512]),
                        reads=["woag"],
                        writes=[("wos", par)], dma=("p3wo", par),
                        tag=f"p3 woload w{w}q{dq}c{c1}")
                else:
                    p.op("sync", lambda e, c1=c1, dq=dq, par=par: e.dma_start(
                        out=WOS[par][:, :],
                        in_=wo_bf[c1 * 128:(c1 + 1) * 128, dq * 512:(dq + 1) * 512]),
                        writes=[("wos", par)], dma=("p3wo", par),
                        tag=f"p3 woload w{w}q{dq}c{c1}",
                        extra_waits=getattr(b, "xw_wt", ()))
                for ts_ in range(4):
                    if getattr(b, "fused", False):
                        lhs = bass.AP(ATW, ts_ * 4 * SEG + c1,
                                      [[ATW.shape[1], 128], [SEG, 4], [16, 32]])
                        p.op("tensor", lambda e, lhs=lhs, c1=c1, ts_=ts_, par=par: e.matmul(
                            PS[ts_][:, :], lhs, WOS[par][:, :],
                            start=(c1 == 0), stop=(c1 == NCH - 1)),
                            reads=[("wos", par), tATW], writes=[tPS[ts_]],
                            tag=f"p3 o mm w{w}q{dq}c{c1}t{ts_}")
                    else:
                        p.op("tensor", lambda e, c1=c1, ts_=ts_, par=par: e.matmul(
                            PS[ts_][:, :],
                            ATW[:, c1 * WT + ts_ * 128: c1 * WT + (ts_ + 1) * 128],
                            WOS[par][:, :],
                            start=(c1 == 0), stop=(c1 == NCH - 1)),
                            reads=[("wos", par), tATW], writes=[tPS[ts_]],
                            tag=f"p3 o mm w{w}q{dq}c{c1}t{ts_}")
            for ts_ in range(4):
                xpar = ts_ % 2
                osl = slice(dq * 512, (dq + 1) * 512)
                xeng = "scalar" if getattr(b, "fused", False) else "gpsimd"
                p.op(xeng, lambda e, w=w, ts_=ts_, xpar=xpar, osl=osl: e.dma_start(
                    out=XR[xpar][:, :],
                    in_=x_rows[w * WT + ts_ * 128: w * WT + (ts_ + 1) * 128, osl]),
                    writes=[("xr", xpar)], dma=("p3x", xpar),
                    tag=f"p3 xload w{w}q{dq}t{ts_}")
                if getattr(b, "dbg", None) is not None and w == 0 and dq == 0 and ts_ == 0:
                    p.op("sync", lambda e: e.dma_start(out=b.dbg["xr"][:, :], in_=XR[0][:, :]),
                         reads=[("xr", 0)], writes=["dbgxr"], dma=("dbg",))
                    p.op("vector", lambda e: e.tensor_copy(OT[1][:, :], PS[0][:, :]),
                         reads=[tPS[0]], writes=[("ot", 1)])
                    p.op("sync", lambda e: e.dma_start(out=b.dbg["o0"][:, :], in_=OT[1][:, :]),
                         reads=[("ot", 1)], writes=["dbgo0"], dma=("dbg",))
                p.op("vector", lambda e, ts_=ts_, osl=osl: e.tensor_mul(
                    OT[ts_ % 2][:, :], PS[ts_][:, :], GABC[:, osl]),
                    reads=[tPS[ts_], tGA], writes=[("ot", ts_ % 2)],
                    tag=f"p3 oga w{w}q{dq}t{ts_}")
                p.op("vector", lambda e, ts_=ts_, dq=dq, xpar=xpar: e.tensor_add(
                    X2T[:, ts_ * D + dq * 512: ts_ * D + (dq + 1) * 512],
                    OT[ts_ % 2][:, :], XR[xpar][:, :]),
                    reads=[("ot", ts_ % 2), ("xr", xpar)], writes=["x2t"],
                    tag=f"p3 x2 w{w}q{dq}t{ts_}")
        if getattr(b, "dbg", None) is not None and w == 0:
            p.op("sync", lambda e: e.dma_start(out=b.dbg["x2t"][:, :], in_=X2T[:, :]),
                 reads=["x2t"], writes=["dbgx"], dma=("dbg",))
            p.op("sync", lambda e: e.dma_start(out=b.dbg["atw"][:, :], in_=ATW[:, :]),
                 reads=["atw"], writes=["dbga"], dma=("dbg",))
        for ts_ in range(4):
            x2sl = slice(ts_ * D, (ts_ + 1) * D)
            p.op("vector", lambda e, x2sl=x2sl: e.tensor_reduce(
                ST[:, 0:1], X2T[:, x2sl], AX.X, ALU.add),
                reads=["x2t"], writes=["p3st"], tag=f"p3 sum w{w}t{ts_}")
            p.op("scalar", lambda e, x2sl=x2sl: e.activation(
                QM[:, 0:D], X2T[:, x2sl], AF.Square, accum_out=ST[:, 1:2]),
                reads=["x2t"], writes=["fb", "p3st"], tag=f"p3 sumsq w{w}t{ts_}")
            p.op("vector", lambda e: e.tensor_single_scalar(
                ST[:, 2:3], ST[:, 0:1], 1.0 / D, ALU.mult),
                reads=["p3st"], writes=["p3st"])
            p.op("vector", lambda e: e.tensor_mul(ST[:, 3:4], ST[:, 2:3], ST[:, 2:3]),
                 reads=["p3st"], writes=["p3st"])
            p.op("vector", lambda e: e.scalar_tensor_tensor(
                ST[:, 4:5], ST[:, 1:2], 1.0 / D, ST[:, 3:4], ALU.mult, ALU.subtract),
                reads=["p3st"], writes=["p3st"])
            p.op("scalar", lambda e: e.activation(
                ST[:, 4:5], ST[:, 4:5], AF.Sqrt, bias=EPS_LN, scale=1.0),
                reads=["p3st", t_eps], writes=["p3st"])
            p.op("vector", lambda e: e.reciprocal(ST[:, 5:6], ST[:, 4:5]),
                 reads=["p3st"], writes=["p3st"])
            p.op("vector", lambda e, x2sl=x2sl: e.tensor_scalar(
                H2TOK[:, :], X2T[:, x2sl], ST[:, 2:3], ST[:, 5:6],
                ALU.subtract, ALU.mult),
                reads=["x2t", "p3st"], writes=["h2tok"], tag=f"p3 norm w{w}t{ts_}")
            p.op("vector", lambda e: e.tensor_mul(H2TOK[:, :], H2TOK[:, :], CBBC[:, :]),
                 reads=["h2tok", tCB], writes=["h2tok"])
            p.op("vector", lambda e: e.tensor_add(H2TOK[:, :], H2TOK[:, :], SBBC[:, :]),
                 reads=["h2tok", tSB], writes=["h2tok"])
            for half in range(2):
                for cc in range(8):
                    c = half * 8 + cc
                    p.op("tensor", lambda e, c=c, cc=cc: e.transpose(
                        PSB[:, cc * 128:(cc + 1) * 128],
                        H2TOK[:, c * 128:(c + 1) * 128], ident[:, :]),
                        reads=["h2tok", "ident"], writes=[tPSB],
                        tag=f"p3 h2t w{w}t{ts_}c{c}")
                p.op("vector", lambda e: e.tensor_copy(H2C[:, :], PSB[:, 0:1024]),
                     reads=[tPSB], writes=["h2c"], tag=f"p3 h2tev w{w}t{ts_}h{half}")
                for g2 in range(2):
                    k = 2 + g2
                    p.op("tensor", lambda e, g2=g2, k=k: e.matmul(
                        PS[k][:, :], H128[:, :], H2C[:, g2 * 512:(g2 + 1) * 512],
                        start=True, stop=True),
                        reads=["h2c", "h128"], writes=[tPS[k]],
                        tag=f"p3 fw1mm w{w}t{ts_}h{half}g{g2}")
                    for j4 in range(4):
                        c = half * 8 + g2 * 4 + j4
                        p.op("scalar", lambda e, c=c, ts_=ts_, k=k, j4=j4: e.copy(
                            FA[:, c * WT + ts_ * 128: c * WT + (ts_ + 1) * 128],
                            PS[k][:, j4 * 128:(j4 + 1) * 128]),
                            reads=[tPS[k]], writes=["fa"],
                            tag=f"p3 fw1ev w{w}t{ts_}c{c}")
        bufs = [(FA, "fa"), (FB, "fb")]
        for st_i, stv in enumerate([8, 4, 2, 1]):
            src_t, tsrc = bufs[st_i % 2]
            dst_t, tdst = bufs[(st_i + 1) % 2]
            for g in range(NCH // (2 * stv)):
                for j in range(stv):
                    a = g * 2 * stv + j
                    bi = a + stv
                    asl = slice(a * WT, (a + 1) * WT)
                    bsl = slice(bi * WT, (bi + 1) * WT)
                    p.op("vector", lambda e, asl=asl, bsl=bsl, src_t=src_t, dst_t=dst_t: e.tensor_add(
                        dst_t[:, asl], src_t[:, asl], src_t[:, bsl]),
                        reads=[tsrc], writes=[tdst], tag=f"p3 bf1 w{w}")
                    p.op("vector", lambda e, asl=asl, bsl=bsl, src_t=src_t, dst_t=dst_t: e.tensor_sub(
                        dst_t[:, bsl], src_t[:, asl], src_t[:, bsl]),
                        reads=[tsrc], writes=[tdst], tag=f"p3 bf1 w{w}")
        if getattr(b, "dbg", None) is not None and w == 0:
            p.op("sync", lambda e: e.dma_start(out=b.dbg["fw1"][:, :], in_=FA[:, 0:NCH * WT]),
                 reads=["fa"], writes=["dbgf"], dma=("dbg",))
        quant(NCH, WT, FA, "fa", lambda c: (XQ, slice(c * WT, (c + 1) * WT)),
              "xq", C1, w, f"p3 q1 w{w}")
        if getattr(b, "dbg", None) is not None and w == 0:
            p.op("sync", lambda e: e.dma_start(out=b.dbg["xq"][:, :], in_=XQ[:, :]),
                 reads=["xq"], writes=["dbgq"], dma=("dbg",))
        for f2 in range(FCH):
            par = f2 % 2
            if getattr(b, "fused", False):
                p.op("sync", lambda e, f2=f2, par=par: e.dma_start(
                    out=WUP[par][:, :], in_=wup_pcj[f2 // 4, f2 % 4, :, :]),
                    reads=["wupag"],
                    writes=[("wup", par)], dma=("p3wup", par),
                    tag=f"p3 wupload w{w}f{f2}")
            else:
                p.op("sync", lambda e, f2=f2, par=par: e.dma_start(
                    out=WUP[par][:, :], in_=wup_pcj[f2, :, :]),
                    writes=[("wup", par)], dma=("p3wup", par), tag=f"p3 wupload w{w}f{f2}",
                    extra_waits=getattr(b, "xw_wt", ()))
            k = 2 + f2 % 2
            for c in range(NCH):
                p.op("tensor", lambda e, f2=f2, c=c, k=k, par=par: e.matmul(
                    PS[k][:, :], WUP[par][:, c * 128:(c + 1) * 128],
                    XQ[:, c * WT:(c + 1) * WT],
                    start=(c == 0), stop=(c == NCH - 1)),
                    reads=[("wup", par), "xq"], writes=[tPS[k]],
                    tag=f"p3 up mm w{w}f{f2}c{c}")
            p.op("vector", lambda e, k=k: e.tensor_single_scalar(
                QU[:, :], PS[k][:, :], 0.0, ALU.max),
                reads=[tPS[k]], writes=["qu"], tag=f"p3 relu w{w}f{f2}")
            p.op("scalar", lambda e, f2=f2: e.activation(
                FA[:, f2 * WT:(f2 + 1) * WT], QU[:, :], AF.Square),
                reads=["qu"], writes=["fa"], tag=f"p3 rsq w{w}f{f2}")
        HT = WT // 2
        for hw in range(2):
            for f2 in range(FCH):
                k = 2 + f2 % 2
                p.op("tensor", lambda e, f2=f2, hw=hw, k=k: e.matmul(
                    PS[k][:, 0:HT], H128[:, :],
                    FA[:, f2 * WT + hw * HT: f2 * WT + (hw + 1) * HT],
                    start=True, stop=True), reads=["fa", "h128"], writes=[tPS[k]],
                    tag=f"p3 fw2mm w{w}h{hw}f{f2}")
                p.op("scalar", lambda e, f2=f2, k=k: e.copy(
                    FB[:, f2 * HT:(f2 + 1) * HT], PS[k][:, 0:HT]),
                    reads=[tPS[k]], writes=["fb"], tag=f"p3 fw2ev w{w}h{hw}f{f2}")
            bufs2 = [(FB, "fb"), (GB, "gb")]
            for st_i, stv in enumerate([16, 8, 4, 2, 1]):
                src_t, tsrc = bufs2[st_i % 2]
                dst_t, tdst = bufs2[(st_i + 1) % 2]
                for g in range(FCH // (2 * stv)):
                    for j in range(stv):
                        a = g * 2 * stv + j
                        bi = a + stv
                        asl = slice(a * HT, (a + 1) * HT)
                        bsl = slice(bi * HT, (bi + 1) * HT)
                        p.op("vector", lambda e, asl=asl, bsl=bsl, src_t=src_t, dst_t=dst_t: e.tensor_add(
                            dst_t[:, asl], src_t[:, asl], src_t[:, bsl]),
                            reads=[tsrc], writes=[tdst], tag=f"p3 bf2 w{w}")
                        p.op("vector", lambda e, asl=asl, bsl=bsl, src_t=src_t, dst_t=dst_t: e.tensor_sub(
                            dst_t[:, bsl], src_t[:, asl], src_t[:, bsl]),
                            reads=[tsrc], writes=[tdst], tag=f"p3 bf2 w{w}")
            quant(FCH, HT, GB, "gb",
                  lambda c, hw=hw: (FA, slice(c * WT + hw * HT, c * WT + (hw + 1) * HT)),
                  "fa", C2, w, f"p3 q2 w{w}h{hw}")
        for dq in range(4):
            for f in range(FCH):
                par = f % 2
                if getattr(b, "fused", False):
                    wsrc = bass.AP(wdnT, (2 * dq) * FF * 256 + f * 128 * 256,
                                   [[256, 128], [FF * 256, 2], [1, 256]])
                    wdst = bass.AP(WDN[par], 0, [[512, 128], [256, 2], [1, 256]])
                    p.op("sync", lambda e, wsrc=wsrc, wdst=wdst: e.dma_start(
                        out=wdst, in_=wsrc),
                        reads=["wdnag"],
                        writes=[("wdn", par)], dma=("p3wdn", par),
                        tag=f"p3 wdnload w{w}q{dq}f{f}")
                else:
                    p.op("sync", lambda e, f=f, dq=dq, par=par: e.dma_start(
                        out=WDN[par][:, :],
                        in_=wdnT[f * 128:(f + 1) * 128, dq * 512:(dq + 1) * 512]),
                        writes=[("wdn", par)], dma=("p3wdn", par),
                        tag=f"p3 wdnload w{w}q{dq}f{f}")
                for ts_ in range(4):
                    p.op("tensor", lambda e, f=f, ts_=ts_, par=par: e.matmul(
                        PS[ts_][:, :],
                        FA[:, f * WT + ts_ * 128: f * WT + (ts_ + 1) * 128],
                        WDN[par][:, :],
                        start=(f == 0), stop=(f == FCH - 1)),
                        reads=[("wdn", par), "fa"], writes=[tPS[ts_]],
                        tag=f"p3 dn mm w{w}q{dq}f{f}t{ts_}")
            for ts_ in range(4):
                osl = slice(dq * 512, (dq + 1) * 512)
                p.op("vector", lambda e, ts_=ts_, osl=osl: e.tensor_mul(
                    OT[ts_ % 2][:, :], PS[ts_][:, :], GBBC[:, osl]),
                    reads=[tPS[ts_], tGB], writes=[("ot", ts_ % 2)],
                    tag=f"p3 dngb w{w}q{dq}t{ts_}")
                p.op("vector", lambda e, ts_=ts_, dq=dq: e.tensor_add(
                    OT[ts_ % 2][:, :], OT[ts_ % 2][:, :],
                    X2T[:, ts_ * D + dq * 512: ts_ * D + (dq + 1) * 512]),
                    reads=[("ot", ts_ % 2), "x2t"], writes=[("ot", ts_ % 2)],
                    tag=f"p3 dnadd w{w}q{dq}t{ts_}")
                alt3 = "scalar" if getattr(b, "fused", False) else "gpsimd"
                eng = "sync" if ts_ % 2 == 0 else alt3
                p.op(eng, lambda e, w=w, ts_=ts_, osl=osl: e.dma_start(
                    out=out_rows[w * WT + ts_ * 128: w * WT + (ts_ + 1) * 128, osl],
                    in_=OT[ts_ % 2][:, :]),
                    reads=[("ot", ts_ % 2)], writes=[("outr", w, ts_, dq)],
                    dma=("p3out", eng), tag=f"p3 out w{w}q{dq}t{ts_}")


def build_phase3_v0(debug=False):
    nc = bass.Bass("TRN2", target_bir_lowering=False, debug=False)
    x_rows = nc.dram_tensor("x_rows", [TC, D], F32, kind="ExternalInput")
    attT = nc.dram_tensor("attT", [NCH, 128, TC], BF16, kind="ExternalInput")
    wo_bf = nc.dram_tensor("wo_bf", [D, D], BF16, kind="ExternalInput")
    wup_pcj = nc.dram_tensor("wup_pcj", [FCH, 128, NCH * 128], BF16, kind="ExternalInput")
    wdnT = nc.dram_tensor("wdnT", [FF, D], BF16, kind="ExternalInput")
    modrow = nc.dram_tensor("modrow", [1, 6 * D], F32, kind="ExternalInput")
    out_rows = nc.dram_tensor("out_rows", [TC, D], F32, kind="ExternalOutput")
    p = Prog(nc)
    b = Bld(nc, p)
    if debug:
        b.dbg = {
            "x2t": nc.dram_tensor("dbg_x2t", [128, 4 * D], F32, kind="ExternalOutput"),
            "fw1": nc.dram_tensor("dbg_fw1", [128, NCH * WT], BF16, kind="ExternalOutput"),
            "xq": nc.dram_tensor("dbg_xq", [128, NCH * WT], BF16, kind="ExternalOutput"),
            "atw": nc.dram_tensor("dbg_atw", [128, NCH * WT], BF16, kind="ExternalOutput"),
            "xr": nc.dram_tensor("dbg_xr", [128, 512], F32, kind="ExternalOutput"),
            "o0": nc.dram_tensor("dbg_o0", [128, 512], F32, kind="ExternalOutput"),
        }
    emit_phase3(b, x_rows, attT, wo_bf, wup_pcj, wdnT, modrow, out_rows)
    p.build()
    return nc



def emit_phase2(b, h_full, wqkv, betas_s, att_s):
    nc, p = b.nc, b.p
    ident = b.need_ident()

    WSTG = b.sb("p2_wstg", (128, 768), F32)
    WQKV = b.sb("p2_wqkv", (128, NCH * 768), BF16)
    for c in range(NCH):
        p.op("sync", lambda e, c=c: e.dma_start(
            out=WSTG[:, :], in_=wqkv[c * 128:(c + 1) * 128, :]),
            writes=["wstg"], dma=("p2w",), tag=f"p2 wload c{c}")
        p.op("scalar", lambda e, c=c: e.copy(WQKV[:, c * 768:(c + 1) * 768], WSTG[:, :]),
             reads=["wstg"], writes=["wqkv"], tag=f"p2 wcast c{c}")

    BROW = b.sb("p2_brow", (1, 2 * 128), BF16)
    BROWF = b.sb("p2_browf", (1, 2 * 128), F32)
    BBC = b.sb("p2_bbc", (128, 2 * 128), F32)
    MBC = b.sb("p2_mbc", (128, 2 * 128), F32)
    for hl in range(HL):
        p.op("sync", lambda e, hl=hl: e.dma_start(
            out=BROWF[0:1, hl * 128:(hl + 1) * 128], in_=betas_s[hl:hl + 1, :]),
            writes=["brow"], dma=("p2wb",), tag=f"p2 bload {hl}")
    p.op("scalar", lambda e: e.activation(BROW[0:1, :], BROWF[0:1, :], AF.Sigmoid),
         reads=["brow"], writes=["brow"])
    for hl in range(HL):
        p.op("tensor", lambda e, hl=hl: e.matmul(
            b.PS[0][:, 0:128], b.ones_row[:, :], BROW[0:1, hl * 128:(hl + 1) * 128],
            start=True, stop=True), reads=["brow", "p3or"], writes=[b.tPS[0]])
        p.op("scalar", lambda e, hl=hl: e.copy(BBC[:, hl * 128:(hl + 1) * 128],
                                               b.PS[0][:, 0:128]),
             reads=[b.tPS[0]], writes=["bbc"])
        p.op("vector", lambda e, hl=hl: e.tensor_scalar(
            MBC[:, hl * 128:(hl + 1) * 128], BBC[:, hl * 128:(hl + 1) * 128],
            -1.0, 1.0, ALU.mult, ALU.add),
            reads=["bbc"], writes=["mbc"])

    mask_np = np.where(np.arange(128)[:, None] >= np.arange(128)[None, :],
                       0.0, -1e9).astype(np.float32)
    mask_dr = nc.inline_tensor(np.asarray(mask_np), name="cmask")
    CMASK = b.sb("p2_cmask", (128, 128), F32)
    p.op("sync", lambda e: e.dma_start(out=CMASK[:, :], in_=mask_dr[:, :]),
         writes=["cmask"], dma=("p2wc",))

    MEMZ = [b.sb(f"p2_memz{i}", (128, 129), F32) for i in range(2 * HL)]
    MEMZB = [b.sb(f"p2_memzb{i}", (128, 129), BF16) for i in range(2 * HL)]
    for pr in range(2 * HL):
        p.op("vector", lambda e, pr=pr: e.memset(MEMZ[pr][:, 0:128], 0.0),
             writes=[("memz", pr)])
        p.op("vector", lambda e, pr=pr: e.memset(MEMZ[pr][:, 128:129], 1.0 / DK),
             writes=[("memz", pr)])
        p.op("scalar", lambda e, pr=pr: e.copy(MEMZB[pr][:, :], MEMZ[pr][:, :]),
             reads=[("memz", pr)], writes=[("memzb", pr)])

    HTB = [b.sb(f"p2_ht{i}", (128, NCH * SEG), BF16) for i in range(2)]
    Q = b.sb("p2_q", (128, SEG), BF16)
    K = b.sb("p2_k", (128, SEG), BF16)
    V = b.sb("p2_v", (128, SEG), BF16)
    EQ = b.sb("p2_eq", (128, SEG), BF16)
    SQ = b.sb("p2_sq", (128, SEG), BF16)
    SK = b.sb("p2_sk", (128, SEG), BF16)
    VT = b.sb("p2_vt", (128, SEG), BF16)
    SKT = b.sb("p2_skt", (128, SEG), BF16)
    DLT = b.sb("p2_dlt", (128, SEG), BF16)
    PT = b.sb("p2_pt", (128, 4 * SEG), BF16)
    PSM = b.sb("p2_psm", (128, SEG), BF16)
    DEN = b.sb("p2_den", (128, 8), F32)
    NMS = b.sb("p2_nms", (128, 8), F32)
    RZ = b.sb("p2_rz", (128, 4), F32)
    T1 = b.sb("p2_t1", (128, 128), F32)
    T2 = b.sb("p2_t2", (128, 128), F32)
    ATT = b.sb("p2_att", (128, 128), BF16)
    ATTT = b.sb("p2_attt", (128, SEG), BF16)

    PS, tPS, PSB, tPSB = b.PS, b.tPS, b.PSB, b.tPSB

    for seg in range(NSEG):
        for bb in range(B):
            par = (seg * B + bb) % 2
            HT_ = HTB[par]
            tHT = ("p2ht", par)
            base = bb * S + seg * SEG
            if getattr(b, "fused", False):
                wv, rv = seg % 4, bb * 4 + seg // 4
                eng = "sync" if par == 0 else "scalar"
                p.op(eng, lambda e, wv=wv, rv=rv, HT_=HT_: e.dma_start(
                    out=HT_[:, :], in_=h_full[wv, rv, :, :]),
                    reads=[("hag", wv)],
                    writes=[tHT], dma=("p2h", par, eng),
                    tag=f"p2 hload s{seg}b{bb}")
            else:
                for c in range(NCH):
                    eng = "sync" if c % 2 == 0 else "gpsimd"
                    p.op(eng, lambda e, c=c, base=base, HT_=HT_: e.dma_start(
                        out=HT_[:, c * SEG:(c + 1) * SEG],
                        in_=h_full[c, :, base:base + SEG]),
                        writes=[tHT], dma=("p2h", par, eng),
                        tag=f"p2 hload s{seg}b{bb}c{c}")
            for hl in range(HL):
                pr = bb * HL + hl
                sfx = f"s{seg}b{bb}h{hl}"
                for pi, (dst, sig, tdst) in enumerate(
                        [(Q, SQ, "q"), (K, SK, "k"), (V, None, "v")]):
                    kk = pi % 2
                    for c in range(NCH):
                        p.op("tensor", lambda e, c=c, pi=pi, hl=hl, kk=kk, HT_=HT_: e.matmul(
                            PS[kk][:, :],
                            WQKV[:, c * 768 + pi * 256 + hl * 128:
                                 c * 768 + pi * 256 + (hl + 1) * 128],
                            HT_[:, c * SEG:(c + 1) * SEG],
                            start=(c == 0), stop=(c == NCH - 1)),
                            reads=["wqkv", tHT], writes=[tPS[kk]],
                            tag=f"p2 {tdst}mm {sfx}c{c}")
                    if pi == 0:
                        p.op("scalar", lambda e, dst=dst, kk=kk: e.activation(
                            dst[:, :], PS[kk][:, :], AF.Copy,
                            scale=1.0 / SCALE_ATT),
                            reads=[tPS[kk]], writes=[tdst], tag=f"p2 {tdst}ev {sfx}")
                    else:
                        p.op("scalar", lambda e, dst=dst, kk=kk: e.copy(dst[:, :], PS[kk][:, :]),
                             reads=[tPS[kk]], writes=[tdst], tag=f"p2 {tdst}ev {sfx}")
                    if sig is not None:
                        p.op("scalar", lambda e, kk=kk: e.activation(
                            EQ[:, :], PS[kk][:, :], AF.Exp),
                            reads=[tPS[kk]], writes=["eq"], tag=f"p2 {tdst}exp {sfx}")
                        p.op("vector", lambda e, sig=sig, kk=kk: e.scalar_tensor_tensor(
                            sig[:, :], PS[kk][:, :], 1.0, EQ[:, :], ALU.add, ALU.min),
                            reads=[tPS[kk], "eq"], writes=[tdst + "s"],
                            tag=f"p2 {tdst}sig {sfx}")
                for src, dst, tsrc, tdst in ((V, VT, "v", "vt"), (SK, SKT, "ks", "skt")):
                    for j in range(4):
                        p.op("tensor", lambda e, j=j, src=src: e.transpose(
                            PSB[:, j * 128:(j + 1) * 128],
                            src[:, j * 128:(j + 1) * 128], ident[:, :]),
                            reads=[tsrc, "ident"], writes=[tPSB],
                            tag=f"p2 tr{tdst} {sfx}j{j}")
                    p.op("vector", lambda e, dst=dst: e.tensor_copy(
                        dst[:, :], PSB[:, 0:512]),
                        reads=[tPSB], writes=[tdst], tag=f"p2 trev{tdst} {sfx}")
                for i in range(4):
                    ncol = (i + 1) * 128
                    p.op("tensor", lambda e, i=i, pr=pr: e.matmul(
                        PS[2][:, 0:129], SQ[:, i * 128:(i + 1) * 128],
                        MEMZB[pr][:, :], start=True, stop=True),
                        reads=["qs", ("memzb", pr)], writes=[tPS[2]],
                        tag=f"p2 am {sfx}i{i}")
                    p.op("tensor", lambda e, i=i, ncol=ncol: e.matmul(
                        PS[3][:, 0:ncol], Q[:, i * 128:(i + 1) * 128], K[:, 0:ncol],
                        start=True, stop=True),
                        reads=["q", "k"], writes=[tPS[3]], tag=f"p2 sc {sfx}i{i}")
                    p.op("vector", lambda e, i=i, ncol=ncol: e.tensor_add(
                        PS[3][:, ncol - 128:ncol], PS[3][:, ncol - 128:ncol],
                        CMASK[:, :]),
                        reads=[tPS[3], "cmask"], writes=[tPS[3]],
                        tag=f"p2 mask {sfx}i{i}")
                    p.op("vector", lambda e, i=i, ncol=ncol: e.tensor_reduce(
                        NMS[:, i:i + 1], PS[3][:, 0:ncol], AX.X, ALU.max,
                        negate=True),
                        reads=[tPS[3]], writes=["nms"], tag=f"p2 max {sfx}i{i}")
                    p.op("scalar", lambda e, i=i, ncol=ncol: e.activation(
                        PSM[:, 0:ncol], PS[3][:, 0:ncol], AF.Exp,
                        bias=NMS[:, i:i + 1], scale=1.0,
                        accum_out=DEN[:, i:i + 1]),
                        reads=[tPS[3], "nms"], writes=["psm", "den"],
                        tag=f"p2 exp {sfx}i{i}")
                    for j in range(i + 1):
                        p.op("tensor", lambda e, j=j: e.transpose(
                            PSB[:, j * 128:(j + 1) * 128],
                            PSM[:, j * 128:(j + 1) * 128], ident[:, :]),
                            reads=["psm", "ident"], writes=[tPSB],
                            tag=f"p2 trp {sfx}i{i}j{j}")
                    p.op("vector", lambda e, i=i: e.tensor_copy(
                        PT[:, 0:(i + 1) * 128], PSB[:, 0:(i + 1) * 128]),
                        reads=[tPSB], writes=["pt"], tag=f"p2 trpev {sfx}i{i}")
                    for j in range(i + 1):
                        p.op("tensor", lambda e, i=i, j=j: e.matmul(
                            PS[4][:, 0:128], PT[:, j * 128:(j + 1) * 128],
                            VT[:, j * 128:(j + 1) * 128],
                            start=(j == 0), stop=(j == i)),
                            reads=["pt", "vt"], writes=[tPS[4]],
                            tag=f"p2 ad {sfx}i{i}j{j}")
                    p.op("vector", lambda e, i=i: e.reciprocal(RZ[:, 0:1],
                                                               PS[2][:, 128:129]),
                         reads=[tPS[2]], writes=["rz"], tag=f"p2 rz {sfx}i{i}")
                    p.op("vector", lambda e, i=i: e.reciprocal(RZ[:, 1:2],
                                                               DEN[:, i:i + 1]),
                         reads=["den"], writes=["rz"])
                    p.op("vector", lambda e, hl=hl: e.scalar_tensor_tensor(
                        T1[:, :], PS[2][:, 0:128], RZ[:, 0:1],
                        BBC[:, hl * 128:(hl + 1) * 128], ALU.mult, ALU.mult),
                        reads=[tPS[2], "rz", "bbc"], writes=["t1x"],
                        tag=f"p2 t1 {sfx}i{i}")
                    p.op("vector", lambda e, hl=hl: e.scalar_tensor_tensor(
                        T2[:, :], PS[4][:, 0:128], RZ[:, 1:2],
                        MBC[:, hl * 128:(hl + 1) * 128], ALU.mult, ALU.mult),
                        reads=[tPS[4], "rz", "mbc"], writes=["t2x"],
                        tag=f"p2 t2 {sfx}i{i}")
                    p.op("vector", lambda e: e.tensor_add(ATT[:, :], T1[:, :], T2[:, :]),
                         reads=["t1x", "t2x"], writes=["attx"], tag=f"p2 cmb {sfx}i{i}")
                    p.op("tensor", lambda e: e.transpose(
                        PSB[:, 512:640], ATT[:, :], ident[:, :]),
                        reads=["attx", "ident"], writes=[tPSB],
                        tag=f"p2 trat {sfx}i{i}")
                    p.op("vector", lambda e, i=i: e.tensor_copy(
                        ATTT[:, i * 128:(i + 1) * 128], PSB[:, 512:640]),
                        reads=[tPSB], writes=["atttx"], tag=f"p2 tratev {sfx}i{i}")
                for j in range(4):
                    p.op("tensor", lambda e, j=j, pr=pr: e.matmul(
                        PS[5][:, 0:129], SK[:, j * 128:(j + 1) * 128],
                        MEMZB[pr][:, :], start=True, stop=True),
                        reads=["ks", ("memzb", pr)], writes=[tPS[5]],
                        tag=f"p2 km {sfx}j{j}")
                    p.op("vector", lambda e: e.reciprocal(RZ[:, 2:3],
                                                          PS[5][:, 128:129]),
                         reads=[tPS[5]], writes=["rz"])
                    p.op("vector", lambda e, j=j: e.scalar_tensor_tensor(
                        DLT[:, j * 128:(j + 1) * 128], PS[5][:, 0:128],
                        RZ[:, 2:3], VT[:, j * 128:(j + 1) * 128],
                        ALU.mult, ALU.subtract),
                        reads=[tPS[5], "rz", "vt"], writes=["dlt"],
                        tag=f"p2 dlt {sfx}j{j}")
                for j in range(4):
                    p.op("tensor", lambda e, j=j: e.matmul(
                        PS[6][:, 0:128], SKT[:, j * 128:(j + 1) * 128],
                        DLT[:, j * 128:(j + 1) * 128],
                        start=(j == 0), stop=(j == 3)),
                        reads=["skt", "dlt"], writes=[tPS[6]],
                        tag=f"p2 upd {sfx}j{j}")
                p.op("vector", lambda e, pr=pr: e.tensor_sub(
                    MEMZ[pr][:, 0:128], MEMZ[pr][:, 0:128], PS[6][:, 0:128]),
                    reads=[tPS[6], ("memz", pr)], writes=[("memz", pr)],
                    tag=f"p2 memup {sfx}")
                p.op("vector", lambda e: e.tensor_reduce(
                    RZ[:, 3:4], SK[:, :], AX.X, ALU.add),
                    reads=["ks"], writes=["rz"], tag=f"p2 zsum {sfx}")
                p.op("vector", lambda e, pr=pr: e.tensor_add(
                    MEMZ[pr][:, 128:129], MEMZ[pr][:, 128:129], RZ[:, 3:4]),
                    reads=["rz", ("memz", pr)], writes=[("memz", pr)],
                    tag=f"p2 zup {sfx}")
                p.op("scalar", lambda e, pr=pr: e.copy(MEMZB[pr][:, :], MEMZ[pr][:, :]),
                     reads=[("memz", pr)], writes=[("memzb", pr)],
                     tag=f"p2 memzb {sfx}")
                if getattr(b, "fused", False):
                    eng = "sync" if pr % 2 == 0 else "scalar"
                    ap = bass.AP(att_s, (seg // 4) * NATQ
                                 + ((bb * 2 + hl) * 4 + seg % 4) * (128 * 512),
                                 [[512, 128], [1, 512]])
                    p.op(eng, lambda e, ap=ap: e.dma_start(
                        out=ap, in_=ATTT[:, :]),
                        reads=["atttx"], writes=[("attout", pr, seg)],
                        dma=("p2att", eng), tag=f"p2 attst {sfx}")
                else:
                    eng = "sync" if pr % 2 == 0 else "gpsimd"
                    p.op(eng, lambda e, hl=hl, bb=bb, base=base, seg=seg: e.dma_start(
                        out=att_s[hl, bb, :, seg * SEG:(seg + 1) * SEG],
                        in_=ATTT[:, :]),
                        reads=["atttx"], writes=[("attout", pr, seg)],
                        dma=("p2att", eng), tag=f"p2 attst {sfx}")
        if getattr(b, "att_qdone", None) is not None and seg % 4 == 3:
            b.att_qdone(seg // 4)


def build_phase2_v0():
    nc = bass.Bass("TRN2", target_bir_lowering=False, debug=False)
    h_full = nc.dram_tensor("h_full", [NCH, 128, B * S], BF16, kind="ExternalInput")
    wqkv = nc.dram_tensor("wqkv", [D, 768], F32, kind="ExternalInput")
    betas_s = nc.dram_tensor("betas_s", [HL, 128], F32, kind="ExternalInput")
    att_s = nc.dram_tensor("att_s", [HL, B, 128, S], BF16, kind="ExternalOutput")
    p = Prog(nc)
    b = Bld(nc, p)
    b.PS = [b.ps(f"p2_ps{k}", (128, 512), F32) for k in range(7)]
    b.tPS = [("PS", k) for k in range(7)]
    b.PSB = b.ps("p2_psb", (128, 1024), BF16)
    b.tPSB = ("PSB",)
    onr = b.sb("p2_ones_row", (1, 128), BF16)
    p.op("vector", lambda e: e.memset(onr[:, :], 1.0), writes=["p3or"])
    b.ones_row = onr
    emit_phase2(b, h_full, wqkv, betas_s, att_s)
    p.build()
    return nc



def emit_phase0(b, w_up, w_dn, wup_sl, wdn_sl, wo_sl, wmod_sl, cT,
                mod_part, wqup_sl, wqdn_sl, wob_sl):
    nc, p = b.nc, b.p
    PS, tPS = b.PS, b.tPS

    onesf = b.sb("p0_onesf", (128, 1), F32)
    p.op("vector", lambda e: e.memset(onesf[:, :], 1.0), writes=["p0of"])

    SC = b.sb("p0_sc", (128, 2 * NCH), BF16)
    SCF = b.sb("p0_scf", (128, 2 * NCH), F32)
    WM = [b.sb(f"p0_wm{i}", (128, 1536), BF16) for i in range(2)]
    WMF = [b.sb(f"p0_wmf{i}", (128, 1536), F32) for i in range(2)]
    MODE = b.sb("p0_mode", (2, 1536), F32)
    for c in range(NCH):
        p.op("sync", lambda e, c=c: e.dma_start(
            out=SCF[:, c * 2:(c + 1) * 2], in_=cT[c, :, :]),
            writes=["p0scf"], dma=("p0c",), tag=f"p0 cload {c}")
    p.op("scalar", lambda e: e.activation(SC[:, :], SCF[:, :], AF.Silu),
         reads=["p0scf"], writes=["p0sc"], tag="p0 silu")
    for c in range(NCH):
        par = c % 2
        p.op("sync", lambda e, c=c, par=par: e.dma_start(
            out=WMF[par][:, :], in_=wmod_sl[c * 128:(c + 1) * 128, :]),
            writes=[("p0wmf", par)], dma=("p0wm", par), tag=f"p0 wmload {c}")
        p.op("scalar", lambda e, par=par: e.copy(WM[par][:, :], WMF[par][:, :]),
             reads=[("p0wmf", par)], writes=[("p0wm", par)], tag=f"p0 wmcast {c}")
        for n3 in range(3):
            p.op("tensor", lambda e, c=c, n3=n3, par=par: e.matmul(
                PS[n3][0:2, :], SC[:, c * 2:(c + 1) * 2],
                WM[par][:, n3 * 512:(n3 + 1) * 512],
                start=(c == 0), stop=(c == NCH - 1)),
                reads=["p0sc", ("p0wm", par)], writes=[tPS[n3]],
                tag=f"p0 modmm c{c}n{n3}")
    for n3 in range(3):
        p.op("vector", lambda e, n3=n3: e.tensor_copy(
            MODE[0:2, n3 * 512:(n3 + 1) * 512], PS[n3][0:2, :]),
            reads=[tPS[n3]], writes=["p0mode"], tag=f"p0 modev {n3}")
    p.op("sync", lambda e: e.dma_start(out=mod_part[:, :], in_=MODE[0:2, :]),
         reads=["p0mode"], writes=["p0modout"], dma=("p0mo",), tag="p0 modst")

    WT_ = [b.sb(f"p0_wt{i}", (128, 4096), F32) for i in range(2)]
    ACC = b.sb("p0_acc", (128, 2), F32)
    SCL = b.sb("p0_scl", (128, 4), F32)
    red = b.sb("p0_red", (128, 1), F32)
    p.op("vector", lambda e: e.memset(ACC[:, :], 0.0), writes=["p0acc"])
    for wi, (wfull, nrow, ncol) in enumerate([(w_up, FF, D), (w_dn, D, FF)]):
        ntile = nrow // 128
        for t in range(ntile):
            par = t % 2
            p.op("sync", lambda e, t=t, par=par, wfull=wfull, ncol=ncol: e.dma_start(
                out=WT_[par][:, 0:ncol], in_=wfull[t * 128:(t + 1) * 128, :]),
                writes=[("p0wt", par)], dma=("p0wr", par), tag=f"p0 wstat {wi}t{t}")
            p.op("vector", lambda e, par=par, ncol=ncol: e.tensor_reduce(
                red[:, :], WT_[par][:, 0:ncol], AX.X, ALU.max if False else ALU.add,
                apply_absolute_value=True),
                reads=[("p0wt", par)], writes=["p0red"], tag=f"p0 wred {wi}t{t}")
            p.op("vector", lambda e, wi=wi: e.tensor_add(
                ACC[:, wi:wi + 1], ACC[:, wi:wi + 1], red[:, :]),
                reads=["p0red", "p0acc"], writes=["p0acc"], tag=f"p0 wacc {wi}t{t}")
    p.op("tensor", lambda e: e.matmul(PS[3][0:1, 0:2], onesf[:, :], ACC[:, :],
                                      start=True, stop=True),
         reads=["p0acc", "p0of"], writes=[tPS[3]], tag="p0 wsum")
    p.op("vector", lambda e: e.tensor_scalar(
        PS[3][0:1, 2:4], PS[3][0:1, 0:2], 1.0 / (FF * D), EPS_Q, ALU.mult, ALU.max),
        reads=[tPS[3]], writes=[tPS[3]], tag="p0 wscale")
    p.op("vector", lambda e: e.tensor_copy(SCL[0:1, 0:2], PS[3][0:1, 2:4]),
         reads=[tPS[3]], writes=["p0scl"], tag="p0 sclev")
    onesfr = b.sb("p0_onesfr", (1, 128), F32)
    p.op("vector", lambda e: e.memset(onesfr[:, :], 1.0), writes=["p0ofr"])
    p.op("tensor", lambda e: e.matmul(PS[4][:, 0:2], onesfr[:, :], SCL[0:1, 0:2],
                                      start=True, stop=True),
         reads=["p0scl", "p0ofr"], writes=[tPS[4]], tag="p0 sbc")
    p.op("vector", lambda e: e.tensor_copy(SCL[:, 0:2], PS[4][:, 0:2]),
         reads=[tPS[4]], writes=["p0scl"])
    p.op("vector", lambda e: e.reciprocal(SCL[:, 2:4], SCL[:, 0:2]),
         reads=["p0scl"], writes=["p0scl"])

    QT = b.sb("p0_qt", (128, 4096), F32)
    QA = b.sb("p0_qa", (128, 4096), F32)
    QO = b.sb("p0_qo", (128, 4096), BF16)
    for wi, (wsl, wqout, nrow, ncol) in enumerate(
            [(wup_sl, wqup_sl, FF // 8, D), (wdn_sl, wqdn_sl, D // 8, FF)]):
        ntile = nrow // 128
        for t in range(ntile):
            par = t % 2
            p.op("sync", lambda e, t=t, par=par, wsl=wsl, ncol=ncol: e.dma_start(
                out=WT_[par][:, 0:ncol], in_=wsl[t * 128:(t + 1) * 128, :]),
                writes=[("p0wt", par)], dma=("p0wq", par), tag=f"p0 wqload {wi}t{t}")
            p.op("vector", lambda e, par=par, wi=wi, ncol=ncol: e.tensor_scalar(
                QT[:, 0:ncol], WT_[par][:, 0:ncol], SCL[:, 2 + wi:3 + wi], None,
                ALU.mult),
                reads=[("p0wt", par), "p0scl"], writes=["p0qt"],
                tag=f"p0 wqs {wi}t{t}")
            p.op("vector", lambda e, ncol=ncol: e.tensor_single_scalar(
                QA[:, 0:ncol], QT[:, 0:ncol], 0.5, ALU.is_gt),
                reads=["p0qt"], writes=["p0qa"], tag=f"p0 wqa {wi}t{t}")
            p.op("vector", lambda e, ncol=ncol: e.tensor_single_scalar(
                QT[:, 0:ncol], QT[:, 0:ncol], -0.5, ALU.is_lt),
                reads=["p0qt"], writes=["p0qt"], tag=f"p0 wqb {wi}t{t}")
            p.op("vector", lambda e, ncol=ncol: e.tensor_sub(
                QA[:, 0:ncol], QA[:, 0:ncol], QT[:, 0:ncol]),
                reads=["p0qa", "p0qt"], writes=["p0qa"], tag=f"p0 wqc {wi}t{t}")
            p.op("vector", lambda e, wi=wi, ncol=ncol: e.tensor_scalar(
                QO[:, 0:ncol], QA[:, 0:ncol], SCL[:, wi:wi + 1], None, ALU.mult),
                reads=["p0qa", "p0scl"], writes=["p0qo"], tag=f"p0 wqd {wi}t{t}")
            p.op("gpsimd", lambda e, t=t, wqout=wqout, ncol=ncol: e.dma_start(
                out=wqout[t * 128:(t + 1) * 128, :], in_=QO[:, 0:ncol]),
                reads=["p0qo"], writes=[("p0qout", wi, t)], dma=("p0qst",),
                tag=f"p0 wqst {wi}t{t}")

    for t in range(2):
        par = t % 2
        p.op("sync", lambda e, t=t, par=par: e.dma_start(
            out=WT_[par][:, 0:D], in_=wo_sl[t * 128:(t + 1) * 128, :]),
            writes=[("p0wt", par)], dma=("p0wo", par), tag=f"p0 woload {t}")
        p.op("scalar", lambda e, par=par: e.copy(QO[:, 0:D], WT_[par][:, 0:D]),
             reads=[("p0wt", par)], writes=["p0qo"], tag=f"p0 wocast {t}")
        p.op("gpsimd", lambda e, t=t: e.dma_start(
            out=wob_sl[t * 128:(t + 1) * 128, :], in_=QO[:, 0:D]),
            reads=["p0qo"], writes=[("p0woout", t)], dma=("p0qst",),
            tag=f"p0 wost {t}")


def build_phase0_v0():
    nc = bass.Bass("TRN2", target_bir_lowering=False, debug=False)
    w_up = nc.dram_tensor("w_up", [FF, D], F32, kind="ExternalInput")
    w_dn = nc.dram_tensor("w_dn", [D, FF], F32, kind="ExternalInput")
    wup_sl = nc.dram_tensor("wup_sl", [FF // 8, D], F32, kind="ExternalInput")
    wdn_sl = nc.dram_tensor("wdn_sl", [D // 8, FF], F32, kind="ExternalInput")
    wo_sl = nc.dram_tensor("wo_sl", [D // 8, D], F32, kind="ExternalInput")
    wmod_sl = nc.dram_tensor("wmod_sl", [D, 1536], F32, kind="ExternalInput")
    cT = nc.dram_tensor("cT", [NCH, 128, 2], F32, kind="ExternalInput")
    mod_part = nc.dram_tensor("mod_part", [2, 1536], F32, kind="ExternalOutput")
    wqup_sl = nc.dram_tensor("wqup_sl", [FF // 8, D], BF16, kind="ExternalOutput")
    wqdn_sl = nc.dram_tensor("wqdn_sl", [D // 8, FF], BF16, kind="ExternalOutput")
    wob_sl = nc.dram_tensor("wob_sl", [D // 8, D], BF16, kind="ExternalOutput")
    p = Prog(nc)
    b = Bld(nc, p)
    b.PS = [b.ps(f"p0_ps{k}", (128, 512), F32) for k in range(5)]
    b.tPS = [("PS", k) for k in range(5)]
    emit_phase0(b, w_up, w_dn, wup_sl, wdn_sl, wo_sl, wmod_sl, cT,
                mod_part, wqup_sl, wqdn_sl, wob_sl)
    p.build()
    return nc



_RUN = [None]


def _runner(nc, in_maps):
    if _RUN[0] is not None:
        return _RUN[0](nc, in_maps)
    return [dict(r) for r in
            run_bass_kernel_spmd(nc, in_maps, core_ids=list(range(NCORE))).results]


def kernel(L0_attn_x, L0_attn_Wq, L0_attn_Wk, L0_attn_Wv, L0_attn_Wo,
           L0_attn_betas, L0_ffn_w_up, L0_ffn_w_down, L0_c, L0_wmod):
    x = np.ascontiguousarray(np.asarray(L0_attn_x, np.float32))
    Wq = np.asarray(L0_attn_Wq, np.float32)
    Wk = np.asarray(L0_attn_Wk, np.float32)
    Wv = np.asarray(L0_attn_Wv, np.float32)
    Wo = np.asarray(L0_attn_Wo, np.float32)
    betas = np.asarray(L0_attn_betas, np.float32)
    w_up = np.ascontiguousarray(np.asarray(L0_ffn_w_up, np.float32))
    w_dn = np.ascontiguousarray(np.asarray(L0_ffn_w_down, np.float32))
    cc = np.asarray(L0_c, np.float32)
    wmod = np.ascontiguousarray(np.asarray(L0_wmod, np.float32))

    cT = np.ascontiguousarray(cc.T).reshape(NCH, 128, 2)

    nc0 = build_phase0_v0()
    in0 = []
    for g in range(NCORE):
        in0.append({
            "w_up": w_up, "w_dn": w_dn,
            "wup_sl": np.ascontiguousarray(w_up[g * FF // 8:(g + 1) * FF // 8]),
            "wdn_sl": np.ascontiguousarray(w_dn[g * D // 8:(g + 1) * D // 8]),
            "wo_sl": np.ascontiguousarray(Wo[g * D // 8:(g + 1) * D // 8]),
            "wmod_sl": np.ascontiguousarray(wmod[:, g * 1536:(g + 1) * 1536]),
            "cT": cT,
        })
    r0 = _runner(nc0, in0)
    mod = np.concatenate([r0[g]["mod_part"] for g in range(NCORE)], axis=1)
    wq_up = np.concatenate([np.asarray(r0[g]["wqup_sl"]).view(BFNP)
                            for g in range(NCORE)], axis=0)
    wq_dn = np.concatenate([np.asarray(r0[g]["wqdn_sl"]).view(BFNP)
                            for g in range(NCORE)], axis=0)
    wo_b = np.concatenate([np.asarray(r0[g]["wob_sl"]).view(BFNP)
                           for g in range(NCORE)], axis=0)

    nc1 = build_phase1_v0()
    in1 = []
    for g in range(NCORE):
        bb, s0 = g // 4, (g % 4) * TC
        xT = np.ascontiguousarray(x[bb, s0:s0 + TC, :].T).reshape(NCH, 128, TC)
        in1.append({"xT": xT, "modrow": np.ascontiguousarray(mod[bb:bb + 1])})
    r1 = _runner(nc1, in1)
    h_full = np.concatenate(
        [np.asarray(r1[g]["h_out"]).view(BFNP) for g in range(NCORE)], axis=2)

    nc2 = build_phase2_v0()
    in2 = []
    for g in range(NCORE):
        wqkv = np.empty((D, 768), np.float32)
        for hl in range(HL):
            hh = HL * g + hl
            wqkv[:, 0 * 256 + hl * 128:0 * 256 + (hl + 1) * 128] = Wq[:, hh * 128:(hh + 1) * 128]
            wqkv[:, 1 * 256 + hl * 128:1 * 256 + (hl + 1) * 128] = Wk[:, hh * 128:(hh + 1) * 128]
            wqkv[:, 2 * 256 + hl * 128:2 * 256 + (hl + 1) * 128] = Wv[:, hh * 128:(hh + 1) * 128]
        bet = np.ascontiguousarray(betas[0, HL * g:HL * g + HL, 0, :])
        in2.append({"h_full": h_full, "wqkv": wqkv, "betas_s": bet})
    r2 = _runner(nc2, in2)

    wup_im = wq_up.reshape(FCH, 128, NCH, 128)
    wup_pcj = np.ascontiguousarray(
        wup_im.transpose(0, 3, 2, 1).reshape(FCH, 128, NCH * 128))
    wdnT = np.ascontiguousarray(wq_dn.T)
    nc3 = build_phase3_v0()
    in3 = []
    att_by_head = [np.asarray(r2[hh // HL]["att_s"]).view(BFNP)[hh % HL]
                   for hh in range(H)]
    for g in range(NCORE):
        bb, s0 = g // 4, (g % 4) * TC
        A = np.stack([a[bb] for a in att_by_head], axis=0)
        A5 = A.reshape(H, 128, NSEG, 32, 16)
        seg0 = s0 // SEG
        Ds = A5[:, :, seg0:seg0 + TC // SEG, :, :]
        attT = np.ascontiguousarray(
            Ds.transpose(4, 1, 2, 0, 3).reshape(NCH, 128, TC))
        in3.append({
            "x_rows": np.ascontiguousarray(x[bb, s0:s0 + TC, :]),
            "attT": attT, "wo_bf": wo_b, "wup_pcj": wup_pcj, "wdnT": wdnT,
            "modrow": np.ascontiguousarray(mod[bb:bb + 1]),
        })
    r3 = _runner(nc3, in3)
    out = np.empty((B, S, D), np.float32)
    for g in range(NCORE):
        bb, s0 = g // 4, (g % 4) * TC
        out[bb, s0:s0 + TC, :] = r3[g]["out_rows"]
    return out



def emit_phase0_fused(b, w_up, w_dn, wup_sl, wdn_sl, wo_sl, wmod_sl, cT,
                      mode_l, modag, wupsrc, wdnsrc, wosrc):
    nc, p = b.nc, b.p
    PS, tPS, PSB, tPSB = b.PS, b.tPS, b.PSB, b.tPSB
    ident = b.need_ident()

    onesf = b.sb("p0_onesf", (128, 1), F32)
    p.op("vector", lambda e: e.memset(onesf[:, :], 1.0), writes=["p0of"])

    SC = b.sb("p0_sc", (128, 2 * NCH), BF16)
    SCF = b.sb("p0_scf", (128, 2 * NCH), F32)
    WT_ = b.sb("p0_wt", (128, 8192), F32)
    WM = [b.sb(f"p0_wm{i}", (128, 1536), BF16) for i in range(2)]
    MODE = b.sb("p0_mode", (2, 1536), F32)
    for c in range(NCH):
        p.op("sync", lambda e, c=c: e.dma_start(
            out=SCF[:, c * 2:(c + 1) * 2], in_=cT[c, :, :]),
            writes=["p0scf"], dma=("p0c",), tag=f"p0 cload {c}")
    p.op("scalar", lambda e: e.activation(SC[:, :], SCF[:, :], AF.Silu),
         reads=["p0scf"], writes=["p0sc"], tag="p0 silu")
    for c in range(NCH):
        par = c % 2
        p.op("sync", lambda e, c=c, par=par: e.dma_start(
            out=WT_[:, par * 4096:par * 4096 + 1536],
            in_=wmod_sl[c * 128:(c + 1) * 128, :]),
            writes=[("p0wt", par)], dma=("p0wm", par), tag=f"p0 wmload {c}")
        p.op("scalar", lambda e, par=par: e.copy(
            WM[par][:, :], WT_[:, par * 4096:par * 4096 + 1536]),
            reads=[("p0wt", par)], writes=[("p0wm", par)], tag=f"p0 wmcast {c}")
        for n3 in range(3):
            p.op("tensor", lambda e, c=c, n3=n3, par=par: e.matmul(
                PS[n3][0:2, :], SC[:, c * 2:(c + 1) * 2],
                WM[par][:, n3 * 512:(n3 + 1) * 512],
                start=(c == 0), stop=(c == NCH - 1)),
                reads=["p0sc", ("p0wm", par)], writes=[tPS[n3]],
                tag=f"p0 modmm c{c}n{n3}")
    for n3 in range(3):
        p.op("vector", lambda e, n3=n3: e.tensor_copy(
            MODE[0:2, n3 * 512:(n3 + 1) * 512], PS[n3][0:2, :]),
            reads=[tPS[n3]], writes=["p0mode"], tag=f"p0 modev {n3}")
    p.op("sync", lambda e: e.dma_start(out=mode_l[:, :], in_=MODE[0:2, :]),
         reads=["p0mode"], writes=["p0modl"], dma=("p0ml",), tag="p0 modst")
    p.op("gpsimd", lambda e: e.collective_compute(
        "AllGather", ALU.bypass, RG8, [mode_l[:, :]], [modag[:, :, :]]),
        reads=["p0modl"], writes=["modag"], mark=True, tag="p0 modag")

    ACC = b.sb("p0_acc", (128, 2), F32)
    ACR = b.sb("p0_acr", (1, 2), F32)
    SCL = b.sb("p0_scl", (128, 4), F32)
    red = b.sb("p0_red", (128, 1), F32)
    p.op("vector", lambda e: e.memset(ACC[:, :], 0.0), writes=["p0acc"])
    if getattr(b, "fused", False):
        stat_srcs = [(wup_sl, FF // 8, D), (wdn_sl, D // 8, FF)]
    else:
        stat_srcs = [(w_up, FF, D), (w_dn, D, FF)]
    for wi, (wfull, nrow, ncol) in enumerate(stat_srcs):
        ntile = nrow // 128
        for t in range(ntile):
            par = t % 2
            p.op("sync", lambda e, t=t, par=par, wfull=wfull, ncol=ncol: e.dma_start(
                out=WT_[:, par * 4096:par * 4096 + ncol],
                in_=wfull[t * 128:(t + 1) * 128, :]),
                writes=[("p0wt", par)], dma=("p0wr", par), tag=f"p0 wstat {wi}t{t}")
            p.op("vector", lambda e, par=par, ncol=ncol: e.tensor_reduce(
                red[:, :], WT_[:, par * 4096:par * 4096 + ncol], AX.X, ALU.add,
                apply_absolute_value=True),
                reads=[("p0wt", par)], writes=["p0red"], tag=f"p0 wred {wi}t{t}")
            p.op("vector", lambda e, wi=wi: e.tensor_add(
                ACC[:, wi:wi + 1], ACC[:, wi:wi + 1], red[:, :]),
                reads=["p0red", "p0acc"], writes=["p0acc"], tag=f"p0 wacc {wi}t{t}")
    p.op("tensor", lambda e: e.matmul(PS[3][0:1, 0:2], onesf[:, :], ACC[:, :],
                                      start=True, stop=True),
         reads=["p0acc", "p0of"], writes=[tPS[3]], tag="p0 wsum")
    if getattr(b, "fused", False):
        p.op("vector", lambda e: e.tensor_copy(ACR[0:1, :], PS[3][0:1, 0:2]),
             reads=[tPS[3]], writes=["p0acr"], tag="p0 acr")
        p.op("sync", lambda e: e.dma_start(out=b.ar_src[:, :], in_=ACR[0:1, :]),
             reads=["p0acr"], writes=["p0ars"], dma=("p0ar",))
        p.op("gpsimd", lambda e: e.collective_compute(
            "AllReduce", ALU.add, RG8, [b.ar_src[:, :]], [b.ar_out[:, :]]),
            reads=["p0ars"], writes=["p0aro"], mark=True, tag="p0 ar")
        p.op("sync", lambda e: e.dma_start(out=ACR[0:1, :], in_=b.ar_out[:, :]),
             reads=["p0aro"], writes=["p0acr2"], dma=("p0ar2",))
        p.op("vector", lambda e: e.tensor_scalar(
            SCL[0:1, 0:2], ACR[0:1, :], 1.0 / (FF * D), EPS_Q, ALU.mult, ALU.max),
            reads=["p0acr2"], writes=["p0scl"], tag="p0 wscale")
    else:
        p.op("vector", lambda e: e.tensor_scalar(
            PS[3][0:1, 2:4], PS[3][0:1, 0:2], 1.0 / (FF * D), EPS_Q, ALU.mult, ALU.max),
            reads=[tPS[3]], writes=[tPS[3]], tag="p0 wscale")
        p.op("vector", lambda e: e.tensor_copy(SCL[0:1, 0:2], PS[3][0:1, 2:4]),
             reads=[tPS[3]], writes=["p0scl"], tag="p0 sclev")
    onesfr = b.sb("p0_onesfr", (1, 128), F32)
    p.op("vector", lambda e: e.memset(onesfr[:, :], 1.0), writes=["p0ofr"])
    p.op("tensor", lambda e: e.matmul(PS[4][:, 0:2], onesfr[:, :], SCL[0:1, 0:2],
                                      start=True, stop=True),
         reads=["p0scl", "p0ofr"], writes=[tPS[4]], tag="p0 sbc")
    p.op("vector", lambda e: e.tensor_copy(SCL[:, 0:2], PS[4][:, 0:2]),
         reads=[tPS[4]], writes=["p0scl"])
    p.op("vector", lambda e: e.reciprocal(SCL[:, 2:4], SCL[:, 0:2]),
         reads=["p0scl"], writes=["p0scl"])

    QT = b.sb("p0_qt", (128, 4096), F32)
    QA = b.sb("p0_qa", (128, 4096), F32)
    QO = [b.sb(f"p0_qo{i}", (128, 4096), BF16) for i in range(2)]
    IMG = [b.sb(f"p0_img{i}", (128, 2048), BF16) for i in range(2)]

    def quant_tile(wi, t, par, wsl, ncol, tag):
        p.op("sync", lambda e, t=t, par=par, wsl=wsl, ncol=ncol: e.dma_start(
            out=WT_[:, par * 4096:par * 4096 + ncol],
            in_=wsl[t * 128:(t + 1) * 128, :]),
            writes=[("p0wt", par)], dma=("p0wq", par), tag=f"{tag} load")
        p.op("vector", lambda e, par=par, wi=wi, ncol=ncol: e.tensor_scalar(
            QT[:, 0:ncol], WT_[:, par * 4096:par * 4096 + ncol],
            SCL[:, 2 + wi:3 + wi], None, ALU.mult),
            reads=[("p0wt", par), "p0scl"], writes=["p0qt"], tag=f"{tag} s")
        p.op("vector", lambda e, ncol=ncol: e.tensor_single_scalar(
            QA[:, 0:ncol], QT[:, 0:ncol], 0.5, ALU.is_gt),
            reads=["p0qt"], writes=["p0qa"], tag=f"{tag} a")
        p.op("vector", lambda e, ncol=ncol: e.tensor_single_scalar(
            QT[:, 0:ncol], QT[:, 0:ncol], -0.5, ALU.is_lt),
            reads=["p0qt"], writes=["p0qt"], tag=f"{tag} b")
        p.op("vector", lambda e, ncol=ncol: e.tensor_sub(
            QA[:, 0:ncol], QA[:, 0:ncol], QT[:, 0:ncol]),
            reads=["p0qa", "p0qt"], writes=["p0qa"], tag=f"{tag} c")
        p.op("vector", lambda e, wi=wi, par=par, ncol=ncol: e.tensor_scalar(
            QO[par][:, 0:ncol], QA[:, 0:ncol], SCL[:, wi:wi + 1], None, ALU.mult),
            reads=["p0qa", "p0scl"], writes=[("p0qo", par)], tag=f"{tag} d")

    for t in range(4):
        par = t % 2
        quant_tile(0, t, par, wup_sl, D, f"p0 up t{t}")
        for c in range(NCH):
            p.op("tensor", lambda e, c=c, par=par: e.transpose(
                PSB[:, (c % 4) * 128:(c % 4 + 1) * 128],
                QO[par][:, c * 128:(c + 1) * 128], ident[:, :]),
                reads=[("p0qo", par), "ident"], writes=[tPSB],
                tag=f"p0 uptr t{t}c{c}")
            p.op("vector", lambda e, c=c, par=par: e.tensor_copy(
                IMG[par][:, c * 128:(c + 1) * 128],
                PSB[:, (c % 4) * 128:(c % 4 + 1) * 128]),
                reads=[tPSB], writes=[("p0img", par)], tag=f"p0 upev t{t}c{c}")
        p.op("sync", lambda e, t=t, par=par: e.dma_start(
            out=wupsrc[t, :, :], in_=IMG[par][:, :]),
            reads=[("p0img", par)], writes=[("wupsrc", t)], dma=("p0st", par),
            tag=f"p0 upst t{t}")

    for t in range(2):
        quant_tile(1, t, t, wdn_sl, FF, f"p0 dn t{t}")
    for fc in range(FCH):
        par = fc % 2
        for t in range(2):
            p.op("tensor", lambda e, fc=fc, t=t, par=par: e.transpose(
                PSB[:, (t + 2 * (fc % 2)) * 128:(t + 2 * (fc % 2) + 1) * 128],
                QO[t][:, fc * 128:(fc + 1) * 128], ident[:, :]),
                reads=[("p0qo", t), "ident"], writes=[tPSB],
                tag=f"p0 dntr f{fc}t{t}")
            p.op("vector", lambda e, fc=fc, t=t, par=par: e.tensor_copy(
                IMG[par][:, t * 128:(t + 1) * 128],
                PSB[:, (t + 2 * (fc % 2)) * 128:(t + 2 * (fc % 2) + 1) * 128]),
                reads=[tPSB], writes=[("p0img", par)], tag=f"p0 dnev f{fc}t{t}")
        p.op("sync", lambda e, fc=fc, par=par: e.dma_start(
            out=wdnsrc[fc * 128:(fc + 1) * 128, :], in_=IMG[par][:, 0:256]),
            reads=[("p0img", par)], writes=[("wdnsrc", fc)], dma=("p0st", par),
            tag=f"p0 dnst f{fc}")

    for t in range(2):
        par = t % 2
        p.op("sync", lambda e, t=t, par=par: e.dma_start(
            out=WT_[:, par * 4096:par * 4096 + D],
            in_=wo_sl[t * 128:(t + 1) * 128, :]),
            writes=[("p0wt", par)], dma=("p0wo", par), tag=f"p0 woload {t}")
        p.op("scalar", lambda e, par=par: e.copy(
            IMG[par][:, :], WT_[:, par * 4096:par * 4096 + D]),
            reads=[("p0wt", par)], writes=[("p0img", par)], tag=f"p0 wocast {t}")
        p.op("sync", lambda e, t=t, par=par: e.dma_start(
            out=wosrc[t, :, :], in_=IMG[par][:, :]),
            reads=[("p0img", par)], writes=[("wosrc", t)], dma=("p0st", par),
            tag=f"p0 wost {t}")


def emit_att_ag_q(b, atq_src, atg, q):
    p = b.p
    toks = [("attout", pr, seg)
            for pr in range(4) for seg in range(4 * q, 4 * q + 4)]
    p.op("gpsimd", lambda e, q=q: e.collective_compute(
        "AllGather", ALU.bypass, RG8, [atq_src[q, :]], [atg[q, :, :]]),
        reads=toks, writes=[("atag", q)], mark=True, tag=f"p2 atag q{q}")


def build_fused():
    nc = bass.Bass("TRN2", target_bir_lowering=False, debug=False,
                   dynamic_dma_scratch_size=8192)
    xT = nc.dram_tensor("xT", [NCH, 128, TC], F32, kind="ExternalInput")
    x_rows = nc.dram_tensor("x_rows", [TC, D], F32, kind="ExternalInput")
    wqkv = nc.dram_tensor("wqkv", [D, 768], F32, kind="ExternalInput")
    betas_s = nc.dram_tensor("betas_s", [HL, 128], F32, kind="ExternalInput")
    w_up = nc.dram_tensor("w_up", [FF, D], F32, kind="ExternalInput")
    w_dn = nc.dram_tensor("w_dn", [D, FF], F32, kind="ExternalInput")
    wup_sl = nc.dram_tensor("wup_sl", [FF // 8, D], F32, kind="ExternalInput")
    wdn_sl = nc.dram_tensor("wdn_sl", [D // 8, FF], F32, kind="ExternalInput")
    wo_sl = nc.dram_tensor("wo_sl", [D // 8, D], F32, kind="ExternalInput")
    wmod_sl = nc.dram_tensor("wmod_sl", [D, 1536], F32, kind="ExternalInput")
    cT = nc.dram_tensor("cT", [NCH, 128, 2], F32, kind="ExternalInput")
    bflag = nc.dram_tensor("bflag", [1, 1], F32, kind="ExternalInput")
    out_rows = nc.dram_tensor("out_rows", [TC, D], F32, kind="ExternalOutput")
    onehot8 = nc.dram_tensor("onehot8", [1, 8], F32, kind="ExternalInput")
    mode_l = nc.dram_tensor("mode_l", [2, 1536], F32, kind="Internal")
    modag = nc.dram_tensor("modag", [8, 2, 1536], F32, kind="Internal")
    hsrc = nc.dram_tensor("hsrc", [4, 128, NCH * WT], BF16, kind="Internal")
    hga = nc.dram_tensor("hga", [4, 8, 128, NCH * WT], BF16, kind="Internal")
    atq_src = nc.dram_tensor("atq_src", [4, NATQ], BF16, kind="Internal")
    atg = nc.dram_tensor("atg", [4, 8, NATQ], BF16, kind="Internal")
    wupsrc = nc.dram_tensor("wupsrc", [4, 128, D], BF16, kind="Internal")
    wupga = nc.dram_tensor("wupga", [8, 4, 128, D], BF16, kind="Internal")
    wdnsrc = nc.dram_tensor("wdnsrc", [FF, 256], BF16, kind="Internal")
    wdnga = nc.dram_tensor("wdnga", [8, FF, 256], BF16, kind="Internal")
    wosrc = nc.dram_tensor("wosrc", [2, 128, D], BF16, kind="Internal")
    woga = nc.dram_tensor("woga", [8, 2, 128, D], BF16, kind="Internal")
    ar_src = nc.dram_tensor("ar_src", [1, 2], F32, kind="Internal")
    ar_out = nc.dram_tensor("ar_out", [1, 2], F32, kind="Internal")

    p = Prog(nc)
    b = Bld(nc, p)
    b.fused = True
    b.modag = modag
    b.ar_src = ar_src
    b.ar_out = ar_out

    A_f32a = nc.alloc_sbuf_tensor("A_f32a", [128, 8192], F32)
    A_bf1 = nc.alloc_sbuf_tensor("A_bf1", [128, 16384], BF16)
    A_bf2 = nc.alloc_sbuf_tensor("A_bf2", [128, 8192], BF16)
    A_bf3 = nc.alloc_sbuf_tensor("A_bf3", [128, 8192], BF16)
    A_bf4 = nc.alloc_sbuf_tensor("A_bf4", [128, 8192], BF16)
    bf1_f32 = A_bf1[:, :].bitcast(F32)

    SCR4K = nc.alloc_sbuf_tensor("SCR4K", [128, 1024], F32)
    T1S = nc.alloc_sbuf_tensor("T1S", [128, 512], F32)
    T2S = nc.alloc_sbuf_tensor("T2S", [128, 512], F32)
    ROWS = nc.alloc_sbuf_tensor("ROWS", [1, 2048], F32)
    b.share = {
        "p3_atw": A_bf4,
        "p1_mrbc": SCR4K,
        "p3_qu": SCR4K[:, 0:512], "p3_qzi": SCR4K[:, 512:1024].bitcast(I32),
        "p1_t1": T1S, "p3_qzf": T1S,
        "p1_t2": T2S, "p3_qc": T2S,
        "p1_stats": ROWS[0:1, 0:1536],
        "p3_gsc": ROWS[0:1, 0:2048],
        "p0_wt": A_f32a,
        "p0_qt": bf1_f32[:, 0:4096], "p0_qa": bf1_f32[:, 4096:8192],
        "p0_qo0": A_bf4[:, 0:4096], "p0_qo1": A_bf4[:, 4096:8192],
        "p0_img0": A_bf2[:, 0:2048], "p0_img1": A_bf2[:, 2048:4096],
        "p0_wm0": A_bf3[:, 0:1536], "p0_wm1": A_bf3[:, 1536:3072],
        "p1_xstg0": A_f32a, "p1_xstg1": bf1_f32,
        "p1_xb0": A_bf2, "p1_xb1": A_bf3, "p1_xsq": A_bf4,
        "p1_hwin0": A_bf1[:, 0:8192], "p1_hwin1": A_bf1[:, 8192:16384],
        "p2_wstg": A_f32a[:, 0:768],
        "p2_wqkv": A_bf1[:, 0:NCH * 768],
        "p2_ht0": A_bf2, "p2_ht1": A_bf3,
        "p2_q": A_bf4[:, 0:512], "p2_k": A_bf4[:, 512:1024],
        "p2_v": A_bf4[:, 1024:1536], "p2_eq": A_bf4[:, 1536:2048],
        "p2_sq": A_bf4[:, 2048:2560], "p2_sk": A_bf4[:, 2560:3072],
        "p2_vt": A_bf4[:, 3072:3584], "p2_skt": A_bf4[:, 3584:4096],
        "p2_dlt": A_bf4[:, 4096:4608], "p2_psm": A_bf4[:, 4608:5120],
        "p2_pt": A_bf4[:, 5120:7168], "p2_attt": A_bf4[:, 7168:7680],
        "p3_x2t": A_f32a, "p3_fa": A_bf1, "p3_fb": A_bf2, "p3_gb": A_bf3,
        "p3_xq": A_bf4,
        "p3_cand80": A_bf2[:, 0:4096], "p3_cand81": A_bf3[:, 0:4096],
    }
    b.cand_toks = ["fb", "gb"]

    b.PS = [b.ps(f"ps{k}", (128, 512), F32) for k in range(7)]
    b.tPS = [("PS", k) for k in range(7)]
    b.PSB = b.ps("psb", (128, 1024), BF16)
    b.tPSB = ("PSB",)
    onr = b.sb("ones_row_b", (1, 128), BF16)
    p.op("vector", lambda e: e.memset(onr[:, :], 1.0), writes=["p3or"])
    b.ones_row = onr

    FB1 = b.sb("fb1", (1, 1), F32)
    FLAG128 = b.sb("flag128", (128, 1), F32)
    onesfr0 = b.sb("onesfr0", (1, 128), F32)
    p.op("vector", lambda e: e.memset(onesfr0[:, :], 1.0), writes=["of0"])
    p.op("sync", lambda e: e.dma_start(out=FB1[:, :], in_=bflag[:, :]),
         writes=["fb1"], dma=("bfl",))
    p.op("tensor", lambda e: e.matmul(b.PS[5][:, 0:1], onesfr0[:, :], FB1[0:1, :],
                                      start=True, stop=True),
         reads=["fb1", "of0"], writes=[b.tPS[5]])
    p.op("vector", lambda e: e.tensor_copy(FLAG128[:, :], b.PS[5][:, 0:1]),
         reads=[b.tPS[5]], writes=["flag128"])
    b.flag128 = FLAG128
    b.tflag = "flag128"
    b.flag1 = FB1
    b.tflag1 = "fb1"
    b.gsc3 = ROWS[0:1, 0:2048]
    b.gsc3_t = ROWS

    OH = b.sb("oh8", (1, 8), F32)
    GF = b.sb("gf8", (128, 8), F32)
    p.op("sync", lambda e: e.dma_start(out=OH[:, :], in_=onehot8[:, :]),
         writes=["oh8"], dma=("oh8",))
    p.op("tensor", lambda e: e.matmul(b.PS[6][:, 0:8], onesfr0[:, :], OH[0:1, :],
                                      start=True, stop=True),
         reads=["oh8", "of0"], writes=[b.tPS[6]])
    p.op("vector", lambda e: e.tensor_copy(GF[:, :], b.PS[6][:, 0:8]),
         reads=[b.tPS[6]], writes=["gf"])
    b.GF = GF

    emit_phase0_fused(b, w_up, w_dn, wup_sl, wdn_sl, wo_sl, wmod_sl, cT,
                      mode_l, modag, wupsrc, wdnsrc, wosrc)
    p.fence()

    def h_store(w, HW, tHW):
        p.op("sync", lambda e, w=w, HW=HW: e.dma_start(
            out=hsrc[w, :, :], in_=HW[:, :]),
            reads=[tHW], writes=[("hsrc", w)], dma=("p1hs", w % 2),
            tag=f"p1 hst w{w}")
        p.op("gpsimd", lambda e, w=w: e.collective_compute(
            "AllGather", ALU.bypass, RG8, [hsrc[w, :, :]], [hga[w, :, :, :]]),
            reads=[("hsrc", w)], writes=[("hag", w)], mark=True, tag=f"p1 hag w{w}")

    emit_phase1(b, xT, None, h_store)
    p.fence()

    p.op("gpsimd", lambda e: e.collective_compute(
        "AllGather", ALU.bypass, RG8, [wupsrc[:, :, :]], [wupga[:, :, :, :]]),
        reads=[("wupsrc", t) for t in range(4)], writes=["wupag"], mark=True, tag="ag wup")
    p.op("gpsimd", lambda e: e.collective_compute(
        "AllGather", ALU.bypass, RG8, [wdnsrc[:, :]], [wdnga[:, :, :]]),
        reads=[("wdnsrc", fc) for fc in range(FCH)], writes=["wdnag"],
        mark=True, tag="ag wdn")
    p.op("gpsimd", lambda e: e.collective_compute(
        "AllGather", ALU.bypass, RG8, [wosrc[:, :, :]], [woga[:, :, :, :]]),
        reads=[("wosrc", t) for t in range(2)], writes=["woag"], mark=True, tag="ag wo")

    b.att_qdone = lambda q: emit_att_ag_q(b, atq_src, atg, q)
    emit_phase2(b, hga, wqkv, betas_s, atq_src)
    b.att_qdone = None
    p.fence()

    emit_phase3(b, x_rows, atg, woga, wupga, wdnga, None, out_rows)
    p.build()
    return nc


def kernel_fused(L0_attn_x, L0_attn_Wq, L0_attn_Wk, L0_attn_Wv, L0_attn_Wo,
                 L0_attn_betas, L0_ffn_w_up, L0_ffn_w_down, L0_c, L0_wmod):
    x = np.ascontiguousarray(np.asarray(L0_attn_x, np.float32))
    Wq = np.asarray(L0_attn_Wq, np.float32)
    Wk = np.asarray(L0_attn_Wk, np.float32)
    Wv = np.asarray(L0_attn_Wv, np.float32)
    Wo = np.ascontiguousarray(np.asarray(L0_attn_Wo, np.float32))
    betas = np.asarray(L0_attn_betas, np.float32)
    w_up = np.ascontiguousarray(np.asarray(L0_ffn_w_up, np.float32))
    w_dn = np.ascontiguousarray(np.asarray(L0_ffn_w_down, np.float32))
    cc = np.asarray(L0_c, np.float32)
    wmod = np.ascontiguousarray(np.asarray(L0_wmod, np.float32))
    cT = np.ascontiguousarray(cc.T).reshape(NCH, 128, 2)
    wmod_v = wmod.reshape(D, 6, 8, 256)
    nc = build_fused()
    in_maps = []
    for g in range(NCORE):
        bb, s0 = g // 4, (g % 4) * TC
        wqkv_g = np.empty((D, 768), np.float32)
        for hl in range(HL):
            hh = HL * g + hl
            wqkv_g[:, 0 * 256 + hl * 128:0 * 256 + (hl + 1) * 128] = Wq[:, hh * 128:(hh + 1) * 128]
            wqkv_g[:, 1 * 256 + hl * 128:1 * 256 + (hl + 1) * 128] = Wk[:, hh * 128:(hh + 1) * 128]
            wqkv_g[:, 2 * 256 + hl * 128:2 * 256 + (hl + 1) * 128] = Wv[:, hh * 128:(hh + 1) * 128]
        in_maps.append({
            "xT": np.ascontiguousarray(x[bb, s0:s0 + TC, :].T).reshape(NCH, 128, TC),
            "x_rows": np.ascontiguousarray(x[bb, s0:s0 + TC, :]),
            "wqkv": wqkv_g,
            "betas_s": np.ascontiguousarray(betas[0, HL * g:HL * g + HL, 0, :]),
            "w_up": w_up, "w_dn": w_dn,
            "wup_sl": np.ascontiguousarray(w_up[g * FF // 8:(g + 1) * FF // 8]),
            "wdn_sl": np.ascontiguousarray(w_dn[g * D // 8:(g + 1) * D // 8]),
            "wo_sl": np.ascontiguousarray(Wo[g * D // 8:(g + 1) * D // 8]),
            "wmod_sl": np.ascontiguousarray(wmod_v[:, :, g, :].reshape(D, 1536)),
            "cT": cT,
            "bflag": np.array([[float(bb)]], np.float32),
            "onehot8": np.eye(8, dtype=np.float32)[g:g + 1],
        })
    res = _runner(nc, in_maps)
    out = np.empty((B, S, D), np.float32)
    for g in range(NCORE):
        bb, s0 = g // 4, (g % 4) * TC
        out[bb, s0:s0 + TC, :] = res[g]["out_rows"]
    return out


_kernel_v0 = kernel
kernel = kernel_fused
